# version 18 (final)
# baseline (speedup 1.0000x reference)
import contextlib
import numpy as np
import ml_dtypes
import concourse.bass as bass
import concourse.mybir as mybir
from concourse.bass_utils import run_bass_kernel_spmd

F32 = mybir.dt.float32
F32R = mybir.dt.float32r
BF16 = mybir.dt.bfloat16
BF = ml_dtypes.bfloat16
AF = mybir.ActivationFunctionType
ALU = mybir.AluOpType
AXL = mybir.AxisListType

SKIP = set()
NTILES = None
D = 1024
NCH = 8
H = 16
HD = 64
T = 4096
TT = 256
NT = T // TT
CB = 128
NHID = 2730
NPD = 2816
NPC = 22
EPS1 = 1e-5
EPSR = 1e-6
INVD = 1.0 / D

R_K2, R_K3, R_K4, R_ONE, R_K1, R_R, R_NMR, R_PAD = range(8)


def _gamma():
    return 1.0 - 2.0 ** (-5.0 - np.arange(H, dtype=np.float64))


def host_consts():
    gam = _gamma()
    n = np.arange(CB, dtype=np.float64)
    DT = np.zeros((128, H * 128), np.float32)
    for h in range(H):
        j, hh = h // 2, h % 2
        DT[:, (hh * 8 + j) * 128:(hh * 8 + j + 1) * 128] = np.tril(
            gam[h] ** (n[:, None] - n[None, :])).T.astype(np.float32)
    GCOLW = np.zeros((128, 1024), np.float32)
    for h in range(H):
        GCOLW[:, h * 64:(h + 1) * 64] = (
            gam[h] ** (127.0 - n)).astype(np.float32)[:, None]
    GROW = np.zeros((128, 8 * 128), np.float32)
    for j in range(8):
        GROW[0:64, j * 128:(j + 1) * 128] = (gam[2 * j] ** (n + 1)).astype(np.float32)[None, :]
        GROW[64:128, j * 128:(j + 1) * 128] = (gam[2 * j + 1] ** (n + 1)).astype(np.float32)[None, :]
    GCOL = np.zeros((128, H), np.float32)
    for h in range(H):
        GCOL[:, h] = (gam[h] ** (127.0 - n)).astype(np.float32)
    GBLK = np.zeros((128, 8), np.float32)
    for j in range(8):
        GBLK[0:64, j] = np.float32(gam[2 * j] ** 128)
        GBLK[64:128, j] = np.float32(gam[2 * j + 1] ** 128)
    sel = np.zeros((8, 8 * 8), np.float32)
    for k in range(8):
        sel[k, 8 * k:8 * k + 8] = 1.0
    selw = np.zeros((8, 8 * 128), np.float32)
    for k in range(8):
        selw[k, 128 * k:128 * (k + 1)] = 1.0
    e3 = np.zeros((1, 4), np.float32)
    e3[0, 3] = 1.0
    return dict(DTc=DT, GROWc=GROW, GCOLc=GCOL, GCOLWc=GCOLW, GBLKc=GBLK,
                identc=np.eye(128, dtype=np.float32), selc=sel, selwc=selw,
                e3c=e3, onesec=np.ones((8, 128), np.float32))


def fmv(v):
    return np.ascontiguousarray(np.asarray(v, np.float32).reshape(NCH, 128).T)


def permute_w_in(w_in, b_in):
    wu, wg = w_in[:, :NHID], w_in[:, NHID:]
    bu, bg = b_in[:NHID], b_in[NHID:]
    wp = np.zeros((D, 2 * NPD), np.float32)
    bp = np.zeros((1, 2 * NPD), np.float32)
    for j in range(NPC):
        lo, hi = j * 128, min((j + 1) * 128, NHID)
        wd = hi - lo
        if wd > 0:
            wp[:, 256 * j:256 * j + wd] = wu[:, lo:hi]
            wp[:, 256 * j + 128:256 * j + 128 + wd] = wg[:, lo:hi]
            bp[0, 256 * j:256 * j + wd] = bu[lo:hi]
            bp[0, 256 * j + 128:256 * j + 128 + wd] = bg[lo:hi]
    return wp, bp


class Sem:
    __slots__ = ("h", "c", "name")

    def __init__(self, h, name):
        self.h, self.c, self.name = h, 0, name


class Ctx:
    ENGS = ("sync", "gpsimd", "vector", "scalar", "tensor")

    def __init__(self, nc):
        self.nc = nc
        self.sems = []
        self.auto_tensor = ()
        self.self_sems = {}
        self.section = None
        self.res = {}
        self.bankmap = {}

    def sem(self, name):
        s = Sem(self.nc.alloc_semaphore(name=name), name)
        self.sems.append(s)
        return s

    def walk(self, blk, fn):
        finals = []
        start = {id(s): s.c for s in self.sems}
        for en in self.ENGS:
            self.eng_name = en
            self._lastw = {}

            def body(e, fn=fn):
                for s in self.sems:
                    s.c = start[id(s)]
                self.eng = e
                fn(self)

            getattr(blk, en)(body)
            finals.append({s.name: s.c for s in self.sems})
        for f in finals[1:]:
            assert f == finals[0], f"sem divergence: {finals[0]} vs {f}"

    def op(self, engine, emit, waits=(), incs=(), reads=(), writes=(), mark=None):
        if self.section is not None and self.section in SKIP:
            return
        if engine == "tensor":
            waits = list(waits) + [(s, s.c) for s in self.auto_tensor]
        is_dma = any(nn >= 16 for (_, nn) in incs)
        if is_dma:
            waits = list(waits) + [(s, s.c) for (s, _) in incs]
        ss = self.self_sems.get(engine)
        if ss is not None and not is_dma:
            waits = list(waits) + [(ss, ss.c)]
            if not any(s is ss for (s, _) in incs):
                incs = list(incs) + [(ss, 1)]
        waits = list(waits) + self.dep_waits(reads, writes)
        if engine == self.eng_name:
            for (s, v) in waits:
                if v > 0 and self._lastw.get(id(s), -1) < v:
                    self.eng.wait_ge(s.h, v)
                    self._lastw[id(s)] = v
            ins = emit(self.eng)
            for (s, nn) in incs:
                ins = ins.then_inc(s.h, nn)
        for (s, nn) in incs:
            s.c += nn
        if reads or writes:
            if mark is None:
                mark = ss if ss is not None else (incs[-1][0] if incs else None)
            self.register(reads, writes, mark)

    def _expand(self, reads, writes):
        bm = self.bankmap
        if not bm:
            return reads, writes
        rb = {bm[k] for k in reads if k in bm}
        wb = {bm[k] for k in writes if k in bm}
        return (tuple(reads) + tuple(sorted(rb - wb)),
                tuple(writes) + tuple(sorted(wb)))

    def dep_waits(self, reads, writes):
        reads, writes = self._expand(reads, writes)
        ws = []
        for k in reads:
            e = self.res.get(k)
            if e is not None and e[0] is not None:
                ws.append(e[0])
        for k in writes:
            e = self.res.get(k)
            if e is not None:
                if e[0] is not None:
                    ws.append(e[0])
                ws.extend(e[1].values())
        return ws

    def register(self, reads, writes, mark):
        assert mark is not None, "tracked op needs a completion sem"
        reads, writes = self._expand(reads, writes)
        mk = (mark, mark.c)
        for k in writes:
            self.res[k] = [mk, {}]
        for k in reads:
            e = self.res.setdefault(k, [None, {}])
            e[1][id(mark)] = mk

    def w(self, s):
        return (s, s.c)


def build(nc, phases=4):
    ctx = Ctx(nc)

    def din(name, shape, dt=F32):
        return nc.dram_tensor(name, list(shape), dt, kind="ExternalInput").ap()

    x_h = din("x", [T, D])
    wmod_h = din("wmod", [D, 6 * D])
    c_h = din("cvec", [128, 8])
    W_h = {k: din(k, [D, D]) for k in ("wq", "wk", "wv", "wg", "wo")}
    winp_h = din("winp", [D, 2 * NPD])
    binp_h = din("binp", [1, 2 * NPD])
    woutp_h = din("woutp", [NPD, D])
    fWq_h = din("fWq", [D, D])
    fWdown_h = din("fWdown", [D, HD])
    fWup_h = din("fWup", [HD, 2 * D])
    fWo_h = din("fWo", [D, D])
    fWg_h = din("fWg", [D, 8])
    fWe_h = din("fWe", [8, D, D])
    FMV = ("w1", "b1", "w2", "bq", "bk", "bo", "b_out", "gamma1", "gamma2", "f_bo")
    fmv_h = {k: din("fm_" + k, [128, NCH]) for k in FMV}
    b2bf_h = din("b2bf", [128, NCH])
    fbdown_h = din("fm_f_bdown", [HD, 1])
    fbg_h = din("fm_f_bg", [8, 1])
    row_h = {"bv": din("row_bv", [1, D]), "bg_": din("row_bg", [1, D]),
             "f_bq": din("row_fbq", [1, D]), "bup": din("row_bup", [1, 2 * D]),
             "bq": din("row_bq", [1, D]), "bk": din("row_bk", [1, D])}
    gnwrep_h = din("gnw_rep", [128, D])
    gnbrep_h = din("gnb_rep", [128, D])
    CN = host_consts()
    cons_h = {k: din(k, v.shape) for k, v in CN.items()}
    out_h = nc.dram_tensor("out", [T, D], F32, kind="ExternalOutput").ap()

    def dsc(name, shape, dt=F32):
        return nc.dram_tensor(name, list(shape), dt, kind="Internal").ap()

    xfm_s = dsc("xfm_s", [NT, 128, NCH * TT])
    xb1_s = dsc("xb1_s", [NT, 128, NCH * TT])
    x1_s = dsc("x1_s", [NT, 128, NCH * TT])
    xf1_s = dsc("xf1_s", [NT, 128, NCH * TT])
    woutbf_s = dsc("woutbf_s", [NCH, 128, NPC * 128], BF16)
    webf_s = dsc("webf_s", [8, 128, NCH * D], BF16)

    S = {}
    for ph in ("s", "1", "2", "3", "4"):
        for nm in ("pe", "v", "a", "g", "dg", "dh", "ds"):
            S[nm + ph] = ctx.sem(nm + ph)
    S["dgw"] = ctx.sem("dgw")
    S["dhw"] = ctx.sem("dhw")
    S["dhw2"] = ctx.sem("dhw2")
    S["dhw3"] = ctx.sem("dhw3")
    S["dgp"] = ctx.sem("dgp")
    S["dwg"] = ctx.sem("dwg")
    S["dg4b"] = ctx.sem("dg4b")
    S["dsA1"] = ctx.sem("dsA1")
    S["dgb2"] = ctx.sem("dgb2")

    big = contextlib.ExitStack()

    def sb(name, shape, dt, stack):
        return stack.enter_context(nc.sbuf_tensor(name, list(shape), dt))

    P = {}
    P["DTc"] = sb("sb_DTc", [128, H * 128], BF16, big)
    P["GROWc"] = sb("sb_GROWc", [128, 8 * 128], BF16, big)
    P["GCOLW"] = sb("sb_GCOLW", [128, 1024], BF16, big)
    P["GBLKc"] = sb("sb_GBLKc", [128, 8], F32, big)
    P["identf"] = sb("identf", [128, 128], F32, big)
    P["identb"] = sb("identb", [128, 128], BF16, big)
    P["sel"] = sb("sel", [8, 64], BF16, big)
    P["onese"] = sb("onese", [8, 128], BF16, big)
    P["selw"] = sb("selw", [8, 8 * 128], BF16, big)
    P["e3"] = sb("e3", [1, 4], BF16, big)
    P["ones128"] = sb("ones128", [128, 128], F32, big)
    P["onesb"] = sb("onesb", [1, 256], BF16, big)
    P["gnwrep"] = sb("gnwrep", [128, D], BF16, big)
    P["gnbrep"] = sb("gnbrep", [128, D], BF16, big)
    for k in list(FMV) + ["A", "sa", "B", "sb_", "ga", "gb", "w1A", "cq", "ck", "bq8"]:
        P["fm_" + k] = sb("v_" + k, [128, NCH], F32, big)
    P["fm_fbdown"] = sb("v_fbdown", [HD, 1], F32, big)
    P["fm_fbg"] = sb("v_fbg", [8, 1], F32, big)
    P["cuv"] = sb("v_cuv", [128, 2 * NPC], F32, big)
    P["mod"] = sb("v_mod", [128, 48], F32, big)
    P["scal"] = sb("v_scal", [128, 16], F32, big)
    P["nscal"] = sb("v_nscal", [128, 16], F32, big)
    for k in ("Arow", "sarow", "Brow", "sbrow"):
        P[k] = sb("r_" + k, [1, NCH * 128], BF16, big)
    for k in ("corr_q", "corr_k", "corr_v", "corr_g"):
        P[k] = sb(k, [128, D], BF16, big)
    P["G"] = sb("v_G", [128, 8 * HD], F32, big)
    P["Gb"] = sb("v_Gb", [128, 8 * HD], BF16, big)
    P["vecs4"] = sb("v_vecs4", [128, NCH * 4], BF16, big)
    P["svA"] = sb("v_svA", [128, NCH * 4], BF16, big)
    P["svA2"] = sb("v_svA2", [128, NCH * 2], BF16, big)
    P["svB"] = sb("v_svB", [128, NCH * 4], BF16, big)
    P["svB2"] = sb("v_svB2", [128, NCH * 2], BF16, big)
    P["sv1"] = sb("v_sv1", [128, NCH * 2], BF16, big)

    pb = [big.enter_context(nc.psum_tensor(f"pb{i}", [128, 512], F32)) for i in range(8)]

    fm = lambda k: P["fm_" + k]
    SC = {k: i for i, k in enumerate(
        ("cA", "cs", "cA2", "cAs", "cs2", "cB", "csb", "cB2", "cBs", "csb2"))}

    def scol(name, neg=False):
        return (P["nscal"] if neg else P["scal"])[:, SC[name]:SC[name] + 1]

    def rowmath(c, sv, sa, spe, statq, statq2, tmT, tmS, side, rms_rho):
        op, w = c.op, c.w
        V = lambda emit, ww=(): op("vector", emit, waits=list(ww), incs=[(sv, 1)])
        A_ = lambda emit, ww=(): op("scalar", emit, waits=list(ww), incs=[(sa, 1)])
        cA, cs, cA2, cAs, cs2 = (("cA", "cs", "cA2", "cAs", "cs2") if side == "A"
                                 else ("cB", "csb", "cB2", "cBs", "csb2"))
        tm = tmT
        V(lambda e: e.tensor_scalar_mul(out=tm[:, 0:1], in0=statq[:, 0:1], scalar1=INVD),
          ww=[w(spe)])
        V(lambda e: e.tensor_scalar_mul(out=tm[:, 4:5], in0=statq2[:, 0:1], scalar1=INVD))
        V(lambda e: e.tensor_mul(out=tm[:, 5:6], in0=tm[:, 0:1], in1=tm[:, 0:1]))
        V(lambda e: e.tensor_sub(out=tm[:, 6:7], in0=tm[:, 4:5], in1=tm[:, 5:6]))
        A_(lambda e: e.activation(out=tm[:, 7:8], in_=tm[:, 6:7], func=AF.Ln, bias=np.float32(EPS1)),
           ww=[w(sv)])
        A_(lambda e: e.activation(out=tm[:, 1:2], in_=tm[:, 7:8], func=AF.Exp, scale=np.float32(-0.5)))
        V(lambda e: e.scalar_tensor_tensor(out=tm[:, 8:9], in0=tm[:, 0:1],
          scalar=scol(cA, True), in1=statq[:, 1:2], op0=ALU.mult, op1=ALU.add), ww=[w(sa)])
        V(lambda e: e.tensor_mul(out=tm[:, 8:9], in0=tm[:, 1:2], in1=tm[:, 8:9]))
        V(lambda e: e.tensor_scalar(out=tm[:, 3:4], in0=tm[:, 8:9], scalar1=1.0,
          scalar2=scol(cs), op0=ALU.mult, op1=ALU.add))
        V(lambda e: e.tensor_scalar_mul(out=tm[:, 3:4], in0=tm[:, 3:4], scalar1=INVD))
        V(lambda e: e.scalar_tensor_tensor(out=tm[:, 8:9], in0=tm[:, 5:6], scalar=scol(cA2),
          in1=statq2[:, 1:2], op0=ALU.mult, op1=ALU.add))
        V(lambda e: e.scalar_tensor_tensor(out=tm[:, 9:10], in0=tm[:, 0:1], scalar=-2.0,
          in1=statq[:, 2:3], op0=ALU.mult, op1=ALU.mult))
        V(lambda e: e.tensor_add(out=tm[:, 8:9], in0=tm[:, 8:9], in1=tm[:, 9:10]))
        V(lambda e: e.tensor_mul(out=tm[:, 9:10], in0=tm[:, 1:2], in1=tm[:, 1:2]))
        V(lambda e: e.tensor_mul(out=tm[:, 8:9], in0=tm[:, 8:9], in1=tm[:, 9:10]))
        V(lambda e: e.scalar_tensor_tensor(out=tm[:, 9:10], in0=tm[:, 0:1],
          scalar=scol(cAs, True), in1=statq[:, 3:4], op0=ALU.mult, op1=ALU.add))
        V(lambda e: e.tensor_mul(out=tm[:, 9:10], in0=tm[:, 1:2], in1=tm[:, 9:10]))
        V(lambda e: e.scalar_tensor_tensor(out=tm[:, 8:9], in0=tm[:, 9:10], scalar=2.0,
          in1=tm[:, 8:9], op0=ALU.mult, op1=ALU.add))
        V(lambda e: e.tensor_scalar(out=tm[:, 8:9], in0=tm[:, 8:9], scalar1=1.0,
          scalar2=scol(cs2), op0=ALU.mult, op1=ALU.add))
        if rms_rho:
            V(lambda e: e.tensor_scalar_mul(out=tm[:, 9:10], in0=tm[:, 8:9], scalar1=INVD))
            A_(lambda e: e.activation(out=tm[:, 7:8], in_=tm[:, 9:10], func=AF.Ln, bias=np.float32(EPSR)),
               ww=[w(sv)])
        else:
            V(lambda e: e.tensor_mul(out=tm[:, 9:10], in0=tm[:, 3:4], in1=tm[:, 3:4]))
            V(lambda e: e.scalar_tensor_tensor(out=tm[:, 9:10], in0=tm[:, 8:9], scalar=INVD,
              in1=tm[:, 9:10], op0=ALU.mult, op1=ALU.subtract))
            A_(lambda e: e.activation(out=tm[:, 7:8], in_=tm[:, 9:10], func=AF.Ln, bias=np.float32(EPS1)),
               ww=[w(sv)])
        A_(lambda e: e.activation(out=tm[:, 2:3], in_=tm[:, 7:8], func=AF.Exp, scale=np.float32(-0.5)))
        V(lambda e: e.tensor_mul(out=tmS[:, R_K1:R_K1 + 1], in0=tm[:, 2:3], in1=tm[:, 1:2]),
          ww=[w(sa)])
        V(lambda e: e.scalar_tensor_tensor(out=tmS[:, R_K2:R_K2 + 1], in0=tmS[:, R_K1:R_K1 + 1],
          scalar=-1.0, in1=tm[:, 0:1], op0=ALU.mult, op1=ALU.mult))
        V(lambda e: e.tensor_copy(out=tmS[:, R_K3:R_K3 + 1], in_=tm[:, 2:3]))
        V(lambda e: e.scalar_tensor_tensor(out=tmS[:, R_K4:R_K4 + 1], in0=tm[:, 2:3],
          scalar=-1.0, in1=tm[:, 3:4], op0=ALU.mult, op1=ALU.mult))
        V(lambda e: e.tensor_copy(out=tmS[:, R_R:R_R + 1], in_=tm[:, 1:2]))
        V(lambda e: e.scalar_tensor_tensor(out=tmS[:, R_NMR:R_NMR + 1], in0=tm[:, 1:2],
          scalar=-1.0, in1=tm[:, 0:1], op0=ALU.mult, op1=ALU.mult))

    def coeff_rows(c, sv, sa, spe, tmS, rows, reps, b, repkinds):
        op, w = c.op, c.w
        op("tensor", lambda e: e.transpose(pb[4][0:8, 384:512], tmS[:], P["identf"][:]),
           waits=[c.w(sv)], incs=[(spe, 1)])
        op("scalar", lambda e, b=b: e.activation(out=rows[0:8, 128 * b:128 * (b + 1)],
           in_=pb[4][0:8, 384:512], func=AF.Copy), waits=[w(spe)], incs=[(sa, 1)])
        for ri, rk in enumerate(repkinds):
            op("tensor", lambda e, b=b, ri=ri, rk=rk: e.matmul(
                pb[4][0:8, 128 * ri:128 * (ri + 1)], P["sel"][:, 8 * rk:8 * rk + 8],
                rows[0:8, 128 * b:128 * (b + 1)], start=True, stop=True),
                waits=[w(sa)], incs=[(spe, 1)])
        nrep = len(repkinds)
        ncols = reps.shape[1] // len(repkinds) if False else None
        op("scalar", lambda e, b=b, nrep=nrep: e.activation(
            out=reps[:].rearrange("p (k t) -> p k t", k=nrep)[:, :, 128 * b:128 * (b + 1)],
            in_=pb[4][0:8, 0:128 * nrep].rearrange("p (k t) -> p k t", t=128),
            func=AF.Copy), waits=[w(spe)], incs=[(sa, 1)])

    wst = contextlib.ExitStack()
    st = contextlib.ExitStack()
    wsb = {k: sb("w_" + k, [128, NCH * D], BF16, wst) for k in ("wq", "wk", "wv", "wg", "wo")}
    c_sb = sb("c_sb", [128, 8], F32, st)
    siluc = sb("siluc", [128, 8], F32, st)
    wstrip = sb("wstrip", [128, NCH * 512], F32, st)
    wstrip2 = sb("wstrip2", [128, NCH * 512], F32, st)
    sgn = sb("sgn", [128, 8], F32, st)
    absv = sb("absv", [128, 8], F32, st)
    scal_in = sb("scal_in", [128, 16], F32, st)
    prod8 = sb("prod8", [128, 8], F32, st)
    brow = sb("brow", [1, 2 * D], F32, st)
    browb = sb("browb", [1, 2 * D], BF16, st)
    P["corr_fbq"] = sb("corr_fbq", [128, D], BF16, st)
    P["corr_bup"] = sb("corr_bup", [128, 2 * D], BF16, st)
    corr3_s = dsc("corr3_s", [128, 3 * D], BF16)
    stage = sb("stage", [128, NCH * D], BF16, st)

    def setup(c):
        op, w = c.op, c.w
        sdg, sdh, sds = S["dgs"], S["dhs"], S["dss"]
        spe, sv, sa = S["pes"], S["vs"], S["as"]
        c.auto_tensor = (sv, sa)
        c.self_sems = {"vector": sv, "scalar": sa, "gpsimd": S["gs"]}
        c.bankmap = {}
        HW = lambda emit: op("sync", emit, incs=[(sdh, 16)])
        HW(lambda e: e.dma_start(out=P["GBLKc"][:], in_=cons_h["GBLKc"]))
        HW(lambda e: e.dma_start(out=P["identf"][:], in_=cons_h["identc"]))
        for k in FMV:
            HW(lambda e, k=k: e.dma_start(out=fm(k)[:], in_=fmv_h[k]))
        HW(lambda e: e.dma_start(out=P["fm_fbdown"][:], in_=fbdown_h))
        HW(lambda e: e.dma_start(out=P["fm_fbg"][:], in_=fbg_h))
        HW(lambda e: e.dma_start(out=c_sb[:], in_=c_h))
        op("vector", lambda e: e.memset(P["ones128"][:], 1.0), incs=[(sv, 1)])
        op("vector", lambda e: e.memset(P["onesb"][:], 1.0), incs=[(sv, 1)])
        GD = lambda emit: op("gpsimd", emit, incs=[(sdg, 16)])
        GD(lambda e: e.dma_start(out=P["DTc"][:], in_=cons_h["DTc"]))
        GD(lambda e: e.dma_start(out=P["GROWc"][:], in_=cons_h["GROWc"]))
        GD(lambda e: e.dma_start(out=P["GCOLW"][:], in_=cons_h["GCOLWc"]))
        GD(lambda e: e.dma_start(out=P["identb"][:], in_=cons_h["identc"]))
        GD(lambda e: e.dma_start(out=P["sel"][:], in_=cons_h["selc"]))
        GD(lambda e: e.dma_start(out=P["onese"][:], in_=cons_h["onesec"]))
        GD(lambda e: e.dma_start(out=P["selw"][:], in_=cons_h["selwc"]))
        GD(lambda e: e.dma_start(out=P["e3"][:], in_=cons_h["e3c"]))
        GD(lambda e: e.dma_start(out=P["gnwrep"][:], in_=gnwrep_h))
        GD(lambda e: e.dma_start(out=P["gnbrep"][:], in_=gnbrep_h))
        for k in ("wq", "wk", "wv", "wg", "wo"):
            GD(lambda e, k=k: e.dma_start(
                out=wsb[k][:].rearrange("p (c n) -> p c n", n=D), in_=W_h[k].rearrange("(c p) n -> p c n", p=128)))
        cdl, gdl = w(sdh), w(sdg)

        op("scalar", lambda e: e.activation(out=siluc[:], in_=c_sb[:], func=AF.Silu),
           waits=[cdl], incs=[(sa, 1)])
        strip_done = {}

        def strip_load(s_):
            ws_ = wstrip if s_ % 2 == 0 else wstrip2
            wsv_ = ws_[:].rearrange("p (c n) -> p c n", n=512)
            src_ = wmod_h[:, 512 * s_:512 * (s_ + 1)].rearrange(
                "(c p) n -> p c n", p=128)
            wmm = [(spe, spe.c)] if s_ >= 2 else []
            op("sync", lambda e, wsv_=wsv_, src_=src_: e.dma_start(
                out=wsv_[:, 0:3, :], in_=src_[:, 0:3, :]),
                waits=list(wmm), incs=[(S["dhw"], 16)])
            op("scalar", lambda e, wsv_=wsv_, src_=src_: e.dma_start(
                out=wsv_[:, 3:6, :], in_=src_[:, 3:6, :]),
                waits=list(wmm), incs=[(S["dhw2"], 16)])
            op("gpsimd", lambda e, wsv_=wsv_, src_=src_: e.dma_start(
                out=wsv_[:, 6:8, :], in_=src_[:, 6:8, :]),
                waits=list(wmm), incs=[(S["dhw3"], 16)])
            strip_done[s_] = [(S["dhw"], S["dhw"].c), (S["dhw2"], S["dhw2"].c),
                              (S["dhw3"], S["dhw3"].c)]

        strip_load(0)
        strip_load(1)
        for s_ in range(12):
            ws_ = wstrip if s_ % 2 == 0 else wstrip2
            for n4 in range(4):
                for k in range(NCH):
                    op("tensor", lambda e, s=s_, n4=n4, k=k, ws_=ws_: e.matmul(
                        pb[0][:, 4 * s + n4:4 * s + n4 + 1],
                        ws_[:, 512 * k + 128 * n4:512 * k + 128 * (n4 + 1)],
                        siluc[:, k:k + 1], start=(k == 0), stop=(k == NCH - 1)),
                        waits=(strip_done[s_] + [w(sa)]
                               if (n4 == 0 and k == 0) else []),
                        incs=[(spe, 1)] if (k == NCH - 1) else [])
            if s_ + 2 < 12:
                strip_load(s_ + 2)
        op("vector", lambda e: e.tensor_copy(out=P["mod"][:, 0:48], in_=pb[0][:, 0:48]),
           waits=[w(spe)], incs=[(sv, 1)])
        md = P["mod"]

        V = lambda emit, ww=(): op("vector", emit, waits=list(ww), incs=[(sv, 1)])
        A_ = lambda emit, ww=(): op("scalar", emit, waits=list(ww), incs=[(sa, 1)])

        def clamp(dst, lo):
            V(lambda e: e.tensor_scalar(out=sgn[:], in0=md[:, lo:lo + 8], scalar1=0.0,
              scalar2=None, op0=ALU.is_ge))
            V(lambda e: e.tensor_scalar(out=sgn[:], in0=sgn[:], scalar1=2.0, scalar2=-1.0,
              op0=ALU.mult, op1=ALU.add))
            A_(lambda e: e.activation(out=absv[:], in_=md[:, lo:lo + 8], func=AF.Abs),
               ww=[w(sv)])
            V(lambda e: e.tensor_scalar_max(out=absv[:], in0=absv[:], scalar1=0.25),
              ww=[w(sa)])
            V(lambda e: e.tensor_mul(out=dst[:], in0=sgn[:], in1=absv[:]))
        clamp(fm("ga"), 16)
        clamp(fm("gb"), 40)
        V(lambda e: e.tensor_copy(out=fm("sa")[:], in_=md[:, 0:8]))
        V(lambda e: e.tensor_copy(out=fm("sb_")[:], in_=md[:, 24:32]))
        V(lambda e: e.tensor_scalar_add(out=fm("A")[:], in0=md[:, 8:16], scalar1=1.0))
        V(lambda e: e.tensor_scalar_add(out=fm("B")[:], in0=md[:, 32:40], scalar1=1.0))
        V(lambda e: e.tensor_mul(out=fm("w1A")[:], in0=fm("A")[:], in1=fm("w1")[:]))

        svA = P["svA"][:].rearrange("p (c k) -> p c k", k=4)
        svA2 = P["svA2"][:].rearrange("p (c k) -> p c k", k=2)
        svB = P["svB"][:].rearrange("p (c k) -> p c k", k=4)
        svB2 = P["svB2"][:].rearrange("p (c k) -> p c k", k=2)
        V(lambda e: e.memset(P["sv1"][:], 1.0))
        for t3 in (svA, svA2, svB, svB2):
            V(lambda e, t3=t3: e.memset(t3[:, :, 0:1], 1.0))
        V(lambda e: e.tensor_copy(out=svA[:, :, 1:2], in_=fm("A")[:].unsqueeze(2)))
        V(lambda e: e.tensor_mul(out=svA[:, :, 2:3], in0=fm("A")[:].unsqueeze(2),
          in1=fm("A")[:].unsqueeze(2)))
        V(lambda e: e.tensor_mul(out=svA[:, :, 3:4], in0=fm("A")[:].unsqueeze(2),
          in1=fm("sa")[:].unsqueeze(2)))
        V(lambda e: e.tensor_copy(out=svA2[:, :, 1:2], in_=svA[:, :, 2:3]))
        V(lambda e: e.tensor_copy(out=svB[:, :, 1:2], in_=fm("B")[:].unsqueeze(2)))
        V(lambda e: e.tensor_mul(out=svB[:, :, 2:3], in0=fm("B")[:].unsqueeze(2),
          in1=fm("B")[:].unsqueeze(2)))
        V(lambda e: e.tensor_mul(out=svB[:, :, 3:4], in0=fm("B")[:].unsqueeze(2),
          in1=fm("sb_")[:].unsqueeze(2)))
        V(lambda e: e.tensor_copy(out=svB2[:, :, 1:2], in_=svB[:, :, 2:3]))

        defs = [("A", None), ("sa", None), ("A", "A"), ("A", "sa"), ("sa", "sa"),
                ("B", None), ("sb_", None), ("B", "B"), ("B", "sb_"), ("sb_", "sb_")]
        for i, (a, b) in enumerate(defs):
            if b is None:
                V(lambda e, i=i, a=a: e.reduce_sum(out=scal_in[:, i:i + 1], in_=fm(a)[:],
                  axis=AXL.X))
            else:
                V(lambda e, a=a, b=b: e.tensor_mul(out=prod8[:], in0=fm(a)[:], in1=fm(b)[:]))
                V(lambda e, i=i: e.reduce_sum(out=scal_in[:, i:i + 1], in_=prod8[:], axis=AXL.X))
        V(lambda e: e.memset(scal_in[:, 10:16], 0.0))
        op("tensor", lambda e: e.matmul(pb[1][:, 0:16], P["ones128"][:], scal_in[:],
           start=True, stop=True), waits=[w(sv)], incs=[(spe, 1)])
        V(lambda e: e.tensor_copy(out=P["scal"][:], in_=pb[1][:, 0:16]), ww=[w(spe)])
        V(lambda e: e.tensor_scalar_mul(out=P["nscal"][:], in0=P["scal"][:], scalar1=-1.0))

        for nm, srcv in (("Arow", "A"), ("sarow", "sa"), ("Brow", "B"), ("sbrow", "sb_")):
            for hh in range(2):
                for k4 in range(4):
                    kc = 4 * hh + k4
                    op("tensor", lambda e, s=srcv, kc=kc, k4=k4: e.transpose(
                        pb[2][0:1, 128 * k4:128 * (k4 + 1)], fm(s)[:, kc:kc + 1],
                        P["identf"][:]), waits=[w(sv), w(sa)],
                        incs=[(spe, 1)] if k4 == 3 else [])
                A_(lambda e, nm=nm, hh=hh: e.activation(
                    out=P[nm][0:1, 512 * hh:512 * (hh + 1)], in_=pb[2][0:1, 0:512],
                    func=AF.Copy), ww=[w(spe)])

        v4 = P["vecs4"][:].rearrange("p (c k) -> p c k", k=4)
        V(lambda e: e.tensor_copy(out=v4[:, :, 0:1], in_=fm("w1A")[:].unsqueeze(2)))
        V(lambda e: e.tensor_mul(out=v4[:, :, 1:2], in0=fm("sa")[:].unsqueeze(2),
          in1=fm("w1")[:].unsqueeze(2)))
        V(lambda e: e.tensor_copy(out=v4[:, :, 2:3], in_=fm("w1")[:].unsqueeze(2)))
        V(lambda e: e.tensor_copy(out=v4[:, :, 3:4], in_=fm("b1")[:].unsqueeze(2)))
        for k in ("corr_q", "corr_k", "corr_v", "corr_g", "corr_fbq", "corr_bup"):
            V(lambda e, k=k: e.memset(P[k][:], 0.0))

        for kname in ("wq", "wk", "wv", "wg"):
            short = {"wq": "q", "wk": "k", "wv": "v", "wg": "g"}[kname]
            brn = {"wq": "bq", "wk": "bk", "wv": "bv", "wg": "bg_"}[kname]
            wv_ = wsb[kname][:].rearrange("p (c n) -> p c n", n=D)
            op("sync", lambda e, brn=brn: e.dma_start(out=brow[0:1, 0:D], in_=row_h[brn]),
               waits=[w(spe)], incs=[(sdh, 16)])
            V(lambda e: e.tensor_copy(out=browb[0:1, 0:D], in_=brow[0:1, 0:D]),
              ww=[w(sdh)])
            for half in range(2):
                for kc in range(NCH):
                    op("tensor", lambda e, wv_=wv_, h=half, kc=kc: e.matmul(
                        pb[3][0:4, 0:512], P["vecs4"][:, 4 * kc:4 * kc + 4],
                        wv_[:, kc, 512 * h:512 * (h + 1)],
                        start=(kc == 0), stop=False),
                        waits=[gdl, w(sv), w(sa)], incs=[])
                op("tensor", lambda e, h=half: e.matmul(
                    pb[3][0:4, 0:512], P["e3"][:],
                    browb[0:1, 512 * h:512 * (h + 1)],
                    start=False, stop=True), incs=[(spe, 1)])
                A_(lambda e, s=short, h=half: e.activation(
                    out=P["corr_" + s][0:4, 512 * h:512 * (h + 1)], in_=pb[3][0:4, 0:512],
                    func=AF.Copy), ww=[w(spe)])

        op("sync", lambda e: e.dma_start(out=brow[0:1, 0:D], in_=row_h["f_bq"]),
           waits=[w(spe)], incs=[(sdh, 16)])
        V(lambda e: e.tensor_copy(out=browb[0:1, 0:D], in_=brow[0:1, 0:D]), ww=[w(sdh)])
        for half in range(2):
            op("tensor", lambda e, h=half: e.matmul(
                pb[3][0:4, 0:512], P["e3"][:], browb[0:1, 512 * h:512 * (h + 1)],
                start=True, stop=True), incs=[(spe, 1)])
            A_(lambda e, h=half: e.activation(
                out=P["corr_fbq"][0:4, 512 * h:512 * (h + 1)], in_=pb[3][0:4, 0:512],
                func=AF.Copy), ww=[w(spe)])
        op("sync", lambda e: e.dma_start(out=brow[0:1, 0:2 * D], in_=row_h["bup"]),
           waits=[w(spe)], incs=[(sdh, 16)])
        V(lambda e: e.tensor_copy(out=browb[0:1, 0:2 * D], in_=brow[0:1, 0:2 * D]),
          ww=[w(sdh)])
        for q4 in range(4):
            op("tensor", lambda e, q4=q4: e.matmul(
                pb[3][0:4, 0:512], P["e3"][:], browb[0:1, 512 * q4:512 * (q4 + 1)],
                start=True, stop=True), incs=[(spe, 1)])
            A_(lambda e, q4=q4: e.activation(
                out=P["corr_bup"][0:4, 512 * q4:512 * (q4 + 1)], in_=pb[3][0:4, 0:512],
                func=AF.Copy), ww=[w(spe)])
        op("sync", lambda e: e.dma_start(out=corr3_s[:, 0:D], in_=P["corr_fbq"][:]),
           waits=[w(sa), w(sv)], incs=[(sds, 16)])
        op("sync", lambda e: e.dma_start(out=corr3_s[:, D:3 * D], in_=P["corr_bup"][:]),
           incs=[(sds, 16)])
        V(lambda e: e.memset(P["G"][:], 0.0))
        V(lambda e: e.memset(P["Gb"][:], 0.0))

        for n8 in range(NCH):
            op("gpsimd", lambda e, n=n8: e.dma_start(
                out=stage[:, 0:NPC * 128].rearrange("p (c n) -> p c n", n=128),
                in_=woutp_h[:, 128 * n:128 * (n + 1)].rearrange("(c p) n -> p c n", p=128)),
                waits=[w(sds)], incs=[(S["dgp"], 16)])
            op("sync", lambda e, n=n8: e.dma_start(out=woutbf_s[n], in_=stage[:, 0:NPC * 128]),
               waits=[(S["dgp"], S["dgp"].c)], incs=[(sds, 16)])
        for ee in range(8):
            op("gpsimd", lambda e, x=ee: e.dma_start(
                out=stage[:].rearrange("p (c n) -> p c n", n=D), in_=fWe_h[x].rearrange("(c p) n -> p c n", p=128)),
                waits=[w(sds)], incs=[(S["dgp"], 16)])
            op("sync", lambda e, x=ee: e.dma_start(out=webf_s[x], in_=stage[:]),
               waits=[(S["dgp"], S["dgp"].c)], incs=[(sds, 16)])
        op("sync", lambda e: e.nop(), waits=[w(sds), w(sdh), (S["dhw"], S["dhw"].c)])
        op("gpsimd", lambda e: e.nop(), waits=[w(sdg), (S["dgp"], S["dgp"].c)])
        op("vector", lambda e: e.nop(), waits=[w(sv)])
        op("scalar", lambda e: e.nop(), waits=[w(sa), (S["dhw2"], S["dhw2"].c)])
        op("gpsimd", lambda e: e.nop(), waits=[(S["dhw3"], S["dhw3"].c)])
        op("tensor", lambda e: e.nop(), waits=[w(spe)])

    with nc.Block() as blk:
        ctx.walk(blk, setup)
    st.close()

    p1s = contextlib.ExitStack()
    x_tm = sb("x_tm", [128, D], F32, p1s)
    x_fm = sb("x_fm", [128, NCH * TT], F32, p1s)
    x_sq = sb("x_sq", [128, NCH * TT], BF16, p1s)
    x_bf = sb("x_bf", [128, NCH * TT], BF16, p1s)
    h_fm = sb("h_fm", [128, NCH * TT], F32, p1s)
    h_fmB = sb("h_fmB", [128, NCH * TT], F32, p1s)
    xb1t = sb("xb1t", [128, NCH * TT], F32, p1s)
    xt_bf = sb("xt_bf", [128, NCH * TT], BF16, p1s)
    xt_bfB = sb("xt_bfB", [128, NCH * TT], BF16, p1s)
    qfm = sb("qfm", [128, NCH * TT], BF16, p1s)
    kfm = sb("kfm", [128, NCH * TT], BF16, p1s)
    ktm = sb("ktm", [128, 2 * D], BF16, p1s)
    vtm = sb("vtm", [128, 2 * D], BF16, p1s)
    gtm = sb("gtm", [128, 2 * D], BF16, p1s)
    rettm = sb("rettm", [128, 2 * D], BF16, p1s)
    qt_bf = sb("qt_bf", [128, 256], BF16, p1s)
    spt = sb("spt", [128, 512], BF16, p1s)
    intr = sb("intr", [128, 256], BF16, p1s)
    gn16 = sb("gn16", [128, 6 * 16], F32, p1s)
    wointm = sb("wointm", [128, 2 * D], BF16, p1s)
    woinfm = sb("woinfm", [128, NCH * TT], BF16, p1s)
    tmS1 = sb("tmS1", [128, 16], F32, p1s)
    tmT1 = sb("tmT1", [128, 32], F32, p1s)
    rows1 = sb("rows1", [128, TT], BF16, p1s)
    rows1B = sb("rows1B", [128, TT], BF16, p1s)
    reps1 = sb("reps1", [8, 3 * TT], BF16, p1s)
    reps1B = sb("reps1B", [8, 3 * TT], BF16, p1s)

    def p1(c):
        op, w = c.op, c.w
        spe, sv, sa = S["pe1"], S["v1"], S["a1"]
        sdh, sds = S["dh1"], S["ds1"]
        c.auto_tensor = ()
        c.self_sems = {"vector": sv, "scalar": sa, "gpsimd": S["g1"]}
        c.res = {}
        c.bankmap = {
            "t2a": "BK2", "t2b": "BK2", "t3b": "BK3", "s10": "BK5", "s10b": "BK5",
            "st4_0": "BK5", "st2_0": "BK5", "st4_1": "BK5", "st2_1": "BK5",
            "s9": "BK4", "s8": "BK4", "ssum": "BK4", "s6": "BK3", "s7": "BK3",
            "s12": "BK6", "s13": "BK7", "p0": "BK0", "p1": "BK1",
            "st0e": "BK7", "st1e": "BK6", "sx0e": "BK1", "sx1e": "BK0"}
        wsv = {k: wsb[k][:].rearrange("p (c n) -> p c n", n=D) for k in wsb}
        xfmv = x_fm[:].rearrange("p (c t) -> p c t", t=TT)
        xbfv = x_bf[:].rearrange("p (c t) -> p c t", t=TT)
        xsqv = x_sq[:].rearrange("p (c t) -> p c t", t=TT)
        hv_0 = h_fm[:].rearrange("p (c t) -> p c t", t=TT)
        hv_1 = h_fmB[:].rearrange("p (c t) -> p c t", t=TT)
        xb1v = xb1t[:].rearrange("p (c t) -> p c t", t=TT)
        xtv_0 = xt_bf[:].rearrange("p (c t) -> p c t", t=TT)
        xtv_1 = xt_bfB[:].rearrange("p (c t) -> p c t", t=TT)

        def pbuf(t):
            if t % 2 == 0:
                return hv_0, xtv_0, rows1, reps1, "h0", "xt0", "rw0", "rp0"
            return hv_1, xtv_1, rows1B, reps1B, "h1", "xt1", "rw1", "rp1"
        qv = qfm[:].rearrange("p (c t) -> p c t", t=TT)
        kv_ = kfm[:].rearrange("p (c t) -> p c t", t=TT)
        wfv = woinfm[:].rearrange("p (c t) -> p c t", t=TT)
        V = lambda emit, rd=(), wr=(), ww=(): op("vector", emit, waits=list(ww),
                                                incs=[(sv, 1)], reads=rd, writes=wr)
        A_ = lambda emit, rd=(), wr=(), ww=(): op("scalar", emit, waits=list(ww),
                                                 incs=[(sa, 1)], reads=rd, writes=wr)
        G1 = lambda emit, rd=(), wr=(), ww=(): op("gpsimd", emit, waits=list(ww),
                                                 incs=[(S["g1"], 1)], reads=rd,
                                                 writes=wr)

        def mmg(mms, rd=(), wr=(), extra=()):
            ws = c.dep_waits(rd, wr) + list(extra)
            for i, mm in enumerate(mms):
                op("tensor", mm, waits=ws if i == 0 else (),
                   incs=[(spe, 1)] if i == len(mms) - 1 else ())
            c.register(rd, wr, spe)

        op("vector", lambda e: e.memset(rows1[:], 0.0), incs=[(sv, 1)])
        op("vector", lambda e: e.memset(rows1B[:], 0.0), incs=[(sv, 1)])
        op("vector", lambda e: e.memset(tmS1[:], 0.0), incs=[(sv, 1)])
        op("vector", lambda e: e.memset(tmS1[:, R_ONE:R_ONE + 1], 1.0), incs=[(sv, 1)])
        op("vector", lambda e: e.memset(tmS1[:, 8 + R_ONE:8 + R_ONE + 1], 1.0),
           incs=[(sv, 1)])

        def head1(t):
            hv, xtv, rows1c, reps1c, hk, xtk, rwk, rpk = pbuf(t)
            for b in range(2):
                xk = "x_tm" if b == 0 else "xsq"
                xbuf = x_tm[:] if b == 0 else x_sq[:].bitcast(F32)
                op("sync", lambda e, t=t, b=b, xbuf=xbuf: e.dma_start(
                    out=xbuf,
                    in_=x_h[TT * t + 128 * b:TT * t + 128 * (b + 1), :]),
                    waits=c.dep_waits((), (xk,)), incs=[(sdh, 16)])
                c.register((), (xk,), sdh)
                for jp in range(4):
                    slot = "t2a" if jp % 2 == 0 else "t3b"
                    pbx = pb[2] if jp % 2 == 0 else pb[3]
                    mmg([(lambda e, j=2 * jp + q, pbx=pbx, q=q, xbuf=xbuf: e.transpose(
                            pbx[:, 128 * q:128 * (q + 1)],
                            xbuf[:, 128 * j:128 * (j + 1)], P["identf"][:]))
                         for q in range(2)],
                        rd=(xk,), wr=(slot,))
                    A_(lambda e, b=b, jp=jp, pbx=pbx: e.activation(
                        out=xfmv[:, 2 * jp:2 * jp + 2, 128 * b:128 * (b + 1)],
                        in_=pbx[:, 0:256].rearrange("p (c t) -> p c t", t=128),
                        func=AF.Copy), rd=(slot,), wr=("xfm",))
            A_(lambda e: e.activation(out=x_sq[:], in_=x_fm[:], func=AF.Square),
               rd=("xfm",), wr=("xsq",))
            V(lambda e: e.tensor_copy(out=x_bf[:], in_=x_fm[:]), rd=("xfm",), wr=("xbf",))
            op("sync", lambda e, t=t: e.dma_start(out=xfm_s[t], in_=x_fm[:]),
               waits=c.dep_waits(("xfm",), ()), incs=[(S["dsA1"], 16)])
            c.register(("xfm",), (), S["dsA1"])

            for b in range(2):
                so = 8 * b
                mmg([(lambda e, b=b, kc=kc, so=so: e.matmul(
                        pb[5][:, so:so + 4], xbfv[:, kc, 128 * b:128 * (b + 1)],
                        P["svA"][:, 4 * kc:4 * kc + 4],
                        start=(kc == 0), stop=(kc == NCH - 1))) for kc in range(NCH)],
                    rd=("xbf",), wr=("st4_%d" % b,))
                mmg([(lambda e, b=b, kc=kc, so=so: e.matmul(
                        pb[5][:, so + 4:so + 6], xsqv[:, kc, 128 * b:128 * (b + 1)],
                        P["svA2"][:, 2 * kc:2 * kc + 2],
                        start=(kc == 0), stop=(kc == NCH - 1))) for kc in range(NCH)],
                    rd=("xsq",), wr=("st2_%d" % b,))

        def headB_gen(t):
            hv, xtv, rows1c, reps1c, hk, xtk, rwk, rpk = pbuf(t)
            for b in range(2):
                so = 8 * b
                ks4 = "st4_%d" % b
                ks2 = "st2_%d" % b
                ktm_ = "tmT%d" % b
                ksm_ = "tmS%d" % b
                tm = tmT1[:, 16 * b:16 * (b + 1)]
                ts_ = tmS1[:, 8 * b:8 * (b + 1)]
                V(lambda e, so=so: e.tensor_scalar_mul(out=tm[:, 0:1],
                  in0=pb[5][:, so:so + 1],
                  scalar1=INVD), rd=(ks4,), wr=(ktm_,))
                yield
                V(lambda e, so=so: e.tensor_scalar_mul(out=tm[:, 4:5],
                  in0=pb[5][:, so + 4:so + 5],
                  scalar1=INVD), rd=(ks2,), wr=(ktm_,))
                yield
                V(lambda e: e.tensor_mul(out=tm[:, 5:6], in0=tm[:, 0:1], in1=tm[:, 0:1]),
                  rd=(ktm_,), wr=(ktm_,))
                yield
                V(lambda e: e.tensor_sub(out=tm[:, 6:7], in0=tm[:, 4:5], in1=tm[:, 5:6]),
                  rd=(ktm_,), wr=(ktm_,))
                yield
                A_(lambda e: e.activation(out=tm[:, 7:8], in_=tm[:, 6:7], func=AF.Ln,
                   bias=np.float32(EPS1)), rd=(ktm_,), wr=(ktm_,))
                yield
                A_(lambda e: e.activation(out=tm[:, 1:2], in_=tm[:, 7:8], func=AF.Exp,
                   scale=np.float32(-0.5)), rd=(ktm_,), wr=(ktm_,))
                yield
                V(lambda e: e.scalar_tensor_tensor(out=tm[:, 8:9], in0=tm[:, 0:1],
                  scalar=scol("cA", True), in1=pb[5][:, so + 1:so + 2],
                  op0=ALU.mult, op1=ALU.add),
                  rd=(ktm_, ks4), wr=(ktm_,))
                yield
                V(lambda e: e.tensor_mul(out=tm[:, 8:9], in0=tm[:, 1:2], in1=tm[:, 8:9]),
                  rd=(ktm_,), wr=(ktm_,))
                yield
                V(lambda e: e.tensor_scalar(out=tm[:, 3:4], in0=tm[:, 8:9], scalar1=1.0,
                  scalar2=scol("cs"), op0=ALU.mult, op1=ALU.add), rd=(ktm_,), wr=(ktm_,))
                yield
                V(lambda e: e.tensor_scalar_mul(out=tm[:, 3:4], in0=tm[:, 3:4], scalar1=INVD),
                  rd=(ktm_,), wr=(ktm_,))
                yield
                V(lambda e: e.scalar_tensor_tensor(out=tm[:, 8:9], in0=tm[:, 5:6],
                  scalar=scol("cA2"), in1=pb[5][:, so + 5:so + 6],
                  op0=ALU.mult, op1=ALU.add),
                  rd=(ktm_, ks2), wr=(ktm_,))
                yield
                V(lambda e: e.scalar_tensor_tensor(out=tm[:, 9:10], in0=tm[:, 0:1],
                  scalar=-2.0, in1=pb[5][:, so + 2:so + 3], op0=ALU.mult, op1=ALU.mult),
                  rd=(ktm_, ks4), wr=(ktm_,))
                yield
                V(lambda e: e.tensor_add(out=tm[:, 8:9], in0=tm[:, 8:9], in1=tm[:, 9:10]),
                  rd=(ktm_,), wr=(ktm_,))
                yield
                V(lambda e: e.tensor_mul(out=tm[:, 9:10], in0=tm[:, 1:2], in1=tm[:, 1:2]),
                  rd=(ktm_,), wr=(ktm_,))
                yield
                V(lambda e: e.tensor_mul(out=tm[:, 8:9], in0=tm[:, 8:9], in1=tm[:, 9:10]),
                  rd=(ktm_,), wr=(ktm_,))
                yield
                V(lambda e: e.scalar_tensor_tensor(out=tm[:, 9:10], in0=tm[:, 0:1],
                  scalar=scol("cAs", True), in1=pb[5][:, so + 3:so + 4],
                  op0=ALU.mult, op1=ALU.add),
                  rd=(ktm_, ks4), wr=(ktm_,))
                yield
                V(lambda e: e.tensor_mul(out=tm[:, 9:10], in0=tm[:, 1:2], in1=tm[:, 9:10]),
                  rd=(ktm_,), wr=(ktm_,))
                yield
                V(lambda e: e.scalar_tensor_tensor(out=tm[:, 8:9], in0=tm[:, 9:10],
                  scalar=2.0, in1=tm[:, 8:9], op0=ALU.mult, op1=ALU.add),
                  rd=(ktm_,), wr=(ktm_,))
                yield
                V(lambda e: e.tensor_scalar(out=tm[:, 8:9], in0=tm[:, 8:9], scalar1=1.0,
                  scalar2=scol("cs2"), op0=ALU.mult, op1=ALU.add), rd=(ktm_,), wr=(ktm_,))
                yield
                V(lambda e: e.tensor_mul(out=tm[:, 9:10], in0=tm[:, 3:4], in1=tm[:, 3:4]),
                  rd=(ktm_,), wr=(ktm_,))
                yield
                V(lambda e: e.scalar_tensor_tensor(out=tm[:, 9:10], in0=tm[:, 8:9],
                  scalar=INVD, in1=tm[:, 9:10], op0=ALU.mult, op1=ALU.subtract),
                  rd=(ktm_,), wr=(ktm_,))
                yield
                A_(lambda e: e.activation(out=tm[:, 7:8], in_=tm[:, 9:10], func=AF.Ln,
                   bias=np.float32(EPS1)), rd=(ktm_,), wr=(ktm_,))
                yield
                A_(lambda e: e.activation(out=tm[:, 2:3], in_=tm[:, 7:8], func=AF.Exp,
                   scale=np.float32(-0.5)), rd=(ktm_,), wr=(ktm_,))
                yield
                V(lambda e: e.tensor_mul(out=ts_[:, R_K1:R_K1 + 1], in0=tm[:, 2:3],
                  in1=tm[:, 1:2]), rd=(ktm_,), wr=(ksm_,))
                yield
                V(lambda e: e.scalar_tensor_tensor(out=ts_[:, R_K2:R_K2 + 1],
                  in0=ts_[:, R_K1:R_K1 + 1], scalar=-1.0, in1=tm[:, 0:1],
                  op0=ALU.mult, op1=ALU.mult), rd=(ktm_, ksm_), wr=(ksm_,))
                yield
                V(lambda e: e.tensor_copy(out=ts_[:, R_K3:R_K3 + 1], in_=tm[:, 2:3]),
                  rd=(ktm_,), wr=(ksm_,))
                yield
                V(lambda e: e.scalar_tensor_tensor(out=ts_[:, R_K4:R_K4 + 1],
                  in0=tm[:, 2:3], scalar=-1.0, in1=tm[:, 3:4], op0=ALU.mult, op1=ALU.mult),
                  rd=(ktm_,), wr=(ksm_,))
                yield
                V(lambda e: e.tensor_copy(out=ts_[:, R_R:R_R + 1], in_=tm[:, 1:2]),
                  rd=(ktm_,), wr=(ksm_,))
                yield
                V(lambda e: e.scalar_tensor_tensor(out=ts_[:, R_NMR:R_NMR + 1],
                  in0=tm[:, 1:2], scalar=-1.0, in1=tm[:, 0:1], op0=ALU.mult, op1=ALU.mult),
                  rd=(ktm_,), wr=(ksm_,))
                yield

            for b in range(2):
                ksm_ = "tmS%d" % b
                mmg([lambda e: e.transpose(pb[4][0:8, 384:512], tmS1[:, 8 * b:8 * (b + 1)],
                     P["identf"][:])],
                    rd=(ksm_,), wr=("s9",))
                yield
                A_(lambda e, b=b: e.activation(out=rows1c[0:8, 128 * b:128 * (b + 1)],
                   in_=pb[4][0:8, 384:512], func=AF.Copy), rd=("s9",), wr=(rwk,))
                yield
                mmg([(lambda e, b=b, ri=ri, rk=rk: e.matmul(
                        pb[4][0:8, 128 * ri:128 * (ri + 1)], P["sel"][:, 8 * rk:8 * rk + 8],
                        rows1c[0:8, 128 * b:128 * (b + 1)], start=True, stop=True))
                     for ri, rk in enumerate((R_R, R_NMR, R_K1))],
                    rd=(rwk,), wr=("s8",))
                yield
                A_(lambda e, b=b: e.activation(
                    out=reps1c[:].rearrange("p (k t) -> p k t", k=3)[:, :, 128 * b:128 * (b + 1)],
                    in_=pb[4][0:8, 0:384].rearrange("p (k t) -> p k t", t=128),
                    func=AF.Copy), rd=("s8",), wr=(rpk,))
                yield

            mmg([lambda e: e.matmul(pb[3][:, 0:TT], P["onesb"][0:1, 0:128],
                 reps1c[0:1, 2 * TT:3 * TT], start=True, stop=True)],
                rd=(rpk,), wr=("s6",))
            yield
            mmg([lambda e: e.matmul(pb[3][:, 256:256 + TT], P["onesb"][0:1, 0:128],
                 reps1c[0:1, 0:TT], start=True, stop=True)],
                rd=(rpk,), wr=("s7",))
            yield
            mmg([lambda e: e.matmul(pb[5][:, 256:256 + TT], P["onesb"][0:1, 0:128],
                 reps1c[0:1, TT:2 * TT], start=True, stop=True)],
                rd=(rpk,), wr=("s10b",))
            yield
            for kc in range(NCH):
                V(lambda e, kc=kc: e.scalar_tensor_tensor(
                    out=xtv[:, kc, :], in0=xfmv[:, kc, :], scalar=fm("w1A")[:, kc:kc + 1],
                    in1=pb[3][:, 0:TT], op0=ALU.mult, op1=ALU.mult),
                    rd=("xfm", "s6"), wr=(xtk,))
                yield

        def headB2(t):
            hv, xtv, rows1c, reps1c, hk, xtk, rwk, rpk = pbuf(t)
            for kc in range(NCH):
                V(lambda e, kc=kc: e.scalar_tensor_tensor(
                    out=hv[:, kc, :], in0=xfmv[:, kc, :], scalar=fm("A")[:, kc:kc + 1],
                    in1=pb[3][:, 256:256 + TT], op0=ALU.mult, op1=ALU.mult),
                    rd=("xfm", "s7"), wr=(hk,))
                yield
                V(lambda e, kc=kc: e.scalar_tensor_tensor(
                    out=hv[:, kc, :], in0=pb[5][:, 256:256 + TT],
                    scalar=fm("A")[:, kc:kc + 1], in1=hv[:, kc, :],
                    op0=ALU.mult, op1=ALU.add),
                    rd=("s10b", hk), wr=(hk,))
                yield
                V(lambda e, kc=kc: e.tensor_scalar(
                    out=hv[:, kc, :], in0=hv[:, kc, :], scalar1=1.0,
                    scalar2=fm("sa")[:, kc:kc + 1], op0=ALU.mult, op1=ALU.add),
                    rd=(hk,), wr=(hk,))
                yield

        def tail1(t, gB=None):
            hv, xtv, rows1c, reps1c, hk, xtk, rwk, rpk = pbuf(t)

            def adv(n):
                if gB is not None:
                    for _ in range(n):
                        next(gB, None)
            for pi, (wn, corr, sc_, dstv, dk) in enumerate(
                    (("wq", "corr_q", 0.125, qv, "q"), ("wk", "corr_k", 1.0, kv_, "k"))):
                for oc in range(NCH):
                    slot = "s12" if (pi * NCH + oc) % 2 == 0 else "s13"
                    off = 0 if slot == "s12" else 0
                    bank = pb[6] if slot == "s12" else pb[7]
                    mmg([(lambda e, wn=wn, oc=oc, kc=kc, bank=bank: e.matmul(
                            bank[:, 0:TT], wsv[wn][:, kc, 128 * oc:128 * (oc + 1)],
                            xtv[:, kc, :], start=(kc == 0), stop=False))
                         for kc in range(NCH)] + [
                         (lambda e, corr=corr, oc=oc, bank=bank: e.matmul(
                            bank[:, 0:TT], P[corr][:, 128 * oc:128 * (oc + 1)],
                            rows1c[:, 0:TT], start=False, stop=True))],
                        rd=(xtk, rwk), wr=(slot,))
                    A_(lambda e, dstv=dstv, oc=oc, sc_=sc_, bank=bank: e.activation(
                        out=dstv[:, oc, :], in_=bank[:, 0:TT], func=AF.Identity,
                        scale=np.float32(sc_)), rd=(slot,), wr=(dk,))
                    adv(4)
            for wn, corr, dstt, fn, dk in (("wv", "corr_v", vtm, AF.Copy, "v"),
                                           ("wg", "corr_g", gtm, AF.Silu, "g")):
                for b in range(2):
                    for hf in range(2):
                        slot = "p0" if hf == 0 else "p1"
                        bank = pb[0] if hf == 0 else pb[1]
                        mmg([(lambda e, wn=wn, b=b, hf=hf, kc=kc, bank=bank: e.matmul(
                                bank[:, 0:512], xtv[:, kc, 128 * b:128 * (b + 1)],
                                wsv[wn][:, kc, 512 * hf:512 * (hf + 1)],
                                start=(kc == 0), stop=False)) for kc in range(NCH)] + [
                             (lambda e, corr=corr, b=b, hf=hf, bank=bank: e.matmul(
                                bank[:, 0:512], rows1c[:, 128 * b:128 * (b + 1)],
                                P[corr][:, 512 * hf:512 * (hf + 1)],
                                start=False, stop=True))],
                            rd=(xtk, rwk), wr=(slot,))
                    for hf in range(2):
                        slot = "p0" if hf == 0 else "p1"
                        bank = pb[0] if hf == 0 else pb[1]
                        A_(lambda e, dstt=dstt, b=b, hf=hf, fn=fn, bank=bank: e.activation(
                            out=dstt[:, D * b + 512 * hf:D * b + 512 * (hf + 1)],
                            in_=bank[:, 0:512], func=fn), rd=(slot,), wr=(dk,))
                        adv(4)
            for b in range(2):
                for jq in range(2):
                    slot = "t2a" if jq == 0 else "t2b"
                    off = 0 if jq == 0 else 512
                    mmg([(lambda e, b=b, j=4 * jq + q, off=off, q=q: e.transpose(
                            pb[2][:].bitcast(BF16)[:, off + 128 * q:off + 128 * (q + 1)],
                            kv_[:, j, 128 * b:128 * (b + 1)], P["identb"][:]))
                         for q in range(4)],
                        rd=("k",), wr=(slot,))
                    A_(lambda e, b=b, jq=jq, off=off: e.activation(
                        out=ktm[:, D * b + 512 * jq:D * b + 512 * (jq + 1)],
                        in_=pb[2][:].bitcast(BF16)[:, off:off + 512], func=AF.Copy),
                        rd=(slot,), wr=("ktm",))
                    adv(4)
                V(lambda e, b=b: e.tensor_mul(
                    out=ktm[:, D * b:D * (b + 1)], in0=ktm[:, D * b:D * (b + 1)],
                    in1=P["GCOLW"][:]), rd=("ktm",), wr=("ktm",))

        def retp(b, jp):
            j0 = 2 * jp
            for hh in range(2):
                bk = pb[7] if hh == 0 else pb[6]
                mmg([(lambda e, b=b, j=j0 + dj, hh=hh, bk=bk, dj=dj: e.matmul(
                        bk[:, 256 + 128 * dj:384 + 128 * dj],
                        kv_[64 * hh:64 * (hh + 1), j, 128 * b:128 * (b + 1)],
                        qv[64 * hh:64 * (hh + 1), j, 128 * b:128 * (b + 1)],
                        start=True, stop=True, tile_position=(64 * hh, 0)))
                     for dj in range(2)],
                    rd=("k", "q"), wr=("st0e" if hh == 0 else "st1e",))
            for hh in range(2):
                bk = pb[7] if hh == 0 else pb[6]
                V(lambda e, jp=jp, hh=hh, bk=bk: e.tensor_mul(
                    out=spt[:, 256 * hh:256 * (hh + 1)],
                    in0=bk[:, 256:512],
                    in1=P["DTc"][:, 128 * (8 * hh + 2 * jp):
                                  128 * (8 * hh + 2 * jp + 2)]),
                    rd=("st0e" if hh == 0 else "st1e",), wr=("spt",))
            G1(lambda e, b=b, j0=j0: e.tensor_tensor(
                out=qt_bf[:].rearrange("p (c t) -> p c t", t=128),
                in0=qv[:, j0:j0 + 2, 128 * b:128 * (b + 1)],
                in1=P["GROWc"][:, 128 * j0:128 * (j0 + 2)].rearrange(
                    "p (c t) -> p c t", t=128),
                op=ALU.mult), rd=("q",), wr=("qt",))
            mmg([(lambda e, b=b, j=j0 + dj, dj=dj, hh=hh: e.matmul(
                    pb[7][:, 128 * dj + 64 * hh:128 * dj + 64 * (hh + 1)],
                    spt[:, 256 * hh + 128 * dj:256 * hh + 128 * (dj + 1)],
                    vtm[:, D * b + 64 * (2 * j + hh):
                         D * b + 64 * (2 * j + hh + 1)],
                    start=True, stop=True))
                 for dj in range(2) for hh in range(2)],
                rd=("spt", "v"), wr=("s13",))
            A_(lambda e: e.activation(out=intr[:], in_=pb[7][:, 0:256],
               func=AF.Copy), rd=("s13",), wr=("intr",))
            for hh in range(2):
                bk = pb[1] if hh == 0 else pb[0]
                mmg([(lambda e, j=j0 + dj, hh=hh, bk=bk, dj=dj: e.matmul(
                        bk[:, 384 + 64 * dj:448 + 64 * dj],
                        qt_bf[64 * hh:64 * (hh + 1), 128 * dj:128 * (dj + 1)],
                        P["Gb"][64 * hh:64 * (hh + 1), 64 * j:64 * (j + 1)],
                        start=True, stop=True, tile_position=(64 * hh, 0)))
                     for dj in range(2)],
                    rd=("qt", "Gb"), wr=("sx0e" if hh == 0 else "sx1e",))
            for hh in range(2):
                bk = pb[1] if hh == 0 else pb[0]
                rv = rettm[:, D * b + 128 * j0:D * b + 128 * (j0 + 2)]
                V(lambda e, rv=rv, hh=hh, bk=bk: e.tensor_tensor(
                    out=rv.rearrange("p (c t) -> p c t", t=128)[
                        :, :, 64 * hh:64 * (hh + 1)],
                    in0=intr[:].rearrange("p (c t) -> p c t", t=128)[
                        :, :, 64 * hh:64 * (hh + 1)],
                    in1=bk[:, 384:512].rearrange("p (c t) -> p c t", t=64),
                    op=ALU.add),
                    rd=("intr", "sx0e" if hh == 0 else "sx1e"), wr=("rettm",))
            mmg([(lambda e, b=b, j=j0 + dj, hh=hh, dj=dj: e.matmul(
                    pb[4][64 * hh:64 * (hh + 1), 128 + 64 * dj:192 + 64 * dj],
                    ktm[:, D * b + 64 * (2 * j + hh):D * b + 64 * (2 * j + hh + 1)],
                    vtm[:, D * b + 64 * (2 * j + hh):D * b + 64 * (2 * j + hh + 1)],
                    start=True, stop=True, tile_position=(0, 64 * hh)))
                 for dj in range(2) for hh in range(2)],
                rd=("ktm", "v"), wr=("ssum",))
            for dj in range(2):
                V(lambda e, j=j0 + dj, dj=dj: e.scalar_tensor_tensor(
                    out=P["G"][:, 64 * j:64 * (j + 1)],
                    in0=P["G"][:, 64 * j:64 * (j + 1)],
                    scalar=P["GBLKc"][:, j:j + 1],
                    in1=pb[4][:, 128 + 64 * dj:192 + 64 * dj],
                    op0=ALU.mult, op1=ALU.add), rd=("ssum", "G"), wr=("G",))
            G1(lambda e, j0=j0: e.tensor_copy(
                out=P["Gb"][:, 64 * j0:64 * (j0 + 2)],
                in_=P["G"][:, 64 * j0:64 * (j0 + 2)]), rd=("G",), wr=("Gb",))


        def gn1(b):
            rb_ = rettm[:, D * b:D * (b + 1)]
            gtmp = wointm[:, D * b:D * (b + 1)]
            V(lambda e, rb_=rb_: e.reduce_sum(out=gn16[:, 0:16],
              in_=rb_.rearrange("p (h d) -> p h d", d=HD), axis=AXL.X),
              rd=("rettm",), wr=("gn16",))
            yield
            A_(lambda e, rb_=rb_: e.activation(out=gtmp[:], in_=rb_, func=AF.Square),
               rd=("rettm",), wr=("wointm",))
            yield
            V(lambda e: e.reduce_sum(out=gn16[:, 16:32],
              in_=gtmp[:].rearrange("p (h d) -> p h d", d=HD), axis=AXL.X),
              rd=("wointm",), wr=("gn16",))
            yield
            V(lambda e: e.tensor_scalar_mul(out=gn16[:, 32:48], in0=gn16[:, 0:16],
              scalar1=1.0 / HD), rd=("gn16",), wr=("gn16",))
            yield
            V(lambda e: e.tensor_mul(out=gn16[:, 48:64], in0=gn16[:, 32:48],
              in1=gn16[:, 32:48]), rd=("gn16",), wr=("gn16",))
            yield
            V(lambda e: e.scalar_tensor_tensor(out=gn16[:, 48:64], in0=gn16[:, 16:32],
              scalar=1.0 / HD, in1=gn16[:, 48:64], op0=ALU.mult, op1=ALU.subtract),
              rd=("gn16",), wr=("gn16",))
            yield
            A_(lambda e: e.activation(out=gn16[:, 64:80], in_=gn16[:, 48:64], func=AF.Ln,
               bias=np.float32(EPS1)), rd=("gn16",), wr=("gn16",))
            yield
            A_(lambda e: e.activation(out=gn16[:, 64:80], in_=gn16[:, 64:80], func=AF.Exp,
               scale=np.float32(-0.5)), rd=("gn16",), wr=("gn16",))
            yield
            V(lambda e: e.tensor_mul(out=gn16[:, 80:96], in0=gn16[:, 32:48],
              in1=gn16[:, 64:80]), rd=("gn16",), wr=("gn16",))
            yield
            G1(lambda e, rb_=rb_, gtmp=gtmp: e.tensor_tensor(
                out=gtmp[:].rearrange("p (h d) -> p h d", d=HD),
                in0=rb_.rearrange("p (h d) -> p h d", d=HD),
                in1=gn16[:, 64:80].unsqueeze(2).broadcast_to([128, 16, HD]),
                op=ALU.mult), rd=("rettm", "gn16"), wr=("wointm",))
            yield
            G1(lambda e, gtmp=gtmp: e.tensor_tensor(
                out=gtmp[:].rearrange("p (h d) -> p h d", d=HD),
                in0=gtmp[:].rearrange("p (h d) -> p h d", d=HD),
                in1=gn16[:, 80:96].unsqueeze(2).broadcast_to([128, 16, HD]),
                op=ALU.subtract), rd=("wointm", "gn16"), wr=("wointm",))
            yield
            G1(lambda e, gtmp=gtmp: e.tensor_mul(
                out=gtmp[:], in0=gtmp[:], in1=P["gnwrep"][:]),
                rd=("wointm",), wr=("wointm",))
            yield
            G1(lambda e, gtmp=gtmp: e.tensor_add(
                out=gtmp[:], in0=gtmp[:], in1=P["gnbrep"][:]),
                rd=("wointm",), wr=("wointm",))
            yield
            G1(lambda e, b=b, gtmp=gtmp: e.tensor_mul(
                out=wointm[:, D * b:D * (b + 1)], in0=gtmp[:],
                in1=gtm[:, D * b:D * (b + 1)]), rd=("wointm", "g"), wr=("wointm",))
            yield

        def woin1(b):
            for jq in range(2):
                slot = "t2a" if jq == 0 else "t2b"
                off = 0 if jq == 0 else 512
                mmg([(lambda e, b=b, j=4 * jq + q, off=off, q=q: e.transpose(
                        pb[2][:].bitcast(BF16)[:, off + 128 * q:off + 128 * (q + 1)],
                        wointm[:, D * b + 128 * j:D * b + 128 * (j + 1)], P["identb"][:]))
                     for q in range(4)],
                    rd=("wointm",), wr=(slot,))
                A_(lambda e, b=b, jq=jq, off=off: e.activation(
                    out=wfv[:, 4 * jq:4 * jq + 4, 128 * b:128 * (b + 1)],
                    in_=pb[2][:].bitcast(BF16)[:, off:off + 512].rearrange(
                        "p (c t) -> p c t", t=128),
                    func=AF.Copy), rd=(slot,), wr=("woinfm",))


        def tailD(t):
            for b in range(2):
                for jp in range(4):
                    retp(b, jp)

        def tailE(t):
            hv, xtv, rows1c, reps1c, hk, xtk, rwk, rpk = pbuf(t)
            woin1(0)
            woin1(1)
            for oc in range(NCH):
                slot = "s12" if oc % 2 == 0 else "s13"
                bank = pb[6] if oc % 2 == 0 else pb[7]
                mmg([(lambda e, oc=oc, kc=kc, bank=bank: e.matmul(
                        bank[:, 0:TT], wsv["wo"][:, kc, 128 * oc:128 * (oc + 1)],
                        wfv[:, kc, :], start=(kc == 0), stop=(kc == NCH - 1)))
                     for kc in range(NCH)],
                    rd=("woinfm",), wr=(slot,))
                V(lambda e, oc=oc, bank=bank: e.scalar_tensor_tensor(
                    out=xb1v[:, oc, :], in0=bank[:, 0:TT],
                    scalar=fm("bo")[:, oc:oc + 1], in1=hv[:, oc, :],
                    op0=ALU.add, op1=ALU.add), rd=(slot, hk), wr=("xb1",))
            op("sync", lambda e, t=t: e.dma_start(out=xb1_s[t], in_=xb1t[:]),
               waits=c.dep_waits(("xb1",), ()), incs=[(sds, 16)])
            c.register(("xb1",), (), sds)

        def gnall():
            for _ in gn1(0):
                yield
            for _ in gn1(1):
                yield

        def chain2(ga, gb):
            if ga is not None:
                for _ in ga:
                    yield
            if gb is not None:
                for _ in gb:
                    yield

        NT1 = NTILES or NT
        head1(0)
        for _ in headB_gen(0):
            pass
        for _ in headB2(0):
            pass
        for t in range(NT1):
            if t + 1 < NT1:
                head1(t + 1)
            if t > 0:
                tailD(t - 1)
            gB = chain2(gnall() if t > 0 else None,
                        headB_gen(t + 1) if t + 1 < NT1 else None)
            tail1(t, gB)
            for _ in gB:
                pass
            if t > 0:
                tailE(t - 1)
            if t + 1 < NT1:
                for _ in headB2(t + 1):
                    pass
        tailD(NT1 - 1)
        for _ in gnall():
            pass
        tailE(NT1 - 1)
        op("sync", lambda e: e.nop(), waits=[w(sds), (S["dsA1"], S["dsA1"].c)])
        op("vector", lambda e: e.nop(), waits=[w(sv)])
        op("scalar", lambda e: e.nop(), waits=[w(sa)])
        op("tensor", lambda e: e.nop(), waits=[w(spe)])


    if phases >= 1:
        with nc.Block() as blk:
            ctx.walk(blk, p1)
    p1s.close()
    wst.close()

    p2s = contextlib.ExitStack()
    TT2 = 512
    winp_sb = sb("winp_sb", [128, NCH * 2 * NPD], BF16, p2s)
    brow2 = sb("brow2", [1, 512], F32, p2s)
    b2bf = sb("b2bfs", [128, NCH], BF16, p2s)
    xfc0 = sb("xfc0", [128, TT2], F32, p2s)
    xfc1 = sb("xfc1", [128, TT2], F32, p2s)
    xb12 = sb("xb12", [128, NCH * TT2], F32, p2s)
    xb_bf = sb("xb_bf", [128, NCH * TT2], BF16, p2s)
    x1c0 = sb("x1c0", [128, TT2], F32, p2s)
    x1c1 = sb("x1c1", [128, TT2], F32, p2s)
    xt2 = sb("xt2", [128, NCH * TT2], BF16, p2s)
    s_bf = sb("s_bf", [128, NPC * TT2], BF16, p2s)
    su_bf = sb("su_bf", [128, TT2], BF16, p2s)
    woutb = sb("woutb", [128, NPC * 128], BF16, p2s)
    woutb2 = sb("woutb2", [128, NPC * 128], BF16, p2s)
    tmS2 = sb("tmS2", [128, 32], F32, p2s)
    tmT2 = sb("tmT2", [128, 16], F32, p2s)
    rows2 = sb("rows2", [128, TT2], BF16, p2s)
    reps2 = sb("reps2", [8, 2 * TT2], BF16, p2s)
    hm2 = sb("hm2", [128, TT2], F32, p2s)

    def p2(c):
        op, w = c.op, c.w
        spe, sv, sa = S["pe2"], S["v2"], S["a2"]
        sdh, sds, sdg = S["dh2"], S["ds2"], S["dg2"]
        c.auto_tensor = ()
        c.self_sems = {"vector": sv, "scalar": sa, "gpsimd": S["g2"]}
        c.res = {}
        c.bankmap = {
            "s4": "BK2", "s6": "BK3", "s7": "BK4", "s9": "BK2", "s8": "BK2",
            "glu": "BK0", "glg": "BK1", "wo_e": "BK6", "wo_o": "BK7",
            "s10_0": "BK5", "s10b_0": "BK5", "s10_1": "BK5", "s10b_1": "BK5",
            "s10_2": "BK5", "s10b_2": "BK5", "s10_3": "BK5", "s10b_3": "BK5",
            "sC": "BK2"}
        winv = winp_sb[:].rearrange("p (c n) -> p c n", n=2 * NPD)
        xbsq = s_bf[:, 0:NCH * TT2]
        rowt2 = x1c0
        xbv = xb12[:].rearrange("p (c t) -> p c t", t=TT2)
        xt2v = xt2[:].rearrange("p (c t) -> p c t", t=TT2)
        sv3 = s_bf[:].rearrange("p (c t) -> p c t", t=TT2)
        xb4v = xb12[:].rearrange("p (c f t) -> p f c t", f=2, t=TT)
        V = lambda emit, rd=(), wr=(), ww=(): op("vector", emit, waits=list(ww),
                                                incs=[(sv, 1)], reads=rd, writes=wr)
        A_ = lambda emit, rd=(), wr=(), ww=(): op("scalar", emit, waits=list(ww),
                                                 incs=[(sa, 1)], reads=rd, writes=wr)

        def mmg(mms, rd=(), wr=(), extra=()):
            ws = c.dep_waits(rd, wr) + list(extra)
            for i, mm in enumerate(mms):
                op("tensor", mm, waits=ws if i == 0 else (),
                   incs=[(spe, 1)] if i == len(mms) - 1 else ())
            c.register(rd, wr, spe)

        op("vector", lambda e: e.memset(tmS2[:], 0.0), incs=[(sv, 1)])
        op("gpsimd", lambda e: e.dma_start(
            out=winp_sb[:].rearrange("p (c n) -> p c n", n=2 * NPD),
            in_=winp_h.rearrange("(c p) n -> p c n", p=128)), incs=[(sdg, 16)])
        c.register((), ("winp",), sdg)
        op("gpsimd", lambda e: e.dma_start(out=b2bf[:], in_=b2bf_h),
           incs=[(S["dgb2"], 16)])
        c.register((), ("b2bf",), S["dgb2"])
        for strip in range(11):
            mmg([(lambda e, s=strip, kc=kc: e.matmul(
                    pb[3][0:1, 0:512], b2bf[:, kc:kc + 1],
                    winv[:, kc, 512 * s:512 * (s + 1)],
                    start=(kc == 0), stop=(kc == NCH - 1))) for kc in range(NCH)],
                rd=("winp", "b2bf"), wr=("s6",))
            op("sync", lambda e, s=strip: e.dma_start(
                out=brow2[0:1, :], in_=binp_h[0:1, 512 * s:512 * (s + 1)]),
                waits=c.dep_waits((), ("brow2",)), incs=[(sdh, 16)])
            c.register((), ("brow2",), sdh)
            V(lambda e: e.tensor_add(out=rowt2[0:1, :], in0=pb[3][0:1, 0:512],
              in1=brow2[0:1, :]), rd=("s6", "brow2"), wr=("rowt2",))
            mmg([(lambda e, s=strip, j4=j4: e.transpose(
                    pb[2][:, 4 * s + j4:4 * s + j4 + 1],
                    rowt2[0:1, 128 * j4:128 * (j4 + 1)], P["identf"][0:1, 0:1]))
                 for j4 in range(4)],
                rd=("rowt2",), wr=("s4",))
        V(lambda e: e.tensor_copy(out=P["cuv"][:], in_=pb[2][:, 0:2 * NPC]),
          rd=("s4",), wr=("cuv",))

        NT2 = (NTILES or NT) // 2
        for t in range(max(NT2, 1)):
            for f in range(2):
                op("sync", lambda e, t=t, f=f: e.dma_start(
                    out=xb4v[:, f], in_=xb1_s[2 * t + f].rearrange(
                        "p (c t) -> p c t", t=TT)),
                    waits=c.dep_waits((), ("xb1",)), incs=[(sdh, 16)])
                c.register((), ("xb1",), sdh)
            A_(lambda e: e.activation(out=xbsq[:], in_=xb12[:], func=AF.Square),
               rd=("xb1",), wr=("s_",))
            V(lambda e: e.tensor_copy(out=xb_bf[:], in_=xb12[:]), rd=("xb1",), wr=("xbbf",))
            for bb in range(4):
                so = 8 * bb
                ks4 = "s10_%d" % bb
                ks2 = "s10b_%d" % bb
                mmg([(lambda e, bb=bb, kc=kc, so=so: e.matmul(
                        pb[5][:, so:so + 2],
                        xb_bf[:].rearrange("p (c t) -> p c t", t=TT2)[
                            :, kc, 128 * bb:128 * (bb + 1)],
                        P["sv1"][:, 2 * kc:2 * kc + 2],
                        start=(kc == 0), stop=(kc == NCH - 1))) for kc in range(NCH)],
                    rd=("xbbf",), wr=(ks4,))
                mmg([(lambda e, bb=bb, kc=kc, so=so: e.matmul(
                        pb[5][:, so + 2:so + 4],
                        xbsq.rearrange("p (c t) -> p c t", t=TT2)[
                            :, kc, 128 * bb:128 * (bb + 1)],
                        P["sv1"][:, 2 * kc:2 * kc + 2],
                        start=(kc == 0), stop=(kc == NCH - 1))) for kc in range(NCH)],
                    rd=("s_",), wr=(ks2,))
            for bb in range(4):
                so = 8 * bb
                ks4 = "s10_%d" % bb
                ks2 = "s10b_%d" % bb
                ts_ = tmS2[:, 8 * bb:8 * (bb + 1)]
                tm = tmT2[:, 4 * bb:4 * (bb + 1)]
                ktm_ = "tmT%d" % bb
                ksm_ = "tmS%d" % bb
                V(lambda e, so=so, ts_=ts_: e.tensor_scalar_mul(
                  out=ts_[:, 0:1], in0=pb[5][:, so:so + 1],
                  scalar1=INVD), rd=(ks4,), wr=(ksm_,))
                V(lambda e, so=so, tm=tm: e.tensor_scalar_mul(
                  out=tm[:, 0:1], in0=pb[5][:, so + 2:so + 3],
                  scalar1=INVD), rd=(ks2,), wr=(ktm_,))
                V(lambda e, tm=tm, ts_=ts_: e.tensor_mul(
                  out=tm[:, 1:2], in0=ts_[:, 0:1],
                  in1=ts_[:, 0:1]), rd=(ksm_, ktm_), wr=(ktm_,))
                V(lambda e, tm=tm: e.tensor_sub(out=tm[:, 2:3], in0=tm[:, 0:1],
                  in1=tm[:, 1:2]), rd=(ktm_,), wr=(ktm_,))
                A_(lambda e, tm=tm: e.activation(out=tm[:, 3:4], in_=tm[:, 2:3],
                   func=AF.Ln,
                   bias=np.float32(EPS1)), rd=(ktm_,), wr=(ktm_,))
                A_(lambda e, tm=tm, ts_=ts_: e.activation(
                   out=ts_[:, 1:2], in_=tm[:, 3:4], func=AF.Exp,
                   scale=np.float32(-0.5)), rd=(ktm_,), wr=(ksm_,))
            for bb in range(4):
                ksm_ = "tmS%d" % bb
                mmg([(lambda e, bb=bb: e.transpose(
                        pb[2][0:8, 384:512], tmS2[:, 8 * bb:8 * (bb + 1)],
                        P["identf"][:]))],
                    rd=(ksm_,), wr=("s9",))
                A_(lambda e, bb=bb: e.activation(
                   out=rows2[0:8, 128 * bb:128 * (bb + 1)],
                   in_=pb[2][0:8, 384:512], func=AF.Copy), rd=("s9",), wr=("rows",))
                mmg([(lambda e, bb=bb, ri=ri: e.matmul(
                        pb[2][0:8, 128 * ri:128 * (ri + 1)],
                        P["sel"][:, 8 * ri:8 * ri + 8],
                        rows2[0:8, 128 * bb:128 * (bb + 1)], start=True, stop=True))
                     for ri in range(2)],
                    rd=("rows",), wr=("s8",))
                A_(lambda e, bb=bb: e.activation(
                    out=reps2[:].rearrange("p (k t) -> p k t", k=2)[
                        :, :, 128 * bb:128 * (bb + 1)],
                    in_=pb[2][0:8, 0:256].rearrange("p (k t) -> p k t", t=128),
                    func=AF.Copy), rd=("s8",), wr=("reps",))
            mmg([lambda e: e.matmul(pb[3][:, 0:TT2], P["onesb"][0:1, 0:128],
                 reps2[0:1, 0:TT2], start=True, stop=True)], rd=("reps",), wr=("s6",))
            mmg([lambda e: e.matmul(pb[4][:, 0:TT2], P["onesb"][0:1, 0:128],
                 reps2[0:1, TT2:2 * TT2], start=True, stop=True)],
                rd=("reps",), wr=("s7",))
            for kc in range(NCH):
                V(lambda e, kc=kc: e.tensor_sub(out=hm2[:], in0=xbv[:, kc, :],
                  in1=pb[3][:, 0:TT2]), rd=("xb1", "s6"), wr=("hm2",))
                V(lambda e, kc=kc: e.scalar_tensor_tensor(
                    out=xt2v[:, kc, :], in0=hm2[:], scalar=fm("w2")[:, kc:kc + 1],
                    in1=pb[4][:, 0:TT2], op0=ALU.mult, op1=ALU.mult),
                    rd=("hm2", "s7"), wr=("xt2",))

            for j in range(NPC):
                for ug in range(2):
                    bank = pb[0] if ug == 0 else pb[1]
                    slot = "glu" if ug == 0 else "glg"
                    mmg([(lambda e, j=j, ug=ug, kc=kc, bank=bank: e.matmul(
                            bank[:, 0:TT2],
                            winv[:, kc, 256 * j + 128 * ug:256 * j + 128 * (ug + 1)],
                            xt2v[:, kc, :], start=(kc == 0), stop=(kc == NCH - 1)))
                         for kc in range(NCH)],
                        rd=("xt2", "winp"), wr=(slot,))
                A_(lambda e, j=j: e.activation(
                    out=su_bf[:], in_=pb[0][:, 0:TT2], func=AF.Silu,
                    bias=P["cuv"][:, 2 * j:2 * j + 1]),
                    rd=("glu", "cuv"), wr=("su",))
                V(lambda e, j=j: e.scalar_tensor_tensor(
                    out=sv3[:, j, :], in0=pb[1][:, 0:TT2],
                    scalar=P["cuv"][:, 2 * j + 1:2 * j + 2], in1=su_bf[:],
                    op0=ALU.add, op1=ALU.mult), rd=("glg", "su", "cuv"), wr=("s_",))

            for oc in range(NCH):
                wb_ = woutb if oc % 2 == 0 else woutb2
                wk_ = "woutb" if oc % 2 == 0 else "woutb2"
                xfc = xfc0 if oc % 2 == 0 else xfc1
                xfk = "xf0" if oc % 2 == 0 else "xf1"
                op("gpsimd", lambda e, oc=oc, wb_=wb_: e.dma_start(
                    out=wb_[:], in_=woutbf_s[oc]),
                    waits=c.dep_waits((), (wk_,)) + [(sdg, sdg.c)],
                    incs=[(sdg, 16)])
                c.register((), (wk_,), sdg)
                for f in range(2):
                    op("sync", lambda e, t=t, f=f, oc=oc, xfc=xfc: e.dma_start(
                        out=xfc[:, TT * f:TT * (f + 1)],
                        in_=xfm_s[2 * t + f][:, TT * oc:TT * (oc + 1)]),
                        waits=c.dep_waits((), (xfk,)), incs=[(sdh, 16)])
                    c.register((), (xfk,), sdh)
                bank = pb[6] if oc % 2 == 0 else pb[7]
                slot = "wo_e" if oc % 2 == 0 else "wo_o"
                mmg([(lambda e, j=j, wb_=wb_, bank=bank: e.matmul(
                        bank[:, 0:TT2], wb_[:, 128 * j:128 * (j + 1)], sv3[:, j, :],
                        start=(j == 0), stop=(j == NPC - 1))) for j in range(NPC)],
                    rd=("s_", wk_), wr=(slot,))
                V(lambda e, oc=oc, bank=bank: e.scalar_tensor_tensor(
                    out=hm2[:], in0=bank[:, 0:TT2],
                    scalar=fm("b_out")[:, oc:oc + 1], in1=xbv[:, oc, :],
                    op0=ALU.add, op1=ALU.add), rd=(slot, "xb1"), wr=("hm2",))
                x1c = x1c0 if oc % 2 == 0 else x1c1
                x1k = "x1_0" if oc % 2 == 0 else "x1_1"
                V(lambda e, oc=oc, xfc=xfc, x1c=x1c: e.scalar_tensor_tensor(
                    out=x1c[:], in0=hm2[:], scalar=fm("ga")[:, oc:oc + 1],
                    in1=xfc[:], op0=ALU.mult, op1=ALU.add),
                    rd=("hm2", xfk), wr=(x1k,))
                for f in range(2):
                    op("sync", lambda e, t=t, f=f, oc=oc, x1c=x1c: e.dma_start(
                        out=x1_s[2 * t + f][:, TT * oc:TT * (oc + 1)],
                        in_=x1c[:, TT * f:TT * (f + 1)]),
                        waits=c.dep_waits((x1k,), ()), incs=[(sds, 16)])
                    c.register((x1k,), (), sds)
        op("sync", lambda e: e.nop(), waits=[w(sds)])
        op("vector", lambda e: e.nop(), waits=[w(sv)])
        op("scalar", lambda e: e.nop(), waits=[w(sa)])
        op("tensor", lambda e: e.nop(), waits=[w(spe)])
        op("gpsimd", lambda e: e.nop(),
           waits=[w(sdg), (S["dgb2"], S["dgb2"].c)])


    if phases >= 2:
        with nc.Block() as blk:
            ctx.walk(blk, p2)
    p2s.close()

    p3s = contextlib.ExitStack()
    fWq_sb = sb("fWq_sb", [128, NCH * D], BF16, p3s)
    fWd_sb = sb("fWd_sb", [128, NCH * HD], BF16, p3s)
    fWu_sb = sb("fWu_sb", [HD, 2 * D], BF16, p3s)
    fWo_sb = sb("fWo_sb", [128, NCH * D], BF16, p3s)
    x13 = sb("x13", [128, NCH * TT], F32, p3s)
    x1sq = sb("x1sq", [128, NCH * TT], BF16, p3s)
    x1_bf = sb("x1_bf", [128, NCH * TT], BF16, p3s)
    h2_fm = sb("h2_fm", [128, NCH * TT], F32, p3s)
    h2_fm2 = sb("h2_fm2", [128, NCH * TT], F32, p3s)
    corr_fbq3 = sb("corr_fbq3", [128, D], BF16, p3s)
    corr_bup3 = sb("corr_bup3", [128, 2 * D], BF16, p3s)
    hn_bf = sb("hn_bf", [128, NCH * TT], BF16, p3s)
    z_bf = sb("z_bf", [HD, TT], BF16, p3s)
    qtm3 = sb("qtm3", [128, 2 * D], BF16, p3s)
    kvtm = sb("kvtm", [128, 4 * D], BF16, p3s)
    prod3 = sb("prod3", [128, D], BF16, p3s)
    prod3g = sb("prod3g", [128, D], BF16, p3s)
    prod3b = sb("prod3b", [128, D], BF16, p3s)
    prod3gb = sb("prod3gb", [128, D], BF16, p3s)
    av3g = sb("av3g", [128, D], F32, p3s)
    att3 = sb("att3", [128, 256], F32, p3s)
    attw = sb("attw", [128, 256], BF16, p3s)
    av3 = sb("av3", [128, D], F32, p3s)
    aotm = sb("aotm", [128, 2 * D], BF16, p3s)
    aotmB = sb("aotmB", [128, 2 * D], BF16, p3s)
    aofm = sb("aofm", [128, NCH * TT], BF16, p3s)
    xf1t = sb("xf1t", [128, NCH * TT], F32, p3s)
    tmS3 = sb("tmS3", [128, 8], F32, p3s)
    tmT3 = sb("tmT3", [128, 16], F32, p3s)
    rows3 = sb("rows3", [128, TT], BF16, p3s)
    reps3 = sb("reps3", [8, 3 * TT], BF16, p3s)
    hm3 = sb("hm3", [128, TT], F32, p3s)
    s163 = sb("s163", [128, 48], F32, p3s)

    def p3(c):
        op, w = c.op, c.w
        spe, sv, sa, sg = S["pe3"], S["v3"], S["a3"], S["g3"]
        sdh, sds, sdg = S["dh3"], S["ds3"], S["dg3"]
        c.auto_tensor = ()
        c.self_sems = {"vector": sv, "scalar": sa, "gpsimd": sg}
        c.res = {}
        c.bankmap = {
            "s10": "BK5", "s10b": "BK5", "s12h": "BK4", "s9": "BK4", "s8": "BK4",
            "s7": "BK3", "s6r": "BK3", "s12": "BK6", "s13": "BK7",
            "p0": "BK0", "p1": "BK1", "t2a": "BK2", "t2b": "BK2"}
        x1v = x13[:].rearrange("p (c t) -> p c t", t=TT)
        h2v_0 = h2_fm[:].rearrange("p (c t) -> p c t", t=TT)
        h2v_1 = h2_fm2[:].rearrange("p (c t) -> p c t", t=TT)
        hnv = hn_bf[:].rearrange("p (c t) -> p c t", t=TT)
        aofv = aofm[:].rearrange("p (c t) -> p c t", t=TT)
        xf1v = xf1t[:].rearrange("p (c t) -> p c t", t=TT)
        wqv = fWq_sb[:].rearrange("p (c n) -> p c n", n=D)
        wdv = fWd_sb[:].rearrange("p (c n) -> p c n", n=HD)
        wov = fWo_sb[:].rearrange("p (c n) -> p c n", n=D)
        V = lambda emit, rd=(), wr=(), ww=(): op("vector", emit, waits=list(ww),
                                                incs=[(sv, 1)], reads=rd, writes=wr)
        A_ = lambda emit, rd=(), wr=(), ww=(): op("scalar", emit, waits=list(ww),
                                                 incs=[(sa, 1)], reads=rd, writes=wr)
        G_ = lambda emit, rd=(), wr=(), ww=(): op("gpsimd", emit, waits=list(ww),
                                                 incs=[(sg, 1)], reads=rd, writes=wr)

        def mmg(mms, rd=(), wr=(), extra=()):
            ws = c.dep_waits(rd, wr) + list(extra)
            for i, mm in enumerate(mms):
                op("tensor", mm, waits=ws if i == 0 else (),
                   incs=[(spe, 1)] if i == len(mms) - 1 else ())
            c.register(rd, wr, spe)

        for nm, hh, key in (("fWq_sb", fWq_h, "wq3"), ("fWd_sb", fWdown_h, "wd3"),
                            ("fWo_sb", fWo_h, "wo3")):
            tile = {"fWq_sb": fWq_sb, "fWd_sb": fWd_sb, "fWo_sb": fWo_sb}[nm]
            nn_ = {"fWq_sb": D, "fWd_sb": HD, "fWo_sb": D}[nm]
            op("gpsimd", lambda e, tile=tile, hh=hh, nn_=nn_: e.dma_start(
                out=tile[:].rearrange("p (c n) -> p c n", n=nn_),
                in_=hh.rearrange("(c p) n -> p c n", p=128)), incs=[(sdg, 16)])
            c.register((), (key,), sdg)
        op("gpsimd", lambda e: e.dma_start(out=fWu_sb[:], in_=fWup_h), incs=[(sdg, 16)])
        c.register((), ("wu3",), sdg)
        op("sync", lambda e: e.dma_start(out=corr_fbq3[:], in_=corr3_s[:, 0:D]),
           incs=[(sdh, 16)])
        c.register((), ("cfb",), sdh)
        op("sync", lambda e: e.dma_start(out=corr_bup3[:], in_=corr3_s[:, D:3 * D]),
           incs=[(sdh, 16)])
        c.register((), ("cbu",), sdh)
        op("vector", lambda e: e.memset(rows3[:], 0.0), incs=[(sv, 1)])
        op("vector", lambda e: e.memset(tmS3[:], 0.0), incs=[(sv, 1)])
        op("vector", lambda e: e.memset(tmS3[:, R_ONE:R_ONE + 1], 1.0), incs=[(sv, 1)])

        def head3(t):
            h2v = h2v_0 if t % 2 == 0 else h2v_1
            h2k = "h2a" if t % 2 == 0 else "h2b"
            op("sync", lambda e, t=t: e.dma_start(out=x13[:], in_=x1_s[t]),
               waits=c.dep_waits((), ("x1",)), incs=[(sdh, 16)])
            c.register((), ("x1",), sdh)
            A_(lambda e: e.activation(out=x1sq[:], in_=x13[:], func=AF.Square),
               rd=("x1",), wr=("x1sq",))
            A_(lambda e: e.activation(out=x1_bf[:], in_=x13[:], func=AF.Copy),
               rd=("x1",), wr=("x1bf",))
            for b in range(2):
                mmg([(lambda e, b=b, kc=kc: e.matmul(
                        pb[5][:, 0:4],
                        x1_bf[:].rearrange("p (c t) -> p c t", t=TT)[
                            :, kc, 128 * b:128 * (b + 1)],
                        P["svB"][:, 4 * kc:4 * kc + 4],
                        start=(kc == 0), stop=(kc == NCH - 1))) for kc in range(NCH)],
                    rd=("x1bf",), wr=("s10",))
                mmg([(lambda e, b=b, kc=kc: e.matmul(
                        pb[5][:, 4:6],
                        x1sq[:].rearrange("p (c t) -> p c t", t=TT)[
                            :, kc, 128 * b:128 * (b + 1)],
                        P["svB2"][:, 2 * kc:2 * kc + 2],
                        start=(kc == 0), stop=(kc == NCH - 1))) for kc in range(NCH)],
                    rd=("x1sq",), wr=("s10b",))
                tm = tmT3
                V(lambda e: e.tensor_scalar_mul(out=tm[:, 0:1], in0=pb[5][:, 0:1],
                  scalar1=INVD), rd=("s10",), wr=("tmT",))
                V(lambda e: e.tensor_scalar_mul(out=tm[:, 4:5], in0=pb[5][:, 4:5],
                  scalar1=INVD), rd=("s10b",), wr=("tmT",))
                V(lambda e: e.tensor_mul(out=tm[:, 5:6], in0=tm[:, 0:1], in1=tm[:, 0:1]),
                  rd=("tmT",), wr=("tmT",))
                V(lambda e: e.tensor_sub(out=tm[:, 6:7], in0=tm[:, 4:5], in1=tm[:, 5:6]),
                  rd=("tmT",), wr=("tmT",))
                A_(lambda e: e.activation(out=tm[:, 7:8], in_=tm[:, 6:7], func=AF.Ln,
                   bias=np.float32(EPS1)), rd=("tmT",), wr=("tmT",))
                A_(lambda e: e.activation(out=tm[:, 1:2], in_=tm[:, 7:8], func=AF.Exp,
                   scale=np.float32(-0.5)), rd=("tmT",), wr=("tmT",))
                V(lambda e: e.scalar_tensor_tensor(out=tm[:, 8:9], in0=tm[:, 5:6],
                  scalar=scol("cB2"), in1=pb[5][:, 5:6], op0=ALU.mult, op1=ALU.add),
                  rd=("tmT", "s10b"), wr=("tmT",))
                V(lambda e: e.scalar_tensor_tensor(out=tm[:, 9:10], in0=tm[:, 0:1],
                  scalar=-2.0, in1=pb[5][:, 2:3], op0=ALU.mult, op1=ALU.mult),
                  rd=("tmT", "s10"), wr=("tmT",))
                V(lambda e: e.tensor_add(out=tm[:, 8:9], in0=tm[:, 8:9], in1=tm[:, 9:10]),
                  rd=("tmT",), wr=("tmT",))
                V(lambda e: e.tensor_mul(out=tm[:, 9:10], in0=tm[:, 1:2], in1=tm[:, 1:2]),
                  rd=("tmT",), wr=("tmT",))
                V(lambda e: e.tensor_mul(out=tm[:, 8:9], in0=tm[:, 8:9], in1=tm[:, 9:10]),
                  rd=("tmT",), wr=("tmT",))
                V(lambda e: e.scalar_tensor_tensor(out=tm[:, 9:10], in0=tm[:, 0:1],
                  scalar=scol("cBs", True), in1=pb[5][:, 3:4], op0=ALU.mult, op1=ALU.add),
                  rd=("tmT", "s10"), wr=("tmT",))
                V(lambda e: e.tensor_mul(out=tm[:, 9:10], in0=tm[:, 1:2], in1=tm[:, 9:10]),
                  rd=("tmT",), wr=("tmT",))
                V(lambda e: e.scalar_tensor_tensor(out=tm[:, 8:9], in0=tm[:, 9:10],
                  scalar=2.0, in1=tm[:, 8:9], op0=ALU.mult, op1=ALU.add),
                  rd=("tmT",), wr=("tmT",))
                V(lambda e: e.tensor_scalar(out=tm[:, 8:9], in0=tm[:, 8:9], scalar1=1.0,
                  scalar2=scol("csb2"), op0=ALU.mult, op1=ALU.add), rd=("tmT",), wr=("tmT",))
                V(lambda e: e.tensor_scalar_mul(out=tm[:, 8:9], in0=tm[:, 8:9],
                  scalar1=INVD), rd=("tmT",), wr=("tmT",))
                A_(lambda e: e.activation(out=tm[:, 7:8], in_=tm[:, 8:9], func=AF.Ln,
                   bias=np.float32(EPSR)), rd=("tmT",), wr=("tmT",))
                A_(lambda e: e.activation(out=tm[:, 2:3], in_=tm[:, 7:8], func=AF.Exp,
                   scale=np.float32(-0.5)), rd=("tmT",), wr=("tmT",))
                V(lambda e: e.tensor_copy(out=tmS3[:, R_K2:R_K2 + 1], in_=tm[:, 1:2]),
                  rd=("tmT",), wr=("tmS",))
                V(lambda e: e.scalar_tensor_tensor(out=tmS3[:, R_K3:R_K3 + 1],
                  in0=tm[:, 1:2], scalar=-1.0, in1=tm[:, 0:1], op0=ALU.mult, op1=ALU.mult),
                  rd=("tmT",), wr=("tmS",))
                V(lambda e: e.tensor_copy(out=tmS3[:, R_K4:R_K4 + 1], in_=tm[:, 2:3]),
                  rd=("tmT",), wr=("tmS",))
                mmg([lambda e: e.transpose(pb[4][0:8, 384:512], tmS3[:], P["identf"][:])],
                    rd=("tmS",), wr=("s9",))
                A_(lambda e, b=b: e.activation(out=rows3[0:8, 128 * b:128 * (b + 1)],
                   in_=pb[4][0:8, 384:512], func=AF.Copy), rd=("s9",), wr=("rows",))
                mmg([(lambda e, b=b, ri=ri, rk=rk: e.matmul(
                        pb[4][0:8, 128 * ri:128 * (ri + 1)], P["sel"][:, 8 * rk:8 * rk + 8],
                        rows3[0:8, 128 * b:128 * (b + 1)], start=True, stop=True))
                     for ri, rk in enumerate((R_K2, R_K3, R_K4))],
                    rd=("rows",), wr=("s8",))
                A_(lambda e, b=b: e.activation(
                    out=reps3[:].rearrange("p (k t) -> p k t", k=3)[
                        :, :, 128 * b:128 * (b + 1)],
                    in_=pb[4][0:8, 0:384].rearrange("p (k t) -> p k t", t=128),
                    func=AF.Copy), rd=("s8",), wr=("reps",))
            mmg([lambda e: e.matmul(pb[3][:, 0:TT], P["onesb"][0:1, 0:128],
                 reps3[0:1, 2 * TT:3 * TT], start=True, stop=True)],
                rd=("reps",), wr=("s6r",))
            mmg([lambda e: e.matmul(pb[3][:, 256:256 + TT], P["onesb"][0:1, 0:128],
                 reps3[0:1, 0:TT], start=True, stop=True)],
                rd=("reps",), wr=("s7",))
            mmg([lambda e: e.matmul(pb[5][:, 256:256 + TT], P["onesb"][0:1, 0:128],
                 reps3[0:1, TT:2 * TT], start=True, stop=True)],
                rd=("reps",), wr=("s10b",))
            for kc in range(NCH):
                V(lambda e, kc=kc, h2v=h2v: e.scalar_tensor_tensor(
                    out=h2v[:, kc, :], in0=x1v[:, kc, :], scalar=fm("B")[:, kc:kc + 1],
                    in1=pb[3][:, 256:256 + TT], op0=ALU.mult, op1=ALU.mult),
                    rd=("x1", "s7"), wr=(h2k,))
                V(lambda e, kc=kc, h2v=h2v: e.scalar_tensor_tensor(
                    out=h2v[:, kc, :], in0=pb[5][:, 256:256 + TT],
                    scalar=fm("B")[:, kc:kc + 1], in1=h2v[:, kc, :],
                    op0=ALU.mult, op1=ALU.add),
                    rd=("s10b", h2k), wr=(h2k,))
                V(lambda e, kc=kc, h2v=h2v: e.tensor_scalar(
                    out=h2v[:, kc, :], in0=h2v[:, kc, :], scalar1=1.0,
                    scalar2=fm("sb_")[:, kc:kc + 1], op0=ALU.mult, op1=ALU.add),
                    rd=(h2k,), wr=(h2k,))
                V(lambda e, kc=kc, h2v=h2v: e.scalar_tensor_tensor(
                    out=hnv[:, kc, :], in0=h2v[:, kc, :],
                    scalar=fm("gamma1")[:, kc:kc + 1], in1=pb[3][:, 0:TT],
                    op0=ALU.mult, op1=ALU.mult), rd=(h2k, "s6r"), wr=("hn",))

        def proj3(t, b):
            h2v = h2v_0 if t % 2 == 0 else h2v_1
            h2k = "h2a" if t % 2 == 0 else "h2b"
            for hf in range(2):
                bank = pb[0] if hf == 0 else pb[1]
                slot = "p0" if hf == 0 else "p1"
                mmg([(lambda e, b=b, hf=hf, kc=kc, bank=bank: e.matmul(
                        bank[:, 0:512], hnv[:, kc, 128 * b:128 * (b + 1)],
                        wqv[:, kc, 512 * hf:512 * (hf + 1)],
                        start=(kc == 0), stop=False)) for kc in range(NCH)] + [
                     (lambda e, b=b, hf=hf, bank=bank: e.matmul(
                        bank[:, 0:512], rows3[:, 128 * b:128 * (b + 1)],
                        corr_fbq3[:, 512 * hf:512 * (hf + 1)],
                        start=False, stop=True))],
                    rd=("hn", "rows", "wq3", "cfb"), wr=(slot,))
            for hf in range(2):
                bank = pb[0] if hf == 0 else pb[1]
                slot = "p0" if hf == 0 else "p1"
                A_(lambda e, b=b, hf=hf, bank=bank: e.activation(
                    out=qtm3[:, D * b + 512 * hf:D * b + 512 * (hf + 1)],
                    in_=bank[:, 0:512], func=AF.Identity, scale=np.float32(0.125)),
                    rd=(slot,), wr=("q3_%d" % b,))
            if b == 0:
                mmg([(lambda e, kc=kc: e.matmul(
                        pb[6][0:HD, 0:TT], wdv[:, kc, :], hnv[:, kc, :],
                        start=(kc == 0), stop=(kc == NCH - 1))) for kc in range(NCH)],
                    rd=("hn", "wd3"), wr=("s12",))
                A_(lambda e: e.activation(out=z_bf[:], in_=pb[6][0:HD, 0:TT],
                   func=AF.Identity, bias=P["fm_fbdown"][:]), rd=("s12",), wr=("z3",))
            for hf in range(2):
                for q2 in range(2):
                    bank = pb[0] if q2 == 0 else pb[1]
                    slot = "p0" if q2 == 0 else "p1"
                    mmg([(lambda e, b=b, hf=hf, q2=q2, bank=bank: e.matmul(
                            bank[:, 0:512], z_bf[:, 128 * b:128 * (b + 1)],
                            fWu_sb[:, 1024 * hf + 512 * q2:1024 * hf + 512 * (q2 + 1)],
                            start=True, stop=False)),
                         (lambda e, b=b, hf=hf, q2=q2, bank=bank: e.matmul(
                            bank[:, 0:512], rows3[:, 128 * b:128 * (b + 1)],
                            corr_bup3[:, 1024 * hf + 512 * q2:
                                       1024 * hf + 512 * (q2 + 1)],
                            start=False, stop=True))],
                        rd=("z3", "rows", "wu3", "cbu"), wr=(slot,))
                    A_(lambda e, b=b, hf=hf, q2=q2, bank=bank: e.activation(
                        out=kvtm[:, 2 * D * b + 1024 * hf + 512 * q2:
                                 2 * D * b + 1024 * hf + 512 * (q2 + 1)],
                        in_=bank[:, 0:512], func=AF.Copy), rd=(slot,), wr=("kv3_%d" % b,))

        def att_b(t, b):
            aotmc = aotm if t % 2 == 0 else aotmB
            aok = "aotm0" if t % 2 == 0 else "aotm1"
            qb = qtm3[:, D * b:D * (b + 1)].rearrange("p (h d) -> p h d", d=HD)
            kvb = kvtm[:, 2 * D * b:2 * D * (b + 1)]
            av = att3[:].rearrange("p (h g) -> p h g", g=H)
            for g in range(H):
                pr0 = (prod3, prod3g, prod3b, prod3gb)[g % 4]
                pk0 = ("prod", "prodg", "prodb", "prodgb")[g % 4]
                eng0 = V if g % 2 == 0 else G_
                eng0(lambda e, g=g, qb=qb, kvb=kvb, pr0=pr0: e.tensor_tensor(
                    out=pr0[:].rearrange("p (h d) -> p h d", d=HD), in0=qb,
                    in1=kvb[:, 128 * g:128 * g + HD].unsqueeze(1).broadcast_to(
                        [128, H, HD]), op=ALU.mult),
                    rd=("q3_%d" % b, "kv3_%d" % b), wr=(pk0,))
                V(lambda e, g=g, pr0=pr0: e.reduce_sum(
                    out=att3[:].rearrange("p (h g) -> p h g", g=H)[:, :, g:g + 1],
                    in_=pr0[:].rearrange("p (h d) -> p h d", d=HD), axis=AXL.X),
                    rd=(pk0,), wr=("att",))
            V(lambda e, av=av: e.reduce_max(out=s163[:, 0:16], in_=av, axis=AXL.X),
              rd=("att",), wr=("s163",))
            V(lambda e, av=av: e.tensor_tensor(
                out=att3[:].rearrange("p (h g) -> p h g", g=H), in0=av,
                in1=s163[:, 0:16].unsqueeze(2).broadcast_to([128, H, H]),
                op=ALU.subtract), rd=("att", "s163"), wr=("att",))
            A_(lambda e: e.activation(out=att3[:], in_=att3[:], func=AF.Exp),
               rd=("att",), wr=("att",))
            V(lambda e, av=av: e.reduce_sum(out=s163[:, 16:32], in_=av, axis=AXL.X),
              rd=("att",), wr=("s163",))
            V(lambda e: e.reciprocal(out=s163[:, 32:48], in_=s163[:, 16:32]),
              rd=("s163",), wr=("s163",))
            V(lambda e, av=av: e.tensor_tensor(
                out=attw[:].rearrange("p (h g) -> p h g", g=H), in0=av,
                in1=s163[:, 32:48].unsqueeze(2).broadcast_to([128, H, H]),
                op=ALU.mult), rd=("att", "s163"), wr=("attw",))
            for g in range(H):
                eng = V if g < 7 else G_
                pr = prod3 if g < 7 else prod3g
                acc = av3 if g < 7 else av3g
                pk = "prod" if g < 7 else "prodg"
                ak = "av" if g < 7 else "avg"
                if g in (0, 7):
                    eng(lambda e, g=g, kvb=kvb, acc=acc: e.tensor_tensor(
                        out=acc[:].rearrange("p (h d) -> p h d", d=HD),
                        in0=attw[:].rearrange("p (h g) -> p h g", g=H)[
                            :, :, g:g + 1].broadcast_to([128, H, HD]),
                        in1=kvb[:, 128 * g + HD:128 * (g + 1)].unsqueeze(1)
                        .broadcast_to([128, H, HD]), op=ALU.mult),
                        rd=("attw", "kv3_%d" % b), wr=(ak,))
                else:
                    eng(lambda e, g=g, kvb=kvb, pr=pr: e.tensor_tensor(
                        out=pr[:].rearrange("p (h d) -> p h d", d=HD),
                        in0=attw[:].rearrange("p (h g) -> p h g", g=H)[
                            :, :, g:g + 1].broadcast_to([128, H, HD]),
                        in1=kvb[:, 128 * g + HD:128 * (g + 1)].unsqueeze(1)
                        .broadcast_to([128, H, HD]), op=ALU.mult),
                        rd=("attw", "kv3_%d" % b), wr=(pk,))
                    eng(lambda e, pr=pr, acc=acc: e.tensor_add(
                        out=acc[:], in0=acc[:], in1=pr[:]), rd=(pk, ak), wr=(ak,))
            V(lambda e, b=b, aotmc=aotmc: e.tensor_add(
              out=aotmc[:, D * b:D * (b + 1)],
              in0=av3[:], in1=av3g[:]), rd=("av", "avg"), wr=(aok,))

        def tail3(t):
            h2v = h2v_0 if t % 2 == 0 else h2v_1
            h2k = "h2a" if t % 2 == 0 else "h2b"
            aotmc = aotm if t % 2 == 0 else aotmB
            aok = "aotm0" if t % 2 == 0 else "aotm1"
            for b in range(2):
                for jq in range(2):
                    slot = "t2a" if jq == 0 else "t2b"
                    off = 0 if jq == 0 else 512
                    mmg([(lambda e, b=b, j=4 * jq + q, off=off, q=q: e.transpose(
                            pb[2][:].bitcast(BF16)[:, off + 128 * q:off + 128 * (q + 1)],
                            aotmc[:, D * b + 128 * j:D * b + 128 * (j + 1)], P["identb"][:]))
                         for q in range(4)],
                        rd=(aok,), wr=(slot,))
                    A_(lambda e, b=b, jq=jq, off=off: e.activation(
                        out=aofv[:, 4 * jq:4 * jq + 4, 128 * b:128 * (b + 1)],
                        in_=pb[2][:].bitcast(BF16)[:, off:off + 512].rearrange(
                            "p (c t) -> p c t", t=128),
                        func=AF.Copy), rd=(slot,), wr=("aofm",))

            for oc in range(NCH):
                bank = pb[6] if oc % 2 == 0 else pb[7]
                slot = "s12" if oc % 2 == 0 else "s13"
                mmg([(lambda e, oc=oc, kc=kc, bank=bank: e.matmul(
                        bank[:, 0:TT], wov[:, kc, 128 * oc:128 * (oc + 1)],
                        aofv[:, kc, :], start=(kc == 0), stop=(kc == NCH - 1)))
                     for kc in range(NCH)],
                    rd=("aofm", "wo3"), wr=(slot,))
                V(lambda e, oc=oc, bank=bank, h2v=h2v: e.scalar_tensor_tensor(
                    out=xf1v[:, oc, :], in0=bank[:, 0:TT],
                    scalar=fm("f_bo")[:, oc:oc + 1], in1=h2v[:, oc, :],
                    op0=ALU.add, op1=ALU.add), rd=(slot, h2k), wr=("xf1",))
            op("sync", lambda e, t=t: e.dma_start(out=xf1_s[t], in_=xf1t[:]),
               waits=c.dep_waits(("xf1",), ()), incs=[(sds, 16)])
            c.register(("xf1",), (), sds)

        NT3 = NTILES or NT
        for t in range(NT3):
            head3(t)
            proj3(t, 0)
            if t > 0:
                att_b(t - 1, 1)
            proj3(t, 1)
            if t > 0:
                tail3(t - 1)
            att_b(t, 0)
        att_b(NT3 - 1, 1)
        tail3(NT3 - 1)
        op("sync", lambda e: e.nop(), waits=[w(sds)])
        op("vector", lambda e: e.nop(), waits=[w(sv)])
        op("scalar", lambda e: e.nop(), waits=[w(sa)])
        op("tensor", lambda e: e.nop(), waits=[w(spe)])
        op("gpsimd", lambda e: e.nop(), waits=[w(sg), w(sdg)])


    if phases >= 3:
        with nc.Block() as blk:
            ctx.walk(blk, p3)
    p3s.close()

    p4s = contextlib.ExitStack()
    TT4 = 512
    NT4 = T // TT4
    x14 = sb("x14", [128, 2 * NCH * TT], F32, p4s)
    xf14 = sb("xf14", [128, 2 * NCH * TT], F32, p4s)
    xfsq = sb("xfsq", [128, NCH * TT], BF16, p4s)
    min_bf = sb("min_bf", [128, 2 * NCH * TT], BF16, p4s)
    he_bf = sb("he_bf", [128, NCH * TT4], BF16, p4s)
    he_bf2 = sb("he_bf2", [128, NCH * TT4], BF16, p4s)
    gb8 = sb("gb8", [128, 8 * TT4], BF16, p4s)
    we_sb = sb("we_sb", [128, NCH * D], BF16, p4s)
    we_sb2 = sb("we_sb2", [128, NCH * D], BF16, p4s)
    fWg_sb = sb("fWg_sb", [128, NCH * 8], BF16, p4s)
    glog = sb("glog", [8, TT4], F32, p4s)
    gtm4 = sb("gtm4", [128, 8], F32, p4s)
    gfm4 = sb("gfm4", [8, TT4], BF16, p4s)
    g16 = sb("g16", [128, 4], F32, p4s)
    tmS4 = sb("tmS4", [128, 8], F32, p4s)
    rows4 = sb("rows4", [8, TT4], BF16, p4s)
    reps4 = sb("reps4", [8, TT4], BF16, p4s)
    zt4 = sb("zt4", [128, TT], F32, p4s)
    outfm = sb("outfm", [128, 2 * NCH * TT], F32, p4s)
    outtm = sb("outtm", [128, D], F32, p4s)

    def p4(c):
        op, w = c.op, c.w
        spe, sv, sa = S["pe4"], S["v4"], S["a4"]
        sdh, sds, sdg = S["dh4"], S["ds4"], S["dg4"]
        c.auto_tensor = ()
        c.self_sems = {"vector": sv, "scalar": sa, "gpsimd": S["g4"]}
        c.res = {}
        c.bankmap = {
            "s4": "BK2", "s4b": "BK2", "s4c": "BK2", "s6": "BK3", "s7": "BK3",
            "so2": "BK2", "so3": "BK3"}
        V = lambda emit, rd=(), wr=(), ww=(): op("vector", emit, waits=list(ww),
                                                incs=[(sv, 1)], reads=rd, writes=wr)
        A_ = lambda emit, rd=(), wr=(), ww=(): op("scalar", emit, waits=list(ww),
                                                 incs=[(sa, 1)], reads=rd, writes=wr)

        def mmg(mms, rd=(), wr=(), extra=()):
            ws = c.dep_waits(rd, wr) + list(extra)
            for i, mm in enumerate(mms):
                op("tensor", mm, waits=ws if i == 0 else (),
                   incs=[(spe, 1)] if i == len(mms) - 1 else ())
            c.register(rd, wr, spe)

        TT4 = 512
        NT4 = T // TT4
        xf4 = xf14[:].rearrange("p (f c t) -> p f c t", c=NCH, t=TT)
        xq4 = xfsq[:].rearrange("p (c t) -> p c t", t=TT)
        x14v = x14[:].rearrange("p (f c t) -> p f c t", c=NCH, t=TT)
        mvk = min_bf[:].rearrange("p (f c t) -> p c f t", c=NCH, t=TT)
        hev_0 = he_bf[:].rearrange("p (c t) -> p c t", t=TT4)
        hev_1 = he_bf2[:].rearrange("p (c t) -> p c t", t=TT4)
        gbv = gb8[:].rearrange("p (e t) -> p e t", t=TT4)
        ofm = outfm[:].rearrange("p (f c t) -> p f c t", c=NCH, t=TT)
        wgv = fWg_sb[:].rearrange("p (c n) -> p c n", n=8)

        op("gpsimd", lambda e: e.dma_start(
            out=fWg_sb[:].rearrange("p (c n) -> p c n", n=8),
            in_=fWg_h.rearrange("(c p) n -> p c n", p=128)), incs=[(S["dwg"], 16)])
        c.register((), ("wg4",), S["dwg"])
        op("vector", lambda e: e.memset(tmS4[:], 0.0), incs=[(sv, 1)])

        for t in range(min(NTILES or NT4, NT4)):
            op("sync", lambda e, t=t: e.dma_start(
                out=x14[:].rearrange("p (f n) -> p f n", f=2),
                in_=x1_s[2 * t:2 * t + 2].rearrange("f p n -> p f n")),
                waits=c.dep_waits((), ("x14",)), incs=[(sdh, 16)])
            c.register((), ("x14",), sdh)
            op("sync", lambda e, t=t: e.dma_start(
                out=xf14[:].rearrange("p (f n) -> p f n", f=2),
                in_=xf1_s[2 * t:2 * t + 2].rearrange("f p n -> p f n")),
                waits=c.dep_waits((), ("xf14",)), incs=[(sdh, 16)])
            c.register((), ("xf14",), sdh)
            for f in range(2):
                A_(lambda e, f=f: e.activation(
                    out=xfsq[:], in_=xf14[:, f * NCH * TT:(f + 1) * NCH * TT],
                    func=AF.Square), rd=("xf14",), wr=("xfsq",))
                for b in range(2):
                    bb = 2 * f + b
                    mmg([(lambda e, b=b, kc=kc: e.matmul(
                            pb[2][:, 0:2], xq4[:, kc, 128 * b:128 * (b + 1)],
                            P["sv1"][:, 2 * kc:2 * kc + 2],
                            start=(kc == 0), stop=(kc == NCH - 1))) for kc in range(NCH)],
                        rd=("xfsq",), wr=("s4",))
                    V(lambda e: e.tensor_scalar_mul(out=g16[:, 0:1], in0=pb[2][:, 0:1],
                      scalar1=INVD), rd=("s4",), wr=("g16",))
                    A_(lambda e: e.activation(out=g16[:, 1:2], in_=g16[:, 0:1], func=AF.Ln,
                       bias=np.float32(EPSR)), rd=("g16",), wr=("g16",))
                    A_(lambda e: e.activation(out=tmS4[:, 0:1], in_=g16[:, 1:2],
                       func=AF.Exp, scale=np.float32(-0.5)), rd=("g16",), wr=("tmS",))
                    mmg([lambda e: e.transpose(pb[2][0:8, 128:256], tmS4[:],
                         P["identf"][:])], rd=("tmS",), wr=("s4b",))
                    A_(lambda e, bb=bb: e.activation(out=rows4[0:8, 128 * bb:128 * (bb + 1)],
                       in_=pb[2][0:8, 128:256], func=AF.Copy), rd=("s4b",), wr=("rows",))
                    mmg([(lambda e, bb=bb: e.matmul(
                            pb[2][0:8, 256:384], P["sel"][:, 0:8],
                            rows4[0:8, 128 * bb:128 * (bb + 1)], start=True, stop=True))],
                        rd=("rows",), wr=("s4c",))
                    A_(lambda e, bb=bb: e.activation(out=reps4[0:8, 128 * bb:128 * (bb + 1)],
                       in_=pb[2][0:8, 256:384], func=AF.Copy), rd=("s4c",), wr=("reps",))
            for f in range(2):
                for b in range(2):
                    bb = 2 * f + b
                    mmg([(lambda e, bb=bb: e.matmul(
                            pb[3][:, 0:128], P["onesb"][0:1, 0:128],
                            reps4[0:1, 128 * bb:128 * (bb + 1)], start=True, stop=True))],
                        rd=("reps",), wr=("s6",))
                    for kc in range(NCH):
                        V(lambda e, f=f, b=b, kc=kc: e.scalar_tensor_tensor(
                            out=min_bf[:].rearrange("p (f c t) -> p f c t", c=NCH, t=TT)[
                                :, f, kc, 128 * b:128 * (b + 1)],
                            in0=xf4[:, f, kc, 128 * b:128 * (b + 1)],
                            scalar=fm("gamma2")[:, kc:kc + 1], in1=pb[3][:, 0:128],
                            op0=ALU.mult, op1=ALU.mult), rd=("xf14", "s6"), wr=("min",))
            for f in range(2):
                mmg([(lambda e, f=f, kc=kc: e.matmul(
                        pb[3][0:8, TT * f:TT * (f + 1)], wgv[:, kc, :],
                        min_bf[:].rearrange("p (f c t) -> p f c t", c=NCH, t=TT)[:, f, kc, :],
                        start=(kc == 0), stop=(kc == NCH - 1))) for kc in range(NCH)],
                    rd=("min", "wg4"), wr=("s6" if f == 0 else "s7",))
            A_(lambda e: e.activation(out=glog[:], in_=pb[3][0:8, 0:TT4], func=AF.Identity,
               bias=P["fm_fbg"][:]), rd=("s6", "s7"), wr=("glog",))
            for bb in range(4):
                mmg([(lambda e, bb=bb: e.transpose(
                        pb[2][:, 0:8], glog[:, 128 * bb:128 * (bb + 1)],
                        P["identf"][0:8, 0:8]))], rd=("glog",), wr=("s4",))
                V(lambda e: e.tensor_copy(out=gtm4[:], in_=pb[2][:, 0:8]),
                  rd=("s4",), wr=("gtm4",))
                V(lambda e: e.reduce_max(out=g16[:, 0:1], in_=gtm4[:], axis=AXL.X),
                  rd=("gtm4",), wr=("g16",))
                V(lambda e: e.tensor_scalar_mul(out=g16[:, 1:2], in0=g16[:, 0:1],
                  scalar1=-1.0), rd=("g16",), wr=("g16",))
                A_(lambda e: e.activation(out=gtm4[:], in_=gtm4[:], func=AF.Exp,
                   bias=g16[:, 1:2]), rd=("gtm4", "g16"), wr=("gtm4",))
                V(lambda e: e.reduce_sum(out=g16[:, 2:3], in_=gtm4[:], axis=AXL.X),
                  rd=("gtm4",), wr=("g16",))
                V(lambda e: e.reciprocal(out=g16[:, 3:4], in_=g16[:, 2:3]),
                  rd=("g16",), wr=("g16",))
                V(lambda e: e.tensor_scalar_mul(out=gtm4[:], in0=gtm4[:],
                  scalar1=g16[:, 3:4]), rd=("gtm4", "g16"), wr=("gtm4",))
                mmg([(lambda e, bb=bb: e.transpose(pb[2][0:8, 128:256], gtm4[:],
                     P["identf"][:]))], rd=("gtm4",), wr=("s4b",))
                A_(lambda e, bb=bb: e.activation(out=gfm4[:, 128 * bb:128 * (bb + 1)],
                   in_=pb[2][0:8, 128:256], func=AF.Copy), rd=("s4b",), wr=("gfm",))
            for e_ in range(8):
                mmg([(lambda e, x=e_: e.matmul(
                        pb[3][:, 0:TT4], P["selw"][:, 128 * x:128 * (x + 1)], gfm4[0:8, :],
                        start=True, stop=True))], rd=("gfm",), wr=("s6",))
                A_(lambda e, x=e_: e.activation(out=gbv[:, x, :], in_=pb[3][:, 0:TT4],
                   func=AF.Copy), rd=("s6",), wr=("gb8",))

            weneed = {}
            op("gpsimd", lambda e: e.dma_start(out=we_sb[:], in_=webf_s[0]),
               waits=c.dep_waits((), ("we0",)), incs=[(sdg, 16)])
            c.register((), ("we0",), sdg)
            weneed[0] = (sdg, sdg.c)
            for e_ in range(8):
                buf = we_sb if e_ % 2 == 0 else we_sb2
                nbuf = we_sb2 if e_ % 2 == 0 else we_sb
                bk = "we0" if e_ % 2 == 0 else "we1"
                nbk = "we1" if e_ % 2 == 0 else "we0"
                hev = hev_0 if e_ % 2 == 0 else hev_1
                hek = "he0" if e_ % 2 == 0 else "he1"
                wevv = buf[:].rearrange("p (c n) -> p c n", n=D)
                for kc in range(NCH):
                    V(lambda e, x=e_, kc=kc, hev=hev: e.tensor_tensor(
                        out=hev[:, kc, :].rearrange("p (f t) -> p f t", t=TT),
                        in0=mvk[:, kc, :, :],
                        in1=gbv[:, x, :].rearrange("p (f t) -> p f t", t=TT),
                        op=ALU.mult), rd=("min", "gb8"), wr=(hek,))
                if e_ < 7:
                    nsem = S["dg4b"] if e_ % 2 == 0 else sdg
                    op("gpsimd", lambda e, x=e_: e.dma_start(out=nbuf[:], in_=webf_s[x + 1]),
                       waits=c.dep_waits((), (nbk,)), incs=[(nsem, 16)])
                    c.register((), (nbk,), nsem)
                    weneed[e_ + 1] = (nsem, nsem.c)
                ws0 = c.dep_waits((hek, bk), ()) + [weneed[e_]]
                for oc in range(NCH):
                    for kc in range(NCH):
                        op("tensor", lambda e, x=e_, oc=oc, kc=kc, wevv=wevv,
                           hev=hev: e.matmul(
                            pb[oc][:, 0:512], wevv[:, kc, 128 * oc:128 * (oc + 1)],
                            hev[:, kc, :],
                            start=(x == 0 and kc == 0), stop=(x == 7 and kc == NCH - 1)),
                            waits=ws0 if (oc == 0 and kc == 0) else (),
                            incs=[(spe, 1)] if (oc == NCH - 1 and kc == NCH - 1) else [])
                c.register((hek, bk), ("moe",) if e_ == 7 else (), spe)
            for oc in range(NCH):
                for f in range(2):
                    V(lambda e, oc=oc, f=f: e.tensor_add(
                        out=zt4[:], in0=pb[oc][:, 256 * f:256 * (f + 1)],
                        in1=xf4[:, f, oc, :]), rd=("moe", "xf14"), wr=("zt4",))
                    V(lambda e, oc=oc, f=f: e.scalar_tensor_tensor(
                        out=ofm[:, f, oc, :], in0=zt4[:], scalar=fm("gb")[:, oc:oc + 1],
                        in1=x14v[:, f, oc, :], op0=ALU.mult, op1=ALU.add),
                        rd=("zt4", "x14"), wr=("ofm",))
            for bb in range(4):
                f, b = bb // 2, bb % 2
                for jq in range(2):
                    slot = "so2" if jq == 0 else "so3"
                    bank_off = 0 if jq == 0 else 0
                    pbx = pb[2] if jq == 0 else pb[3]
                    mmg([(lambda e, f=f, b=b, j=4 * jq + q, pbx=pbx, q=q: e.transpose(
                            pbx[:, 128 * q:128 * (q + 1)],
                            ofm[:, f, j, 128 * b:128 * (b + 1)], P["identf"][:]))
                         for q in range(4)],
                        rd=("ofm",), wr=(slot,))
                    A_(lambda e, jq=jq, pbx=pbx: e.activation(
                        out=outtm[:, 512 * jq:512 * (jq + 1)], in_=pbx[:, 0:512],
                        func=AF.Copy), rd=(slot,), wr=("outtm",))
                op("sync", lambda e, t=t, bb=bb: e.dma_start(
                    out=out_h[TT4 * t + 128 * bb:TT4 * t + 128 * (bb + 1), :],
                    in_=outtm[:]), waits=c.dep_waits(("outtm",), ()), incs=[(sds, 16)])
                c.register(("outtm",), (), sds)
        op("gpsimd", lambda e: e.nop(), waits=[w(sdg), (S["dg4b"], S["dg4b"].c)])
        op("vector", lambda e: e.nop(), waits=[w(sv)])
        op("scalar", lambda e: e.nop(), waits=[w(sa)])
        op("tensor", lambda e: e.nop(), waits=[w(spe)])
        op("sync", lambda e: e.nop(), waits=[w(sds)])


    if phases >= 4:
        with nc.Block() as blk:
            ctx.walk(blk, p4)
    p4s.close()
    big.close()
    return nc, out_h.name


def make_shared(inputs):
    g = lambda k: np.ascontiguousarray(np.asarray(inputs[k], np.float32))
    wp, bp = permute_w_in(g("L0_attn_w_in"), g("L0_attn_b_in"))
    woutp = np.zeros((NPD, D), np.float32)
    woutp[:NHID] = g("L0_attn_w_out")
    CN = host_consts()
    return dict(
        wmod=g("L0_wmod"),
        wq=g("L0_attn_wq"), wk=g("L0_attn_wk"), wv=g("L0_attn_wv"),
        wg=g("L0_attn_wg"), wo=g("L0_attn_wo"),
        winp=wp, binp=bp, woutp=woutp,
        fWq=g("L0_ffn_Wq"), fWdown=g("L0_ffn_Wdown"), fWup=g("L0_ffn_Wup"),
        fWo=g("L0_ffn_Wo"), fWg=g("L0_ffn_Wg"), fWe=g("L0_ffn_We"),
        fm_w1=fmv(g("L0_attn_norm1_w")), fm_b1=fmv(g("L0_attn_norm1_b")),
        fm_w2=fmv(g("L0_attn_norm2_w")),
        b2bf=fmv(g("L0_attn_norm2_b")),
        fm_bq=fmv(g("L0_attn_bq")), fm_bk=fmv(g("L0_attn_bk")),
        fm_bo=fmv(g("L0_attn_bo")), fm_b_out=fmv(g("L0_attn_b_out")),
        fm_gamma1=fmv(g("L0_ffn_gamma1")), fm_gamma2=fmv(g("L0_ffn_gamma2")),
        fm_f_bo=fmv(g("L0_ffn_bo")),
        fm_f_bdown=g("L0_ffn_bdown").reshape(HD, 1),
        fm_f_bg=g("L0_ffn_bg").reshape(8, 1),
        row_bv=g("L0_attn_bv").reshape(1, D), row_bg=g("L0_attn_bg").reshape(1, D),
        row_fbq=g("L0_ffn_bq").reshape(1, D), row_bup=g("L0_ffn_bup").reshape(1, 2 * D),
        row_bq=g("L0_attn_bq").reshape(1, D), row_bk=g("L0_attn_bk").reshape(1, D),
        gnw_rep=np.ascontiguousarray(np.broadcast_to(g("L0_attn_gn_w")[None, :], (128, D))),
        gnb_rep=np.ascontiguousarray(np.broadcast_to(g("L0_attn_gn_b")[None, :], (128, D))),
        **CN,
    )


def kernel(**inputs):
    shared = make_shared(inputs)
    in_maps = []
    for b in range(8):
        m = dict(shared)
        m["x"] = np.ascontiguousarray(np.asarray(inputs["L0_attn_x"][b], np.float32))
        m["cvec"] = fmv(np.asarray(inputs["L0_c"][b], np.float32))
        in_maps.append(m)
    nc = bass.Bass("TRN2", target_bir_lowering=False)
    nc, out_name = build(nc)
    res = run_bass_kernel_spmd(nc, in_maps, core_ids=list(range(8)))
    out = np.stack([np.asarray(r["out"], np.float32) for r in res.results], axis=0)
    return out


# version 19
# speedup vs baseline: 1.0070x; 1.0070x over previous
import contextlib
import numpy as np
import ml_dtypes
import concourse.bass as bass
import concourse.mybir as mybir
from concourse.bass_utils import run_bass_kernel_spmd

F32 = mybir.dt.float32
F32R = mybir.dt.float32r
BF16 = mybir.dt.bfloat16
BF = ml_dtypes.bfloat16
AF = mybir.ActivationFunctionType
ALU = mybir.AluOpType
AXL = mybir.AxisListType

SKIP = set()
NTILES = None
D = 1024
NCH = 8
H = 16
HD = 64
T = 4096
TT = 256
NT = T // TT
CB = 128
NHID = 2730
NPD = 2816
NPC = 22
EPS1 = 1e-5
EPSR = 1e-6
INVD = 1.0 / D

R_K2, R_K3, R_K4, R_ONE, R_K1, R_R, R_NMR, R_PAD = range(8)


def _gamma():
    return 1.0 - 2.0 ** (-5.0 - np.arange(H, dtype=np.float64))


def host_consts():
    gam = _gamma()
    n = np.arange(CB, dtype=np.float64)
    DT = np.zeros((128, H * 128), np.float32)
    for h in range(H):
        j, hh = h // 2, h % 2
        DT[:, (hh * 8 + j) * 128:(hh * 8 + j + 1) * 128] = np.tril(
            gam[h] ** (n[:, None] - n[None, :])).T.astype(np.float32)
    GCOLW = np.zeros((128, 1024), np.float32)
    for h in range(H):
        GCOLW[:, h * 64:(h + 1) * 64] = (
            gam[h] ** (127.0 - n)).astype(np.float32)[:, None]
    GROW = np.zeros((128, 8 * 128), np.float32)
    for j in range(8):
        GROW[0:64, j * 128:(j + 1) * 128] = (gam[2 * j] ** (n + 1)).astype(np.float32)[None, :]
        GROW[64:128, j * 128:(j + 1) * 128] = (gam[2 * j + 1] ** (n + 1)).astype(np.float32)[None, :]
    GCOL = np.zeros((128, H), np.float32)
    for h in range(H):
        GCOL[:, h] = (gam[h] ** (127.0 - n)).astype(np.float32)
    GBLK = np.zeros((128, 8), np.float32)
    for j in range(8):
        GBLK[0:64, j] = np.float32(gam[2 * j] ** 128)
        GBLK[64:128, j] = np.float32(gam[2 * j + 1] ** 128)
    sel = np.zeros((8, 8 * 8), np.float32)
    for k in range(8):
        sel[k, 8 * k:8 * k + 8] = 1.0
    selw = np.zeros((8, 8 * 128), np.float32)
    for k in range(8):
        selw[k, 128 * k:128 * (k + 1)] = 1.0
    e3 = np.zeros((1, 4), np.float32)
    e3[0, 3] = 1.0
    return dict(DTc=DT, GROWc=GROW, GCOLc=GCOL, GCOLWc=GCOLW, GBLKc=GBLK,
                identc=np.eye(128, dtype=np.float32), selc=sel, selwc=selw,
                e3c=e3, onesec=np.ones((8, 128), np.float32))


def fmv(v):
    return np.ascontiguousarray(np.asarray(v, np.float32).reshape(NCH, 128).T)


def permute_w_in(w_in, b_in):
    wu, wg = w_in[:, :NHID], w_in[:, NHID:]
    bu, bg = b_in[:NHID], b_in[NHID:]
    wp = np.zeros((D, 2 * NPD), np.float32)
    bp = np.zeros((1, 2 * NPD), np.float32)
    for j in range(NPC):
        lo, hi = j * 128, min((j + 1) * 128, NHID)
        wd = hi - lo
        if wd > 0:
            wp[:, 256 * j:256 * j + wd] = wu[:, lo:hi]
            wp[:, 256 * j + 128:256 * j + 128 + wd] = wg[:, lo:hi]
            bp[0, 256 * j:256 * j + wd] = bu[lo:hi]
            bp[0, 256 * j + 128:256 * j + 128 + wd] = bg[lo:hi]
    return wp, bp


class Sem:
    __slots__ = ("h", "c", "name")

    def __init__(self, h, name):
        self.h, self.c, self.name = h, 0, name


class Ctx:
    ENGS = ("sync", "gpsimd", "vector", "scalar", "tensor")

    def __init__(self, nc):
        self.nc = nc
        self.sems = []
        self.auto_tensor = ()
        self.self_sems = {}
        self.section = None
        self.res = {}
        self.bankmap = {}

    def sem(self, name):
        s = Sem(self.nc.alloc_semaphore(name=name), name)
        self.sems.append(s)
        return s

    def walk(self, blk, fn):
        finals = []
        start = {id(s): s.c for s in self.sems}
        for en in self.ENGS:
            self.eng_name = en
            self._lastw = {}

            def body(e, fn=fn):
                for s in self.sems:
                    s.c = start[id(s)]
                self.eng = e
                fn(self)

            getattr(blk, en)(body)
            finals.append({s.name: s.c for s in self.sems})
        for f in finals[1:]:
            assert f == finals[0], f"sem divergence: {finals[0]} vs {f}"

    def op(self, engine, emit, waits=(), incs=(), reads=(), writes=(), mark=None):
        if self.section is not None and self.section in SKIP:
            return
        if engine == "tensor":
            waits = list(waits) + [(s, s.c) for s in self.auto_tensor]
        is_dma = any(nn >= 16 for (_, nn) in incs)
        if is_dma:
            waits = list(waits) + [(s, s.c) for (s, _) in incs]
        ss = self.self_sems.get(engine)
        if ss is not None and not is_dma:
            waits = list(waits) + [(ss, ss.c)]
            if not any(s is ss for (s, _) in incs):
                incs = list(incs) + [(ss, 1)]
        waits = list(waits) + self.dep_waits(reads, writes)
        if engine == self.eng_name:
            for (s, v) in waits:
                if v > 0 and self._lastw.get(id(s), -1) < v:
                    self.eng.wait_ge(s.h, v)
                    self._lastw[id(s)] = v
            ins = emit(self.eng)
            for (s, nn) in incs:
                ins = ins.then_inc(s.h, nn)
        for (s, nn) in incs:
            s.c += nn
        if reads or writes:
            if mark is None:
                mark = ss if ss is not None else (incs[-1][0] if incs else None)
            self.register(reads, writes, mark)

    def _expand(self, reads, writes):
        bm = self.bankmap
        if not bm:
            return reads, writes
        rb = {bm[k] for k in reads if k in bm}
        wb = {bm[k] for k in writes if k in bm}
        return (tuple(reads) + tuple(sorted(rb - wb)),
                tuple(writes) + tuple(sorted(wb)))

    def dep_waits(self, reads, writes):
        reads, writes = self._expand(reads, writes)
        ws = []
        for k in reads:
            e = self.res.get(k)
            if e is not None and e[0] is not None:
                ws.append(e[0])
        for k in writes:
            e = self.res.get(k)
            if e is not None:
                if e[0] is not None:
                    ws.append(e[0])
                ws.extend(e[1].values())
        return ws

    def register(self, reads, writes, mark):
        assert mark is not None, "tracked op needs a completion sem"
        reads, writes = self._expand(reads, writes)
        mk = (mark, mark.c)
        for k in writes:
            self.res[k] = [mk, {}]
        for k in reads:
            e = self.res.setdefault(k, [None, {}])
            e[1][id(mark)] = mk

    def w(self, s):
        return (s, s.c)


def build(nc, phases=4):
    ctx = Ctx(nc)

    def din(name, shape, dt=F32):
        return nc.dram_tensor(name, list(shape), dt, kind="ExternalInput").ap()

    x_h = din("x", [T, D])
    wmod_h = din("wmod", [D, 6 * D])
    c_h = din("cvec", [128, 8])
    W_h = {k: din(k, [D, D]) for k in ("wq", "wk", "wv", "wg", "wo")}
    winp_h = din("winp", [D, 2 * NPD])
    binp_h = din("binp", [1, 2 * NPD])
    woutp_h = din("woutp", [NPD, D])
    fWq_h = din("fWq", [D, D])
    fWdown_h = din("fWdown", [D, HD])
    fWup_h = din("fWup", [HD, 2 * D])
    fWo_h = din("fWo", [D, D])
    fWg_h = din("fWg", [D, 8])
    fWe_h = din("fWe", [8, D, D])
    FMV = ("w1", "b1", "w2", "bq", "bk", "bo", "b_out", "gamma1", "gamma2", "f_bo")
    fmv_h = {k: din("fm_" + k, [128, NCH]) for k in FMV}
    b2bf_h = din("b2bf", [128, NCH])
    fbdown_h = din("fm_f_bdown", [HD, 1])
    fbg_h = din("fm_f_bg", [8, 1])
    row_h = {"bv": din("row_bv", [1, D]), "bg_": din("row_bg", [1, D]),
             "f_bq": din("row_fbq", [1, D]), "bup": din("row_bup", [1, 2 * D]),
             "bq": din("row_bq", [1, D]), "bk": din("row_bk", [1, D])}
    gnwrep_h = din("gnw_rep", [128, D])
    gnbrep_h = din("gnb_rep", [128, D])
    CN = host_consts()
    cons_h = {k: din(k, v.shape) for k, v in CN.items()}
    out_h = nc.dram_tensor("out", [T, D], F32, kind="ExternalOutput").ap()

    def dsc(name, shape, dt=F32):
        return nc.dram_tensor(name, list(shape), dt, kind="Internal").ap()

    xfm_s = dsc("xfm_s", [NT, 128, NCH * TT])
    xb1_s = dsc("xb1_s", [NT, 128, NCH * TT])
    x1_s = dsc("x1_s", [NT, 128, NCH * TT])
    xf1_s = dsc("xf1_s", [NT, 128, NCH * TT])
    woutbf_s = dsc("woutbf_s", [NCH, 128, NPC * 128], BF16)
    webf_s = dsc("webf_s", [8, 128, NCH * D], BF16)

    S = {}
    for ph in ("s", "1", "2", "3", "4"):
        for nm in ("pe", "v", "a", "g", "dg", "dh", "ds"):
            S[nm + ph] = ctx.sem(nm + ph)
    S["dgw"] = ctx.sem("dgw")
    S["dhw"] = ctx.sem("dhw")
    S["dhw2"] = ctx.sem("dhw2")
    S["dhw3"] = ctx.sem("dhw3")
    S["dgp"] = ctx.sem("dgp")
    S["dwg"] = ctx.sem("dwg")
    S["dg4b"] = ctx.sem("dg4b")
    S["dsA1"] = ctx.sem("dsA1")
    S["dgb2"] = ctx.sem("dgb2")

    big = contextlib.ExitStack()

    def sb(name, shape, dt, stack):
        return stack.enter_context(nc.sbuf_tensor(name, list(shape), dt))

    P = {}
    P["DTc"] = sb("sb_DTc", [128, H * 128], BF16, big)
    P["GROWc"] = sb("sb_GROWc", [128, 8 * 128], BF16, big)
    P["GCOLW"] = sb("sb_GCOLW", [128, 1024], BF16, big)
    P["GBLKc"] = sb("sb_GBLKc", [128, 8], F32, big)
    P["identf"] = sb("identf", [128, 128], F32, big)
    P["identb"] = sb("identb", [128, 128], BF16, big)
    P["sel"] = sb("sel", [8, 64], BF16, big)
    P["onese"] = sb("onese", [8, 128], BF16, big)
    P["selw"] = sb("selw", [8, 8 * 128], BF16, big)
    P["e3"] = sb("e3", [1, 4], BF16, big)
    P["ones128"] = sb("ones128", [128, 128], F32, big)
    P["onesb"] = sb("onesb", [1, 256], BF16, big)
    P["gnwrep"] = sb("gnwrep", [128, D], BF16, big)
    P["gnbrep"] = sb("gnbrep", [128, D], BF16, big)
    for k in list(FMV) + ["A", "sa", "B", "sb_", "ga", "gb", "w1A", "cq", "ck", "bq8"]:
        P["fm_" + k] = sb("v_" + k, [128, NCH], F32, big)
    P["fm_fbdown"] = sb("v_fbdown", [HD, 1], F32, big)
    P["fm_fbg"] = sb("v_fbg", [8, 1], F32, big)
    P["cuv"] = sb("v_cuv", [128, 2 * NPC], F32, big)
    P["mod"] = sb("v_mod", [128, 48], F32, big)
    P["scal"] = sb("v_scal", [128, 16], F32, big)
    P["nscal"] = sb("v_nscal", [128, 16], F32, big)
    for k in ("Arow", "sarow", "Brow", "sbrow"):
        P[k] = sb("r_" + k, [1, NCH * 128], BF16, big)
    for k in ("corr_q", "corr_k", "corr_v", "corr_g"):
        P[k] = sb(k, [128, D], BF16, big)
    P["G"] = sb("v_G", [128, 8 * HD], F32, big)
    P["Gb"] = sb("v_Gb", [128, 8 * HD], BF16, big)
    P["vecs4"] = sb("v_vecs4", [128, NCH * 4], BF16, big)
    P["svA"] = sb("v_svA", [128, NCH * 4], BF16, big)
    P["svA2"] = sb("v_svA2", [128, NCH * 2], BF16, big)
    P["svB"] = sb("v_svB", [128, NCH * 4], BF16, big)
    P["svB2"] = sb("v_svB2", [128, NCH * 2], BF16, big)
    P["sv1"] = sb("v_sv1", [128, NCH * 2], BF16, big)

    pb = [big.enter_context(nc.psum_tensor(f"pb{i}", [128, 512], F32)) for i in range(8)]

    fm = lambda k: P["fm_" + k]
    SC = {k: i for i, k in enumerate(
        ("cA", "cs", "cA2", "cAs", "cs2", "cB", "csb", "cB2", "cBs", "csb2"))}

    def scol(name, neg=False):
        return (P["nscal"] if neg else P["scal"])[:, SC[name]:SC[name] + 1]

    def rowmath(c, sv, sa, spe, statq, statq2, tmT, tmS, side, rms_rho):
        op, w = c.op, c.w
        V = lambda emit, ww=(): op("vector", emit, waits=list(ww), incs=[(sv, 1)])
        A_ = lambda emit, ww=(): op("scalar", emit, waits=list(ww), incs=[(sa, 1)])
        cA, cs, cA2, cAs, cs2 = (("cA", "cs", "cA2", "cAs", "cs2") if side == "A"
                                 else ("cB", "csb", "cB2", "cBs", "csb2"))
        tm = tmT
        V(lambda e: e.tensor_scalar_mul(out=tm[:, 0:1], in0=statq[:, 0:1], scalar1=INVD),
          ww=[w(spe)])
        V(lambda e: e.tensor_scalar_mul(out=tm[:, 4:5], in0=statq2[:, 0:1], scalar1=INVD))
        V(lambda e: e.tensor_mul(out=tm[:, 5:6], in0=tm[:, 0:1], in1=tm[:, 0:1]))
        V(lambda e: e.tensor_sub(out=tm[:, 6:7], in0=tm[:, 4:5], in1=tm[:, 5:6]))
        A_(lambda e: e.activation(out=tm[:, 7:8], in_=tm[:, 6:7], func=AF.Ln, bias=np.float32(EPS1)),
           ww=[w(sv)])
        A_(lambda e: e.activation(out=tm[:, 1:2], in_=tm[:, 7:8], func=AF.Exp, scale=np.float32(-0.5)))
        V(lambda e: e.scalar_tensor_tensor(out=tm[:, 8:9], in0=tm[:, 0:1],
          scalar=scol(cA, True), in1=statq[:, 1:2], op0=ALU.mult, op1=ALU.add), ww=[w(sa)])
        V(lambda e: e.tensor_mul(out=tm[:, 8:9], in0=tm[:, 1:2], in1=tm[:, 8:9]))
        V(lambda e: e.tensor_scalar(out=tm[:, 3:4], in0=tm[:, 8:9], scalar1=1.0,
          scalar2=scol(cs), op0=ALU.mult, op1=ALU.add))
        V(lambda e: e.tensor_scalar_mul(out=tm[:, 3:4], in0=tm[:, 3:4], scalar1=INVD))
        V(lambda e: e.scalar_tensor_tensor(out=tm[:, 8:9], in0=tm[:, 5:6], scalar=scol(cA2),
          in1=statq2[:, 1:2], op0=ALU.mult, op1=ALU.add))
        V(lambda e: e.scalar_tensor_tensor(out=tm[:, 9:10], in0=tm[:, 0:1], scalar=-2.0,
          in1=statq[:, 2:3], op0=ALU.mult, op1=ALU.mult))
        V(lambda e: e.tensor_add(out=tm[:, 8:9], in0=tm[:, 8:9], in1=tm[:, 9:10]))
        V(lambda e: e.tensor_mul(out=tm[:, 9:10], in0=tm[:, 1:2], in1=tm[:, 1:2]))
        V(lambda e: e.tensor_mul(out=tm[:, 8:9], in0=tm[:, 8:9], in1=tm[:, 9:10]))
        V(lambda e: e.scalar_tensor_tensor(out=tm[:, 9:10], in0=tm[:, 0:1],
          scalar=scol(cAs, True), in1=statq[:, 3:4], op0=ALU.mult, op1=ALU.add))
        V(lambda e: e.tensor_mul(out=tm[:, 9:10], in0=tm[:, 1:2], in1=tm[:, 9:10]))
        V(lambda e: e.scalar_tensor_tensor(out=tm[:, 8:9], in0=tm[:, 9:10], scalar=2.0,
          in1=tm[:, 8:9], op0=ALU.mult, op1=ALU.add))
        V(lambda e: e.tensor_scalar(out=tm[:, 8:9], in0=tm[:, 8:9], scalar1=1.0,
          scalar2=scol(cs2), op0=ALU.mult, op1=ALU.add))
        if rms_rho:
            V(lambda e: e.tensor_scalar_mul(out=tm[:, 9:10], in0=tm[:, 8:9], scalar1=INVD))
            A_(lambda e: e.activation(out=tm[:, 7:8], in_=tm[:, 9:10], func=AF.Ln, bias=np.float32(EPSR)),
               ww=[w(sv)])
        else:
            V(lambda e: e.tensor_mul(out=tm[:, 9:10], in0=tm[:, 3:4], in1=tm[:, 3:4]))
            V(lambda e: e.scalar_tensor_tensor(out=tm[:, 9:10], in0=tm[:, 8:9], scalar=INVD,
              in1=tm[:, 9:10], op0=ALU.mult, op1=ALU.subtract))
            A_(lambda e: e.activation(out=tm[:, 7:8], in_=tm[:, 9:10], func=AF.Ln, bias=np.float32(EPS1)),
               ww=[w(sv)])
        A_(lambda e: e.activation(out=tm[:, 2:3], in_=tm[:, 7:8], func=AF.Exp, scale=np.float32(-0.5)))
        V(lambda e: e.tensor_mul(out=tmS[:, R_K1:R_K1 + 1], in0=tm[:, 2:3], in1=tm[:, 1:2]),
          ww=[w(sa)])
        V(lambda e: e.scalar_tensor_tensor(out=tmS[:, R_K2:R_K2 + 1], in0=tmS[:, R_K1:R_K1 + 1],
          scalar=-1.0, in1=tm[:, 0:1], op0=ALU.mult, op1=ALU.mult))
        V(lambda e: e.tensor_copy(out=tmS[:, R_K3:R_K3 + 1], in_=tm[:, 2:3]))
        V(lambda e: e.scalar_tensor_tensor(out=tmS[:, R_K4:R_K4 + 1], in0=tm[:, 2:3],
          scalar=-1.0, in1=tm[:, 3:4], op0=ALU.mult, op1=ALU.mult))
        V(lambda e: e.tensor_copy(out=tmS[:, R_R:R_R + 1], in_=tm[:, 1:2]))
        V(lambda e: e.scalar_tensor_tensor(out=tmS[:, R_NMR:R_NMR + 1], in0=tm[:, 1:2],
          scalar=-1.0, in1=tm[:, 0:1], op0=ALU.mult, op1=ALU.mult))

    def coeff_rows(c, sv, sa, spe, tmS, rows, reps, b, repkinds):
        op, w = c.op, c.w
        op("tensor", lambda e: e.transpose(pb[4][0:8, 384:512], tmS[:], P["identf"][:]),
           waits=[c.w(sv)], incs=[(spe, 1)])
        op("scalar", lambda e, b=b: e.activation(out=rows[0:8, 128 * b:128 * (b + 1)],
           in_=pb[4][0:8, 384:512], func=AF.Copy), waits=[w(spe)], incs=[(sa, 1)])
        for ri, rk in enumerate(repkinds):
            op("tensor", lambda e, b=b, ri=ri, rk=rk: e.matmul(
                pb[4][0:8, 128 * ri:128 * (ri + 1)], P["sel"][:, 8 * rk:8 * rk + 8],
                rows[0:8, 128 * b:128 * (b + 1)], start=True, stop=True),
                waits=[w(sa)], incs=[(spe, 1)])
        nrep = len(repkinds)
        ncols = reps.shape[1] // len(repkinds) if False else None
        op("scalar", lambda e, b=b, nrep=nrep: e.activation(
            out=reps[:].rearrange("p (k t) -> p k t", k=nrep)[:, :, 128 * b:128 * (b + 1)],
            in_=pb[4][0:8, 0:128 * nrep].rearrange("p (k t) -> p k t", t=128),
            func=AF.Copy), waits=[w(spe)], incs=[(sa, 1)])

    wst = contextlib.ExitStack()
    st = contextlib.ExitStack()
    wsb = {k: sb("w_" + k, [128, NCH * D], BF16, wst) for k in ("wq", "wk", "wv", "wg", "wo")}
    c_sb = sb("c_sb", [128, 8], F32, st)
    siluc = sb("siluc", [128, 8], F32, st)
    wstrip = sb("wstrip", [128, NCH * 512], F32, st)
    wstrip2 = sb("wstrip2", [128, NCH * 512], F32, st)
    sgn = sb("sgn", [128, 8], F32, st)
    absv = sb("absv", [128, 8], F32, st)
    scal_in = sb("scal_in", [128, 16], F32, st)
    prod8 = sb("prod8", [128, 8], F32, st)
    brow = sb("brow", [1, 2 * D], F32, st)
    browb = sb("browb", [1, 2 * D], BF16, st)
    P["corr_fbq"] = sb("corr_fbq", [128, D], BF16, st)
    P["corr_bup"] = sb("corr_bup", [128, 2 * D], BF16, st)
    corr3_s = dsc("corr3_s", [128, 3 * D], BF16)
    stage = sb("stage", [128, NCH * D], BF16, st)

    def setup(c):
        op, w = c.op, c.w
        sdg, sdh, sds = S["dgs"], S["dhs"], S["dss"]
        spe, sv, sa = S["pes"], S["vs"], S["as"]
        c.auto_tensor = (sv, sa)
        c.self_sems = {"vector": sv, "scalar": sa, "gpsimd": S["gs"]}
        c.bankmap = {}
        HW = lambda emit: op("sync", emit, incs=[(sdh, 16)])
        HW(lambda e: e.dma_start(out=P["GBLKc"][:], in_=cons_h["GBLKc"]))
        HW(lambda e: e.dma_start(out=P["identf"][:], in_=cons_h["identc"]))
        for k in FMV:
            HW(lambda e, k=k: e.dma_start(out=fm(k)[:], in_=fmv_h[k]))
        HW(lambda e: e.dma_start(out=P["fm_fbdown"][:], in_=fbdown_h))
        HW(lambda e: e.dma_start(out=P["fm_fbg"][:], in_=fbg_h))
        HW(lambda e: e.dma_start(out=c_sb[:], in_=c_h))
        op("vector", lambda e: e.memset(P["ones128"][:], 1.0), incs=[(sv, 1)])
        op("vector", lambda e: e.memset(P["onesb"][:], 1.0), incs=[(sv, 1)])
        GD = lambda emit: op("gpsimd", emit, incs=[(sdg, 16)])
        GD(lambda e: e.dma_start(out=P["DTc"][:], in_=cons_h["DTc"]))
        GD(lambda e: e.dma_start(out=P["GROWc"][:], in_=cons_h["GROWc"]))
        GD(lambda e: e.dma_start(out=P["GCOLW"][:], in_=cons_h["GCOLWc"]))
        GD(lambda e: e.dma_start(out=P["identb"][:], in_=cons_h["identc"]))
        GD(lambda e: e.dma_start(out=P["sel"][:], in_=cons_h["selc"]))
        GD(lambda e: e.dma_start(out=P["onese"][:], in_=cons_h["onesec"]))
        GD(lambda e: e.dma_start(out=P["selw"][:], in_=cons_h["selwc"]))
        GD(lambda e: e.dma_start(out=P["e3"][:], in_=cons_h["e3c"]))
        GD(lambda e: e.dma_start(out=P["gnwrep"][:], in_=gnwrep_h))
        GD(lambda e: e.dma_start(out=P["gnbrep"][:], in_=gnbrep_h))
        for k in ("wq", "wk", "wv", "wg", "wo"):
            GD(lambda e, k=k: e.dma_start(
                out=wsb[k][:].rearrange("p (c n) -> p c n", n=D), in_=W_h[k].rearrange("(c p) n -> p c n", p=128)))
        cdl, gdl = w(sdh), w(sdg)

        op("scalar", lambda e: e.activation(out=siluc[:], in_=c_sb[:], func=AF.Silu),
           waits=[cdl], incs=[(sa, 1)])
        strip_done = {}

        def strip_load(s_):
            ws_ = wstrip if s_ % 2 == 0 else wstrip2
            wsv_ = ws_[:].rearrange("p (c n) -> p c n", n=512)
            src_ = wmod_h[:, 512 * s_:512 * (s_ + 1)].rearrange(
                "(c p) n -> p c n", p=128)
            wmm = [(spe, spe.c)] if s_ >= 2 else []
            op("sync", lambda e, wsv_=wsv_, src_=src_: e.dma_start(
                out=wsv_[:, 0:3, :], in_=src_[:, 0:3, :]),
                waits=list(wmm), incs=[(S["dhw"], 16)])
            op("scalar", lambda e, wsv_=wsv_, src_=src_: e.dma_start(
                out=wsv_[:, 3:6, :], in_=src_[:, 3:6, :]),
                waits=list(wmm), incs=[(S["dhw2"], 16)])
            op("gpsimd", lambda e, wsv_=wsv_, src_=src_: e.dma_start(
                out=wsv_[:, 6:8, :], in_=src_[:, 6:8, :]),
                waits=list(wmm), incs=[(S["dhw3"], 16)])
            strip_done[s_] = [(S["dhw"], S["dhw"].c), (S["dhw2"], S["dhw2"].c),
                              (S["dhw3"], S["dhw3"].c)]

        strip_load(0)
        strip_load(1)
        for s_ in range(12):
            ws_ = wstrip if s_ % 2 == 0 else wstrip2
            for n4 in range(4):
                for k in range(NCH):
                    op("tensor", lambda e, s=s_, n4=n4, k=k, ws_=ws_: e.matmul(
                        pb[0][:, 4 * s + n4:4 * s + n4 + 1],
                        ws_[:, 512 * k + 128 * n4:512 * k + 128 * (n4 + 1)],
                        siluc[:, k:k + 1], start=(k == 0), stop=(k == NCH - 1)),
                        waits=(strip_done[s_] + [w(sa)]
                               if (n4 == 0 and k == 0) else []),
                        incs=[(spe, 1)] if (k == NCH - 1) else [])
            if s_ + 2 < 12:
                strip_load(s_ + 2)
        op("vector", lambda e: e.tensor_copy(out=P["mod"][:, 0:48], in_=pb[0][:, 0:48]),
           waits=[w(spe)], incs=[(sv, 1)])
        md = P["mod"]

        V = lambda emit, ww=(): op("vector", emit, waits=list(ww), incs=[(sv, 1)])
        A_ = lambda emit, ww=(): op("scalar", emit, waits=list(ww), incs=[(sa, 1)])

        def clamp(dst, lo):
            V(lambda e: e.tensor_scalar(out=sgn[:], in0=md[:, lo:lo + 8], scalar1=0.0,
              scalar2=None, op0=ALU.is_ge))
            V(lambda e: e.tensor_scalar(out=sgn[:], in0=sgn[:], scalar1=2.0, scalar2=-1.0,
              op0=ALU.mult, op1=ALU.add))
            A_(lambda e: e.activation(out=absv[:], in_=md[:, lo:lo + 8], func=AF.Abs),
               ww=[w(sv)])
            V(lambda e: e.tensor_scalar_max(out=absv[:], in0=absv[:], scalar1=0.25),
              ww=[w(sa)])
            V(lambda e: e.tensor_mul(out=dst[:], in0=sgn[:], in1=absv[:]))
        clamp(fm("ga"), 16)
        clamp(fm("gb"), 40)
        V(lambda e: e.tensor_copy(out=fm("sa")[:], in_=md[:, 0:8]))
        V(lambda e: e.tensor_copy(out=fm("sb_")[:], in_=md[:, 24:32]))
        V(lambda e: e.tensor_scalar_add(out=fm("A")[:], in0=md[:, 8:16], scalar1=1.0))
        V(lambda e: e.tensor_scalar_add(out=fm("B")[:], in0=md[:, 32:40], scalar1=1.0))
        V(lambda e: e.tensor_mul(out=fm("w1A")[:], in0=fm("A")[:], in1=fm("w1")[:]))

        svA = P["svA"][:].rearrange("p (c k) -> p c k", k=4)
        svA2 = P["svA2"][:].rearrange("p (c k) -> p c k", k=2)
        svB = P["svB"][:].rearrange("p (c k) -> p c k", k=4)
        svB2 = P["svB2"][:].rearrange("p (c k) -> p c k", k=2)
        V(lambda e: e.memset(P["sv1"][:], 1.0))
        for t3 in (svA, svA2, svB, svB2):
            V(lambda e, t3=t3: e.memset(t3[:, :, 0:1], 1.0))
        V(lambda e: e.tensor_copy(out=svA[:, :, 1:2], in_=fm("A")[:].unsqueeze(2)))
        V(lambda e: e.tensor_mul(out=svA[:, :, 2:3], in0=fm("A")[:].unsqueeze(2),
          in1=fm("A")[:].unsqueeze(2)))
        V(lambda e: e.tensor_mul(out=svA[:, :, 3:4], in0=fm("A")[:].unsqueeze(2),
          in1=fm("sa")[:].unsqueeze(2)))
        V(lambda e: e.tensor_copy(out=svA2[:, :, 1:2], in_=svA[:, :, 2:3]))
        V(lambda e: e.tensor_copy(out=svB[:, :, 1:2], in_=fm("B")[:].unsqueeze(2)))
        V(lambda e: e.tensor_mul(out=svB[:, :, 2:3], in0=fm("B")[:].unsqueeze(2),
          in1=fm("B")[:].unsqueeze(2)))
        V(lambda e: e.tensor_mul(out=svB[:, :, 3:4], in0=fm("B")[:].unsqueeze(2),
          in1=fm("sb_")[:].unsqueeze(2)))
        V(lambda e: e.tensor_copy(out=svB2[:, :, 1:2], in_=svB[:, :, 2:3]))

        defs = [("A", None), ("sa", None), ("A", "A"), ("A", "sa"), ("sa", "sa"),
                ("B", None), ("sb_", None), ("B", "B"), ("B", "sb_"), ("sb_", "sb_")]
        for i, (a, b) in enumerate(defs):
            if b is None:
                V(lambda e, i=i, a=a: e.reduce_sum(out=scal_in[:, i:i + 1], in_=fm(a)[:],
                  axis=AXL.X))
            else:
                V(lambda e, a=a, b=b: e.tensor_mul(out=prod8[:], in0=fm(a)[:], in1=fm(b)[:]))
                V(lambda e, i=i: e.reduce_sum(out=scal_in[:, i:i + 1], in_=prod8[:], axis=AXL.X))
        V(lambda e: e.memset(scal_in[:, 10:16], 0.0))
        op("tensor", lambda e: e.matmul(pb[1][:, 0:16], P["ones128"][:], scal_in[:],
           start=True, stop=True), waits=[w(sv)], incs=[(spe, 1)])
        V(lambda e: e.tensor_copy(out=P["scal"][:], in_=pb[1][:, 0:16]), ww=[w(spe)])
        V(lambda e: e.tensor_scalar_mul(out=P["nscal"][:], in0=P["scal"][:], scalar1=-1.0))

        for nm, srcv in (("Arow", "A"), ("sarow", "sa"), ("Brow", "B"), ("sbrow", "sb_")):
            for hh in range(2):
                for k4 in range(4):
                    kc = 4 * hh + k4
                    op("tensor", lambda e, s=srcv, kc=kc, k4=k4: e.transpose(
                        pb[2][0:1, 128 * k4:128 * (k4 + 1)], fm(s)[:, kc:kc + 1],
                        P["identf"][:]), waits=[w(sv), w(sa)],
                        incs=[(spe, 1)] if k4 == 3 else [])
                A_(lambda e, nm=nm, hh=hh: e.activation(
                    out=P[nm][0:1, 512 * hh:512 * (hh + 1)], in_=pb[2][0:1, 0:512],
                    func=AF.Copy), ww=[w(spe)])

        v4 = P["vecs4"][:].rearrange("p (c k) -> p c k", k=4)
        V(lambda e: e.tensor_copy(out=v4[:, :, 0:1], in_=fm("w1A")[:].unsqueeze(2)))
        V(lambda e: e.tensor_mul(out=v4[:, :, 1:2], in0=fm("sa")[:].unsqueeze(2),
          in1=fm("w1")[:].unsqueeze(2)))
        V(lambda e: e.tensor_copy(out=v4[:, :, 2:3], in_=fm("w1")[:].unsqueeze(2)))
        V(lambda e: e.tensor_copy(out=v4[:, :, 3:4], in_=fm("b1")[:].unsqueeze(2)))
        for k in ("corr_q", "corr_k", "corr_v", "corr_g", "corr_fbq", "corr_bup"):
            V(lambda e, k=k: e.memset(P[k][:], 0.0))

        for kname in ("wq", "wk", "wv", "wg"):
            short = {"wq": "q", "wk": "k", "wv": "v", "wg": "g"}[kname]
            brn = {"wq": "bq", "wk": "bk", "wv": "bv", "wg": "bg_"}[kname]
            wv_ = wsb[kname][:].rearrange("p (c n) -> p c n", n=D)
            op("sync", lambda e, brn=brn: e.dma_start(out=brow[0:1, 0:D], in_=row_h[brn]),
               waits=[w(spe)], incs=[(sdh, 16)])
            V(lambda e: e.tensor_copy(out=browb[0:1, 0:D], in_=brow[0:1, 0:D]),
              ww=[w(sdh)])
            for half in range(2):
                for kc in range(NCH):
                    op("tensor", lambda e, wv_=wv_, h=half, kc=kc: e.matmul(
                        pb[3][0:4, 0:512], P["vecs4"][:, 4 * kc:4 * kc + 4],
                        wv_[:, kc, 512 * h:512 * (h + 1)],
                        start=(kc == 0), stop=False),
                        waits=[gdl, w(sv), w(sa)], incs=[])
                op("tensor", lambda e, h=half: e.matmul(
                    pb[3][0:4, 0:512], P["e3"][:],
                    browb[0:1, 512 * h:512 * (h + 1)],
                    start=False, stop=True), incs=[(spe, 1)])
                A_(lambda e, s=short, h=half: e.activation(
                    out=P["corr_" + s][0:4, 512 * h:512 * (h + 1)], in_=pb[3][0:4, 0:512],
                    func=AF.Copy), ww=[w(spe)])

        op("sync", lambda e: e.dma_start(out=brow[0:1, 0:D], in_=row_h["f_bq"]),
           waits=[w(spe)], incs=[(sdh, 16)])
        V(lambda e: e.tensor_copy(out=browb[0:1, 0:D], in_=brow[0:1, 0:D]), ww=[w(sdh)])
        for half in range(2):
            op("tensor", lambda e, h=half: e.matmul(
                pb[3][0:4, 0:512], P["e3"][:], browb[0:1, 512 * h:512 * (h + 1)],
                start=True, stop=True), incs=[(spe, 1)])
            A_(lambda e, h=half: e.activation(
                out=P["corr_fbq"][0:4, 512 * h:512 * (h + 1)], in_=pb[3][0:4, 0:512],
                func=AF.Copy), ww=[w(spe)])
        op("sync", lambda e: e.dma_start(out=brow[0:1, 0:2 * D], in_=row_h["bup"]),
           waits=[w(spe)], incs=[(sdh, 16)])
        V(lambda e: e.tensor_copy(out=browb[0:1, 0:2 * D], in_=brow[0:1, 0:2 * D]),
          ww=[w(sdh)])
        for q4 in range(4):
            op("tensor", lambda e, q4=q4: e.matmul(
                pb[3][0:4, 0:512], P["e3"][:], browb[0:1, 512 * q4:512 * (q4 + 1)],
                start=True, stop=True), incs=[(spe, 1)])
            A_(lambda e, q4=q4: e.activation(
                out=P["corr_bup"][0:4, 512 * q4:512 * (q4 + 1)], in_=pb[3][0:4, 0:512],
                func=AF.Copy), ww=[w(spe)])
        op("sync", lambda e: e.dma_start(out=corr3_s[:, 0:D], in_=P["corr_fbq"][:]),
           waits=[w(sa), w(sv)], incs=[(sds, 16)])
        op("sync", lambda e: e.dma_start(out=corr3_s[:, D:3 * D], in_=P["corr_bup"][:]),
           incs=[(sds, 16)])
        V(lambda e: e.memset(P["G"][:], 0.0))
        V(lambda e: e.memset(P["Gb"][:], 0.0))

        for n8 in range(NCH):
            op("gpsimd", lambda e, n=n8: e.dma_start(
                out=stage[:, 0:NPC * 128].rearrange("p (c n) -> p c n", n=128),
                in_=woutp_h[:, 128 * n:128 * (n + 1)].rearrange("(c p) n -> p c n", p=128)),
                waits=[w(sds)], incs=[(S["dgp"], 16)])
            op("sync", lambda e, n=n8: e.dma_start(out=woutbf_s[n], in_=stage[:, 0:NPC * 128]),
               waits=[(S["dgp"], S["dgp"].c)], incs=[(sds, 16)])
        for ee in range(8):
            op("gpsimd", lambda e, x=ee: e.dma_start(
                out=stage[:].rearrange("p (c n) -> p c n", n=D), in_=fWe_h[x].rearrange("(c p) n -> p c n", p=128)),
                waits=[w(sds)], incs=[(S["dgp"], 16)])
            op("sync", lambda e, x=ee: e.dma_start(out=webf_s[x], in_=stage[:]),
               waits=[(S["dgp"], S["dgp"].c)], incs=[(sds, 16)])
        op("sync", lambda e: e.nop(), waits=[w(sds), w(sdh), (S["dhw"], S["dhw"].c)])
        op("gpsimd", lambda e: e.nop(), waits=[w(sdg), (S["dgp"], S["dgp"].c)])
        op("vector", lambda e: e.nop(), waits=[w(sv)])
        op("scalar", lambda e: e.nop(), waits=[w(sa), (S["dhw2"], S["dhw2"].c)])
        op("gpsimd", lambda e: e.nop(), waits=[(S["dhw3"], S["dhw3"].c)])
        op("tensor", lambda e: e.nop(), waits=[w(spe)])

    with nc.Block() as blk:
        ctx.walk(blk, setup)
    st.close()

    p1s = contextlib.ExitStack()
    x_tm = sb("x_tm", [128, D], F32, p1s)
    x_fm = sb("x_fm", [128, NCH * TT], F32, p1s)
    x_sq = sb("x_sq", [128, NCH * TT], BF16, p1s)
    x_bf = sb("x_bf", [128, NCH * TT], BF16, p1s)
    h_fm = sb("h_fm", [128, NCH * TT], F32, p1s)
    h_fmB = sb("h_fmB", [128, NCH * TT], F32, p1s)
    xb1t = sb("xb1t", [128, NCH * TT], F32, p1s)
    xt_bf = sb("xt_bf", [128, NCH * TT], BF16, p1s)
    xt_bfB = sb("xt_bfB", [128, NCH * TT], BF16, p1s)
    qfm = sb("qfm", [128, NCH * TT], BF16, p1s)
    kfm = sb("kfm", [128, NCH * TT], BF16, p1s)
    ktm = sb("ktm", [128, 2 * D], BF16, p1s)
    vtm = sb("vtm", [128, 2 * D], BF16, p1s)
    gtm = sb("gtm", [128, 2 * D], BF16, p1s)
    rettm = sb("rettm", [128, 2 * D], BF16, p1s)
    qt_bf = sb("qt_bf", [128, 256], BF16, p1s)
    spt = sb("spt", [128, 512], BF16, p1s)
    intr = sb("intr", [128, 256], BF16, p1s)
    gn16 = sb("gn16", [128, 6 * 16], F32, p1s)
    wointm = sb("wointm", [128, 2 * D], BF16, p1s)
    woinfm = sb("woinfm", [128, NCH * TT], BF16, p1s)
    tmS1 = sb("tmS1", [128, 16], F32, p1s)
    tmT1 = sb("tmT1", [128, 32], F32, p1s)
    rows1 = sb("rows1", [128, TT], BF16, p1s)
    rows1B = sb("rows1B", [128, TT], BF16, p1s)
    reps1 = sb("reps1", [8, 3 * TT], BF16, p1s)
    reps1B = sb("reps1B", [8, 3 * TT], BF16, p1s)

    def p1(c):
        op, w = c.op, c.w
        spe, sv, sa = S["pe1"], S["v1"], S["a1"]
        sdh, sds = S["dh1"], S["ds1"]
        c.auto_tensor = ()
        c.self_sems = {"vector": sv, "scalar": sa, "gpsimd": S["g1"]}
        c.res = {}
        c.bankmap = {
            "t2a": "BK2", "t2b": "BK2", "t3b": "BK3", "s10": "BK5", "s10b": "BK5",
            "st4_0": "BK5", "st2_0": "BK5", "st4_1": "BK5", "st2_1": "BK5",
            "s9": "BK4", "s8": "BK4", "ssum": "BK4", "s6": "BK3", "s7": "BK3",
            "s12": "BK6", "s13": "BK7", "p0": "BK0", "p1": "BK1",
            "st0e": "BK7", "st1e": "BK6", "sx0e": "BK1", "sx1e": "BK0"}
        wsv = {k: wsb[k][:].rearrange("p (c n) -> p c n", n=D) for k in wsb}
        xfmv = x_fm[:].rearrange("p (c t) -> p c t", t=TT)
        xbfv = x_bf[:].rearrange("p (c t) -> p c t", t=TT)
        xsqv = x_sq[:].rearrange("p (c t) -> p c t", t=TT)
        hv_0 = h_fm[:].rearrange("p (c t) -> p c t", t=TT)
        hv_1 = h_fmB[:].rearrange("p (c t) -> p c t", t=TT)
        xb1v = xb1t[:].rearrange("p (c t) -> p c t", t=TT)
        xtv_0 = xt_bf[:].rearrange("p (c t) -> p c t", t=TT)
        xtv_1 = xt_bfB[:].rearrange("p (c t) -> p c t", t=TT)

        def pbuf(t):
            if t % 2 == 0:
                return hv_0, xtv_0, rows1, reps1, "h0", "xt0", "rw0", "rp0"
            return hv_1, xtv_1, rows1B, reps1B, "h1", "xt1", "rw1", "rp1"
        qv = qfm[:].rearrange("p (c t) -> p c t", t=TT)
        kv_ = kfm[:].rearrange("p (c t) -> p c t", t=TT)
        wfv = woinfm[:].rearrange("p (c t) -> p c t", t=TT)
        V = lambda emit, rd=(), wr=(), ww=(): op("vector", emit, waits=list(ww),
                                                incs=[(sv, 1)], reads=rd, writes=wr)
        A_ = lambda emit, rd=(), wr=(), ww=(): op("scalar", emit, waits=list(ww),
                                                 incs=[(sa, 1)], reads=rd, writes=wr)
        G1 = lambda emit, rd=(), wr=(), ww=(): op("gpsimd", emit, waits=list(ww),
                                                 incs=[(S["g1"], 1)], reads=rd,
                                                 writes=wr)

        def mmg(mms, rd=(), wr=(), extra=()):
            ws = c.dep_waits(rd, wr) + list(extra)
            for i, mm in enumerate(mms):
                op("tensor", mm, waits=ws if i == 0 else (),
                   incs=[(spe, 1)] if i == len(mms) - 1 else ())
            c.register(rd, wr, spe)

        op("vector", lambda e: e.memset(rows1[:], 0.0), incs=[(sv, 1)])
        op("vector", lambda e: e.memset(rows1B[:], 0.0), incs=[(sv, 1)])
        op("vector", lambda e: e.memset(tmS1[:], 0.0), incs=[(sv, 1)])
        op("vector", lambda e: e.memset(tmS1[:, R_ONE:R_ONE + 1], 1.0), incs=[(sv, 1)])
        op("vector", lambda e: e.memset(tmS1[:, 8 + R_ONE:8 + R_ONE + 1], 1.0),
           incs=[(sv, 1)])

        def head1(t):
            hv, xtv, rows1c, reps1c, hk, xtk, rwk, rpk = pbuf(t)
            for b in range(2):
                xk = "x_tm" if b == 0 else "xsq"
                xbuf = x_tm[:] if b == 0 else x_sq[:].bitcast(F32)
                op("sync", lambda e, t=t, b=b, xbuf=xbuf: e.dma_start(
                    out=xbuf,
                    in_=x_h[TT * t + 128 * b:TT * t + 128 * (b + 1), :]),
                    waits=c.dep_waits((), (xk,)), incs=[(sdh, 16)])
                c.register((), (xk,), sdh)
                for jp in range(4):
                    slot = "t2a" if jp % 2 == 0 else "t3b"
                    pbx = pb[2] if jp % 2 == 0 else pb[3]
                    mmg([(lambda e, j=2 * jp + q, pbx=pbx, q=q, xbuf=xbuf: e.transpose(
                            pbx[:, 128 * q:128 * (q + 1)],
                            xbuf[:, 128 * j:128 * (j + 1)], P["identf"][:]))
                         for q in range(2)],
                        rd=(xk,), wr=(slot,))
                    A_(lambda e, b=b, jp=jp, pbx=pbx: e.activation(
                        out=xfmv[:, 2 * jp:2 * jp + 2, 128 * b:128 * (b + 1)],
                        in_=pbx[:, 0:256].rearrange("p (c t) -> p c t", t=128),
                        func=AF.Copy), rd=(slot,), wr=("xfm",))
            A_(lambda e: e.activation(out=x_sq[:], in_=x_fm[:], func=AF.Square),
               rd=("xfm",), wr=("xsq",))
            V(lambda e: e.tensor_copy(out=x_bf[:], in_=x_fm[:]), rd=("xfm",), wr=("xbf",))
            op("sync", lambda e, t=t: e.dma_start(out=xfm_s[t], in_=x_fm[:]),
               waits=c.dep_waits(("xfm",), ()), incs=[(S["dsA1"], 16)])
            c.register(("xfm",), (), S["dsA1"])

            for b in range(2):
                so = 8 * b
                mmg([(lambda e, b=b, kc=kc, so=so: e.matmul(
                        pb[5][:, so:so + 4], xbfv[:, kc, 128 * b:128 * (b + 1)],
                        P["svA"][:, 4 * kc:4 * kc + 4],
                        start=(kc == 0), stop=(kc == NCH - 1))) for kc in range(NCH)],
                    rd=("xbf",), wr=("st4_%d" % b,))
                mmg([(lambda e, b=b, kc=kc, so=so: e.matmul(
                        pb[5][:, so + 4:so + 6], xsqv[:, kc, 128 * b:128 * (b + 1)],
                        P["svA2"][:, 2 * kc:2 * kc + 2],
                        start=(kc == 0), stop=(kc == NCH - 1))) for kc in range(NCH)],
                    rd=("xsq",), wr=("st2_%d" % b,))

        def headB_gen(t):
            hv, xtv, rows1c, reps1c, hk, xtk, rwk, rpk = pbuf(t)
            for b in range(2):
                so = 8 * b
                ks4 = "st4_%d" % b
                ks2 = "st2_%d" % b
                ktm_ = "tmT%d" % b
                ksm_ = "tmS%d" % b
                tm = tmT1[:, 16 * b:16 * (b + 1)]
                ts_ = tmS1[:, 8 * b:8 * (b + 1)]
                V(lambda e, so=so: e.tensor_scalar_mul(out=tm[:, 0:1],
                  in0=pb[5][:, so:so + 1],
                  scalar1=INVD), rd=(ks4,), wr=(ktm_,))
                yield
                V(lambda e, so=so: e.tensor_scalar_mul(out=tm[:, 4:5],
                  in0=pb[5][:, so + 4:so + 5],
                  scalar1=INVD), rd=(ks2,), wr=(ktm_,))
                yield
                V(lambda e: e.tensor_mul(out=tm[:, 5:6], in0=tm[:, 0:1], in1=tm[:, 0:1]),
                  rd=(ktm_,), wr=(ktm_,))
                yield
                V(lambda e: e.tensor_sub(out=tm[:, 6:7], in0=tm[:, 4:5], in1=tm[:, 5:6]),
                  rd=(ktm_,), wr=(ktm_,))
                yield
                A_(lambda e: e.activation(out=tm[:, 7:8], in_=tm[:, 6:7], func=AF.Ln,
                   bias=np.float32(EPS1)), rd=(ktm_,), wr=(ktm_,))
                yield
                A_(lambda e: e.activation(out=tm[:, 1:2], in_=tm[:, 7:8], func=AF.Exp,
                   scale=np.float32(-0.5)), rd=(ktm_,), wr=(ktm_,))
                yield
                V(lambda e: e.scalar_tensor_tensor(out=tm[:, 8:9], in0=tm[:, 0:1],
                  scalar=scol("cA", True), in1=pb[5][:, so + 1:so + 2],
                  op0=ALU.mult, op1=ALU.add),
                  rd=(ktm_, ks4), wr=(ktm_,))
                yield
                V(lambda e: e.tensor_mul(out=tm[:, 8:9], in0=tm[:, 1:2], in1=tm[:, 8:9]),
                  rd=(ktm_,), wr=(ktm_,))
                yield
                V(lambda e: e.tensor_scalar(out=tm[:, 3:4], in0=tm[:, 8:9], scalar1=1.0,
                  scalar2=scol("cs"), op0=ALU.mult, op1=ALU.add), rd=(ktm_,), wr=(ktm_,))
                yield
                V(lambda e: e.tensor_scalar_mul(out=tm[:, 3:4], in0=tm[:, 3:4], scalar1=INVD),
                  rd=(ktm_,), wr=(ktm_,))
                yield
                V(lambda e: e.scalar_tensor_tensor(out=tm[:, 8:9], in0=tm[:, 5:6],
                  scalar=scol("cA2"), in1=pb[5][:, so + 5:so + 6],
                  op0=ALU.mult, op1=ALU.add),
                  rd=(ktm_, ks2), wr=(ktm_,))
                yield
                V(lambda e: e.scalar_tensor_tensor(out=tm[:, 9:10], in0=tm[:, 0:1],
                  scalar=-2.0, in1=pb[5][:, so + 2:so + 3], op0=ALU.mult, op1=ALU.mult),
                  rd=(ktm_, ks4), wr=(ktm_,))
                yield
                V(lambda e: e.tensor_add(out=tm[:, 8:9], in0=tm[:, 8:9], in1=tm[:, 9:10]),
                  rd=(ktm_,), wr=(ktm_,))
                yield
                V(lambda e: e.tensor_mul(out=tm[:, 9:10], in0=tm[:, 1:2], in1=tm[:, 1:2]),
                  rd=(ktm_,), wr=(ktm_,))
                yield
                V(lambda e: e.tensor_mul(out=tm[:, 8:9], in0=tm[:, 8:9], in1=tm[:, 9:10]),
                  rd=(ktm_,), wr=(ktm_,))
                yield
                V(lambda e: e.scalar_tensor_tensor(out=tm[:, 9:10], in0=tm[:, 0:1],
                  scalar=scol("cAs", True), in1=pb[5][:, so + 3:so + 4],
                  op0=ALU.mult, op1=ALU.add),
                  rd=(ktm_, ks4), wr=(ktm_,))
                yield
                V(lambda e: e.tensor_mul(out=tm[:, 9:10], in0=tm[:, 1:2], in1=tm[:, 9:10]),
                  rd=(ktm_,), wr=(ktm_,))
                yield
                V(lambda e: e.scalar_tensor_tensor(out=tm[:, 8:9], in0=tm[:, 9:10],
                  scalar=2.0, in1=tm[:, 8:9], op0=ALU.mult, op1=ALU.add),
                  rd=(ktm_,), wr=(ktm_,))
                yield
                V(lambda e: e.tensor_scalar(out=tm[:, 8:9], in0=tm[:, 8:9], scalar1=1.0,
                  scalar2=scol("cs2"), op0=ALU.mult, op1=ALU.add), rd=(ktm_,), wr=(ktm_,))
                yield
                V(lambda e: e.tensor_mul(out=tm[:, 9:10], in0=tm[:, 3:4], in1=tm[:, 3:4]),
                  rd=(ktm_,), wr=(ktm_,))
                yield
                V(lambda e: e.scalar_tensor_tensor(out=tm[:, 9:10], in0=tm[:, 8:9],
                  scalar=INVD, in1=tm[:, 9:10], op0=ALU.mult, op1=ALU.subtract),
                  rd=(ktm_,), wr=(ktm_,))
                yield
                A_(lambda e: e.activation(out=tm[:, 7:8], in_=tm[:, 9:10], func=AF.Ln,
                   bias=np.float32(EPS1)), rd=(ktm_,), wr=(ktm_,))
                yield
                A_(lambda e: e.activation(out=tm[:, 2:3], in_=tm[:, 7:8], func=AF.Exp,
                   scale=np.float32(-0.5)), rd=(ktm_,), wr=(ktm_,))
                yield
                V(lambda e: e.tensor_mul(out=ts_[:, R_K1:R_K1 + 1], in0=tm[:, 2:3],
                  in1=tm[:, 1:2]), rd=(ktm_,), wr=(ksm_,))
                yield
                V(lambda e: e.scalar_tensor_tensor(out=ts_[:, R_K2:R_K2 + 1],
                  in0=ts_[:, R_K1:R_K1 + 1], scalar=-1.0, in1=tm[:, 0:1],
                  op0=ALU.mult, op1=ALU.mult), rd=(ktm_, ksm_), wr=(ksm_,))
                yield
                V(lambda e: e.tensor_copy(out=ts_[:, R_K3:R_K3 + 1], in_=tm[:, 2:3]),
                  rd=(ktm_,), wr=(ksm_,))
                yield
                V(lambda e: e.scalar_tensor_tensor(out=ts_[:, R_K4:R_K4 + 1],
                  in0=tm[:, 2:3], scalar=-1.0, in1=tm[:, 3:4], op0=ALU.mult, op1=ALU.mult),
                  rd=(ktm_,), wr=(ksm_,))
                yield
                V(lambda e: e.tensor_copy(out=ts_[:, R_R:R_R + 1], in_=tm[:, 1:2]),
                  rd=(ktm_,), wr=(ksm_,))
                yield
                V(lambda e: e.scalar_tensor_tensor(out=ts_[:, R_NMR:R_NMR + 1],
                  in0=tm[:, 1:2], scalar=-1.0, in1=tm[:, 0:1], op0=ALU.mult, op1=ALU.mult),
                  rd=(ktm_,), wr=(ksm_,))
                yield

            for b in range(2):
                ksm_ = "tmS%d" % b
                mmg([lambda e: e.transpose(pb[4][0:8, 384:512], tmS1[:, 8 * b:8 * (b + 1)],
                     P["identf"][:])],
                    rd=(ksm_,), wr=("s9",))
                yield
                A_(lambda e, b=b: e.activation(out=rows1c[0:8, 128 * b:128 * (b + 1)],
                   in_=pb[4][0:8, 384:512], func=AF.Copy), rd=("s9",), wr=(rwk,))
                yield
                mmg([(lambda e, b=b, ri=ri, rk=rk: e.matmul(
                        pb[4][0:8, 128 * ri:128 * (ri + 1)], P["sel"][:, 8 * rk:8 * rk + 8],
                        rows1c[0:8, 128 * b:128 * (b + 1)], start=True, stop=True))
                     for ri, rk in enumerate((R_R, R_NMR, R_K1))],
                    rd=(rwk,), wr=("s8",))
                yield
                A_(lambda e, b=b: e.activation(
                    out=reps1c[:].rearrange("p (k t) -> p k t", k=3)[:, :, 128 * b:128 * (b + 1)],
                    in_=pb[4][0:8, 0:384].rearrange("p (k t) -> p k t", t=128),
                    func=AF.Copy), rd=("s8",), wr=(rpk,))
                yield

            mmg([lambda e: e.matmul(pb[3][:, 0:TT], P["onesb"][0:1, 0:128],
                 reps1c[0:1, 2 * TT:3 * TT], start=True, stop=True)],
                rd=(rpk,), wr=("s6",))
            yield
            mmg([lambda e: e.matmul(pb[3][:, 256:256 + TT], P["onesb"][0:1, 0:128],
                 reps1c[0:1, 0:TT], start=True, stop=True)],
                rd=(rpk,), wr=("s7",))
            yield
            mmg([lambda e: e.matmul(pb[5][:, 256:256 + TT], P["onesb"][0:1, 0:128],
                 reps1c[0:1, TT:2 * TT], start=True, stop=True)],
                rd=(rpk,), wr=("s10b",))
            yield
            for kc in range(NCH):
                V(lambda e, kc=kc: e.scalar_tensor_tensor(
                    out=xtv[:, kc, :], in0=xfmv[:, kc, :], scalar=fm("w1A")[:, kc:kc + 1],
                    in1=pb[3][:, 0:TT], op0=ALU.mult, op1=ALU.mult),
                    rd=("xfm", "s6"), wr=(xtk,))
                yield

        def headB2(t):
            hv, xtv, rows1c, reps1c, hk, xtk, rwk, rpk = pbuf(t)
            for kc in range(NCH):
                V(lambda e, kc=kc: e.scalar_tensor_tensor(
                    out=hv[:, kc, :], in0=xfmv[:, kc, :], scalar=fm("A")[:, kc:kc + 1],
                    in1=pb[3][:, 256:256 + TT], op0=ALU.mult, op1=ALU.mult),
                    rd=("xfm", "s7"), wr=(hk,))
                yield
                V(lambda e, kc=kc: e.scalar_tensor_tensor(
                    out=hv[:, kc, :], in0=pb[5][:, 256:256 + TT],
                    scalar=fm("A")[:, kc:kc + 1], in1=hv[:, kc, :],
                    op0=ALU.mult, op1=ALU.add),
                    rd=("s10b", hk), wr=(hk,))
                yield
                V(lambda e, kc=kc: e.tensor_scalar(
                    out=hv[:, kc, :], in0=hv[:, kc, :], scalar1=1.0,
                    scalar2=fm("sa")[:, kc:kc + 1], op0=ALU.mult, op1=ALU.add),
                    rd=(hk,), wr=(hk,))
                yield

        def tail1(t, gB=None):
            hv, xtv, rows1c, reps1c, hk, xtk, rwk, rpk = pbuf(t)

            def adv(n):
                if gB is not None:
                    for _ in range(n):
                        next(gB, None)
            for pi, (wn, corr, sc_, dstv, dk) in enumerate(
                    (("wq", "corr_q", 0.125, qv, "q"), ("wk", "corr_k", 1.0, kv_, "k"))):
                for oc in range(NCH):
                    slot = "s12" if (pi * NCH + oc) % 2 == 0 else "s13"
                    off = 0 if slot == "s12" else 0
                    bank = pb[6] if slot == "s12" else pb[7]
                    mmg([(lambda e, wn=wn, oc=oc, kc=kc, bank=bank: e.matmul(
                            bank[:, 0:TT], wsv[wn][:, kc, 128 * oc:128 * (oc + 1)],
                            xtv[:, kc, :], start=(kc == 0), stop=False))
                         for kc in range(NCH)] + [
                         (lambda e, corr=corr, oc=oc, bank=bank: e.matmul(
                            bank[:, 0:TT], P[corr][:, 128 * oc:128 * (oc + 1)],
                            rows1c[:, 0:TT], start=False, stop=True))],
                        rd=(xtk, rwk), wr=(slot,))
                    A_(lambda e, dstv=dstv, oc=oc, sc_=sc_, bank=bank: e.activation(
                        out=dstv[:, oc, :], in_=bank[:, 0:TT], func=AF.Identity,
                        scale=np.float32(sc_)), rd=(slot,), wr=(dk,))
                    adv(4)
            for wn, corr, dstt, fn, dk in (("wv", "corr_v", vtm, AF.Copy, "v"),
                                           ("wg", "corr_g", gtm, AF.Silu, "g")):
                for b in range(2):
                    for hf in range(2):
                        slot = "p0" if hf == 0 else "p1"
                        bank = pb[0] if hf == 0 else pb[1]
                        mmg([(lambda e, wn=wn, b=b, hf=hf, kc=kc, bank=bank: e.matmul(
                                bank[:, 0:512], xtv[:, kc, 128 * b:128 * (b + 1)],
                                wsv[wn][:, kc, 512 * hf:512 * (hf + 1)],
                                start=(kc == 0), stop=False)) for kc in range(NCH)] + [
                             (lambda e, corr=corr, b=b, hf=hf, bank=bank: e.matmul(
                                bank[:, 0:512], rows1c[:, 128 * b:128 * (b + 1)],
                                P[corr][:, 512 * hf:512 * (hf + 1)],
                                start=False, stop=True))],
                            rd=(xtk, rwk), wr=(slot,))
                    for hf in range(2):
                        slot = "p0" if hf == 0 else "p1"
                        bank = pb[0] if hf == 0 else pb[1]
                        A_(lambda e, dstt=dstt, b=b, hf=hf, fn=fn, bank=bank: e.activation(
                            out=dstt[:, D * b + 512 * hf:D * b + 512 * (hf + 1)],
                            in_=bank[:, 0:512], func=fn), rd=(slot,), wr=(dk,))
                        adv(4)
            for b in range(2):
                for jq in range(2):
                    slot = "t2a" if jq == 0 else "t2b"
                    off = 0 if jq == 0 else 512
                    mmg([(lambda e, b=b, j=4 * jq + q, off=off, q=q: e.transpose(
                            pb[2][:].bitcast(BF16)[:, off + 128 * q:off + 128 * (q + 1)],
                            kv_[:, j, 128 * b:128 * (b + 1)], P["identb"][:]))
                         for q in range(4)],
                        rd=("k",), wr=(slot,))
                    A_(lambda e, b=b, jq=jq, off=off: e.activation(
                        out=ktm[:, D * b + 512 * jq:D * b + 512 * (jq + 1)],
                        in_=pb[2][:].bitcast(BF16)[:, off:off + 512], func=AF.Copy),
                        rd=(slot,), wr=("ktm",))
                    adv(4)
                V(lambda e, b=b: e.tensor_mul(
                    out=ktm[:, D * b:D * (b + 1)], in0=ktm[:, D * b:D * (b + 1)],
                    in1=P["GCOLW"][:]), rd=("ktm",), wr=("ktm",))

        def retp(b, jp):
            j0 = 2 * jp
            for hh in range(2):
                bk = pb[7] if hh == 0 else pb[6]
                mmg([(lambda e, b=b, j=j0 + dj, hh=hh, bk=bk, dj=dj: e.matmul(
                        bk[:, 256 + 128 * dj:384 + 128 * dj],
                        kv_[64 * hh:64 * (hh + 1), j, 128 * b:128 * (b + 1)],
                        qv[64 * hh:64 * (hh + 1), j, 128 * b:128 * (b + 1)],
                        start=True, stop=True, tile_position=(64 * hh, 0)))
                     for dj in range(2)],
                    rd=("k", "q"), wr=("st0e" if hh == 0 else "st1e",))
            for hh in range(2):
                bk = pb[7] if hh == 0 else pb[6]
                V(lambda e, jp=jp, hh=hh, bk=bk: e.tensor_mul(
                    out=spt[:, 256 * hh:256 * (hh + 1)],
                    in0=bk[:, 256:512],
                    in1=P["DTc"][:, 128 * (8 * hh + 2 * jp):
                                  128 * (8 * hh + 2 * jp + 2)]),
                    rd=("st0e" if hh == 0 else "st1e",), wr=("spt",))
            V(lambda e, b=b, j0=j0: e.tensor_tensor(
                out=qt_bf[:].rearrange("p (c t) -> p c t", t=128),
                in0=qv[:, j0:j0 + 2, 128 * b:128 * (b + 1)],
                in1=P["GROWc"][:, 128 * j0:128 * (j0 + 2)].rearrange(
                    "p (c t) -> p c t", t=128),
                op=ALU.mult), rd=("q",), wr=("qt",))
            mmg([(lambda e, b=b, j=j0 + dj, dj=dj, hh=hh: e.matmul(
                    pb[7][:, 128 * dj + 64 * hh:128 * dj + 64 * (hh + 1)],
                    spt[:, 256 * hh + 128 * dj:256 * hh + 128 * (dj + 1)],
                    vtm[:, D * b + 64 * (2 * j + hh):
                         D * b + 64 * (2 * j + hh + 1)],
                    start=True, stop=True))
                 for dj in range(2) for hh in range(2)],
                rd=("spt", "v"), wr=("s13",))
            A_(lambda e: e.activation(out=intr[:], in_=pb[7][:, 0:256],
               func=AF.Copy), rd=("s13",), wr=("intr",))
            for hh in range(2):
                bk = pb[1] if hh == 0 else pb[0]
                mmg([(lambda e, j=j0 + dj, hh=hh, bk=bk, dj=dj: e.matmul(
                        bk[:, 384 + 64 * dj:448 + 64 * dj],
                        qt_bf[64 * hh:64 * (hh + 1), 128 * dj:128 * (dj + 1)],
                        P["Gb"][64 * hh:64 * (hh + 1), 64 * j:64 * (j + 1)],
                        start=True, stop=True, tile_position=(64 * hh, 0)))
                     for dj in range(2)],
                    rd=("qt", "Gb"), wr=("sx0e" if hh == 0 else "sx1e",))
            for hh in range(2):
                bk = pb[1] if hh == 0 else pb[0]
                rv = rettm[:, D * b + 128 * j0:D * b + 128 * (j0 + 2)]
                V(lambda e, rv=rv, hh=hh, bk=bk: e.tensor_tensor(
                    out=rv.rearrange("p (c t) -> p c t", t=128)[
                        :, :, 64 * hh:64 * (hh + 1)],
                    in0=intr[:].rearrange("p (c t) -> p c t", t=128)[
                        :, :, 64 * hh:64 * (hh + 1)],
                    in1=bk[:, 384:512].rearrange("p (c t) -> p c t", t=64),
                    op=ALU.add),
                    rd=("intr", "sx0e" if hh == 0 else "sx1e"), wr=("rettm",))
            mmg([(lambda e, b=b, j=j0 + dj, hh=hh, dj=dj: e.matmul(
                    pb[4][64 * hh:64 * (hh + 1), 128 + 64 * dj:192 + 64 * dj],
                    ktm[:, D * b + 64 * (2 * j + hh):D * b + 64 * (2 * j + hh + 1)],
                    vtm[:, D * b + 64 * (2 * j + hh):D * b + 64 * (2 * j + hh + 1)],
                    start=True, stop=True, tile_position=(0, 64 * hh)))
                 for dj in range(2) for hh in range(2)],
                rd=("ktm", "v"), wr=("ssum",))
            for dj in range(2):
                V(lambda e, j=j0 + dj, dj=dj: e.scalar_tensor_tensor(
                    out=P["G"][:, 64 * j:64 * (j + 1)],
                    in0=P["G"][:, 64 * j:64 * (j + 1)],
                    scalar=P["GBLKc"][:, j:j + 1],
                    in1=pb[4][:, 128 + 64 * dj:192 + 64 * dj],
                    op0=ALU.mult, op1=ALU.add), rd=("ssum", "G"), wr=("G",))
            V(lambda e, j0=j0: e.tensor_copy(
                out=P["Gb"][:, 64 * j0:64 * (j0 + 2)],
                in_=P["G"][:, 64 * j0:64 * (j0 + 2)]), rd=("G",), wr=("Gb",))


        def gn1(b):
            rb_ = rettm[:, D * b:D * (b + 1)]
            gtmp = wointm[:, D * b:D * (b + 1)]
            V(lambda e, rb_=rb_: e.reduce_sum(out=gn16[:, 0:16],
              in_=rb_.rearrange("p (h d) -> p h d", d=HD), axis=AXL.X),
              rd=("rettm",), wr=("gn16",))
            yield
            A_(lambda e, rb_=rb_: e.activation(out=gtmp[:], in_=rb_, func=AF.Square),
               rd=("rettm",), wr=("wointm",))
            yield
            V(lambda e: e.reduce_sum(out=gn16[:, 16:32],
              in_=gtmp[:].rearrange("p (h d) -> p h d", d=HD), axis=AXL.X),
              rd=("wointm",), wr=("gn16",))
            yield
            V(lambda e: e.tensor_scalar_mul(out=gn16[:, 32:48], in0=gn16[:, 0:16],
              scalar1=1.0 / HD), rd=("gn16",), wr=("gn16",))
            yield
            V(lambda e: e.tensor_mul(out=gn16[:, 48:64], in0=gn16[:, 32:48],
              in1=gn16[:, 32:48]), rd=("gn16",), wr=("gn16",))
            yield
            V(lambda e: e.scalar_tensor_tensor(out=gn16[:, 48:64], in0=gn16[:, 16:32],
              scalar=1.0 / HD, in1=gn16[:, 48:64], op0=ALU.mult, op1=ALU.subtract),
              rd=("gn16",), wr=("gn16",))
            yield
            A_(lambda e: e.activation(out=gn16[:, 64:80], in_=gn16[:, 48:64], func=AF.Ln,
               bias=np.float32(EPS1)), rd=("gn16",), wr=("gn16",))
            yield
            A_(lambda e: e.activation(out=gn16[:, 64:80], in_=gn16[:, 64:80], func=AF.Exp,
               scale=np.float32(-0.5)), rd=("gn16",), wr=("gn16",))
            yield
            V(lambda e: e.tensor_mul(out=gn16[:, 80:96], in0=gn16[:, 32:48],
              in1=gn16[:, 64:80]), rd=("gn16",), wr=("gn16",))
            yield
            V(lambda e, rb_=rb_, gtmp=gtmp: e.tensor_tensor(
                out=gtmp[:].rearrange("p (h d) -> p h d", d=HD),
                in0=rb_.rearrange("p (h d) -> p h d", d=HD),
                in1=gn16[:, 64:80].unsqueeze(2).broadcast_to([128, 16, HD]),
                op=ALU.mult), rd=("rettm", "gn16"), wr=("wointm",))
            yield
            V(lambda e, gtmp=gtmp: e.tensor_tensor(
                out=gtmp[:].rearrange("p (h d) -> p h d", d=HD),
                in0=gtmp[:].rearrange("p (h d) -> p h d", d=HD),
                in1=gn16[:, 80:96].unsqueeze(2).broadcast_to([128, 16, HD]),
                op=ALU.subtract), rd=("wointm", "gn16"), wr=("wointm",))
            yield
            V(lambda e, gtmp=gtmp: e.tensor_mul(
                out=gtmp[:], in0=gtmp[:], in1=P["gnwrep"][:]),
                rd=("wointm",), wr=("wointm",))
            yield
            V(lambda e, gtmp=gtmp: e.tensor_add(
                out=gtmp[:], in0=gtmp[:], in1=P["gnbrep"][:]),
                rd=("wointm",), wr=("wointm",))
            yield
            V(lambda e, b=b, gtmp=gtmp: e.tensor_mul(
                out=wointm[:, D * b:D * (b + 1)], in0=gtmp[:],
                in1=gtm[:, D * b:D * (b + 1)]), rd=("wointm", "g"), wr=("wointm",))
            yield

        def woin1(b):
            for jq in range(2):
                slot = "t2a" if jq == 0 else "t2b"
                off = 0 if jq == 0 else 512
                mmg([(lambda e, b=b, j=4 * jq + q, off=off, q=q: e.transpose(
                        pb[2][:].bitcast(BF16)[:, off + 128 * q:off + 128 * (q + 1)],
                        wointm[:, D * b + 128 * j:D * b + 128 * (j + 1)], P["identb"][:]))
                     for q in range(4)],
                    rd=("wointm",), wr=(slot,))
                A_(lambda e, b=b, jq=jq, off=off: e.activation(
                    out=wfv[:, 4 * jq:4 * jq + 4, 128 * b:128 * (b + 1)],
                    in_=pb[2][:].bitcast(BF16)[:, off:off + 512].rearrange(
                        "p (c t) -> p c t", t=128),
                    func=AF.Copy), rd=(slot,), wr=("woinfm",))


        def tailD(t):
            for b in range(2):
                for jp in range(4):
                    retp(b, jp)

        def tailE(t):
            hv, xtv, rows1c, reps1c, hk, xtk, rwk, rpk = pbuf(t)
            woin1(0)
            woin1(1)
            for oc in range(NCH):
                slot = "s12" if oc % 2 == 0 else "s13"
                bank = pb[6] if oc % 2 == 0 else pb[7]
                mmg([(lambda e, oc=oc, kc=kc, bank=bank: e.matmul(
                        bank[:, 0:TT], wsv["wo"][:, kc, 128 * oc:128 * (oc + 1)],
                        wfv[:, kc, :], start=(kc == 0), stop=(kc == NCH - 1)))
                     for kc in range(NCH)],
                    rd=("woinfm",), wr=(slot,))
                V(lambda e, oc=oc, bank=bank: e.scalar_tensor_tensor(
                    out=xb1v[:, oc, :], in0=bank[:, 0:TT],
                    scalar=fm("bo")[:, oc:oc + 1], in1=hv[:, oc, :],
                    op0=ALU.add, op1=ALU.add), rd=(slot, hk), wr=("xb1",))
            op("sync", lambda e, t=t: e.dma_start(out=xb1_s[t], in_=xb1t[:]),
               waits=c.dep_waits(("xb1",), ()), incs=[(sds, 16)])
            c.register(("xb1",), (), sds)

        def gnall():
            for _ in gn1(0):
                yield
            for _ in gn1(1):
                yield

        def chain2(ga, gb):
            if ga is not None:
                for _ in ga:
                    yield
            if gb is not None:
                for _ in gb:
                    yield

        NT1 = NTILES or NT
        head1(0)
        for _ in headB_gen(0):
            pass
        for _ in headB2(0):
            pass
        for t in range(NT1):
            if t + 1 < NT1:
                head1(t + 1)
            if t > 0:
                tailD(t - 1)
            gB = chain2(gnall() if t > 0 else None,
                        headB_gen(t + 1) if t + 1 < NT1 else None)
            tail1(t, gB)
            for _ in gB:
                pass
            if t > 0:
                tailE(t - 1)
            if t + 1 < NT1:
                for _ in headB2(t + 1):
                    pass
        tailD(NT1 - 1)
        for _ in gnall():
            pass
        tailE(NT1 - 1)
        op("sync", lambda e: e.nop(), waits=[w(sds), (S["dsA1"], S["dsA1"].c)])
        op("vector", lambda e: e.nop(), waits=[w(sv)])
        op("scalar", lambda e: e.nop(), waits=[w(sa)])
        op("tensor", lambda e: e.nop(), waits=[w(spe)])


    if phases >= 1:
        with nc.Block() as blk:
            ctx.walk(blk, p1)
    p1s.close()
    wst.close()

    p2s = contextlib.ExitStack()
    TT2 = 512
    winp_sb = sb("winp_sb", [128, NCH * 2 * NPD], BF16, p2s)
    brow2 = sb("brow2", [1, 512], F32, p2s)
    b2bf = sb("b2bfs", [128, NCH], BF16, p2s)
    xfc0 = sb("xfc0", [128, TT2], F32, p2s)
    xfc1 = sb("xfc1", [128, TT2], F32, p2s)
    xb12 = sb("xb12", [128, NCH * TT2], F32, p2s)
    xb_bf = sb("xb_bf", [128, NCH * TT2], BF16, p2s)
    x1c0 = sb("x1c0", [128, TT2], F32, p2s)
    x1c1 = sb("x1c1", [128, TT2], F32, p2s)
    xt2 = sb("xt2", [128, NCH * TT2], BF16, p2s)
    s_bf = sb("s_bf", [128, NPC * TT2], BF16, p2s)
    su_bf = sb("su_bf", [128, TT2], BF16, p2s)
    woutb = sb("woutb", [128, NPC * 128], BF16, p2s)
    woutb2 = sb("woutb2", [128, NPC * 128], BF16, p2s)
    tmS2 = sb("tmS2", [128, 32], F32, p2s)
    tmT2 = sb("tmT2", [128, 16], F32, p2s)
    rows2 = sb("rows2", [128, TT2], BF16, p2s)
    reps2 = sb("reps2", [8, 2 * TT2], BF16, p2s)
    hm2 = sb("hm2", [128, TT2], F32, p2s)

    def p2(c):
        op, w = c.op, c.w
        spe, sv, sa = S["pe2"], S["v2"], S["a2"]
        sdh, sds, sdg = S["dh2"], S["ds2"], S["dg2"]
        c.auto_tensor = ()
        c.self_sems = {"vector": sv, "scalar": sa, "gpsimd": S["g2"]}
        c.res = {}
        c.bankmap = {
            "s4": "BK2", "s6": "BK3", "s7": "BK4", "s9": "BK2", "s8": "BK2",
            "glu": "BK0", "glg": "BK1", "wo_e": "BK6", "wo_o": "BK7",
            "s10_0": "BK5", "s10b_0": "BK5", "s10_1": "BK5", "s10b_1": "BK5",
            "s10_2": "BK5", "s10b_2": "BK5", "s10_3": "BK5", "s10b_3": "BK5",
            "sC": "BK2"}
        winv = winp_sb[:].rearrange("p (c n) -> p c n", n=2 * NPD)
        xbsq = s_bf[:, 0:NCH * TT2]
        rowt2 = x1c0
        xbv = xb12[:].rearrange("p (c t) -> p c t", t=TT2)
        xt2v = xt2[:].rearrange("p (c t) -> p c t", t=TT2)
        sv3 = s_bf[:].rearrange("p (c t) -> p c t", t=TT2)
        xb4v = xb12[:].rearrange("p (c f t) -> p f c t", f=2, t=TT)
        V = lambda emit, rd=(), wr=(), ww=(): op("vector", emit, waits=list(ww),
                                                incs=[(sv, 1)], reads=rd, writes=wr)
        A_ = lambda emit, rd=(), wr=(), ww=(): op("scalar", emit, waits=list(ww),
                                                 incs=[(sa, 1)], reads=rd, writes=wr)

        def mmg(mms, rd=(), wr=(), extra=()):
            ws = c.dep_waits(rd, wr) + list(extra)
            for i, mm in enumerate(mms):
                op("tensor", mm, waits=ws if i == 0 else (),
                   incs=[(spe, 1)] if i == len(mms) - 1 else ())
            c.register(rd, wr, spe)

        op("vector", lambda e: e.memset(tmS2[:], 0.0), incs=[(sv, 1)])
        op("gpsimd", lambda e: e.dma_start(
            out=winp_sb[:].rearrange("p (c n) -> p c n", n=2 * NPD),
            in_=winp_h.rearrange("(c p) n -> p c n", p=128)), incs=[(sdg, 16)])
        c.register((), ("winp",), sdg)
        op("gpsimd", lambda e: e.dma_start(out=b2bf[:], in_=b2bf_h),
           incs=[(S["dgb2"], 16)])
        c.register((), ("b2bf",), S["dgb2"])
        for strip in range(11):
            mmg([(lambda e, s=strip, kc=kc: e.matmul(
                    pb[3][0:1, 0:512], b2bf[:, kc:kc + 1],
                    winv[:, kc, 512 * s:512 * (s + 1)],
                    start=(kc == 0), stop=(kc == NCH - 1))) for kc in range(NCH)],
                rd=("winp", "b2bf"), wr=("s6",))
            op("sync", lambda e, s=strip: e.dma_start(
                out=brow2[0:1, :], in_=binp_h[0:1, 512 * s:512 * (s + 1)]),
                waits=c.dep_waits((), ("brow2",)), incs=[(sdh, 16)])
            c.register((), ("brow2",), sdh)
            V(lambda e: e.tensor_add(out=rowt2[0:1, :], in0=pb[3][0:1, 0:512],
              in1=brow2[0:1, :]), rd=("s6", "brow2"), wr=("rowt2",))
            mmg([(lambda e, s=strip, j4=j4: e.transpose(
                    pb[2][:, 4 * s + j4:4 * s + j4 + 1],
                    rowt2[0:1, 128 * j4:128 * (j4 + 1)], P["identf"][0:1, 0:1]))
                 for j4 in range(4)],
                rd=("rowt2",), wr=("s4",))
        V(lambda e: e.tensor_copy(out=P["cuv"][:], in_=pb[2][:, 0:2 * NPC]),
          rd=("s4",), wr=("cuv",))

        NT2 = (NTILES or NT) // 2
        for t in range(max(NT2, 1)):
            for f in range(2):
                op("sync", lambda e, t=t, f=f: e.dma_start(
                    out=xb4v[:, f], in_=xb1_s[2 * t + f].rearrange(
                        "p (c t) -> p c t", t=TT)),
                    waits=c.dep_waits((), ("xb1",)), incs=[(sdh, 16)])
                c.register((), ("xb1",), sdh)
            A_(lambda e: e.activation(out=xbsq[:], in_=xb12[:], func=AF.Square),
               rd=("xb1",), wr=("s_",))
            V(lambda e: e.tensor_copy(out=xb_bf[:], in_=xb12[:]), rd=("xb1",), wr=("xbbf",))
            for bb in range(4):
                so = 8 * bb
                ks4 = "s10_%d" % bb
                ks2 = "s10b_%d" % bb
                mmg([(lambda e, bb=bb, kc=kc, so=so: e.matmul(
                        pb[5][:, so:so + 2],
                        xb_bf[:].rearrange("p (c t) -> p c t", t=TT2)[
                            :, kc, 128 * bb:128 * (bb + 1)],
                        P["sv1"][:, 2 * kc:2 * kc + 2],
                        start=(kc == 0), stop=(kc == NCH - 1))) for kc in range(NCH)],
                    rd=("xbbf",), wr=(ks4,))
                mmg([(lambda e, bb=bb, kc=kc, so=so: e.matmul(
                        pb[5][:, so + 2:so + 4],
                        xbsq.rearrange("p (c t) -> p c t", t=TT2)[
                            :, kc, 128 * bb:128 * (bb + 1)],
                        P["sv1"][:, 2 * kc:2 * kc + 2],
                        start=(kc == 0), stop=(kc == NCH - 1))) for kc in range(NCH)],
                    rd=("s_",), wr=(ks2,))
            for bb in range(4):
                so = 8 * bb
                ks4 = "s10_%d" % bb
                ks2 = "s10b_%d" % bb
                ts_ = tmS2[:, 8 * bb:8 * (bb + 1)]
                tm = tmT2[:, 4 * bb:4 * (bb + 1)]
                ktm_ = "tmT%d" % bb
                ksm_ = "tmS%d" % bb
                V(lambda e, so=so, ts_=ts_: e.tensor_scalar_mul(
                  out=ts_[:, 0:1], in0=pb[5][:, so:so + 1],
                  scalar1=INVD), rd=(ks4,), wr=(ksm_,))
                V(lambda e, so=so, tm=tm: e.tensor_scalar_mul(
                  out=tm[:, 0:1], in0=pb[5][:, so + 2:so + 3],
                  scalar1=INVD), rd=(ks2,), wr=(ktm_,))
                V(lambda e, tm=tm, ts_=ts_: e.tensor_mul(
                  out=tm[:, 1:2], in0=ts_[:, 0:1],
                  in1=ts_[:, 0:1]), rd=(ksm_, ktm_), wr=(ktm_,))
                V(lambda e, tm=tm: e.tensor_sub(out=tm[:, 2:3], in0=tm[:, 0:1],
                  in1=tm[:, 1:2]), rd=(ktm_,), wr=(ktm_,))
                A_(lambda e, tm=tm: e.activation(out=tm[:, 3:4], in_=tm[:, 2:3],
                   func=AF.Ln,
                   bias=np.float32(EPS1)), rd=(ktm_,), wr=(ktm_,))
                A_(lambda e, tm=tm, ts_=ts_: e.activation(
                   out=ts_[:, 1:2], in_=tm[:, 3:4], func=AF.Exp,
                   scale=np.float32(-0.5)), rd=(ktm_,), wr=(ksm_,))
            for bb in range(4):
                ksm_ = "tmS%d" % bb
                mmg([(lambda e, bb=bb: e.transpose(
                        pb[2][0:8, 384:512], tmS2[:, 8 * bb:8 * (bb + 1)],
                        P["identf"][:]))],
                    rd=(ksm_,), wr=("s9",))
                A_(lambda e, bb=bb: e.activation(
                   out=rows2[0:8, 128 * bb:128 * (bb + 1)],
                   in_=pb[2][0:8, 384:512], func=AF.Copy), rd=("s9",), wr=("rows",))
                mmg([(lambda e, bb=bb, ri=ri: e.matmul(
                        pb[2][0:8, 128 * ri:128 * (ri + 1)],
                        P["sel"][:, 8 * ri:8 * ri + 8],
                        rows2[0:8, 128 * bb:128 * (bb + 1)], start=True, stop=True))
                     for ri in range(2)],
                    rd=("rows",), wr=("s8",))
                A_(lambda e, bb=bb: e.activation(
                    out=reps2[:].rearrange("p (k t) -> p k t", k=2)[
                        :, :, 128 * bb:128 * (bb + 1)],
                    in_=pb[2][0:8, 0:256].rearrange("p (k t) -> p k t", t=128),
                    func=AF.Copy), rd=("s8",), wr=("reps",))
            mmg([lambda e: e.matmul(pb[3][:, 0:TT2], P["onesb"][0:1, 0:128],
                 reps2[0:1, 0:TT2], start=True, stop=True)], rd=("reps",), wr=("s6",))
            mmg([lambda e: e.matmul(pb[4][:, 0:TT2], P["onesb"][0:1, 0:128],
                 reps2[0:1, TT2:2 * TT2], start=True, stop=True)],
                rd=("reps",), wr=("s7",))
            for kc in range(NCH):
                V(lambda e, kc=kc: e.tensor_sub(out=hm2[:], in0=xbv[:, kc, :],
                  in1=pb[3][:, 0:TT2]), rd=("xb1", "s6"), wr=("hm2",))
                V(lambda e, kc=kc: e.scalar_tensor_tensor(
                    out=xt2v[:, kc, :], in0=hm2[:], scalar=fm("w2")[:, kc:kc + 1],
                    in1=pb[4][:, 0:TT2], op0=ALU.mult, op1=ALU.mult),
                    rd=("hm2", "s7"), wr=("xt2",))

            for j in range(NPC):
                for ug in range(2):
                    bank = pb[0] if ug == 0 else pb[1]
                    slot = "glu" if ug == 0 else "glg"
                    mmg([(lambda e, j=j, ug=ug, kc=kc, bank=bank: e.matmul(
                            bank[:, 0:TT2],
                            winv[:, kc, 256 * j + 128 * ug:256 * j + 128 * (ug + 1)],
                            xt2v[:, kc, :], start=(kc == 0), stop=(kc == NCH - 1)))
                         for kc in range(NCH)],
                        rd=("xt2", "winp"), wr=(slot,))
                A_(lambda e, j=j: e.activation(
                    out=su_bf[:], in_=pb[0][:, 0:TT2], func=AF.Silu,
                    bias=P["cuv"][:, 2 * j:2 * j + 1]),
                    rd=("glu", "cuv"), wr=("su",))
                V(lambda e, j=j: e.scalar_tensor_tensor(
                    out=sv3[:, j, :], in0=pb[1][:, 0:TT2],
                    scalar=P["cuv"][:, 2 * j + 1:2 * j + 2], in1=su_bf[:],
                    op0=ALU.add, op1=ALU.mult), rd=("glg", "su", "cuv"), wr=("s_",))

            for oc in range(NCH):
                wb_ = woutb if oc % 2 == 0 else woutb2
                wk_ = "woutb" if oc % 2 == 0 else "woutb2"
                xfc = xfc0 if oc % 2 == 0 else xfc1
                xfk = "xf0" if oc % 2 == 0 else "xf1"
                op("gpsimd", lambda e, oc=oc, wb_=wb_: e.dma_start(
                    out=wb_[:], in_=woutbf_s[oc]),
                    waits=c.dep_waits((), (wk_,)) + [(sdg, sdg.c)],
                    incs=[(sdg, 16)])
                c.register((), (wk_,), sdg)
                for f in range(2):
                    op("sync", lambda e, t=t, f=f, oc=oc, xfc=xfc: e.dma_start(
                        out=xfc[:, TT * f:TT * (f + 1)],
                        in_=xfm_s[2 * t + f][:, TT * oc:TT * (oc + 1)]),
                        waits=c.dep_waits((), (xfk,)), incs=[(sdh, 16)])
                    c.register((), (xfk,), sdh)
                bank = pb[6] if oc % 2 == 0 else pb[7]
                slot = "wo_e" if oc % 2 == 0 else "wo_o"
                mmg([(lambda e, j=j, wb_=wb_, bank=bank: e.matmul(
                        bank[:, 0:TT2], wb_[:, 128 * j:128 * (j + 1)], sv3[:, j, :],
                        start=(j == 0), stop=(j == NPC - 1))) for j in range(NPC)],
                    rd=("s_", wk_), wr=(slot,))
                V(lambda e, oc=oc, bank=bank: e.scalar_tensor_tensor(
                    out=hm2[:], in0=bank[:, 0:TT2],
                    scalar=fm("b_out")[:, oc:oc + 1], in1=xbv[:, oc, :],
                    op0=ALU.add, op1=ALU.add), rd=(slot, "xb1"), wr=("hm2",))
                x1c = x1c0 if oc % 2 == 0 else x1c1
                x1k = "x1_0" if oc % 2 == 0 else "x1_1"
                V(lambda e, oc=oc, xfc=xfc, x1c=x1c: e.scalar_tensor_tensor(
                    out=x1c[:], in0=hm2[:], scalar=fm("ga")[:, oc:oc + 1],
                    in1=xfc[:], op0=ALU.mult, op1=ALU.add),
                    rd=("hm2", xfk), wr=(x1k,))
                for f in range(2):
                    op("sync", lambda e, t=t, f=f, oc=oc, x1c=x1c: e.dma_start(
                        out=x1_s[2 * t + f][:, TT * oc:TT * (oc + 1)],
                        in_=x1c[:, TT * f:TT * (f + 1)]),
                        waits=c.dep_waits((x1k,), ()), incs=[(sds, 16)])
                    c.register((x1k,), (), sds)
        op("sync", lambda e: e.nop(), waits=[w(sds)])
        op("vector", lambda e: e.nop(), waits=[w(sv)])
        op("scalar", lambda e: e.nop(), waits=[w(sa)])
        op("tensor", lambda e: e.nop(), waits=[w(spe)])
        op("gpsimd", lambda e: e.nop(),
           waits=[w(sdg), (S["dgb2"], S["dgb2"].c)])


    if phases >= 2:
        with nc.Block() as blk:
            ctx.walk(blk, p2)
    p2s.close()

    p3s = contextlib.ExitStack()
    fWq_sb = sb("fWq_sb", [128, NCH * D], BF16, p3s)
    fWd_sb = sb("fWd_sb", [128, NCH * HD], BF16, p3s)
    fWu_sb = sb("fWu_sb", [HD, 2 * D], BF16, p3s)
    fWo_sb = sb("fWo_sb", [128, NCH * D], BF16, p3s)
    x13 = sb("x13", [128, NCH * TT], F32, p3s)
    x1sq = sb("x1sq", [128, NCH * TT], BF16, p3s)
    x1_bf = sb("x1_bf", [128, NCH * TT], BF16, p3s)
    h2_fm = sb("h2_fm", [128, NCH * TT], F32, p3s)
    h2_fm2 = sb("h2_fm2", [128, NCH * TT], F32, p3s)
    corr_fbq3 = sb("corr_fbq3", [128, D], BF16, p3s)
    corr_bup3 = sb("corr_bup3", [128, 2 * D], BF16, p3s)
    hn_bf = sb("hn_bf", [128, NCH * TT], BF16, p3s)
    z_bf = sb("z_bf", [HD, TT], BF16, p3s)
    qtm3 = sb("qtm3", [128, 2 * D], BF16, p3s)
    kvtm = sb("kvtm", [128, 4 * D], BF16, p3s)
    prod3 = sb("prod3", [128, D], BF16, p3s)
    prod3g = sb("prod3g", [128, D], BF16, p3s)
    prod3b = sb("prod3b", [128, D], BF16, p3s)
    prod3gb = sb("prod3gb", [128, D], BF16, p3s)
    av3g = sb("av3g", [128, D], F32, p3s)
    att3 = sb("att3", [128, 256], F32, p3s)
    attw = sb("attw", [128, 256], BF16, p3s)
    av3 = sb("av3", [128, D], F32, p3s)
    aotm = sb("aotm", [128, 2 * D], BF16, p3s)
    aotmB = sb("aotmB", [128, 2 * D], BF16, p3s)
    aofm = sb("aofm", [128, NCH * TT], BF16, p3s)
    xf1t = sb("xf1t", [128, NCH * TT], F32, p3s)
    tmS3 = sb("tmS3", [128, 8], F32, p3s)
    tmT3 = sb("tmT3", [128, 16], F32, p3s)
    rows3 = sb("rows3", [128, TT], BF16, p3s)
    reps3 = sb("reps3", [8, 3 * TT], BF16, p3s)
    hm3 = sb("hm3", [128, TT], F32, p3s)
    s163 = sb("s163", [128, 48], F32, p3s)

    def p3(c):
        op, w = c.op, c.w
        spe, sv, sa, sg = S["pe3"], S["v3"], S["a3"], S["g3"]
        sdh, sds, sdg = S["dh3"], S["ds3"], S["dg3"]
        c.auto_tensor = ()
        c.self_sems = {"vector": sv, "scalar": sa, "gpsimd": sg}
        c.res = {}
        c.bankmap = {
            "s10": "BK5", "s10b": "BK5", "s12h": "BK4", "s9": "BK4", "s8": "BK4",
            "s7": "BK3", "s6r": "BK3", "s12": "BK6", "s13": "BK7",
            "p0": "BK0", "p1": "BK1", "t2a": "BK2", "t2b": "BK2"}
        x1v = x13[:].rearrange("p (c t) -> p c t", t=TT)
        h2v_0 = h2_fm[:].rearrange("p (c t) -> p c t", t=TT)
        h2v_1 = h2_fm2[:].rearrange("p (c t) -> p c t", t=TT)
        hnv = hn_bf[:].rearrange("p (c t) -> p c t", t=TT)
        aofv = aofm[:].rearrange("p (c t) -> p c t", t=TT)
        xf1v = xf1t[:].rearrange("p (c t) -> p c t", t=TT)
        wqv = fWq_sb[:].rearrange("p (c n) -> p c n", n=D)
        wdv = fWd_sb[:].rearrange("p (c n) -> p c n", n=HD)
        wov = fWo_sb[:].rearrange("p (c n) -> p c n", n=D)
        V = lambda emit, rd=(), wr=(), ww=(): op("vector", emit, waits=list(ww),
                                                incs=[(sv, 1)], reads=rd, writes=wr)
        A_ = lambda emit, rd=(), wr=(), ww=(): op("scalar", emit, waits=list(ww),
                                                 incs=[(sa, 1)], reads=rd, writes=wr)
        G_ = lambda emit, rd=(), wr=(), ww=(): op("gpsimd", emit, waits=list(ww),
                                                 incs=[(sg, 1)], reads=rd, writes=wr)

        def mmg(mms, rd=(), wr=(), extra=()):
            ws = c.dep_waits(rd, wr) + list(extra)
            for i, mm in enumerate(mms):
                op("tensor", mm, waits=ws if i == 0 else (),
                   incs=[(spe, 1)] if i == len(mms) - 1 else ())
            c.register(rd, wr, spe)

        for nm, hh, key in (("fWq_sb", fWq_h, "wq3"), ("fWd_sb", fWdown_h, "wd3"),
                            ("fWo_sb", fWo_h, "wo3")):
            tile = {"fWq_sb": fWq_sb, "fWd_sb": fWd_sb, "fWo_sb": fWo_sb}[nm]
            nn_ = {"fWq_sb": D, "fWd_sb": HD, "fWo_sb": D}[nm]
            op("gpsimd", lambda e, tile=tile, hh=hh, nn_=nn_: e.dma_start(
                out=tile[:].rearrange("p (c n) -> p c n", n=nn_),
                in_=hh.rearrange("(c p) n -> p c n", p=128)), incs=[(sdg, 16)])
            c.register((), (key,), sdg)
        op("gpsimd", lambda e: e.dma_start(out=fWu_sb[:], in_=fWup_h), incs=[(sdg, 16)])
        c.register((), ("wu3",), sdg)
        op("sync", lambda e: e.dma_start(out=corr_fbq3[:], in_=corr3_s[:, 0:D]),
           incs=[(sdh, 16)])
        c.register((), ("cfb",), sdh)
        op("sync", lambda e: e.dma_start(out=corr_bup3[:], in_=corr3_s[:, D:3 * D]),
           incs=[(sdh, 16)])
        c.register((), ("cbu",), sdh)
        op("vector", lambda e: e.memset(rows3[:], 0.0), incs=[(sv, 1)])
        op("vector", lambda e: e.memset(tmS3[:], 0.0), incs=[(sv, 1)])
        op("vector", lambda e: e.memset(tmS3[:, R_ONE:R_ONE + 1], 1.0), incs=[(sv, 1)])

        def head3(t):
            h2v = h2v_0 if t % 2 == 0 else h2v_1
            h2k = "h2a" if t % 2 == 0 else "h2b"
            op("sync", lambda e, t=t: e.dma_start(out=x13[:], in_=x1_s[t]),
               waits=c.dep_waits((), ("x1",)), incs=[(sdh, 16)])
            c.register((), ("x1",), sdh)
            A_(lambda e: e.activation(out=x1sq[:], in_=x13[:], func=AF.Square),
               rd=("x1",), wr=("x1sq",))
            A_(lambda e: e.activation(out=x1_bf[:], in_=x13[:], func=AF.Copy),
               rd=("x1",), wr=("x1bf",))
            for b in range(2):
                mmg([(lambda e, b=b, kc=kc: e.matmul(
                        pb[5][:, 0:4],
                        x1_bf[:].rearrange("p (c t) -> p c t", t=TT)[
                            :, kc, 128 * b:128 * (b + 1)],
                        P["svB"][:, 4 * kc:4 * kc + 4],
                        start=(kc == 0), stop=(kc == NCH - 1))) for kc in range(NCH)],
                    rd=("x1bf",), wr=("s10",))
                mmg([(lambda e, b=b, kc=kc: e.matmul(
                        pb[5][:, 4:6],
                        x1sq[:].rearrange("p (c t) -> p c t", t=TT)[
                            :, kc, 128 * b:128 * (b + 1)],
                        P["svB2"][:, 2 * kc:2 * kc + 2],
                        start=(kc == 0), stop=(kc == NCH - 1))) for kc in range(NCH)],
                    rd=("x1sq",), wr=("s10b",))
                tm = tmT3
                V(lambda e: e.tensor_scalar_mul(out=tm[:, 0:1], in0=pb[5][:, 0:1],
                  scalar1=INVD), rd=("s10",), wr=("tmT",))
                V(lambda e: e.tensor_scalar_mul(out=tm[:, 4:5], in0=pb[5][:, 4:5],
                  scalar1=INVD), rd=("s10b",), wr=("tmT",))
                V(lambda e: e.tensor_mul(out=tm[:, 5:6], in0=tm[:, 0:1], in1=tm[:, 0:1]),
                  rd=("tmT",), wr=("tmT",))
                V(lambda e: e.tensor_sub(out=tm[:, 6:7], in0=tm[:, 4:5], in1=tm[:, 5:6]),
                  rd=("tmT",), wr=("tmT",))
                A_(lambda e: e.activation(out=tm[:, 7:8], in_=tm[:, 6:7], func=AF.Ln,
                   bias=np.float32(EPS1)), rd=("tmT",), wr=("tmT",))
                A_(lambda e: e.activation(out=tm[:, 1:2], in_=tm[:, 7:8], func=AF.Exp,
                   scale=np.float32(-0.5)), rd=("tmT",), wr=("tmT",))
                V(lambda e: e.scalar_tensor_tensor(out=tm[:, 8:9], in0=tm[:, 5:6],
                  scalar=scol("cB2"), in1=pb[5][:, 5:6], op0=ALU.mult, op1=ALU.add),
                  rd=("tmT", "s10b"), wr=("tmT",))
                V(lambda e: e.scalar_tensor_tensor(out=tm[:, 9:10], in0=tm[:, 0:1],
                  scalar=-2.0, in1=pb[5][:, 2:3], op0=ALU.mult, op1=ALU.mult),
                  rd=("tmT", "s10"), wr=("tmT",))
                V(lambda e: e.tensor_add(out=tm[:, 8:9], in0=tm[:, 8:9], in1=tm[:, 9:10]),
                  rd=("tmT",), wr=("tmT",))
                V(lambda e: e.tensor_mul(out=tm[:, 9:10], in0=tm[:, 1:2], in1=tm[:, 1:2]),
                  rd=("tmT",), wr=("tmT",))
                V(lambda e: e.tensor_mul(out=tm[:, 8:9], in0=tm[:, 8:9], in1=tm[:, 9:10]),
                  rd=("tmT",), wr=("tmT",))
                V(lambda e: e.scalar_tensor_tensor(out=tm[:, 9:10], in0=tm[:, 0:1],
                  scalar=scol("cBs", True), in1=pb[5][:, 3:4], op0=ALU.mult, op1=ALU.add),
                  rd=("tmT", "s10"), wr=("tmT",))
                V(lambda e: e.tensor_mul(out=tm[:, 9:10], in0=tm[:, 1:2], in1=tm[:, 9:10]),
                  rd=("tmT",), wr=("tmT",))
                V(lambda e: e.scalar_tensor_tensor(out=tm[:, 8:9], in0=tm[:, 9:10],
                  scalar=2.0, in1=tm[:, 8:9], op0=ALU.mult, op1=ALU.add),
                  rd=("tmT",), wr=("tmT",))
                V(lambda e: e.tensor_scalar(out=tm[:, 8:9], in0=tm[:, 8:9], scalar1=1.0,
                  scalar2=scol("csb2"), op0=ALU.mult, op1=ALU.add), rd=("tmT",), wr=("tmT",))
                V(lambda e: e.tensor_scalar_mul(out=tm[:, 8:9], in0=tm[:, 8:9],
                  scalar1=INVD), rd=("tmT",), wr=("tmT",))
                A_(lambda e: e.activation(out=tm[:, 7:8], in_=tm[:, 8:9], func=AF.Ln,
                   bias=np.float32(EPSR)), rd=("tmT",), wr=("tmT",))
                A_(lambda e: e.activation(out=tm[:, 2:3], in_=tm[:, 7:8], func=AF.Exp,
                   scale=np.float32(-0.5)), rd=("tmT",), wr=("tmT",))
                V(lambda e: e.tensor_copy(out=tmS3[:, R_K2:R_K2 + 1], in_=tm[:, 1:2]),
                  rd=("tmT",), wr=("tmS",))
                V(lambda e: e.scalar_tensor_tensor(out=tmS3[:, R_K3:R_K3 + 1],
                  in0=tm[:, 1:2], scalar=-1.0, in1=tm[:, 0:1], op0=ALU.mult, op1=ALU.mult),
                  rd=("tmT",), wr=("tmS",))
                V(lambda e: e.tensor_copy(out=tmS3[:, R_K4:R_K4 + 1], in_=tm[:, 2:3]),
                  rd=("tmT",), wr=("tmS",))
                mmg([lambda e: e.transpose(pb[4][0:8, 384:512], tmS3[:], P["identf"][:])],
                    rd=("tmS",), wr=("s9",))
                A_(lambda e, b=b: e.activation(out=rows3[0:8, 128 * b:128 * (b + 1)],
                   in_=pb[4][0:8, 384:512], func=AF.Copy), rd=("s9",), wr=("rows",))
                mmg([(lambda e, b=b, ri=ri, rk=rk: e.matmul(
                        pb[4][0:8, 128 * ri:128 * (ri + 1)], P["sel"][:, 8 * rk:8 * rk + 8],
                        rows3[0:8, 128 * b:128 * (b + 1)], start=True, stop=True))
                     for ri, rk in enumerate((R_K2, R_K3, R_K4))],
                    rd=("rows",), wr=("s8",))
                A_(lambda e, b=b: e.activation(
                    out=reps3[:].rearrange("p (k t) -> p k t", k=3)[
                        :, :, 128 * b:128 * (b + 1)],
                    in_=pb[4][0:8, 0:384].rearrange("p (k t) -> p k t", t=128),
                    func=AF.Copy), rd=("s8",), wr=("reps",))
            mmg([lambda e: e.matmul(pb[3][:, 0:TT], P["onesb"][0:1, 0:128],
                 reps3[0:1, 2 * TT:3 * TT], start=True, stop=True)],
                rd=("reps",), wr=("s6r",))
            mmg([lambda e: e.matmul(pb[3][:, 256:256 + TT], P["onesb"][0:1, 0:128],
                 reps3[0:1, 0:TT], start=True, stop=True)],
                rd=("reps",), wr=("s7",))
            mmg([lambda e: e.matmul(pb[5][:, 256:256 + TT], P["onesb"][0:1, 0:128],
                 reps3[0:1, TT:2 * TT], start=True, stop=True)],
                rd=("reps",), wr=("s10b",))
            for kc in range(NCH):
                V(lambda e, kc=kc, h2v=h2v: e.scalar_tensor_tensor(
                    out=h2v[:, kc, :], in0=x1v[:, kc, :], scalar=fm("B")[:, kc:kc + 1],
                    in1=pb[3][:, 256:256 + TT], op0=ALU.mult, op1=ALU.mult),
                    rd=("x1", "s7"), wr=(h2k,))
                V(lambda e, kc=kc, h2v=h2v: e.scalar_tensor_tensor(
                    out=h2v[:, kc, :], in0=pb[5][:, 256:256 + TT],
                    scalar=fm("B")[:, kc:kc + 1], in1=h2v[:, kc, :],
                    op0=ALU.mult, op1=ALU.add),
                    rd=("s10b", h2k), wr=(h2k,))
                V(lambda e, kc=kc, h2v=h2v: e.tensor_scalar(
                    out=h2v[:, kc, :], in0=h2v[:, kc, :], scalar1=1.0,
                    scalar2=fm("sb_")[:, kc:kc + 1], op0=ALU.mult, op1=ALU.add),
                    rd=(h2k,), wr=(h2k,))
                V(lambda e, kc=kc, h2v=h2v: e.scalar_tensor_tensor(
                    out=hnv[:, kc, :], in0=h2v[:, kc, :],
                    scalar=fm("gamma1")[:, kc:kc + 1], in1=pb[3][:, 0:TT],
                    op0=ALU.mult, op1=ALU.mult), rd=(h2k, "s6r"), wr=("hn",))

        def proj3(t, b):
            h2v = h2v_0 if t % 2 == 0 else h2v_1
            h2k = "h2a" if t % 2 == 0 else "h2b"
            for hf in range(2):
                bank = pb[0] if hf == 0 else pb[1]
                slot = "p0" if hf == 0 else "p1"
                mmg([(lambda e, b=b, hf=hf, kc=kc, bank=bank: e.matmul(
                        bank[:, 0:512], hnv[:, kc, 128 * b:128 * (b + 1)],
                        wqv[:, kc, 512 * hf:512 * (hf + 1)],
                        start=(kc == 0), stop=False)) for kc in range(NCH)] + [
                     (lambda e, b=b, hf=hf, bank=bank: e.matmul(
                        bank[:, 0:512], rows3[:, 128 * b:128 * (b + 1)],
                        corr_fbq3[:, 512 * hf:512 * (hf + 1)],
                        start=False, stop=True))],
                    rd=("hn", "rows", "wq3", "cfb"), wr=(slot,))
            for hf in range(2):
                bank = pb[0] if hf == 0 else pb[1]
                slot = "p0" if hf == 0 else "p1"
                A_(lambda e, b=b, hf=hf, bank=bank: e.activation(
                    out=qtm3[:, D * b + 512 * hf:D * b + 512 * (hf + 1)],
                    in_=bank[:, 0:512], func=AF.Identity, scale=np.float32(0.125)),
                    rd=(slot,), wr=("q3_%d" % b,))
            if b == 0:
                mmg([(lambda e, kc=kc: e.matmul(
                        pb[6][0:HD, 0:TT], wdv[:, kc, :], hnv[:, kc, :],
                        start=(kc == 0), stop=(kc == NCH - 1))) for kc in range(NCH)],
                    rd=("hn", "wd3"), wr=("s12",))
                A_(lambda e: e.activation(out=z_bf[:], in_=pb[6][0:HD, 0:TT],
                   func=AF.Identity, bias=P["fm_fbdown"][:]), rd=("s12",), wr=("z3",))
            for hf in range(2):
                for q2 in range(2):
                    bank = pb[0] if q2 == 0 else pb[1]
                    slot = "p0" if q2 == 0 else "p1"
                    mmg([(lambda e, b=b, hf=hf, q2=q2, bank=bank: e.matmul(
                            bank[:, 0:512], z_bf[:, 128 * b:128 * (b + 1)],
                            fWu_sb[:, 1024 * hf + 512 * q2:1024 * hf + 512 * (q2 + 1)],
                            start=True, stop=False)),
                         (lambda e, b=b, hf=hf, q2=q2, bank=bank: e.matmul(
                            bank[:, 0:512], rows3[:, 128 * b:128 * (b + 1)],
                            corr_bup3[:, 1024 * hf + 512 * q2:
                                       1024 * hf + 512 * (q2 + 1)],
                            start=False, stop=True))],
                        rd=("z3", "rows", "wu3", "cbu"), wr=(slot,))
                    A_(lambda e, b=b, hf=hf, q2=q2, bank=bank: e.activation(
                        out=kvtm[:, 2 * D * b + 1024 * hf + 512 * q2:
                                 2 * D * b + 1024 * hf + 512 * (q2 + 1)],
                        in_=bank[:, 0:512], func=AF.Copy), rd=(slot,), wr=("kv3_%d" % b,))

        def att_b(t, b):
            aotmc = aotm if t % 2 == 0 else aotmB
            aok = "aotm0" if t % 2 == 0 else "aotm1"
            qb = qtm3[:, D * b:D * (b + 1)].rearrange("p (h d) -> p h d", d=HD)
            kvb = kvtm[:, 2 * D * b:2 * D * (b + 1)]
            av = att3[:].rearrange("p (h g) -> p h g", g=H)
            for g in range(H):
                pr0 = (prod3, prod3g, prod3b, prod3gb)[g % 4]
                pk0 = ("prod", "prodg", "prodb", "prodgb")[g % 4]
                eng0 = V if g % 2 == 0 else G_
                eng0(lambda e, g=g, qb=qb, kvb=kvb, pr0=pr0: e.tensor_tensor(
                    out=pr0[:].rearrange("p (h d) -> p h d", d=HD), in0=qb,
                    in1=kvb[:, 128 * g:128 * g + HD].unsqueeze(1).broadcast_to(
                        [128, H, HD]), op=ALU.mult),
                    rd=("q3_%d" % b, "kv3_%d" % b), wr=(pk0,))
                V(lambda e, g=g, pr0=pr0: e.reduce_sum(
                    out=att3[:].rearrange("p (h g) -> p h g", g=H)[:, :, g:g + 1],
                    in_=pr0[:].rearrange("p (h d) -> p h d", d=HD), axis=AXL.X),
                    rd=(pk0,), wr=("att",))
            V(lambda e, av=av: e.reduce_max(out=s163[:, 0:16], in_=av, axis=AXL.X),
              rd=("att",), wr=("s163",))
            V(lambda e, av=av: e.tensor_tensor(
                out=att3[:].rearrange("p (h g) -> p h g", g=H), in0=av,
                in1=s163[:, 0:16].unsqueeze(2).broadcast_to([128, H, H]),
                op=ALU.subtract), rd=("att", "s163"), wr=("att",))
            A_(lambda e: e.activation(out=att3[:], in_=att3[:], func=AF.Exp),
               rd=("att",), wr=("att",))
            V(lambda e, av=av: e.reduce_sum(out=s163[:, 16:32], in_=av, axis=AXL.X),
              rd=("att",), wr=("s163",))
            V(lambda e: e.reciprocal(out=s163[:, 32:48], in_=s163[:, 16:32]),
              rd=("s163",), wr=("s163",))
            V(lambda e, av=av: e.tensor_tensor(
                out=attw[:].rearrange("p (h g) -> p h g", g=H), in0=av,
                in1=s163[:, 32:48].unsqueeze(2).broadcast_to([128, H, H]),
                op=ALU.mult), rd=("att", "s163"), wr=("attw",))
            for g in range(H):
                eng = V if g < 9 else G_
                pr = prod3 if g < 9 else prod3g
                acc = av3 if g < 9 else av3g
                pk = "prod" if g < 9 else "prodg"
                ak = "av" if g < 9 else "avg"
                if g in (0, 9):
                    eng(lambda e, g=g, kvb=kvb, acc=acc: e.tensor_tensor(
                        out=acc[:].rearrange("p (h d) -> p h d", d=HD),
                        in0=attw[:].rearrange("p (h g) -> p h g", g=H)[
                            :, :, g:g + 1].broadcast_to([128, H, HD]),
                        in1=kvb[:, 128 * g + HD:128 * (g + 1)].unsqueeze(1)
                        .broadcast_to([128, H, HD]), op=ALU.mult),
                        rd=("attw", "kv3_%d" % b), wr=(ak,))
                else:
                    eng(lambda e, g=g, kvb=kvb, pr=pr: e.tensor_tensor(
                        out=pr[:].rearrange("p (h d) -> p h d", d=HD),
                        in0=attw[:].rearrange("p (h g) -> p h g", g=H)[
                            :, :, g:g + 1].broadcast_to([128, H, HD]),
                        in1=kvb[:, 128 * g + HD:128 * (g + 1)].unsqueeze(1)
                        .broadcast_to([128, H, HD]), op=ALU.mult),
                        rd=("attw", "kv3_%d" % b), wr=(pk,))
                    eng(lambda e, pr=pr, acc=acc: e.tensor_add(
                        out=acc[:], in0=acc[:], in1=pr[:]), rd=(pk, ak), wr=(ak,))
            V(lambda e, b=b, aotmc=aotmc: e.tensor_add(
              out=aotmc[:, D * b:D * (b + 1)],
              in0=av3[:], in1=av3g[:]), rd=("av", "avg"), wr=(aok,))

        def tail3(t):
            h2v = h2v_0 if t % 2 == 0 else h2v_1
            h2k = "h2a" if t % 2 == 0 else "h2b"
            aotmc = aotm if t % 2 == 0 else aotmB
            aok = "aotm0" if t % 2 == 0 else "aotm1"
            for b in range(2):
                for jq in range(2):
                    slot = "t2a" if jq == 0 else "t2b"
                    off = 0 if jq == 0 else 512
                    mmg([(lambda e, b=b, j=4 * jq + q, off=off, q=q: e.transpose(
                            pb[2][:].bitcast(BF16)[:, off + 128 * q:off + 128 * (q + 1)],
                            aotmc[:, D * b + 128 * j:D * b + 128 * (j + 1)], P["identb"][:]))
                         for q in range(4)],
                        rd=(aok,), wr=(slot,))
                    A_(lambda e, b=b, jq=jq, off=off: e.activation(
                        out=aofv[:, 4 * jq:4 * jq + 4, 128 * b:128 * (b + 1)],
                        in_=pb[2][:].bitcast(BF16)[:, off:off + 512].rearrange(
                            "p (c t) -> p c t", t=128),
                        func=AF.Copy), rd=(slot,), wr=("aofm",))

            for oc in range(NCH):
                bank = pb[6] if oc % 2 == 0 else pb[7]
                slot = "s12" if oc % 2 == 0 else "s13"
                mmg([(lambda e, oc=oc, kc=kc, bank=bank: e.matmul(
                        bank[:, 0:TT], wov[:, kc, 128 * oc:128 * (oc + 1)],
                        aofv[:, kc, :], start=(kc == 0), stop=(kc == NCH - 1)))
                     for kc in range(NCH)],
                    rd=("aofm", "wo3"), wr=(slot,))
                V(lambda e, oc=oc, bank=bank, h2v=h2v: e.scalar_tensor_tensor(
                    out=xf1v[:, oc, :], in0=bank[:, 0:TT],
                    scalar=fm("f_bo")[:, oc:oc + 1], in1=h2v[:, oc, :],
                    op0=ALU.add, op1=ALU.add), rd=(slot, h2k), wr=("xf1",))
            op("sync", lambda e, t=t: e.dma_start(out=xf1_s[t], in_=xf1t[:]),
               waits=c.dep_waits(("xf1",), ()), incs=[(sds, 16)])
            c.register(("xf1",), (), sds)

        NT3 = NTILES or NT
        for t in range(NT3):
            head3(t)
            proj3(t, 0)
            if t > 0:
                att_b(t - 1, 1)
            proj3(t, 1)
            if t > 0:
                tail3(t - 1)
            att_b(t, 0)
        att_b(NT3 - 1, 1)
        tail3(NT3 - 1)
        op("sync", lambda e: e.nop(), waits=[w(sds)])
        op("vector", lambda e: e.nop(), waits=[w(sv)])
        op("scalar", lambda e: e.nop(), waits=[w(sa)])
        op("tensor", lambda e: e.nop(), waits=[w(spe)])
        op("gpsimd", lambda e: e.nop(), waits=[w(sg), w(sdg)])


    if phases >= 3:
        with nc.Block() as blk:
            ctx.walk(blk, p3)
    p3s.close()

    p4s = contextlib.ExitStack()
    TT4 = 512
    NT4 = T // TT4
    x14 = sb("x14", [128, 2 * NCH * TT], F32, p4s)
    xf14 = sb("xf14", [128, 2 * NCH * TT], F32, p4s)
    xfsq = sb("xfsq", [128, NCH * TT], BF16, p4s)
    min_bf = sb("min_bf", [128, 2 * NCH * TT], BF16, p4s)
    he_bf = sb("he_bf", [128, NCH * TT4], BF16, p4s)
    he_bf2 = sb("he_bf2", [128, NCH * TT4], BF16, p4s)
    gb8 = sb("gb8", [128, 8 * TT4], BF16, p4s)
    we_sb = sb("we_sb", [128, NCH * D], BF16, p4s)
    we_sb2 = sb("we_sb2", [128, NCH * D], BF16, p4s)
    fWg_sb = sb("fWg_sb", [128, NCH * 8], BF16, p4s)
    glog = sb("glog", [8, TT4], F32, p4s)
    gtm4 = sb("gtm4", [128, 8], F32, p4s)
    gfm4 = sb("gfm4", [8, TT4], BF16, p4s)
    g16 = sb("g16", [128, 4], F32, p4s)
    tmS4 = sb("tmS4", [128, 8], F32, p4s)
    rows4 = sb("rows4", [8, TT4], BF16, p4s)
    reps4 = sb("reps4", [8, TT4], BF16, p4s)
    zt4 = sb("zt4", [128, TT], F32, p4s)
    outfm = sb("outfm", [128, 2 * NCH * TT], F32, p4s)
    outtm = sb("outtm", [128, D], F32, p4s)

    def p4(c):
        op, w = c.op, c.w
        spe, sv, sa = S["pe4"], S["v4"], S["a4"]
        sdh, sds, sdg = S["dh4"], S["ds4"], S["dg4"]
        c.auto_tensor = ()
        c.self_sems = {"vector": sv, "scalar": sa, "gpsimd": S["g4"]}
        c.res = {}
        c.bankmap = {
            "s4": "BK2", "s4b": "BK2", "s4c": "BK2", "s6": "BK3", "s7": "BK3",
            "so2": "BK2", "so3": "BK3"}
        V = lambda emit, rd=(), wr=(), ww=(): op("vector", emit, waits=list(ww),
                                                incs=[(sv, 1)], reads=rd, writes=wr)
        A_ = lambda emit, rd=(), wr=(), ww=(): op("scalar", emit, waits=list(ww),
                                                 incs=[(sa, 1)], reads=rd, writes=wr)

        def mmg(mms, rd=(), wr=(), extra=()):
            ws = c.dep_waits(rd, wr) + list(extra)
            for i, mm in enumerate(mms):
                op("tensor", mm, waits=ws if i == 0 else (),
                   incs=[(spe, 1)] if i == len(mms) - 1 else ())
            c.register(rd, wr, spe)

        TT4 = 512
        NT4 = T // TT4
        xf4 = xf14[:].rearrange("p (f c t) -> p f c t", c=NCH, t=TT)
        xq4 = xfsq[:].rearrange("p (c t) -> p c t", t=TT)
        x14v = x14[:].rearrange("p (f c t) -> p f c t", c=NCH, t=TT)
        mvk = min_bf[:].rearrange("p (f c t) -> p c f t", c=NCH, t=TT)
        hev_0 = he_bf[:].rearrange("p (c t) -> p c t", t=TT4)
        hev_1 = he_bf2[:].rearrange("p (c t) -> p c t", t=TT4)
        gbv = gb8[:].rearrange("p (e t) -> p e t", t=TT4)
        ofm = outfm[:].rearrange("p (f c t) -> p f c t", c=NCH, t=TT)
        wgv = fWg_sb[:].rearrange("p (c n) -> p c n", n=8)

        op("gpsimd", lambda e: e.dma_start(
            out=fWg_sb[:].rearrange("p (c n) -> p c n", n=8),
            in_=fWg_h.rearrange("(c p) n -> p c n", p=128)), incs=[(S["dwg"], 16)])
        c.register((), ("wg4",), S["dwg"])
        op("vector", lambda e: e.memset(tmS4[:], 0.0), incs=[(sv, 1)])

        for t in range(min(NTILES or NT4, NT4)):
            op("sync", lambda e, t=t: e.dma_start(
                out=x14[:].rearrange("p (f n) -> p f n", f=2),
                in_=x1_s[2 * t:2 * t + 2].rearrange("f p n -> p f n")),
                waits=c.dep_waits((), ("x14",)), incs=[(sdh, 16)])
            c.register((), ("x14",), sdh)
            op("sync", lambda e, t=t: e.dma_start(
                out=xf14[:].rearrange("p (f n) -> p f n", f=2),
                in_=xf1_s[2 * t:2 * t + 2].rearrange("f p n -> p f n")),
                waits=c.dep_waits((), ("xf14",)), incs=[(sdh, 16)])
            c.register((), ("xf14",), sdh)
            for f in range(2):
                A_(lambda e, f=f: e.activation(
                    out=xfsq[:], in_=xf14[:, f * NCH * TT:(f + 1) * NCH * TT],
                    func=AF.Square), rd=("xf14",), wr=("xfsq",))
                for b in range(2):
                    bb = 2 * f + b
                    mmg([(lambda e, b=b, kc=kc: e.matmul(
                            pb[2][:, 0:2], xq4[:, kc, 128 * b:128 * (b + 1)],
                            P["sv1"][:, 2 * kc:2 * kc + 2],
                            start=(kc == 0), stop=(kc == NCH - 1))) for kc in range(NCH)],
                        rd=("xfsq",), wr=("s4",))
                    V(lambda e: e.tensor_scalar_mul(out=g16[:, 0:1], in0=pb[2][:, 0:1],
                      scalar1=INVD), rd=("s4",), wr=("g16",))
                    A_(lambda e: e.activation(out=g16[:, 1:2], in_=g16[:, 0:1], func=AF.Ln,
                       bias=np.float32(EPSR)), rd=("g16",), wr=("g16",))
                    A_(lambda e: e.activation(out=tmS4[:, 0:1], in_=g16[:, 1:2],
                       func=AF.Exp, scale=np.float32(-0.5)), rd=("g16",), wr=("tmS",))
                    mmg([lambda e: e.transpose(pb[2][0:8, 128:256], tmS4[:],
                         P["identf"][:])], rd=("tmS",), wr=("s4b",))
                    A_(lambda e, bb=bb: e.activation(out=rows4[0:8, 128 * bb:128 * (bb + 1)],
                       in_=pb[2][0:8, 128:256], func=AF.Copy), rd=("s4b",), wr=("rows",))
                    mmg([(lambda e, bb=bb: e.matmul(
                            pb[2][0:8, 256:384], P["sel"][:, 0:8],
                            rows4[0:8, 128 * bb:128 * (bb + 1)], start=True, stop=True))],
                        rd=("rows",), wr=("s4c",))
                    A_(lambda e, bb=bb: e.activation(out=reps4[0:8, 128 * bb:128 * (bb + 1)],
                       in_=pb[2][0:8, 256:384], func=AF.Copy), rd=("s4c",), wr=("reps",))
            for f in range(2):
                for b in range(2):
                    bb = 2 * f + b
                    mmg([(lambda e, bb=bb: e.matmul(
                            pb[3][:, 0:128], P["onesb"][0:1, 0:128],
                            reps4[0:1, 128 * bb:128 * (bb + 1)], start=True, stop=True))],
                        rd=("reps",), wr=("s6",))
                    for kc in range(NCH):
                        V(lambda e, f=f, b=b, kc=kc: e.scalar_tensor_tensor(
                            out=min_bf[:].rearrange("p (f c t) -> p f c t", c=NCH, t=TT)[
                                :, f, kc, 128 * b:128 * (b + 1)],
                            in0=xf4[:, f, kc, 128 * b:128 * (b + 1)],
                            scalar=fm("gamma2")[:, kc:kc + 1], in1=pb[3][:, 0:128],
                            op0=ALU.mult, op1=ALU.mult), rd=("xf14", "s6"), wr=("min",))
            for f in range(2):
                mmg([(lambda e, f=f, kc=kc: e.matmul(
                        pb[3][0:8, TT * f:TT * (f + 1)], wgv[:, kc, :],
                        min_bf[:].rearrange("p (f c t) -> p f c t", c=NCH, t=TT)[:, f, kc, :],
                        start=(kc == 0), stop=(kc == NCH - 1))) for kc in range(NCH)],
                    rd=("min", "wg4"), wr=("s6" if f == 0 else "s7",))
            A_(lambda e: e.activation(out=glog[:], in_=pb[3][0:8, 0:TT4], func=AF.Identity,
               bias=P["fm_fbg"][:]), rd=("s6", "s7"), wr=("glog",))
            for bb in range(4):
                mmg([(lambda e, bb=bb: e.transpose(
                        pb[2][:, 0:8], glog[:, 128 * bb:128 * (bb + 1)],
                        P["identf"][0:8, 0:8]))], rd=("glog",), wr=("s4",))
                V(lambda e: e.tensor_copy(out=gtm4[:], in_=pb[2][:, 0:8]),
                  rd=("s4",), wr=("gtm4",))
                V(lambda e: e.reduce_max(out=g16[:, 0:1], in_=gtm4[:], axis=AXL.X),
                  rd=("gtm4",), wr=("g16",))
                V(lambda e: e.tensor_scalar_mul(out=g16[:, 1:2], in0=g16[:, 0:1],
                  scalar1=-1.0), rd=("g16",), wr=("g16",))
                A_(lambda e: e.activation(out=gtm4[:], in_=gtm4[:], func=AF.Exp,
                   bias=g16[:, 1:2]), rd=("gtm4", "g16"), wr=("gtm4",))
                V(lambda e: e.reduce_sum(out=g16[:, 2:3], in_=gtm4[:], axis=AXL.X),
                  rd=("gtm4",), wr=("g16",))
                V(lambda e: e.reciprocal(out=g16[:, 3:4], in_=g16[:, 2:3]),
                  rd=("g16",), wr=("g16",))
                V(lambda e: e.tensor_scalar_mul(out=gtm4[:], in0=gtm4[:],
                  scalar1=g16[:, 3:4]), rd=("gtm4", "g16"), wr=("gtm4",))
                mmg([(lambda e, bb=bb: e.transpose(pb[2][0:8, 128:256], gtm4[:],
                     P["identf"][:]))], rd=("gtm4",), wr=("s4b",))
                A_(lambda e, bb=bb: e.activation(out=gfm4[:, 128 * bb:128 * (bb + 1)],
                   in_=pb[2][0:8, 128:256], func=AF.Copy), rd=("s4b",), wr=("gfm",))
            for e_ in range(8):
                mmg([(lambda e, x=e_: e.matmul(
                        pb[3][:, 0:TT4], P["selw"][:, 128 * x:128 * (x + 1)], gfm4[0:8, :],
                        start=True, stop=True))], rd=("gfm",), wr=("s6",))
                A_(lambda e, x=e_: e.activation(out=gbv[:, x, :], in_=pb[3][:, 0:TT4],
                   func=AF.Copy), rd=("s6",), wr=("gb8",))

            weneed = {}
            op("gpsimd", lambda e: e.dma_start(out=we_sb[:], in_=webf_s[0]),
               waits=c.dep_waits((), ("we0",)), incs=[(sdg, 16)])
            c.register((), ("we0",), sdg)
            weneed[0] = (sdg, sdg.c)
            for e_ in range(8):
                buf = we_sb if e_ % 2 == 0 else we_sb2
                nbuf = we_sb2 if e_ % 2 == 0 else we_sb
                bk = "we0" if e_ % 2 == 0 else "we1"
                nbk = "we1" if e_ % 2 == 0 else "we0"
                hev = hev_0 if e_ % 2 == 0 else hev_1
                hek = "he0" if e_ % 2 == 0 else "he1"
                wevv = buf[:].rearrange("p (c n) -> p c n", n=D)
                for kc in range(NCH):
                    V(lambda e, x=e_, kc=kc, hev=hev: e.tensor_tensor(
                        out=hev[:, kc, :].rearrange("p (f t) -> p f t", t=TT),
                        in0=mvk[:, kc, :, :],
                        in1=gbv[:, x, :].rearrange("p (f t) -> p f t", t=TT),
                        op=ALU.mult), rd=("min", "gb8"), wr=(hek,))
                if e_ < 7:
                    nsem = S["dg4b"] if e_ % 2 == 0 else sdg
                    op("gpsimd", lambda e, x=e_: e.dma_start(out=nbuf[:], in_=webf_s[x + 1]),
                       waits=c.dep_waits((), (nbk,)), incs=[(nsem, 16)])
                    c.register((), (nbk,), nsem)
                    weneed[e_ + 1] = (nsem, nsem.c)
                ws0 = c.dep_waits((hek, bk), ()) + [weneed[e_]]
                for oc in range(NCH):
                    for kc in range(NCH):
                        op("tensor", lambda e, x=e_, oc=oc, kc=kc, wevv=wevv,
                           hev=hev: e.matmul(
                            pb[oc][:, 0:512], wevv[:, kc, 128 * oc:128 * (oc + 1)],
                            hev[:, kc, :],
                            start=(x == 0 and kc == 0), stop=(x == 7 and kc == NCH - 1)),
                            waits=ws0 if (oc == 0 and kc == 0) else (),
                            incs=[(spe, 1)] if (oc == NCH - 1 and kc == NCH - 1) else [])
                c.register((hek, bk), ("moe",) if e_ == 7 else (), spe)
            for oc in range(NCH):
                for f in range(2):
                    V(lambda e, oc=oc, f=f: e.tensor_add(
                        out=zt4[:], in0=pb[oc][:, 256 * f:256 * (f + 1)],
                        in1=xf4[:, f, oc, :]), rd=("moe", "xf14"), wr=("zt4",))
                    V(lambda e, oc=oc, f=f: e.scalar_tensor_tensor(
                        out=ofm[:, f, oc, :], in0=zt4[:], scalar=fm("gb")[:, oc:oc + 1],
                        in1=x14v[:, f, oc, :], op0=ALU.mult, op1=ALU.add),
                        rd=("zt4", "x14"), wr=("ofm",))
            for bb in range(4):
                f, b = bb // 2, bb % 2
                for jq in range(2):
                    slot = "so2" if jq == 0 else "so3"
                    bank_off = 0 if jq == 0 else 0
                    pbx = pb[2] if jq == 0 else pb[3]
                    mmg([(lambda e, f=f, b=b, j=4 * jq + q, pbx=pbx, q=q: e.transpose(
                            pbx[:, 128 * q:128 * (q + 1)],
                            ofm[:, f, j, 128 * b:128 * (b + 1)], P["identf"][:]))
                         for q in range(4)],
                        rd=("ofm",), wr=(slot,))
                    A_(lambda e, jq=jq, pbx=pbx: e.activation(
                        out=outtm[:, 512 * jq:512 * (jq + 1)], in_=pbx[:, 0:512],
                        func=AF.Copy), rd=(slot,), wr=("outtm",))
                op("sync", lambda e, t=t, bb=bb: e.dma_start(
                    out=out_h[TT4 * t + 128 * bb:TT4 * t + 128 * (bb + 1), :],
                    in_=outtm[:]), waits=c.dep_waits(("outtm",), ()), incs=[(sds, 16)])
                c.register(("outtm",), (), sds)
        op("gpsimd", lambda e: e.nop(), waits=[w(sdg), (S["dg4b"], S["dg4b"].c)])
        op("vector", lambda e: e.nop(), waits=[w(sv)])
        op("scalar", lambda e: e.nop(), waits=[w(sa)])
        op("tensor", lambda e: e.nop(), waits=[w(spe)])
        op("sync", lambda e: e.nop(), waits=[w(sds)])


    if phases >= 4:
        with nc.Block() as blk:
            ctx.walk(blk, p4)
    p4s.close()
    big.close()
    return nc, out_h.name


def make_shared(inputs):
    g = lambda k: np.ascontiguousarray(np.asarray(inputs[k], np.float32))
    wp, bp = permute_w_in(g("L0_attn_w_in"), g("L0_attn_b_in"))
    woutp = np.zeros((NPD, D), np.float32)
    woutp[:NHID] = g("L0_attn_w_out")
    CN = host_consts()
    return dict(
        wmod=g("L0_wmod"),
        wq=g("L0_attn_wq"), wk=g("L0_attn_wk"), wv=g("L0_attn_wv"),
        wg=g("L0_attn_wg"), wo=g("L0_attn_wo"),
        winp=wp, binp=bp, woutp=woutp,
        fWq=g("L0_ffn_Wq"), fWdown=g("L0_ffn_Wdown"), fWup=g("L0_ffn_Wup"),
        fWo=g("L0_ffn_Wo"), fWg=g("L0_ffn_Wg"), fWe=g("L0_ffn_We"),
        fm_w1=fmv(g("L0_attn_norm1_w")), fm_b1=fmv(g("L0_attn_norm1_b")),
        fm_w2=fmv(g("L0_attn_norm2_w")),
        b2bf=fmv(g("L0_attn_norm2_b")),
        fm_bq=fmv(g("L0_attn_bq")), fm_bk=fmv(g("L0_attn_bk")),
        fm_bo=fmv(g("L0_attn_bo")), fm_b_out=fmv(g("L0_attn_b_out")),
        fm_gamma1=fmv(g("L0_ffn_gamma1")), fm_gamma2=fmv(g("L0_ffn_gamma2")),
        fm_f_bo=fmv(g("L0_ffn_bo")),
        fm_f_bdown=g("L0_ffn_bdown").reshape(HD, 1),
        fm_f_bg=g("L0_ffn_bg").reshape(8, 1),
        row_bv=g("L0_attn_bv").reshape(1, D), row_bg=g("L0_attn_bg").reshape(1, D),
        row_fbq=g("L0_ffn_bq").reshape(1, D), row_bup=g("L0_ffn_bup").reshape(1, 2 * D),
        row_bq=g("L0_attn_bq").reshape(1, D), row_bk=g("L0_attn_bk").reshape(1, D),
        gnw_rep=np.ascontiguousarray(np.broadcast_to(g("L0_attn_gn_w")[None, :], (128, D))),
        gnb_rep=np.ascontiguousarray(np.broadcast_to(g("L0_attn_gn_b")[None, :], (128, D))),
        **CN,
    )


def kernel(**inputs):
    shared = make_shared(inputs)
    in_maps = []
    for b in range(8):
        m = dict(shared)
        m["x"] = np.ascontiguousarray(np.asarray(inputs["L0_attn_x"][b], np.float32))
        m["cvec"] = fmv(np.asarray(inputs["L0_c"][b], np.float32))
        in_maps.append(m)
    nc = bass.Bass("TRN2", target_bir_lowering=False)
    nc, out_name = build(nc)
    res = run_bass_kernel_spmd(nc, in_maps, core_ids=list(range(8)))
    out = np.stack([np.asarray(r["out"], np.float32) for r in res.results], axis=0)
    return out


# version 20
# speedup vs baseline: 1.0160x; 1.0089x over previous
import contextlib
import numpy as np
import ml_dtypes
import concourse.bass as bass
import concourse.mybir as mybir
from concourse.bass_utils import run_bass_kernel_spmd

F32 = mybir.dt.float32
F32R = mybir.dt.float32r
BF16 = mybir.dt.bfloat16
BF = ml_dtypes.bfloat16
AF = mybir.ActivationFunctionType
ALU = mybir.AluOpType
AXL = mybir.AxisListType

SKIP = set()
NTILES = None
D = 1024
NCH = 8
H = 16
HD = 64
T = 4096
TT = 256
NT = T // TT
CB = 128
NHID = 2730
NPD = 2816
NPC = 22
EPS1 = 1e-5
EPSR = 1e-6
INVD = 1.0 / D

R_K2, R_K3, R_K4, R_ONE, R_K1, R_R, R_NMR, R_PAD = range(8)


def _gamma():
    return 1.0 - 2.0 ** (-5.0 - np.arange(H, dtype=np.float64))


def host_consts():
    gam = _gamma()
    n = np.arange(CB, dtype=np.float64)
    DT = np.zeros((128, H * 128), np.float32)
    for h in range(H):
        j, hh = h // 2, h % 2
        DT[:, (hh * 8 + j) * 128:(hh * 8 + j + 1) * 128] = np.tril(
            gam[h] ** (n[:, None] - n[None, :])).T.astype(np.float32)
    GCOLW = np.zeros((128, 1024), np.float32)
    for h in range(H):
        GCOLW[:, h * 64:(h + 1) * 64] = (
            gam[h] ** (127.0 - n)).astype(np.float32)[:, None]
    GROW = np.zeros((128, 8 * 128), np.float32)
    for j in range(8):
        GROW[0:64, j * 128:(j + 1) * 128] = (gam[2 * j] ** (n + 1)).astype(np.float32)[None, :]
        GROW[64:128, j * 128:(j + 1) * 128] = (gam[2 * j + 1] ** (n + 1)).astype(np.float32)[None, :]
    GCOL = np.zeros((128, H), np.float32)
    for h in range(H):
        GCOL[:, h] = (gam[h] ** (127.0 - n)).astype(np.float32)
    GBLK = np.zeros((128, 8), np.float32)
    for j in range(8):
        GBLK[0:64, j] = np.float32(gam[2 * j] ** 128)
        GBLK[64:128, j] = np.float32(gam[2 * j + 1] ** 128)
    sel = np.zeros((8, 8 * 8), np.float32)
    for k in range(8):
        sel[k, 8 * k:8 * k + 8] = 1.0
    selw = np.zeros((8, 8 * 128), np.float32)
    for k in range(8):
        selw[k, 128 * k:128 * (k + 1)] = 1.0
    e3 = np.zeros((1, 4), np.float32)
    e3[0, 3] = 1.0
    return dict(DTc=DT, GROWc=GROW, GCOLc=GCOL, GCOLWc=GCOLW, GBLKc=GBLK,
                identc=np.eye(128, dtype=np.float32), selc=sel, selwc=selw,
                e3c=e3, onesec=np.ones((8, 128), np.float32))


def fmv(v):
    return np.ascontiguousarray(np.asarray(v, np.float32).reshape(NCH, 128).T)


def permute_w_in(w_in, b_in):
    wu, wg = w_in[:, :NHID], w_in[:, NHID:]
    bu, bg = b_in[:NHID], b_in[NHID:]
    wp = np.zeros((D, 2 * NPD), np.float32)
    bp = np.zeros((1, 2 * NPD), np.float32)
    for j in range(NPC):
        lo, hi = j * 128, min((j + 1) * 128, NHID)
        wd = hi - lo
        if wd > 0:
            wp[:, 256 * j:256 * j + wd] = wu[:, lo:hi]
            wp[:, 256 * j + 128:256 * j + 128 + wd] = wg[:, lo:hi]
            bp[0, 256 * j:256 * j + wd] = bu[lo:hi]
            bp[0, 256 * j + 128:256 * j + 128 + wd] = bg[lo:hi]
    return wp, bp


class Sem:
    __slots__ = ("h", "c", "name")

    def __init__(self, h, name):
        self.h, self.c, self.name = h, 0, name


class Ctx:
    ENGS = ("sync", "gpsimd", "vector", "scalar", "tensor")

    def __init__(self, nc):
        self.nc = nc
        self.sems = []
        self.auto_tensor = ()
        self.self_sems = {}
        self.section = None
        self.res = {}
        self.bankmap = {}

    def sem(self, name):
        s = Sem(self.nc.alloc_semaphore(name=name), name)
        self.sems.append(s)
        return s

    def walk(self, blk, fn):
        finals = []
        start = {id(s): s.c for s in self.sems}
        for en in self.ENGS:
            self.eng_name = en
            self._lastw = {}

            def body(e, fn=fn):
                for s in self.sems:
                    s.c = start[id(s)]
                self.eng = e
                fn(self)

            getattr(blk, en)(body)
            finals.append({s.name: s.c for s in self.sems})
        for f in finals[1:]:
            assert f == finals[0], f"sem divergence: {finals[0]} vs {f}"

    def op(self, engine, emit, waits=(), incs=(), reads=(), writes=(), mark=None):
        if self.section is not None and self.section in SKIP:
            return
        if engine == "tensor":
            waits = list(waits) + [(s, s.c) for s in self.auto_tensor]
        is_dma = any(nn >= 16 for (_, nn) in incs)
        if is_dma:
            waits = list(waits) + [(s, s.c) for (s, _) in incs]
        ss = self.self_sems.get(engine)
        if ss is not None and not is_dma:
            waits = list(waits) + [(ss, ss.c)]
            if not any(s is ss for (s, _) in incs):
                incs = list(incs) + [(ss, 1)]
        waits = list(waits) + self.dep_waits(reads, writes)
        if engine == self.eng_name:
            for (s, v) in waits:
                if v > 0 and self._lastw.get(id(s), -1) < v:
                    self.eng.wait_ge(s.h, v)
                    self._lastw[id(s)] = v
            ins = emit(self.eng)
            for (s, nn) in incs:
                ins = ins.then_inc(s.h, nn)
        for (s, nn) in incs:
            s.c += nn
        if reads or writes:
            if mark is None:
                mark = ss if ss is not None else (incs[-1][0] if incs else None)
            self.register(reads, writes, mark)

    def _expand(self, reads, writes):
        bm = self.bankmap
        if not bm:
            return reads, writes
        rb = {bm[k] for k in reads if k in bm}
        wb = {bm[k] for k in writes if k in bm}
        return (tuple(reads) + tuple(sorted(rb - wb)),
                tuple(writes) + tuple(sorted(wb)))

    def dep_waits(self, reads, writes):
        reads, writes = self._expand(reads, writes)
        ws = []
        for k in reads:
            e = self.res.get(k)
            if e is not None and e[0] is not None:
                ws.append(e[0])
        for k in writes:
            e = self.res.get(k)
            if e is not None:
                if e[0] is not None:
                    ws.append(e[0])
                ws.extend(e[1].values())
        return ws

    def register(self, reads, writes, mark):
        assert mark is not None, "tracked op needs a completion sem"
        reads, writes = self._expand(reads, writes)
        mk = (mark, mark.c)
        for k in writes:
            self.res[k] = [mk, {}]
        for k in reads:
            e = self.res.setdefault(k, [None, {}])
            e[1][id(mark)] = mk

    def w(self, s):
        return (s, s.c)


def build(nc, phases=4):
    ctx = Ctx(nc)

    def din(name, shape, dt=F32):
        return nc.dram_tensor(name, list(shape), dt, kind="ExternalInput").ap()

    x_h = din("x", [T, D])
    wmod_h = din("wmod", [D, 6 * D])
    c_h = din("cvec", [128, 8])
    W_h = {k: din(k, [D, D]) for k in ("wq", "wk", "wv", "wg", "wo")}
    winp_h = din("winp", [D, 2 * NPD])
    binp_h = din("binp", [1, 2 * NPD])
    woutp_h = din("woutp", [NPD, D])
    fWq_h = din("fWq", [D, D])
    fWdown_h = din("fWdown", [D, HD])
    fWup_h = din("fWup", [HD, 2 * D])
    fWo_h = din("fWo", [D, D])
    fWg_h = din("fWg", [D, 8])
    fWe_h = din("fWe", [8, D, D])
    FMV = ("w1", "b1", "w2", "bq", "bk", "bo", "b_out", "gamma1", "gamma2", "f_bo")
    fmv_h = {k: din("fm_" + k, [128, NCH]) for k in FMV}
    b2bf_h = din("b2bf", [128, NCH])
    fbdown_h = din("fm_f_bdown", [HD, 1])
    fbg_h = din("fm_f_bg", [8, 1])
    row_h = {"bv": din("row_bv", [1, D]), "bg_": din("row_bg", [1, D]),
             "f_bq": din("row_fbq", [1, D]), "bup": din("row_bup", [1, 2 * D]),
             "bq": din("row_bq", [1, D]), "bk": din("row_bk", [1, D])}
    gnwrep_h = din("gnw_rep", [128, D])
    gnbrep_h = din("gnb_rep", [128, D])
    CN = host_consts()
    cons_h = {k: din(k, v.shape) for k, v in CN.items()}
    out_h = nc.dram_tensor("out", [T, D], F32, kind="ExternalOutput").ap()

    def dsc(name, shape, dt=F32):
        return nc.dram_tensor(name, list(shape), dt, kind="Internal").ap()

    xfm_s = dsc("xfm_s", [NT, 128, NCH * TT])
    xb1_s = dsc("xb1_s", [NT, 128, NCH * TT])
    x1_s = dsc("x1_s", [NT, 128, NCH * TT])
    xf1_s = dsc("xf1_s", [NT, 128, NCH * TT])
    woutbf_s = dsc("woutbf_s", [NCH, 128, NPC * 128], BF16)
    webf_s = dsc("webf_s", [8, 128, NCH * D], BF16)

    S = {}
    for ph in ("s", "1", "2", "3", "4"):
        for nm in ("pe", "v", "a", "g", "dg", "dh", "ds"):
            S[nm + ph] = ctx.sem(nm + ph)
    S["dgw"] = ctx.sem("dgw")
    S["dhw"] = ctx.sem("dhw")
    S["dhw2"] = ctx.sem("dhw2")
    S["dhw3"] = ctx.sem("dhw3")
    S["dgp"] = ctx.sem("dgp")
    S["dwg"] = ctx.sem("dwg")
    S["dg4b"] = ctx.sem("dg4b")
    S["dsA1"] = ctx.sem("dsA1")
    S["dgb2"] = ctx.sem("dgb2")

    big = contextlib.ExitStack()

    def sb(name, shape, dt, stack):
        return stack.enter_context(nc.sbuf_tensor(name, list(shape), dt))

    P = {}
    P["DTc"] = sb("sb_DTc", [128, H * 128], BF16, big)
    P["GROWc"] = sb("sb_GROWc", [128, 8 * 128], BF16, big)
    P["GCOLW"] = sb("sb_GCOLW", [128, 1024], BF16, big)
    P["GBLKc"] = sb("sb_GBLKc", [128, 8], F32, big)
    P["identf"] = sb("identf", [128, 128], F32, big)
    P["identb"] = sb("identb", [128, 128], BF16, big)
    P["sel"] = sb("sel", [8, 64], BF16, big)
    P["onese"] = sb("onese", [8, 128], BF16, big)
    P["selw"] = sb("selw", [8, 8 * 128], BF16, big)
    P["e3"] = sb("e3", [1, 4], BF16, big)
    P["ones128"] = sb("ones128", [128, 128], F32, big)
    P["onesb"] = sb("onesb", [1, 256], BF16, big)
    P["gnwrep"] = sb("gnwrep", [128, D], BF16, big)
    P["gnbrep"] = sb("gnbrep", [128, D], BF16, big)
    for k in list(FMV) + ["A", "sa", "B", "sb_", "ga", "gb", "w1A", "cq", "ck", "bq8"]:
        P["fm_" + k] = sb("v_" + k, [128, NCH], F32, big)
    P["fm_fbdown"] = sb("v_fbdown", [HD, 1], F32, big)
    P["fm_fbg"] = sb("v_fbg", [8, 1], F32, big)
    P["cuv"] = sb("v_cuv", [128, 2 * NPC], F32, big)
    P["mod"] = sb("v_mod", [128, 48], F32, big)
    P["scal"] = sb("v_scal", [128, 16], F32, big)
    P["nscal"] = sb("v_nscal", [128, 16], F32, big)
    for k in ("Arow", "sarow", "Brow", "sbrow"):
        P[k] = sb("r_" + k, [1, NCH * 128], BF16, big)
    for k in ("corr_q", "corr_k", "corr_v", "corr_g"):
        P[k] = sb(k, [128, D], BF16, big)
    P["G"] = sb("v_G", [128, 8 * HD], F32, big)
    P["Gb"] = sb("v_Gb", [128, 8 * HD], BF16, big)
    P["vecs4"] = sb("v_vecs4", [128, NCH * 4], BF16, big)
    P["svA"] = sb("v_svA", [128, NCH * 4], BF16, big)
    P["svA2"] = sb("v_svA2", [128, NCH * 2], BF16, big)
    P["svB"] = sb("v_svB", [128, NCH * 4], BF16, big)
    P["svB2"] = sb("v_svB2", [128, NCH * 2], BF16, big)
    P["sv1"] = sb("v_sv1", [128, NCH * 2], BF16, big)

    pb = [big.enter_context(nc.psum_tensor(f"pb{i}", [128, 512], F32)) for i in range(8)]

    fm = lambda k: P["fm_" + k]
    SC = {k: i for i, k in enumerate(
        ("cA", "cs", "cA2", "cAs", "cs2", "cB", "csb", "cB2", "cBs", "csb2"))}

    def scol(name, neg=False):
        return (P["nscal"] if neg else P["scal"])[:, SC[name]:SC[name] + 1]

    def rowmath(c, sv, sa, spe, statq, statq2, tmT, tmS, side, rms_rho):
        op, w = c.op, c.w
        V = lambda emit, ww=(): op("vector", emit, waits=list(ww), incs=[(sv, 1)])
        A_ = lambda emit, ww=(): op("scalar", emit, waits=list(ww), incs=[(sa, 1)])
        cA, cs, cA2, cAs, cs2 = (("cA", "cs", "cA2", "cAs", "cs2") if side == "A"
                                 else ("cB", "csb", "cB2", "cBs", "csb2"))
        tm = tmT
        V(lambda e: e.tensor_scalar_mul(out=tm[:, 0:1], in0=statq[:, 0:1], scalar1=INVD),
          ww=[w(spe)])
        V(lambda e: e.tensor_scalar_mul(out=tm[:, 4:5], in0=statq2[:, 0:1], scalar1=INVD))
        V(lambda e: e.tensor_mul(out=tm[:, 5:6], in0=tm[:, 0:1], in1=tm[:, 0:1]))
        V(lambda e: e.tensor_sub(out=tm[:, 6:7], in0=tm[:, 4:5], in1=tm[:, 5:6]))
        A_(lambda e: e.activation(out=tm[:, 7:8], in_=tm[:, 6:7], func=AF.Ln, bias=np.float32(EPS1)),
           ww=[w(sv)])
        A_(lambda e: e.activation(out=tm[:, 1:2], in_=tm[:, 7:8], func=AF.Exp, scale=np.float32(-0.5)))
        V(lambda e: e.scalar_tensor_tensor(out=tm[:, 8:9], in0=tm[:, 0:1],
          scalar=scol(cA, True), in1=statq[:, 1:2], op0=ALU.mult, op1=ALU.add), ww=[w(sa)])
        V(lambda e: e.tensor_mul(out=tm[:, 8:9], in0=tm[:, 1:2], in1=tm[:, 8:9]))
        V(lambda e: e.tensor_scalar(out=tm[:, 3:4], in0=tm[:, 8:9], scalar1=1.0,
          scalar2=scol(cs), op0=ALU.mult, op1=ALU.add))
        V(lambda e: e.tensor_scalar_mul(out=tm[:, 3:4], in0=tm[:, 3:4], scalar1=INVD))
        V(lambda e: e.scalar_tensor_tensor(out=tm[:, 8:9], in0=tm[:, 5:6], scalar=scol(cA2),
          in1=statq2[:, 1:2], op0=ALU.mult, op1=ALU.add))
        V(lambda e: e.scalar_tensor_tensor(out=tm[:, 9:10], in0=tm[:, 0:1], scalar=-2.0,
          in1=statq[:, 2:3], op0=ALU.mult, op1=ALU.mult))
        V(lambda e: e.tensor_add(out=tm[:, 8:9], in0=tm[:, 8:9], in1=tm[:, 9:10]))
        V(lambda e: e.tensor_mul(out=tm[:, 9:10], in0=tm[:, 1:2], in1=tm[:, 1:2]))
        V(lambda e: e.tensor_mul(out=tm[:, 8:9], in0=tm[:, 8:9], in1=tm[:, 9:10]))
        V(lambda e: e.scalar_tensor_tensor(out=tm[:, 9:10], in0=tm[:, 0:1],
          scalar=scol(cAs, True), in1=statq[:, 3:4], op0=ALU.mult, op1=ALU.add))
        V(lambda e: e.tensor_mul(out=tm[:, 9:10], in0=tm[:, 1:2], in1=tm[:, 9:10]))
        V(lambda e: e.scalar_tensor_tensor(out=tm[:, 8:9], in0=tm[:, 9:10], scalar=2.0,
          in1=tm[:, 8:9], op0=ALU.mult, op1=ALU.add))
        V(lambda e: e.tensor_scalar(out=tm[:, 8:9], in0=tm[:, 8:9], scalar1=1.0,
          scalar2=scol(cs2), op0=ALU.mult, op1=ALU.add))
        if rms_rho:
            V(lambda e: e.tensor_scalar_mul(out=tm[:, 9:10], in0=tm[:, 8:9], scalar1=INVD))
            A_(lambda e: e.activation(out=tm[:, 7:8], in_=tm[:, 9:10], func=AF.Ln, bias=np.float32(EPSR)),
               ww=[w(sv)])
        else:
            V(lambda e: e.tensor_mul(out=tm[:, 9:10], in0=tm[:, 3:4], in1=tm[:, 3:4]))
            V(lambda e: e.scalar_tensor_tensor(out=tm[:, 9:10], in0=tm[:, 8:9], scalar=INVD,
              in1=tm[:, 9:10], op0=ALU.mult, op1=ALU.subtract))
            A_(lambda e: e.activation(out=tm[:, 7:8], in_=tm[:, 9:10], func=AF.Ln, bias=np.float32(EPS1)),
               ww=[w(sv)])
        A_(lambda e: e.activation(out=tm[:, 2:3], in_=tm[:, 7:8], func=AF.Exp, scale=np.float32(-0.5)))
        V(lambda e: e.tensor_mul(out=tmS[:, R_K1:R_K1 + 1], in0=tm[:, 2:3], in1=tm[:, 1:2]),
          ww=[w(sa)])
        V(lambda e: e.scalar_tensor_tensor(out=tmS[:, R_K2:R_K2 + 1], in0=tmS[:, R_K1:R_K1 + 1],
          scalar=-1.0, in1=tm[:, 0:1], op0=ALU.mult, op1=ALU.mult))
        V(lambda e: e.tensor_copy(out=tmS[:, R_K3:R_K3 + 1], in_=tm[:, 2:3]))
        V(lambda e: e.scalar_tensor_tensor(out=tmS[:, R_K4:R_K4 + 1], in0=tm[:, 2:3],
          scalar=-1.0, in1=tm[:, 3:4], op0=ALU.mult, op1=ALU.mult))
        V(lambda e: e.tensor_copy(out=tmS[:, R_R:R_R + 1], in_=tm[:, 1:2]))
        V(lambda e: e.scalar_tensor_tensor(out=tmS[:, R_NMR:R_NMR + 1], in0=tm[:, 1:2],
          scalar=-1.0, in1=tm[:, 0:1], op0=ALU.mult, op1=ALU.mult))

    def coeff_rows(c, sv, sa, spe, tmS, rows, reps, b, repkinds):
        op, w = c.op, c.w
        op("tensor", lambda e: e.transpose(pb[4][0:8, 384:512], tmS[:], P["identf"][:]),
           waits=[c.w(sv)], incs=[(spe, 1)])
        op("scalar", lambda e, b=b: e.activation(out=rows[0:8, 128 * b:128 * (b + 1)],
           in_=pb[4][0:8, 384:512], func=AF.Copy), waits=[w(spe)], incs=[(sa, 1)])
        for ri, rk in enumerate(repkinds):
            op("tensor", lambda e, b=b, ri=ri, rk=rk: e.matmul(
                pb[4][0:8, 128 * ri:128 * (ri + 1)], P["sel"][:, 8 * rk:8 * rk + 8],
                rows[0:8, 128 * b:128 * (b + 1)], start=True, stop=True),
                waits=[w(sa)], incs=[(spe, 1)])
        nrep = len(repkinds)
        ncols = reps.shape[1] // len(repkinds) if False else None
        op("scalar", lambda e, b=b, nrep=nrep: e.activation(
            out=reps[:].rearrange("p (k t) -> p k t", k=nrep)[:, :, 128 * b:128 * (b + 1)],
            in_=pb[4][0:8, 0:128 * nrep].rearrange("p (k t) -> p k t", t=128),
            func=AF.Copy), waits=[w(spe)], incs=[(sa, 1)])

    wst = contextlib.ExitStack()
    st = contextlib.ExitStack()
    wsb = {k: sb("w_" + k, [128, NCH * D], BF16, wst) for k in ("wq", "wk", "wv", "wg", "wo")}
    c_sb = sb("c_sb", [128, 8], F32, st)
    siluc = sb("siluc", [128, 8], F32, st)
    wstrip = sb("wstrip", [128, NCH * 512], F32, st)
    wstrip2 = sb("wstrip2", [128, NCH * 512], F32, st)
    sgn = sb("sgn", [128, 8], F32, st)
    absv = sb("absv", [128, 8], F32, st)
    scal_in = sb("scal_in", [128, 16], F32, st)
    prod8 = sb("prod8", [128, 8], F32, st)
    brow = sb("brow", [1, 2 * D], F32, st)
    browb = sb("browb", [1, 2 * D], BF16, st)
    P["corr_fbq"] = sb("corr_fbq", [128, D], BF16, st)
    P["corr_bup"] = sb("corr_bup", [128, 2 * D], BF16, st)
    corr3_s = dsc("corr3_s", [128, 3 * D], BF16)
    stage = sb("stage", [128, NCH * D], BF16, st)

    def setup(c):
        op, w = c.op, c.w
        sdg, sdh, sds = S["dgs"], S["dhs"], S["dss"]
        spe, sv, sa = S["pes"], S["vs"], S["as"]
        c.auto_tensor = (sv, sa)
        c.self_sems = {"vector": sv, "scalar": sa, "gpsimd": S["gs"]}
        c.bankmap = {}
        HW = lambda emit: op("sync", emit, incs=[(sdh, 16)])
        HW(lambda e: e.dma_start(out=P["GBLKc"][:], in_=cons_h["GBLKc"]))
        HW(lambda e: e.dma_start(out=P["identf"][:], in_=cons_h["identc"]))
        for k in FMV:
            HW(lambda e, k=k: e.dma_start(out=fm(k)[:], in_=fmv_h[k]))
        HW(lambda e: e.dma_start(out=P["fm_fbdown"][:], in_=fbdown_h))
        HW(lambda e: e.dma_start(out=P["fm_fbg"][:], in_=fbg_h))
        HW(lambda e: e.dma_start(out=c_sb[:], in_=c_h))
        op("vector", lambda e: e.memset(P["ones128"][:], 1.0), incs=[(sv, 1)])
        op("vector", lambda e: e.memset(P["onesb"][:], 1.0), incs=[(sv, 1)])
        GD = lambda emit: op("gpsimd", emit, incs=[(sdg, 16)])
        GD(lambda e: e.dma_start(out=P["DTc"][:], in_=cons_h["DTc"]))
        GD(lambda e: e.dma_start(out=P["GROWc"][:], in_=cons_h["GROWc"]))
        GD(lambda e: e.dma_start(out=P["GCOLW"][:], in_=cons_h["GCOLWc"]))
        GD(lambda e: e.dma_start(out=P["identb"][:], in_=cons_h["identc"]))
        GD(lambda e: e.dma_start(out=P["sel"][:], in_=cons_h["selc"]))
        GD(lambda e: e.dma_start(out=P["onese"][:], in_=cons_h["onesec"]))
        GD(lambda e: e.dma_start(out=P["selw"][:], in_=cons_h["selwc"]))
        GD(lambda e: e.dma_start(out=P["e3"][:], in_=cons_h["e3c"]))
        GD(lambda e: e.dma_start(out=P["gnwrep"][:], in_=gnwrep_h))
        GD(lambda e: e.dma_start(out=P["gnbrep"][:], in_=gnbrep_h))
        for k in ("wq", "wk", "wv", "wg", "wo"):
            GD(lambda e, k=k: e.dma_start(
                out=wsb[k][:].rearrange("p (c n) -> p c n", n=D), in_=W_h[k].rearrange("(c p) n -> p c n", p=128)))
        cdl, gdl = w(sdh), w(sdg)

        op("scalar", lambda e: e.activation(out=siluc[:], in_=c_sb[:], func=AF.Silu),
           waits=[cdl], incs=[(sa, 1)])
        strip_done = {}

        def strip_load(s_):
            ws_ = wstrip if s_ % 2 == 0 else wstrip2
            wsv_ = ws_[:].rearrange("p (c n) -> p c n", n=512)
            src_ = wmod_h[:, 512 * s_:512 * (s_ + 1)].rearrange(
                "(c p) n -> p c n", p=128)
            wmm = [(spe, spe.c)] if s_ >= 2 else []
            op("sync", lambda e, wsv_=wsv_, src_=src_: e.dma_start(
                out=wsv_[:, 0:3, :], in_=src_[:, 0:3, :]),
                waits=list(wmm), incs=[(S["dhw"], 16)])
            op("scalar", lambda e, wsv_=wsv_, src_=src_: e.dma_start(
                out=wsv_[:, 3:6, :], in_=src_[:, 3:6, :]),
                waits=list(wmm), incs=[(S["dhw2"], 16)])
            op("gpsimd", lambda e, wsv_=wsv_, src_=src_: e.dma_start(
                out=wsv_[:, 6:8, :], in_=src_[:, 6:8, :]),
                waits=list(wmm), incs=[(S["dhw3"], 16)])
            strip_done[s_] = [(S["dhw"], S["dhw"].c), (S["dhw2"], S["dhw2"].c),
                              (S["dhw3"], S["dhw3"].c)]

        strip_load(0)
        strip_load(1)
        for s_ in range(12):
            ws_ = wstrip if s_ % 2 == 0 else wstrip2
            for n4 in range(4):
                for k in range(NCH):
                    op("tensor", lambda e, s=s_, n4=n4, k=k, ws_=ws_: e.matmul(
                        pb[0][:, 4 * s + n4:4 * s + n4 + 1],
                        ws_[:, 512 * k + 128 * n4:512 * k + 128 * (n4 + 1)],
                        siluc[:, k:k + 1], start=(k == 0), stop=(k == NCH - 1)),
                        waits=(strip_done[s_] + [w(sa)]
                               if (n4 == 0 and k == 0) else []),
                        incs=[(spe, 1)] if (k == NCH - 1) else [])
            if s_ + 2 < 12:
                strip_load(s_ + 2)
        op("vector", lambda e: e.tensor_copy(out=P["mod"][:, 0:48], in_=pb[0][:, 0:48]),
           waits=[w(spe)], incs=[(sv, 1)])
        md = P["mod"]

        V = lambda emit, ww=(): op("vector", emit, waits=list(ww), incs=[(sv, 1)])
        A_ = lambda emit, ww=(): op("scalar", emit, waits=list(ww), incs=[(sa, 1)])

        def clamp(dst, lo):
            V(lambda e: e.tensor_scalar(out=sgn[:], in0=md[:, lo:lo + 8], scalar1=0.0,
              scalar2=None, op0=ALU.is_ge))
            V(lambda e: e.tensor_scalar(out=sgn[:], in0=sgn[:], scalar1=2.0, scalar2=-1.0,
              op0=ALU.mult, op1=ALU.add))
            A_(lambda e: e.activation(out=absv[:], in_=md[:, lo:lo + 8], func=AF.Abs),
               ww=[w(sv)])
            V(lambda e: e.tensor_scalar_max(out=absv[:], in0=absv[:], scalar1=0.25),
              ww=[w(sa)])
            V(lambda e: e.tensor_mul(out=dst[:], in0=sgn[:], in1=absv[:]))
        clamp(fm("ga"), 16)
        clamp(fm("gb"), 40)
        V(lambda e: e.tensor_copy(out=fm("sa")[:], in_=md[:, 0:8]))
        V(lambda e: e.tensor_copy(out=fm("sb_")[:], in_=md[:, 24:32]))
        V(lambda e: e.tensor_scalar_add(out=fm("A")[:], in0=md[:, 8:16], scalar1=1.0))
        V(lambda e: e.tensor_scalar_add(out=fm("B")[:], in0=md[:, 32:40], scalar1=1.0))
        V(lambda e: e.tensor_mul(out=fm("w1A")[:], in0=fm("A")[:], in1=fm("w1")[:]))

        svA = P["svA"][:].rearrange("p (c k) -> p c k", k=4)
        svA2 = P["svA2"][:].rearrange("p (c k) -> p c k", k=2)
        svB = P["svB"][:].rearrange("p (c k) -> p c k", k=4)
        svB2 = P["svB2"][:].rearrange("p (c k) -> p c k", k=2)
        V(lambda e: e.memset(P["sv1"][:], 1.0))
        for t3 in (svA, svA2, svB, svB2):
            V(lambda e, t3=t3: e.memset(t3[:, :, 0:1], 1.0))
        V(lambda e: e.tensor_copy(out=svA[:, :, 1:2], in_=fm("A")[:].unsqueeze(2)))
        V(lambda e: e.tensor_mul(out=svA[:, :, 2:3], in0=fm("A")[:].unsqueeze(2),
          in1=fm("A")[:].unsqueeze(2)))
        V(lambda e: e.tensor_mul(out=svA[:, :, 3:4], in0=fm("A")[:].unsqueeze(2),
          in1=fm("sa")[:].unsqueeze(2)))
        V(lambda e: e.tensor_copy(out=svA2[:, :, 1:2], in_=svA[:, :, 2:3]))
        V(lambda e: e.tensor_copy(out=svB[:, :, 1:2], in_=fm("B")[:].unsqueeze(2)))
        V(lambda e: e.tensor_mul(out=svB[:, :, 2:3], in0=fm("B")[:].unsqueeze(2),
          in1=fm("B")[:].unsqueeze(2)))
        V(lambda e: e.tensor_mul(out=svB[:, :, 3:4], in0=fm("B")[:].unsqueeze(2),
          in1=fm("sb_")[:].unsqueeze(2)))
        V(lambda e: e.tensor_copy(out=svB2[:, :, 1:2], in_=svB[:, :, 2:3]))

        defs = [("A", None), ("sa", None), ("A", "A"), ("A", "sa"), ("sa", "sa"),
                ("B", None), ("sb_", None), ("B", "B"), ("B", "sb_"), ("sb_", "sb_")]
        for i, (a, b) in enumerate(defs):
            if b is None:
                V(lambda e, i=i, a=a: e.reduce_sum(out=scal_in[:, i:i + 1], in_=fm(a)[:],
                  axis=AXL.X))
            else:
                V(lambda e, a=a, b=b: e.tensor_mul(out=prod8[:], in0=fm(a)[:], in1=fm(b)[:]))
                V(lambda e, i=i: e.reduce_sum(out=scal_in[:, i:i + 1], in_=prod8[:], axis=AXL.X))
        V(lambda e: e.memset(scal_in[:, 10:16], 0.0))
        op("tensor", lambda e: e.matmul(pb[1][:, 0:16], P["ones128"][:], scal_in[:],
           start=True, stop=True), waits=[w(sv)], incs=[(spe, 1)])
        V(lambda e: e.tensor_copy(out=P["scal"][:], in_=pb[1][:, 0:16]), ww=[w(spe)])
        V(lambda e: e.tensor_scalar_mul(out=P["nscal"][:], in0=P["scal"][:], scalar1=-1.0))

        for nm, srcv in (("Arow", "A"), ("sarow", "sa"), ("Brow", "B"), ("sbrow", "sb_")):
            for hh in range(2):
                for k4 in range(4):
                    kc = 4 * hh + k4
                    op("tensor", lambda e, s=srcv, kc=kc, k4=k4: e.transpose(
                        pb[2][0:1, 128 * k4:128 * (k4 + 1)], fm(s)[:, kc:kc + 1],
                        P["identf"][:]), waits=[w(sv), w(sa)],
                        incs=[(spe, 1)] if k4 == 3 else [])
                A_(lambda e, nm=nm, hh=hh: e.activation(
                    out=P[nm][0:1, 512 * hh:512 * (hh + 1)], in_=pb[2][0:1, 0:512],
                    func=AF.Copy), ww=[w(spe)])

        v4 = P["vecs4"][:].rearrange("p (c k) -> p c k", k=4)
        V(lambda e: e.tensor_copy(out=v4[:, :, 0:1], in_=fm("w1A")[:].unsqueeze(2)))
        V(lambda e: e.tensor_mul(out=v4[:, :, 1:2], in0=fm("sa")[:].unsqueeze(2),
          in1=fm("w1")[:].unsqueeze(2)))
        V(lambda e: e.tensor_copy(out=v4[:, :, 2:3], in_=fm("w1")[:].unsqueeze(2)))
        V(lambda e: e.tensor_copy(out=v4[:, :, 3:4], in_=fm("b1")[:].unsqueeze(2)))
        for k in ("corr_q", "corr_k", "corr_v", "corr_g", "corr_fbq", "corr_bup"):
            V(lambda e, k=k: e.memset(P[k][:], 0.0))

        for kname in ("wq", "wk", "wv", "wg"):
            short = {"wq": "q", "wk": "k", "wv": "v", "wg": "g"}[kname]
            brn = {"wq": "bq", "wk": "bk", "wv": "bv", "wg": "bg_"}[kname]
            wv_ = wsb[kname][:].rearrange("p (c n) -> p c n", n=D)
            op("sync", lambda e, brn=brn: e.dma_start(out=brow[0:1, 0:D], in_=row_h[brn]),
               waits=[w(spe)], incs=[(sdh, 16)])
            V(lambda e: e.tensor_copy(out=browb[0:1, 0:D], in_=brow[0:1, 0:D]),
              ww=[w(sdh)])
            for half in range(2):
                for kc in range(NCH):
                    op("tensor", lambda e, wv_=wv_, h=half, kc=kc: e.matmul(
                        pb[3][0:4, 0:512], P["vecs4"][:, 4 * kc:4 * kc + 4],
                        wv_[:, kc, 512 * h:512 * (h + 1)],
                        start=(kc == 0), stop=False),
                        waits=[gdl, w(sv), w(sa)], incs=[])
                op("tensor", lambda e, h=half: e.matmul(
                    pb[3][0:4, 0:512], P["e3"][:],
                    browb[0:1, 512 * h:512 * (h + 1)],
                    start=False, stop=True), incs=[(spe, 1)])
                A_(lambda e, s=short, h=half: e.activation(
                    out=P["corr_" + s][0:4, 512 * h:512 * (h + 1)], in_=pb[3][0:4, 0:512],
                    func=AF.Copy), ww=[w(spe)])

        op("sync", lambda e: e.dma_start(out=brow[0:1, 0:D], in_=row_h["f_bq"]),
           waits=[w(spe)], incs=[(sdh, 16)])
        V(lambda e: e.tensor_copy(out=browb[0:1, 0:D], in_=brow[0:1, 0:D]), ww=[w(sdh)])
        for half in range(2):
            op("tensor", lambda e, h=half: e.matmul(
                pb[3][0:4, 0:512], P["e3"][:], browb[0:1, 512 * h:512 * (h + 1)],
                start=True, stop=True), incs=[(spe, 1)])
            A_(lambda e, h=half: e.activation(
                out=P["corr_fbq"][0:4, 512 * h:512 * (h + 1)], in_=pb[3][0:4, 0:512],
                func=AF.Copy), ww=[w(spe)])
        op("sync", lambda e: e.dma_start(out=brow[0:1, 0:2 * D], in_=row_h["bup"]),
           waits=[w(spe)], incs=[(sdh, 16)])
        V(lambda e: e.tensor_copy(out=browb[0:1, 0:2 * D], in_=brow[0:1, 0:2 * D]),
          ww=[w(sdh)])
        for q4 in range(4):
            op("tensor", lambda e, q4=q4: e.matmul(
                pb[3][0:4, 0:512], P["e3"][:], browb[0:1, 512 * q4:512 * (q4 + 1)],
                start=True, stop=True), incs=[(spe, 1)])
            A_(lambda e, q4=q4: e.activation(
                out=P["corr_bup"][0:4, 512 * q4:512 * (q4 + 1)], in_=pb[3][0:4, 0:512],
                func=AF.Copy), ww=[w(spe)])
        op("sync", lambda e: e.dma_start(out=corr3_s[:, 0:D], in_=P["corr_fbq"][:]),
           waits=[w(sa), w(sv)], incs=[(sds, 16)])
        op("sync", lambda e: e.dma_start(out=corr3_s[:, D:3 * D], in_=P["corr_bup"][:]),
           incs=[(sds, 16)])
        V(lambda e: e.memset(P["G"][:], 0.0))
        V(lambda e: e.memset(P["Gb"][:], 0.0))

        for n8 in range(NCH):
            op("gpsimd", lambda e, n=n8: e.dma_start(
                out=stage[:, 0:NPC * 128].rearrange("p (c n) -> p c n", n=128),
                in_=woutp_h[:, 128 * n:128 * (n + 1)].rearrange("(c p) n -> p c n", p=128)),
                waits=[w(sds)], incs=[(S["dgp"], 16)])
            op("sync", lambda e, n=n8: e.dma_start(out=woutbf_s[n], in_=stage[:, 0:NPC * 128]),
               waits=[(S["dgp"], S["dgp"].c)], incs=[(sds, 16)])
        for ee in range(8):
            op("gpsimd", lambda e, x=ee: e.dma_start(
                out=stage[:].rearrange("p (c n) -> p c n", n=D), in_=fWe_h[x].rearrange("(c p) n -> p c n", p=128)),
                waits=[w(sds)], incs=[(S["dgp"], 16)])
            op("sync", lambda e, x=ee: e.dma_start(out=webf_s[x], in_=stage[:]),
               waits=[(S["dgp"], S["dgp"].c)], incs=[(sds, 16)])
        op("sync", lambda e: e.nop(), waits=[w(sds), w(sdh), (S["dhw"], S["dhw"].c)])
        op("gpsimd", lambda e: e.nop(), waits=[w(sdg), (S["dgp"], S["dgp"].c)])
        op("vector", lambda e: e.nop(), waits=[w(sv)])
        op("scalar", lambda e: e.nop(), waits=[w(sa), (S["dhw2"], S["dhw2"].c)])
        op("gpsimd", lambda e: e.nop(), waits=[(S["dhw3"], S["dhw3"].c)])
        op("tensor", lambda e: e.nop(), waits=[w(spe)])

    with nc.Block() as blk:
        ctx.walk(blk, setup)
    st.close()

    p1s = contextlib.ExitStack()
    x_tm = sb("x_tm", [128, D], F32, p1s)
    x_fm = sb("x_fm", [128, NCH * TT], F32, p1s)
    x_sq = sb("x_sq", [128, NCH * TT], BF16, p1s)
    x_bf = sb("x_bf", [128, NCH * TT], BF16, p1s)
    h_fm = sb("h_fm", [128, NCH * TT], F32, p1s)
    h_fmB = sb("h_fmB", [128, NCH * TT], F32, p1s)
    xb1t = sb("xb1t", [128, NCH * TT], F32, p1s)
    xt_bf = sb("xt_bf", [128, NCH * TT], BF16, p1s)
    xt_bfB = sb("xt_bfB", [128, NCH * TT], BF16, p1s)
    qfm = sb("qfm", [128, NCH * TT], BF16, p1s)
    kfm = sb("kfm", [128, NCH * TT], BF16, p1s)
    ktm = sb("ktm", [128, 2 * D], BF16, p1s)
    vtm = sb("vtm", [128, 2 * D], BF16, p1s)
    gtm = sb("gtm", [128, 2 * D], BF16, p1s)
    rettm = sb("rettm", [128, 2 * D], BF16, p1s)
    qt_bf = sb("qt_bf", [128, 256], BF16, p1s)
    spt = sb("spt", [128, 512], BF16, p1s)
    intr = sb("intr", [128, 256], BF16, p1s)
    gn16 = sb("gn16", [128, 6 * 16], F32, p1s)
    wointm = sb("wointm", [128, 2 * D], BF16, p1s)
    woinfm = sb("woinfm", [128, NCH * TT], BF16, p1s)
    tmS1 = sb("tmS1", [128, 16], F32, p1s)
    tmT1 = sb("tmT1", [128, 32], F32, p1s)
    rows1 = sb("rows1", [128, TT], BF16, p1s)
    rows1B = sb("rows1B", [128, TT], BF16, p1s)
    reps1 = sb("reps1", [8, 3 * TT], BF16, p1s)
    reps1B = sb("reps1B", [8, 3 * TT], BF16, p1s)

    def p1(c):
        op, w = c.op, c.w
        spe, sv, sa = S["pe1"], S["v1"], S["a1"]
        sdh, sds = S["dh1"], S["ds1"]
        c.auto_tensor = ()
        c.self_sems = {"vector": sv, "scalar": sa, "gpsimd": S["g1"]}
        c.res = {}
        c.bankmap = {
            "t2a": "BK2", "t2b": "BK2", "t3b": "BK3", "s10": "BK5", "s10b": "BK5",
            "st4_0": "BK5", "st2_0": "BK5", "st4_1": "BK5", "st2_1": "BK5",
            "s9": "BK4", "s8": "BK4", "ssum": "BK4", "s6": "BK3", "s7": "BK3",
            "s12": "BK6", "s13": "BK7", "p0": "BK0", "p1": "BK1",
            "st0e": "BK7", "st1e": "BK6", "sx0e": "BK1", "sx1e": "BK0"}
        wsv = {k: wsb[k][:].rearrange("p (c n) -> p c n", n=D) for k in wsb}
        xfmv = x_fm[:].rearrange("p (c t) -> p c t", t=TT)
        xbfv = x_bf[:].rearrange("p (c t) -> p c t", t=TT)
        xsqv = x_sq[:].rearrange("p (c t) -> p c t", t=TT)
        hv_0 = h_fm[:].rearrange("p (c t) -> p c t", t=TT)
        hv_1 = h_fmB[:].rearrange("p (c t) -> p c t", t=TT)
        xb1v = xb1t[:].rearrange("p (c t) -> p c t", t=TT)
        xtv_0 = xt_bf[:].rearrange("p (c t) -> p c t", t=TT)
        xtv_1 = xt_bfB[:].rearrange("p (c t) -> p c t", t=TT)

        def pbuf(t):
            if t % 2 == 0:
                return hv_0, xtv_0, rows1, reps1, "h0", "xt0", "rw0", "rp0"
            return hv_1, xtv_1, rows1B, reps1B, "h1", "xt1", "rw1", "rp1"
        qv = qfm[:].rearrange("p (c t) -> p c t", t=TT)
        kv_ = kfm[:].rearrange("p (c t) -> p c t", t=TT)
        wfv = woinfm[:].rearrange("p (c t) -> p c t", t=TT)
        V = lambda emit, rd=(), wr=(), ww=(): op("vector", emit, waits=list(ww),
                                                incs=[(sv, 1)], reads=rd, writes=wr)
        A_ = lambda emit, rd=(), wr=(), ww=(): op("scalar", emit, waits=list(ww),
                                                 incs=[(sa, 1)], reads=rd, writes=wr)
        G1 = lambda emit, rd=(), wr=(), ww=(): op("gpsimd", emit, waits=list(ww),
                                                 incs=[(S["g1"], 1)], reads=rd,
                                                 writes=wr)

        def mmg(mms, rd=(), wr=(), extra=()):
            ws = c.dep_waits(rd, wr) + list(extra)
            for i, mm in enumerate(mms):
                op("tensor", mm, waits=ws if i == 0 else (),
                   incs=[(spe, 1)] if i == len(mms) - 1 else ())
            c.register(rd, wr, spe)

        op("vector", lambda e: e.memset(rows1[:], 0.0), incs=[(sv, 1)])
        op("vector", lambda e: e.memset(rows1B[:], 0.0), incs=[(sv, 1)])
        op("vector", lambda e: e.memset(tmS1[:], 0.0), incs=[(sv, 1)])
        op("vector", lambda e: e.memset(tmS1[:, R_ONE:R_ONE + 1], 1.0), incs=[(sv, 1)])
        op("vector", lambda e: e.memset(tmS1[:, 8 + R_ONE:8 + R_ONE + 1], 1.0),
           incs=[(sv, 1)])

        def head1(t):
            hv, xtv, rows1c, reps1c, hk, xtk, rwk, rpk = pbuf(t)
            for b in range(2):
                xk = "x_tm" if b == 0 else "xsq"
                xbuf = x_tm[:] if b == 0 else x_sq[:].bitcast(F32)
                op("sync", lambda e, t=t, b=b, xbuf=xbuf: e.dma_start(
                    out=xbuf,
                    in_=x_h[TT * t + 128 * b:TT * t + 128 * (b + 1), :]),
                    waits=c.dep_waits((), (xk,)), incs=[(sdh, 16)])
                c.register((), (xk,), sdh)
                for jp in range(4):
                    slot = "t2a" if jp % 2 == 0 else "t3b"
                    pbx = pb[2] if jp % 2 == 0 else pb[3]
                    mmg([(lambda e, j=2 * jp + q, pbx=pbx, q=q, xbuf=xbuf: e.transpose(
                            pbx[:, 128 * q:128 * (q + 1)],
                            xbuf[:, 128 * j:128 * (j + 1)], P["identf"][:]))
                         for q in range(2)],
                        rd=(xk,), wr=(slot,))
                    A_(lambda e, b=b, jp=jp, pbx=pbx: e.activation(
                        out=xfmv[:, 2 * jp:2 * jp + 2, 128 * b:128 * (b + 1)],
                        in_=pbx[:, 0:256].rearrange("p (c t) -> p c t", t=128),
                        func=AF.Copy), rd=(slot,), wr=("xfm",))
            A_(lambda e: e.activation(out=x_sq[:], in_=x_fm[:], func=AF.Square),
               rd=("xfm",), wr=("xsq",))
            V(lambda e: e.tensor_copy(out=x_bf[:], in_=x_fm[:]), rd=("xfm",), wr=("xbf",))
            op("sync", lambda e, t=t: e.dma_start(out=xfm_s[t], in_=x_fm[:]),
               waits=c.dep_waits(("xfm",), ()), incs=[(S["dsA1"], 16)])
            c.register(("xfm",), (), S["dsA1"])

            for b in range(2):
                so = 8 * b
                mmg([(lambda e, b=b, kc=kc, so=so: e.matmul(
                        pb[5][:, so:so + 4], xbfv[:, kc, 128 * b:128 * (b + 1)],
                        P["svA"][:, 4 * kc:4 * kc + 4],
                        start=(kc == 0), stop=(kc == NCH - 1))) for kc in range(NCH)],
                    rd=("xbf",), wr=("st4_%d" % b,))
                mmg([(lambda e, b=b, kc=kc, so=so: e.matmul(
                        pb[5][:, so + 4:so + 6], xsqv[:, kc, 128 * b:128 * (b + 1)],
                        P["svA2"][:, 2 * kc:2 * kc + 2],
                        start=(kc == 0), stop=(kc == NCH - 1))) for kc in range(NCH)],
                    rd=("xsq",), wr=("st2_%d" % b,))

        def headB_gen(t):
            hv, xtv, rows1c, reps1c, hk, xtk, rwk, rpk = pbuf(t)
            for b in range(2):
                so = 8 * b
                ks4 = "st4_%d" % b
                ks2 = "st2_%d" % b
                ktm_ = "tmT%d" % b
                ksm_ = "tmS%d" % b
                tm = tmT1[:, 16 * b:16 * (b + 1)]
                ts_ = tmS1[:, 8 * b:8 * (b + 1)]
                V(lambda e, so=so: e.tensor_scalar_mul(out=tm[:, 0:1],
                  in0=pb[5][:, so:so + 1],
                  scalar1=INVD), rd=(ks4,), wr=(ktm_,))
                yield
                V(lambda e, so=so: e.tensor_scalar_mul(out=tm[:, 4:5],
                  in0=pb[5][:, so + 4:so + 5],
                  scalar1=INVD), rd=(ks2,), wr=(ktm_,))
                yield
                V(lambda e: e.tensor_mul(out=tm[:, 5:6], in0=tm[:, 0:1], in1=tm[:, 0:1]),
                  rd=(ktm_,), wr=(ktm_,))
                yield
                V(lambda e: e.tensor_sub(out=tm[:, 6:7], in0=tm[:, 4:5], in1=tm[:, 5:6]),
                  rd=(ktm_,), wr=(ktm_,))
                yield
                A_(lambda e: e.activation(out=tm[:, 7:8], in_=tm[:, 6:7], func=AF.Ln,
                   bias=np.float32(EPS1)), rd=(ktm_,), wr=(ktm_,))
                yield
                A_(lambda e: e.activation(out=tm[:, 1:2], in_=tm[:, 7:8], func=AF.Exp,
                   scale=np.float32(-0.5)), rd=(ktm_,), wr=(ktm_,))
                yield
                V(lambda e: e.scalar_tensor_tensor(out=tm[:, 8:9], in0=tm[:, 0:1],
                  scalar=scol("cA", True), in1=pb[5][:, so + 1:so + 2],
                  op0=ALU.mult, op1=ALU.add),
                  rd=(ktm_, ks4), wr=(ktm_,))
                yield
                V(lambda e: e.tensor_mul(out=tm[:, 8:9], in0=tm[:, 1:2], in1=tm[:, 8:9]),
                  rd=(ktm_,), wr=(ktm_,))
                yield
                V(lambda e: e.tensor_scalar(out=tm[:, 3:4], in0=tm[:, 8:9], scalar1=1.0,
                  scalar2=scol("cs"), op0=ALU.mult, op1=ALU.add), rd=(ktm_,), wr=(ktm_,))
                yield
                V(lambda e: e.tensor_scalar_mul(out=tm[:, 3:4], in0=tm[:, 3:4], scalar1=INVD),
                  rd=(ktm_,), wr=(ktm_,))
                yield
                V(lambda e: e.scalar_tensor_tensor(out=tm[:, 8:9], in0=tm[:, 5:6],
                  scalar=scol("cA2"), in1=pb[5][:, so + 5:so + 6],
                  op0=ALU.mult, op1=ALU.add),
                  rd=(ktm_, ks2), wr=(ktm_,))
                yield
                V(lambda e: e.scalar_tensor_tensor(out=tm[:, 9:10], in0=tm[:, 0:1],
                  scalar=-2.0, in1=pb[5][:, so + 2:so + 3], op0=ALU.mult, op1=ALU.mult),
                  rd=(ktm_, ks4), wr=(ktm_,))
                yield
                V(lambda e: e.tensor_add(out=tm[:, 8:9], in0=tm[:, 8:9], in1=tm[:, 9:10]),
                  rd=(ktm_,), wr=(ktm_,))
                yield
                V(lambda e: e.tensor_mul(out=tm[:, 9:10], in0=tm[:, 1:2], in1=tm[:, 1:2]),
                  rd=(ktm_,), wr=(ktm_,))
                yield
                V(lambda e: e.tensor_mul(out=tm[:, 8:9], in0=tm[:, 8:9], in1=tm[:, 9:10]),
                  rd=(ktm_,), wr=(ktm_,))
                yield
                V(lambda e: e.scalar_tensor_tensor(out=tm[:, 9:10], in0=tm[:, 0:1],
                  scalar=scol("cAs", True), in1=pb[5][:, so + 3:so + 4],
                  op0=ALU.mult, op1=ALU.add),
                  rd=(ktm_, ks4), wr=(ktm_,))
                yield
                V(lambda e: e.tensor_mul(out=tm[:, 9:10], in0=tm[:, 1:2], in1=tm[:, 9:10]),
                  rd=(ktm_,), wr=(ktm_,))
                yield
                V(lambda e: e.scalar_tensor_tensor(out=tm[:, 8:9], in0=tm[:, 9:10],
                  scalar=2.0, in1=tm[:, 8:9], op0=ALU.mult, op1=ALU.add),
                  rd=(ktm_,), wr=(ktm_,))
                yield
                V(lambda e: e.tensor_scalar(out=tm[:, 8:9], in0=tm[:, 8:9], scalar1=1.0,
                  scalar2=scol("cs2"), op0=ALU.mult, op1=ALU.add), rd=(ktm_,), wr=(ktm_,))
                yield
                V(lambda e: e.tensor_mul(out=tm[:, 9:10], in0=tm[:, 3:4], in1=tm[:, 3:4]),
                  rd=(ktm_,), wr=(ktm_,))
                yield
                V(lambda e: e.scalar_tensor_tensor(out=tm[:, 9:10], in0=tm[:, 8:9],
                  scalar=INVD, in1=tm[:, 9:10], op0=ALU.mult, op1=ALU.subtract),
                  rd=(ktm_,), wr=(ktm_,))
                yield
                A_(lambda e: e.activation(out=tm[:, 7:8], in_=tm[:, 9:10], func=AF.Ln,
                   bias=np.float32(EPS1)), rd=(ktm_,), wr=(ktm_,))
                yield
                A_(lambda e: e.activation(out=tm[:, 2:3], in_=tm[:, 7:8], func=AF.Exp,
                   scale=np.float32(-0.5)), rd=(ktm_,), wr=(ktm_,))
                yield
                V(lambda e: e.tensor_mul(out=ts_[:, R_K1:R_K1 + 1], in0=tm[:, 2:3],
                  in1=tm[:, 1:2]), rd=(ktm_,), wr=(ksm_,))
                yield
                V(lambda e: e.scalar_tensor_tensor(out=ts_[:, R_K2:R_K2 + 1],
                  in0=ts_[:, R_K1:R_K1 + 1], scalar=-1.0, in1=tm[:, 0:1],
                  op0=ALU.mult, op1=ALU.mult), rd=(ktm_, ksm_), wr=(ksm_,))
                yield
                V(lambda e: e.tensor_copy(out=ts_[:, R_K3:R_K3 + 1], in_=tm[:, 2:3]),
                  rd=(ktm_,), wr=(ksm_,))
                yield
                V(lambda e: e.scalar_tensor_tensor(out=ts_[:, R_K4:R_K4 + 1],
                  in0=tm[:, 2:3], scalar=-1.0, in1=tm[:, 3:4], op0=ALU.mult, op1=ALU.mult),
                  rd=(ktm_,), wr=(ksm_,))
                yield
                V(lambda e: e.tensor_copy(out=ts_[:, R_R:R_R + 1], in_=tm[:, 1:2]),
                  rd=(ktm_,), wr=(ksm_,))
                yield
                V(lambda e: e.scalar_tensor_tensor(out=ts_[:, R_NMR:R_NMR + 1],
                  in0=tm[:, 1:2], scalar=-1.0, in1=tm[:, 0:1], op0=ALU.mult, op1=ALU.mult),
                  rd=(ktm_,), wr=(ksm_,))
                yield

            for b in range(2):
                ksm_ = "tmS%d" % b
                mmg([lambda e: e.transpose(pb[4][0:8, 384:512], tmS1[:, 8 * b:8 * (b + 1)],
                     P["identf"][:])],
                    rd=(ksm_,), wr=("s9",))
                yield
                A_(lambda e, b=b: e.activation(out=rows1c[0:8, 128 * b:128 * (b + 1)],
                   in_=pb[4][0:8, 384:512], func=AF.Copy), rd=("s9",), wr=(rwk,))
                yield
                mmg([(lambda e, b=b, ri=ri, rk=rk: e.matmul(
                        pb[4][0:8, 128 * ri:128 * (ri + 1)], P["sel"][:, 8 * rk:8 * rk + 8],
                        rows1c[0:8, 128 * b:128 * (b + 1)], start=True, stop=True))
                     for ri, rk in enumerate((R_R, R_NMR, R_K1))],
                    rd=(rwk,), wr=("s8",))
                yield
                A_(lambda e, b=b: e.activation(
                    out=reps1c[:].rearrange("p (k t) -> p k t", k=3)[:, :, 128 * b:128 * (b + 1)],
                    in_=pb[4][0:8, 0:384].rearrange("p (k t) -> p k t", t=128),
                    func=AF.Copy), rd=("s8",), wr=(rpk,))
                yield

            mmg([lambda e: e.matmul(pb[3][:, 0:TT], P["onesb"][0:1, 0:128],
                 reps1c[0:1, 2 * TT:3 * TT], start=True, stop=True)],
                rd=(rpk,), wr=("s6",))
            yield
            mmg([lambda e: e.matmul(pb[3][:, 256:256 + TT], P["onesb"][0:1, 0:128],
                 reps1c[0:1, 0:TT], start=True, stop=True)],
                rd=(rpk,), wr=("s7",))
            yield
            mmg([lambda e: e.matmul(pb[5][:, 256:256 + TT], P["onesb"][0:1, 0:128],
                 reps1c[0:1, TT:2 * TT], start=True, stop=True)],
                rd=(rpk,), wr=("s10b",))
            yield
            for kc in range(NCH):
                V(lambda e, kc=kc: e.scalar_tensor_tensor(
                    out=xtv[:, kc, :], in0=xfmv[:, kc, :], scalar=fm("w1A")[:, kc:kc + 1],
                    in1=pb[3][:, 0:TT], op0=ALU.mult, op1=ALU.mult),
                    rd=("xfm", "s6"), wr=(xtk,))
                yield

        def headB2(t):
            hv, xtv, rows1c, reps1c, hk, xtk, rwk, rpk = pbuf(t)
            for kc in range(NCH):
                V(lambda e, kc=kc: e.scalar_tensor_tensor(
                    out=hv[:, kc, :], in0=xfmv[:, kc, :], scalar=fm("A")[:, kc:kc + 1],
                    in1=pb[3][:, 256:256 + TT], op0=ALU.mult, op1=ALU.mult),
                    rd=("xfm", "s7"), wr=(hk,))
                yield
                V(lambda e, kc=kc: e.scalar_tensor_tensor(
                    out=hv[:, kc, :], in0=pb[5][:, 256:256 + TT],
                    scalar=fm("A")[:, kc:kc + 1], in1=hv[:, kc, :],
                    op0=ALU.mult, op1=ALU.add),
                    rd=("s10b", hk), wr=(hk,))
                yield
                V(lambda e, kc=kc: e.tensor_scalar(
                    out=hv[:, kc, :], in0=hv[:, kc, :], scalar1=1.0,
                    scalar2=fm("sa")[:, kc:kc + 1], op0=ALU.mult, op1=ALU.add),
                    rd=(hk,), wr=(hk,))
                yield

        def tail1(t, gB=None):
            hv, xtv, rows1c, reps1c, hk, xtk, rwk, rpk = pbuf(t)

            def adv(n):
                if gB is not None:
                    for _ in range(n):
                        next(gB, None)
            for pi, (wn, corr, sc_, dstv, dk) in enumerate(
                    (("wq", "corr_q", 0.125, qv, "q"), ("wk", "corr_k", 1.0, kv_, "k"))):
                for oc in range(NCH):
                    slot = "s12" if (pi * NCH + oc) % 2 == 0 else "s13"
                    off = 0 if slot == "s12" else 0
                    bank = pb[6] if slot == "s12" else pb[7]
                    mmg([(lambda e, wn=wn, oc=oc, kc=kc, bank=bank: e.matmul(
                            bank[:, 0:TT], wsv[wn][:, kc, 128 * oc:128 * (oc + 1)],
                            xtv[:, kc, :], start=(kc == 0), stop=False))
                         for kc in range(NCH)] + [
                         (lambda e, corr=corr, oc=oc, bank=bank: e.matmul(
                            bank[:, 0:TT], P[corr][:, 128 * oc:128 * (oc + 1)],
                            rows1c[:, 0:TT], start=False, stop=True))],
                        rd=(xtk, rwk), wr=(slot,))
                    A_(lambda e, dstv=dstv, oc=oc, sc_=sc_, bank=bank: e.activation(
                        out=dstv[:, oc, :], in_=bank[:, 0:TT], func=AF.Identity,
                        scale=np.float32(sc_)), rd=(slot,), wr=(dk,))
                    adv(4)
            for wn, corr, dstt, fn, dk in (("wv", "corr_v", vtm, AF.Copy, "v"),
                                           ("wg", "corr_g", gtm, AF.Silu, "g")):
                for b in range(2):
                    for hf in range(2):
                        slot = "p0" if hf == 0 else "p1"
                        bank = pb[0] if hf == 0 else pb[1]
                        mmg([(lambda e, wn=wn, b=b, hf=hf, kc=kc, bank=bank: e.matmul(
                                bank[:, 0:512], xtv[:, kc, 128 * b:128 * (b + 1)],
                                wsv[wn][:, kc, 512 * hf:512 * (hf + 1)],
                                start=(kc == 0), stop=False)) for kc in range(NCH)] + [
                             (lambda e, corr=corr, b=b, hf=hf, bank=bank: e.matmul(
                                bank[:, 0:512], rows1c[:, 128 * b:128 * (b + 1)],
                                P[corr][:, 512 * hf:512 * (hf + 1)],
                                start=False, stop=True))],
                            rd=(xtk, rwk), wr=(slot,))
                    for hf in range(2):
                        slot = "p0" if hf == 0 else "p1"
                        bank = pb[0] if hf == 0 else pb[1]
                        A_(lambda e, dstt=dstt, b=b, hf=hf, fn=fn, bank=bank: e.activation(
                            out=dstt[:, D * b + 512 * hf:D * b + 512 * (hf + 1)],
                            in_=bank[:, 0:512], func=fn), rd=(slot,), wr=(dk,))
                        adv(4)
            for b in range(2):
                for jq in range(2):
                    slot = "t2a" if jq == 0 else "t2b"
                    off = 0 if jq == 0 else 512
                    mmg([(lambda e, b=b, j=4 * jq + q, off=off, q=q: e.transpose(
                            pb[2][:].bitcast(BF16)[:, off + 128 * q:off + 128 * (q + 1)],
                            kv_[:, j, 128 * b:128 * (b + 1)], P["identb"][:]))
                         for q in range(4)],
                        rd=("k",), wr=(slot,))
                    A_(lambda e, b=b, jq=jq, off=off: e.activation(
                        out=ktm[:, D * b + 512 * jq:D * b + 512 * (jq + 1)],
                        in_=pb[2][:].bitcast(BF16)[:, off:off + 512], func=AF.Copy),
                        rd=(slot,), wr=("ktm",))
                    adv(4)
                V(lambda e, b=b: e.tensor_mul(
                    out=ktm[:, D * b:D * (b + 1)], in0=ktm[:, D * b:D * (b + 1)],
                    in1=P["GCOLW"][:]), rd=("ktm",), wr=("ktm",))

        def retp(b, jp):
            j0 = 2 * jp
            for hh in range(2):
                bk = pb[7] if hh == 0 else pb[6]
                mmg([(lambda e, b=b, j=j0 + dj, hh=hh, bk=bk, dj=dj: e.matmul(
                        bk[:, 256 + 128 * dj:384 + 128 * dj],
                        kv_[64 * hh:64 * (hh + 1), j, 128 * b:128 * (b + 1)],
                        qv[64 * hh:64 * (hh + 1), j, 128 * b:128 * (b + 1)],
                        start=True, stop=True, tile_position=(64 * hh, 0)))
                     for dj in range(2)],
                    rd=("k", "q"), wr=("st0e" if hh == 0 else "st1e",))
            for hh in range(2):
                bk = pb[7] if hh == 0 else pb[6]
                V(lambda e, jp=jp, hh=hh, bk=bk: e.tensor_mul(
                    out=spt[:, 256 * hh:256 * (hh + 1)],
                    in0=bk[:, 256:512],
                    in1=P["DTc"][:, 128 * (8 * hh + 2 * jp):
                                  128 * (8 * hh + 2 * jp + 2)]),
                    rd=("st0e" if hh == 0 else "st1e",), wr=("spt",))
            V(lambda e, b=b, j0=j0: e.tensor_tensor(
                out=qt_bf[:].rearrange("p (c t) -> p c t", t=128),
                in0=qv[:, j0:j0 + 2, 128 * b:128 * (b + 1)],
                in1=P["GROWc"][:, 128 * j0:128 * (j0 + 2)].rearrange(
                    "p (c t) -> p c t", t=128),
                op=ALU.mult), rd=("q",), wr=("qt",))
            mmg([(lambda e, b=b, j=j0 + dj, dj=dj, hh=hh: e.matmul(
                    pb[7][:, 128 * dj + 64 * hh:128 * dj + 64 * (hh + 1)],
                    spt[:, 256 * hh + 128 * dj:256 * hh + 128 * (dj + 1)],
                    vtm[:, D * b + 64 * (2 * j + hh):
                         D * b + 64 * (2 * j + hh + 1)],
                    start=True, stop=True))
                 for dj in range(2) for hh in range(2)],
                rd=("spt", "v"), wr=("s13",))
            A_(lambda e: e.activation(out=intr[:], in_=pb[7][:, 0:256],
               func=AF.Copy), rd=("s13",), wr=("intr",))
            for hh in range(2):
                bk = pb[1] if hh == 0 else pb[0]
                mmg([(lambda e, j=j0 + dj, hh=hh, bk=bk, dj=dj: e.matmul(
                        bk[:, 384 + 64 * dj:448 + 64 * dj],
                        qt_bf[64 * hh:64 * (hh + 1), 128 * dj:128 * (dj + 1)],
                        P["Gb"][64 * hh:64 * (hh + 1), 64 * j:64 * (j + 1)],
                        start=True, stop=True, tile_position=(64 * hh, 0)))
                     for dj in range(2)],
                    rd=("qt", "Gb"), wr=("sx0e" if hh == 0 else "sx1e",))
            for hh in range(2):
                bk = pb[1] if hh == 0 else pb[0]
                rv = rettm[:, D * b + 128 * j0:D * b + 128 * (j0 + 2)]
                V(lambda e, rv=rv, hh=hh, bk=bk: e.tensor_tensor(
                    out=rv.rearrange("p (c t) -> p c t", t=128)[
                        :, :, 64 * hh:64 * (hh + 1)],
                    in0=intr[:].rearrange("p (c t) -> p c t", t=128)[
                        :, :, 64 * hh:64 * (hh + 1)],
                    in1=bk[:, 384:512].rearrange("p (c t) -> p c t", t=64),
                    op=ALU.add),
                    rd=("intr", "sx0e" if hh == 0 else "sx1e"), wr=("rettm",))
            mmg([(lambda e, b=b, j=j0 + dj, hh=hh, dj=dj: e.matmul(
                    pb[4][64 * hh:64 * (hh + 1), 128 + 64 * dj:192 + 64 * dj],
                    ktm[:, D * b + 64 * (2 * j + hh):D * b + 64 * (2 * j + hh + 1)],
                    vtm[:, D * b + 64 * (2 * j + hh):D * b + 64 * (2 * j + hh + 1)],
                    start=True, stop=True, tile_position=(0, 64 * hh)))
                 for dj in range(2) for hh in range(2)],
                rd=("ktm", "v"), wr=("ssum",))
            for dj in range(2):
                V(lambda e, j=j0 + dj, dj=dj: e.scalar_tensor_tensor(
                    out=P["G"][:, 64 * j:64 * (j + 1)],
                    in0=P["G"][:, 64 * j:64 * (j + 1)],
                    scalar=P["GBLKc"][:, j:j + 1],
                    in1=pb[4][:, 128 + 64 * dj:192 + 64 * dj],
                    op0=ALU.mult, op1=ALU.add), rd=("ssum", "G"), wr=("G",))
            V(lambda e, j0=j0: e.tensor_copy(
                out=P["Gb"][:, 64 * j0:64 * (j0 + 2)],
                in_=P["G"][:, 64 * j0:64 * (j0 + 2)]), rd=("G",), wr=("Gb",))


        def gn1(b):
            rb_ = rettm[:, D * b:D * (b + 1)]
            gtmp = wointm[:, D * b:D * (b + 1)]
            V(lambda e, rb_=rb_: e.reduce_sum(out=gn16[:, 0:16],
              in_=rb_.rearrange("p (h d) -> p h d", d=HD), axis=AXL.X),
              rd=("rettm",), wr=("gn16",))
            yield
            A_(lambda e, rb_=rb_: e.activation(out=gtmp[:], in_=rb_, func=AF.Square),
               rd=("rettm",), wr=("wointm",))
            yield
            V(lambda e: e.reduce_sum(out=gn16[:, 16:32],
              in_=gtmp[:].rearrange("p (h d) -> p h d", d=HD), axis=AXL.X),
              rd=("wointm",), wr=("gn16",))
            yield
            V(lambda e: e.tensor_scalar_mul(out=gn16[:, 32:48], in0=gn16[:, 0:16],
              scalar1=1.0 / HD), rd=("gn16",), wr=("gn16",))
            yield
            V(lambda e: e.tensor_mul(out=gn16[:, 48:64], in0=gn16[:, 32:48],
              in1=gn16[:, 32:48]), rd=("gn16",), wr=("gn16",))
            yield
            V(lambda e: e.scalar_tensor_tensor(out=gn16[:, 48:64], in0=gn16[:, 16:32],
              scalar=1.0 / HD, in1=gn16[:, 48:64], op0=ALU.mult, op1=ALU.subtract),
              rd=("gn16",), wr=("gn16",))
            yield
            A_(lambda e: e.activation(out=gn16[:, 64:80], in_=gn16[:, 48:64], func=AF.Ln,
               bias=np.float32(EPS1)), rd=("gn16",), wr=("gn16",))
            yield
            A_(lambda e: e.activation(out=gn16[:, 64:80], in_=gn16[:, 64:80], func=AF.Exp,
               scale=np.float32(-0.5)), rd=("gn16",), wr=("gn16",))
            yield
            V(lambda e: e.tensor_mul(out=gn16[:, 80:96], in0=gn16[:, 32:48],
              in1=gn16[:, 64:80]), rd=("gn16",), wr=("gn16",))
            yield
            V(lambda e, rb_=rb_, gtmp=gtmp: e.tensor_tensor(
                out=gtmp[:].rearrange("p (h d) -> p h d", d=HD),
                in0=rb_.rearrange("p (h d) -> p h d", d=HD),
                in1=gn16[:, 64:80].unsqueeze(2).broadcast_to([128, 16, HD]),
                op=ALU.mult), rd=("rettm", "gn16"), wr=("wointm",))
            yield
            V(lambda e, gtmp=gtmp: e.tensor_tensor(
                out=gtmp[:].rearrange("p (h d) -> p h d", d=HD),
                in0=gtmp[:].rearrange("p (h d) -> p h d", d=HD),
                in1=gn16[:, 80:96].unsqueeze(2).broadcast_to([128, 16, HD]),
                op=ALU.subtract), rd=("wointm", "gn16"), wr=("wointm",))
            yield
            V(lambda e, gtmp=gtmp: e.tensor_mul(
                out=gtmp[:], in0=gtmp[:], in1=P["gnwrep"][:]),
                rd=("wointm",), wr=("wointm",))
            yield
            V(lambda e, gtmp=gtmp: e.tensor_add(
                out=gtmp[:], in0=gtmp[:], in1=P["gnbrep"][:]),
                rd=("wointm",), wr=("wointm",))
            yield
            V(lambda e, b=b, gtmp=gtmp: e.tensor_mul(
                out=wointm[:, D * b:D * (b + 1)], in0=gtmp[:],
                in1=gtm[:, D * b:D * (b + 1)]), rd=("wointm", "g"), wr=("wointm",))
            yield

        def woin1(b):
            for jq in range(2):
                slot = "t2a" if jq == 0 else "t2b"
                off = 0 if jq == 0 else 512
                mmg([(lambda e, b=b, j=4 * jq + q, off=off, q=q: e.transpose(
                        pb[2][:].bitcast(BF16)[:, off + 128 * q:off + 128 * (q + 1)],
                        wointm[:, D * b + 128 * j:D * b + 128 * (j + 1)], P["identb"][:]))
                     for q in range(4)],
                    rd=("wointm",), wr=(slot,))
                A_(lambda e, b=b, jq=jq, off=off: e.activation(
                    out=wfv[:, 4 * jq:4 * jq + 4, 128 * b:128 * (b + 1)],
                    in_=pb[2][:].bitcast(BF16)[:, off:off + 512].rearrange(
                        "p (c t) -> p c t", t=128),
                    func=AF.Copy), rd=(slot,), wr=("woinfm",))


        def tailD(t):
            for b in range(2):
                for jp in range(4):
                    retp(b, jp)

        def tailE(t):
            hv, xtv, rows1c, reps1c, hk, xtk, rwk, rpk = pbuf(t)
            woin1(0)
            woin1(1)
            for oc in range(NCH):
                slot = "s12" if oc % 2 == 0 else "s13"
                bank = pb[6] if oc % 2 == 0 else pb[7]
                mmg([(lambda e, oc=oc, kc=kc, bank=bank: e.matmul(
                        bank[:, 0:TT], wsv["wo"][:, kc, 128 * oc:128 * (oc + 1)],
                        wfv[:, kc, :], start=(kc == 0), stop=(kc == NCH - 1)))
                     for kc in range(NCH)],
                    rd=("woinfm",), wr=(slot,))
                V(lambda e, oc=oc, bank=bank: e.scalar_tensor_tensor(
                    out=xb1v[:, oc, :], in0=bank[:, 0:TT],
                    scalar=fm("bo")[:, oc:oc + 1], in1=hv[:, oc, :],
                    op0=ALU.add, op1=ALU.add), rd=(slot, hk), wr=("xb1",))
            op("sync", lambda e, t=t: e.dma_start(out=xb1_s[t], in_=xb1t[:]),
               waits=c.dep_waits(("xb1",), ()), incs=[(sds, 16)])
            c.register(("xb1",), (), sds)

        def gnall():
            for _ in gn1(0):
                yield
            for _ in gn1(1):
                yield

        def chain2(ga, gb):
            if ga is not None:
                for _ in ga:
                    yield
            if gb is not None:
                for _ in gb:
                    yield

        NT1 = NTILES or NT
        head1(0)
        for _ in headB_gen(0):
            pass
        for _ in headB2(0):
            pass
        for t in range(NT1):
            if t + 1 < NT1:
                head1(t + 1)
            if t > 0:
                tailD(t - 1)
            gB = chain2(gnall() if t > 0 else None,
                        headB_gen(t + 1) if t + 1 < NT1 else None)
            tail1(t, gB)
            for _ in gB:
                pass
            if t > 0:
                tailE(t - 1)
            if t + 1 < NT1:
                for _ in headB2(t + 1):
                    pass
        tailD(NT1 - 1)
        for _ in gnall():
            pass
        tailE(NT1 - 1)
        op("sync", lambda e: e.nop(), waits=[w(sds), (S["dsA1"], S["dsA1"].c)])
        op("vector", lambda e: e.nop(), waits=[w(sv)])
        op("scalar", lambda e: e.nop(), waits=[w(sa)])
        op("tensor", lambda e: e.nop(), waits=[w(spe)])


    if phases >= 1:
        with nc.Block() as blk:
            ctx.walk(blk, p1)
    p1s.close()
    wst.close()

    p2s = contextlib.ExitStack()
    TT2 = 512
    winp_sb = sb("winp_sb", [128, NCH * 2 * NPD], BF16, p2s)
    brow2 = sb("brow2", [1, 512], F32, p2s)
    b2bf = sb("b2bfs", [128, NCH], BF16, p2s)
    xfc0 = sb("xfc0", [128, TT2], F32, p2s)
    xfc1 = sb("xfc1", [128, TT2], F32, p2s)
    xb12 = sb("xb12", [128, NCH * TT2], F32, p2s)
    xb_bf = sb("xb_bf", [128, NCH * TT2], BF16, p2s)
    x1c0 = sb("x1c0", [128, TT2], F32, p2s)
    x1c1 = sb("x1c1", [128, TT2], F32, p2s)
    xt2 = sb("xt2", [128, NCH * TT2], BF16, p2s)
    s_bf = sb("s_bf", [128, NPC * TT2], BF16, p2s)
    su_bf = sb("su_bf", [128, TT2], BF16, p2s)
    woutb = sb("woutb", [128, NPC * 128], BF16, p2s)
    woutb2 = sb("woutb2", [128, NPC * 128], BF16, p2s)
    tmS2 = sb("tmS2", [128, 32], F32, p2s)
    tmT2 = sb("tmT2", [128, 16], F32, p2s)
    rows2 = sb("rows2", [128, TT2], BF16, p2s)
    reps2 = sb("reps2", [8, 2 * TT2], BF16, p2s)
    hm2 = sb("hm2", [128, TT2], F32, p2s)

    def p2(c):
        op, w = c.op, c.w
        spe, sv, sa = S["pe2"], S["v2"], S["a2"]
        sdh, sds, sdg = S["dh2"], S["ds2"], S["dg2"]
        c.auto_tensor = ()
        c.self_sems = {"vector": sv, "scalar": sa, "gpsimd": S["g2"]}
        c.res = {}
        c.bankmap = {
            "s4": "BK2", "s6": "BK3", "s7": "BK4", "s9": "BK2", "s8": "BK2",
            "glu": "BK0", "glg": "BK1", "wo_e": "BK6", "wo_o": "BK7",
            "s10_0": "BK5", "s10b_0": "BK5", "s10_1": "BK5", "s10b_1": "BK5",
            "s10_2": "BK5", "s10b_2": "BK5", "s10_3": "BK5", "s10b_3": "BK5",
            "sC": "BK2"}
        winv = winp_sb[:].rearrange("p (c n) -> p c n", n=2 * NPD)
        xbsq = s_bf[:, 0:NCH * TT2]
        rowt2 = x1c0
        xbv = xb12[:].rearrange("p (c t) -> p c t", t=TT2)
        xt2v = xt2[:].rearrange("p (c t) -> p c t", t=TT2)
        sv3 = s_bf[:].rearrange("p (c t) -> p c t", t=TT2)
        xb4v = xb12[:].rearrange("p (c f t) -> p f c t", f=2, t=TT)
        V = lambda emit, rd=(), wr=(), ww=(): op("vector", emit, waits=list(ww),
                                                incs=[(sv, 1)], reads=rd, writes=wr)
        A_ = lambda emit, rd=(), wr=(), ww=(): op("scalar", emit, waits=list(ww),
                                                 incs=[(sa, 1)], reads=rd, writes=wr)

        def mmg(mms, rd=(), wr=(), extra=()):
            ws = c.dep_waits(rd, wr) + list(extra)
            for i, mm in enumerate(mms):
                op("tensor", mm, waits=ws if i == 0 else (),
                   incs=[(spe, 1)] if i == len(mms) - 1 else ())
            c.register(rd, wr, spe)

        op("vector", lambda e: e.memset(tmS2[:], 0.0), incs=[(sv, 1)])
        op("gpsimd", lambda e: e.dma_start(
            out=winp_sb[:].rearrange("p (c n) -> p c n", n=2 * NPD),
            in_=winp_h.rearrange("(c p) n -> p c n", p=128)), incs=[(sdg, 16)])
        c.register((), ("winp",), sdg)
        op("gpsimd", lambda e: e.dma_start(out=b2bf[:], in_=b2bf_h),
           incs=[(S["dgb2"], 16)])
        c.register((), ("b2bf",), S["dgb2"])
        for strip in range(11):
            mmg([(lambda e, s=strip, kc=kc: e.matmul(
                    pb[3][0:1, 0:512], b2bf[:, kc:kc + 1],
                    winv[:, kc, 512 * s:512 * (s + 1)],
                    start=(kc == 0), stop=(kc == NCH - 1))) for kc in range(NCH)],
                rd=("winp", "b2bf"), wr=("s6",))
            op("sync", lambda e, s=strip: e.dma_start(
                out=brow2[0:1, :], in_=binp_h[0:1, 512 * s:512 * (s + 1)]),
                waits=c.dep_waits((), ("brow2",)), incs=[(sdh, 16)])
            c.register((), ("brow2",), sdh)
            V(lambda e: e.tensor_add(out=rowt2[0:1, :], in0=pb[3][0:1, 0:512],
              in1=brow2[0:1, :]), rd=("s6", "brow2"), wr=("rowt2",))
            mmg([(lambda e, s=strip, j4=j4: e.transpose(
                    pb[2][:, 4 * s + j4:4 * s + j4 + 1],
                    rowt2[0:1, 128 * j4:128 * (j4 + 1)], P["identf"][0:1, 0:1]))
                 for j4 in range(4)],
                rd=("rowt2",), wr=("s4",))
        V(lambda e: e.tensor_copy(out=P["cuv"][:], in_=pb[2][:, 0:2 * NPC]),
          rd=("s4",), wr=("cuv",))

        NT2 = (NTILES or NT) // 2
        for t in range(max(NT2, 1)):
            for f in range(2):
                op("sync", lambda e, t=t, f=f: e.dma_start(
                    out=xb4v[:, f], in_=xb1_s[2 * t + f].rearrange(
                        "p (c t) -> p c t", t=TT)),
                    waits=c.dep_waits((), ("xb1",)), incs=[(sdh, 16)])
                c.register((), ("xb1",), sdh)
            A_(lambda e: e.activation(out=xbsq[:], in_=xb12[:], func=AF.Square),
               rd=("xb1",), wr=("s_",))
            V(lambda e: e.tensor_copy(out=xb_bf[:], in_=xb12[:]), rd=("xb1",), wr=("xbbf",))
            for bb in range(4):
                so = 8 * bb
                ks4 = "s10_%d" % bb
                ks2 = "s10b_%d" % bb
                mmg([(lambda e, bb=bb, kc=kc, so=so: e.matmul(
                        pb[5][:, so:so + 2],
                        xb_bf[:].rearrange("p (c t) -> p c t", t=TT2)[
                            :, kc, 128 * bb:128 * (bb + 1)],
                        P["sv1"][:, 2 * kc:2 * kc + 2],
                        start=(kc == 0), stop=(kc == NCH - 1))) for kc in range(NCH)],
                    rd=("xbbf",), wr=(ks4,))
                mmg([(lambda e, bb=bb, kc=kc, so=so: e.matmul(
                        pb[5][:, so + 2:so + 4],
                        xbsq.rearrange("p (c t) -> p c t", t=TT2)[
                            :, kc, 128 * bb:128 * (bb + 1)],
                        P["sv1"][:, 2 * kc:2 * kc + 2],
                        start=(kc == 0), stop=(kc == NCH - 1))) for kc in range(NCH)],
                    rd=("s_",), wr=(ks2,))
            for bb in range(4):
                so = 8 * bb
                ks4 = "s10_%d" % bb
                ks2 = "s10b_%d" % bb
                ts_ = tmS2[:, 8 * bb:8 * (bb + 1)]
                tm = tmT2[:, 4 * bb:4 * (bb + 1)]
                ktm_ = "tmT%d" % bb
                ksm_ = "tmS%d" % bb
                V(lambda e, so=so, ts_=ts_: e.tensor_scalar_mul(
                  out=ts_[:, 0:1], in0=pb[5][:, so:so + 1],
                  scalar1=INVD), rd=(ks4,), wr=(ksm_,))
                V(lambda e, so=so, tm=tm: e.tensor_scalar_mul(
                  out=tm[:, 0:1], in0=pb[5][:, so + 2:so + 3],
                  scalar1=INVD), rd=(ks2,), wr=(ktm_,))
                V(lambda e, tm=tm, ts_=ts_: e.tensor_mul(
                  out=tm[:, 1:2], in0=ts_[:, 0:1],
                  in1=ts_[:, 0:1]), rd=(ksm_, ktm_), wr=(ktm_,))
                V(lambda e, tm=tm: e.tensor_sub(out=tm[:, 2:3], in0=tm[:, 0:1],
                  in1=tm[:, 1:2]), rd=(ktm_,), wr=(ktm_,))
                A_(lambda e, tm=tm: e.activation(out=tm[:, 3:4], in_=tm[:, 2:3],
                   func=AF.Ln,
                   bias=np.float32(EPS1)), rd=(ktm_,), wr=(ktm_,))
                A_(lambda e, tm=tm, ts_=ts_: e.activation(
                   out=ts_[:, 1:2], in_=tm[:, 3:4], func=AF.Exp,
                   scale=np.float32(-0.5)), rd=(ktm_,), wr=(ksm_,))
            for bb in range(4):
                ksm_ = "tmS%d" % bb
                mmg([(lambda e, bb=bb: e.transpose(
                        pb[2][0:8, 384:512], tmS2[:, 8 * bb:8 * (bb + 1)],
                        P["identf"][:]))],
                    rd=(ksm_,), wr=("s9",))
                A_(lambda e, bb=bb: e.activation(
                   out=rows2[0:8, 128 * bb:128 * (bb + 1)],
                   in_=pb[2][0:8, 384:512], func=AF.Copy), rd=("s9",), wr=("rows",))
                mmg([(lambda e, bb=bb, ri=ri: e.matmul(
                        pb[2][0:8, 128 * ri:128 * (ri + 1)],
                        P["sel"][:, 8 * ri:8 * ri + 8],
                        rows2[0:8, 128 * bb:128 * (bb + 1)], start=True, stop=True))
                     for ri in range(2)],
                    rd=("rows",), wr=("s8",))
                A_(lambda e, bb=bb: e.activation(
                    out=reps2[:].rearrange("p (k t) -> p k t", k=2)[
                        :, :, 128 * bb:128 * (bb + 1)],
                    in_=pb[2][0:8, 0:256].rearrange("p (k t) -> p k t", t=128),
                    func=AF.Copy), rd=("s8",), wr=("reps",))
            mmg([lambda e: e.matmul(pb[3][:, 0:TT2], P["onesb"][0:1, 0:128],
                 reps2[0:1, 0:TT2], start=True, stop=True)], rd=("reps",), wr=("s6",))
            mmg([lambda e: e.matmul(pb[4][:, 0:TT2], P["onesb"][0:1, 0:128],
                 reps2[0:1, TT2:2 * TT2], start=True, stop=True)],
                rd=("reps",), wr=("s7",))
            for kc in range(NCH):
                V(lambda e, kc=kc: e.tensor_sub(out=hm2[:], in0=xbv[:, kc, :],
                  in1=pb[3][:, 0:TT2]), rd=("xb1", "s6"), wr=("hm2",))
                V(lambda e, kc=kc: e.scalar_tensor_tensor(
                    out=xt2v[:, kc, :], in0=hm2[:], scalar=fm("w2")[:, kc:kc + 1],
                    in1=pb[4][:, 0:TT2], op0=ALU.mult, op1=ALU.mult),
                    rd=("hm2", "s7"), wr=("xt2",))

            for j in range(NPC):
                for ug in range(2):
                    bank = pb[0] if ug == 0 else pb[1]
                    slot = "glu" if ug == 0 else "glg"
                    mmg([(lambda e, j=j, ug=ug, kc=kc, bank=bank: e.matmul(
                            bank[:, 0:TT2],
                            winv[:, kc, 256 * j + 128 * ug:256 * j + 128 * (ug + 1)],
                            xt2v[:, kc, :], start=(kc == 0), stop=(kc == NCH - 1)))
                         for kc in range(NCH)],
                        rd=("xt2", "winp"), wr=(slot,))
                A_(lambda e, j=j: e.activation(
                    out=su_bf[:], in_=pb[0][:, 0:TT2], func=AF.Silu,
                    bias=P["cuv"][:, 2 * j:2 * j + 1]),
                    rd=("glu", "cuv"), wr=("su",))
                V(lambda e, j=j: e.scalar_tensor_tensor(
                    out=sv3[:, j, :], in0=pb[1][:, 0:TT2],
                    scalar=P["cuv"][:, 2 * j + 1:2 * j + 2], in1=su_bf[:],
                    op0=ALU.add, op1=ALU.mult), rd=("glg", "su", "cuv"), wr=("s_",))

            for oc in range(NCH):
                wb_ = woutb if oc % 2 == 0 else woutb2
                wk_ = "woutb" if oc % 2 == 0 else "woutb2"
                xfc = xfc0 if oc % 2 == 0 else xfc1
                xfk = "xf0" if oc % 2 == 0 else "xf1"
                op("gpsimd", lambda e, oc=oc, wb_=wb_: e.dma_start(
                    out=wb_[:], in_=woutbf_s[oc]),
                    waits=c.dep_waits((), (wk_,)) + [(sdg, sdg.c)],
                    incs=[(sdg, 16)])
                c.register((), (wk_,), sdg)
                for f in range(2):
                    op("sync", lambda e, t=t, f=f, oc=oc, xfc=xfc: e.dma_start(
                        out=xfc[:, TT * f:TT * (f + 1)],
                        in_=xfm_s[2 * t + f][:, TT * oc:TT * (oc + 1)]),
                        waits=c.dep_waits((), (xfk,)), incs=[(sdh, 16)])
                    c.register((), (xfk,), sdh)
                bank = pb[6] if oc % 2 == 0 else pb[7]
                slot = "wo_e" if oc % 2 == 0 else "wo_o"
                mmg([(lambda e, j=j, wb_=wb_, bank=bank: e.matmul(
                        bank[:, 0:TT2], wb_[:, 128 * j:128 * (j + 1)], sv3[:, j, :],
                        start=(j == 0), stop=(j == NPC - 1))) for j in range(NPC)],
                    rd=("s_", wk_), wr=(slot,))
                V(lambda e, oc=oc, bank=bank: e.scalar_tensor_tensor(
                    out=hm2[:], in0=bank[:, 0:TT2],
                    scalar=fm("b_out")[:, oc:oc + 1], in1=xbv[:, oc, :],
                    op0=ALU.add, op1=ALU.add), rd=(slot, "xb1"), wr=("hm2",))
                x1c = x1c0 if oc % 2 == 0 else x1c1
                x1k = "x1_0" if oc % 2 == 0 else "x1_1"
                V(lambda e, oc=oc, xfc=xfc, x1c=x1c: e.scalar_tensor_tensor(
                    out=x1c[:], in0=hm2[:], scalar=fm("ga")[:, oc:oc + 1],
                    in1=xfc[:], op0=ALU.mult, op1=ALU.add),
                    rd=("hm2", xfk), wr=(x1k,))
                for f in range(2):
                    op("sync", lambda e, t=t, f=f, oc=oc, x1c=x1c: e.dma_start(
                        out=x1_s[2 * t + f][:, TT * oc:TT * (oc + 1)],
                        in_=x1c[:, TT * f:TT * (f + 1)]),
                        waits=c.dep_waits((x1k,), ()), incs=[(sds, 16)])
                    c.register((x1k,), (), sds)
        op("sync", lambda e: e.nop(), waits=[w(sds)])
        op("vector", lambda e: e.nop(), waits=[w(sv)])
        op("scalar", lambda e: e.nop(), waits=[w(sa)])
        op("tensor", lambda e: e.nop(), waits=[w(spe)])
        op("gpsimd", lambda e: e.nop(),
           waits=[w(sdg), (S["dgb2"], S["dgb2"].c)])


    if phases >= 2:
        with nc.Block() as blk:
            ctx.walk(blk, p2)
    p2s.close()

    p3s = contextlib.ExitStack()
    fWq_sb = sb("fWq_sb", [128, NCH * D], BF16, p3s)
    fWd_sb = sb("fWd_sb", [128, NCH * HD], BF16, p3s)
    fWu_sb = sb("fWu_sb", [HD, 2 * D], BF16, p3s)
    fWo_sb = sb("fWo_sb", [128, NCH * D], BF16, p3s)
    x13 = sb("x13", [128, NCH * TT], F32, p3s)
    x1sq = sb("x1sq", [128, NCH * TT], BF16, p3s)
    x1_bf = sb("x1_bf", [128, NCH * TT], BF16, p3s)
    h2_fm = sb("h2_fm", [128, NCH * TT], F32, p3s)
    h2_fm2 = sb("h2_fm2", [128, NCH * TT], F32, p3s)
    corr_fbq3 = sb("corr_fbq3", [128, D], BF16, p3s)
    corr_bup3 = sb("corr_bup3", [128, 2 * D], BF16, p3s)
    hn_bf = sb("hn_bf", [128, NCH * TT], BF16, p3s)
    z_bf = sb("z_bf", [HD, TT], BF16, p3s)
    qtm3 = sb("qtm3", [128, 2 * D], BF16, p3s)
    kvtm = sb("kvtm", [128, 4 * D], BF16, p3s)
    prod3 = sb("prod3", [128, D], BF16, p3s)
    prod3g = sb("prod3g", [128, D], BF16, p3s)
    prod3b = sb("prod3b", [128, D], BF16, p3s)
    prod3gb = sb("prod3gb", [128, D], BF16, p3s)
    av3g = sb("av3g", [128, D], F32, p3s)
    att3 = sb("att3", [128, 256], F32, p3s)
    attw = sb("attw", [128, 256], BF16, p3s)
    av3 = sb("av3", [128, D], F32, p3s)
    aotm = sb("aotm", [128, 2 * D], BF16, p3s)
    aotmB = sb("aotmB", [128, 2 * D], BF16, p3s)
    aofm = sb("aofm", [128, NCH * TT], BF16, p3s)
    xf1t = sb("xf1t", [128, NCH * TT], F32, p3s)
    tmS3 = sb("tmS3", [128, 8], F32, p3s)
    tmT3 = sb("tmT3", [128, 16], F32, p3s)
    rows3 = sb("rows3", [128, TT], BF16, p3s)
    reps3 = sb("reps3", [8, 3 * TT], BF16, p3s)
    hm3 = sb("hm3", [128, TT], F32, p3s)
    s163 = sb("s163", [128, 48], F32, p3s)

    def p3(c):
        op, w = c.op, c.w
        spe, sv, sa, sg = S["pe3"], S["v3"], S["a3"], S["g3"]
        sdh, sds, sdg = S["dh3"], S["ds3"], S["dg3"]
        c.auto_tensor = ()
        c.self_sems = {"vector": sv, "scalar": sa, "gpsimd": sg}
        c.res = {}
        c.bankmap = {
            "s10": "BK5", "s10b": "BK5", "s12h": "BK4", "s9": "BK4", "s8": "BK4",
            "s7": "BK3", "s6r": "BK3", "s12": "BK6", "s13": "BK7",
            "p0": "BK0", "p1": "BK1", "t2a": "BK2", "t2b": "BK2"}
        x1v = x13[:].rearrange("p (c t) -> p c t", t=TT)
        h2v_0 = h2_fm[:].rearrange("p (c t) -> p c t", t=TT)
        h2v_1 = h2_fm2[:].rearrange("p (c t) -> p c t", t=TT)
        hnv = hn_bf[:].rearrange("p (c t) -> p c t", t=TT)
        aofv = aofm[:].rearrange("p (c t) -> p c t", t=TT)
        xf1v = xf1t[:].rearrange("p (c t) -> p c t", t=TT)
        wqv = fWq_sb[:].rearrange("p (c n) -> p c n", n=D)
        wdv = fWd_sb[:].rearrange("p (c n) -> p c n", n=HD)
        wov = fWo_sb[:].rearrange("p (c n) -> p c n", n=D)
        V = lambda emit, rd=(), wr=(), ww=(): op("vector", emit, waits=list(ww),
                                                incs=[(sv, 1)], reads=rd, writes=wr)
        A_ = lambda emit, rd=(), wr=(), ww=(): op("scalar", emit, waits=list(ww),
                                                 incs=[(sa, 1)], reads=rd, writes=wr)
        G_ = lambda emit, rd=(), wr=(), ww=(): op("gpsimd", emit, waits=list(ww),
                                                 incs=[(sg, 1)], reads=rd, writes=wr)

        def mmg(mms, rd=(), wr=(), extra=()):
            ws = c.dep_waits(rd, wr) + list(extra)
            for i, mm in enumerate(mms):
                op("tensor", mm, waits=ws if i == 0 else (),
                   incs=[(spe, 1)] if i == len(mms) - 1 else ())
            c.register(rd, wr, spe)

        for nm, hh, key in (("fWq_sb", fWq_h, "wq3"), ("fWd_sb", fWdown_h, "wd3"),
                            ("fWo_sb", fWo_h, "wo3")):
            tile = {"fWq_sb": fWq_sb, "fWd_sb": fWd_sb, "fWo_sb": fWo_sb}[nm]
            nn_ = {"fWq_sb": D, "fWd_sb": HD, "fWo_sb": D}[nm]
            op("gpsimd", lambda e, tile=tile, hh=hh, nn_=nn_: e.dma_start(
                out=tile[:].rearrange("p (c n) -> p c n", n=nn_),
                in_=hh.rearrange("(c p) n -> p c n", p=128)), incs=[(sdg, 16)])
            c.register((), (key,), sdg)
        op("gpsimd", lambda e: e.dma_start(out=fWu_sb[:], in_=fWup_h), incs=[(sdg, 16)])
        c.register((), ("wu3",), sdg)
        op("sync", lambda e: e.dma_start(out=corr_fbq3[:], in_=corr3_s[:, 0:D]),
           incs=[(sdh, 16)])
        c.register((), ("cfb",), sdh)
        op("sync", lambda e: e.dma_start(out=corr_bup3[:], in_=corr3_s[:, D:3 * D]),
           incs=[(sdh, 16)])
        c.register((), ("cbu",), sdh)
        op("vector", lambda e: e.memset(rows3[:], 0.0), incs=[(sv, 1)])
        op("vector", lambda e: e.memset(tmS3[:], 0.0), incs=[(sv, 1)])
        op("vector", lambda e: e.memset(tmS3[:, R_ONE:R_ONE + 1], 1.0), incs=[(sv, 1)])

        def head3(t):
            h2v = h2v_0 if t % 2 == 0 else h2v_1
            h2k = "h2a" if t % 2 == 0 else "h2b"
            op("sync", lambda e, t=t: e.dma_start(out=x13[:], in_=x1_s[t]),
               waits=c.dep_waits((), ("x1",)), incs=[(sdh, 16)])
            c.register((), ("x1",), sdh)
            A_(lambda e: e.activation(out=x1sq[:], in_=x13[:], func=AF.Square),
               rd=("x1",), wr=("x1sq",))
            A_(lambda e: e.activation(out=x1_bf[:], in_=x13[:], func=AF.Copy),
               rd=("x1",), wr=("x1bf",))
            for b in range(2):
                mmg([(lambda e, b=b, kc=kc: e.matmul(
                        pb[5][:, 0:4],
                        x1_bf[:].rearrange("p (c t) -> p c t", t=TT)[
                            :, kc, 128 * b:128 * (b + 1)],
                        P["svB"][:, 4 * kc:4 * kc + 4],
                        start=(kc == 0), stop=(kc == NCH - 1))) for kc in range(NCH)],
                    rd=("x1bf",), wr=("s10",))
                mmg([(lambda e, b=b, kc=kc: e.matmul(
                        pb[5][:, 4:6],
                        x1sq[:].rearrange("p (c t) -> p c t", t=TT)[
                            :, kc, 128 * b:128 * (b + 1)],
                        P["svB2"][:, 2 * kc:2 * kc + 2],
                        start=(kc == 0), stop=(kc == NCH - 1))) for kc in range(NCH)],
                    rd=("x1sq",), wr=("s10b",))
                tm = tmT3
                V(lambda e: e.tensor_scalar_mul(out=tm[:, 0:1], in0=pb[5][:, 0:1],
                  scalar1=INVD), rd=("s10",), wr=("tmT",))
                V(lambda e: e.tensor_scalar_mul(out=tm[:, 4:5], in0=pb[5][:, 4:5],
                  scalar1=INVD), rd=("s10b",), wr=("tmT",))
                V(lambda e: e.tensor_mul(out=tm[:, 5:6], in0=tm[:, 0:1], in1=tm[:, 0:1]),
                  rd=("tmT",), wr=("tmT",))
                V(lambda e: e.tensor_sub(out=tm[:, 6:7], in0=tm[:, 4:5], in1=tm[:, 5:6]),
                  rd=("tmT",), wr=("tmT",))
                A_(lambda e: e.activation(out=tm[:, 7:8], in_=tm[:, 6:7], func=AF.Ln,
                   bias=np.float32(EPS1)), rd=("tmT",), wr=("tmT",))
                A_(lambda e: e.activation(out=tm[:, 1:2], in_=tm[:, 7:8], func=AF.Exp,
                   scale=np.float32(-0.5)), rd=("tmT",), wr=("tmT",))
                V(lambda e: e.scalar_tensor_tensor(out=tm[:, 8:9], in0=tm[:, 5:6],
                  scalar=scol("cB2"), in1=pb[5][:, 5:6], op0=ALU.mult, op1=ALU.add),
                  rd=("tmT", "s10b"), wr=("tmT",))
                V(lambda e: e.scalar_tensor_tensor(out=tm[:, 9:10], in0=tm[:, 0:1],
                  scalar=-2.0, in1=pb[5][:, 2:3], op0=ALU.mult, op1=ALU.mult),
                  rd=("tmT", "s10"), wr=("tmT",))
                V(lambda e: e.tensor_add(out=tm[:, 8:9], in0=tm[:, 8:9], in1=tm[:, 9:10]),
                  rd=("tmT",), wr=("tmT",))
                V(lambda e: e.tensor_mul(out=tm[:, 9:10], in0=tm[:, 1:2], in1=tm[:, 1:2]),
                  rd=("tmT",), wr=("tmT",))
                V(lambda e: e.tensor_mul(out=tm[:, 8:9], in0=tm[:, 8:9], in1=tm[:, 9:10]),
                  rd=("tmT",), wr=("tmT",))
                V(lambda e: e.scalar_tensor_tensor(out=tm[:, 9:10], in0=tm[:, 0:1],
                  scalar=scol("cBs", True), in1=pb[5][:, 3:4], op0=ALU.mult, op1=ALU.add),
                  rd=("tmT", "s10"), wr=("tmT",))
                V(lambda e: e.tensor_mul(out=tm[:, 9:10], in0=tm[:, 1:2], in1=tm[:, 9:10]),
                  rd=("tmT",), wr=("tmT",))
                V(lambda e: e.scalar_tensor_tensor(out=tm[:, 8:9], in0=tm[:, 9:10],
                  scalar=2.0, in1=tm[:, 8:9], op0=ALU.mult, op1=ALU.add),
                  rd=("tmT",), wr=("tmT",))
                V(lambda e: e.tensor_scalar(out=tm[:, 8:9], in0=tm[:, 8:9], scalar1=1.0,
                  scalar2=scol("csb2"), op0=ALU.mult, op1=ALU.add), rd=("tmT",), wr=("tmT",))
                V(lambda e: e.tensor_scalar_mul(out=tm[:, 8:9], in0=tm[:, 8:9],
                  scalar1=INVD), rd=("tmT",), wr=("tmT",))
                A_(lambda e: e.activation(out=tm[:, 7:8], in_=tm[:, 8:9], func=AF.Ln,
                   bias=np.float32(EPSR)), rd=("tmT",), wr=("tmT",))
                A_(lambda e: e.activation(out=tm[:, 2:3], in_=tm[:, 7:8], func=AF.Exp,
                   scale=np.float32(-0.5)), rd=("tmT",), wr=("tmT",))
                V(lambda e: e.tensor_copy(out=tmS3[:, R_K2:R_K2 + 1], in_=tm[:, 1:2]),
                  rd=("tmT",), wr=("tmS",))
                V(lambda e: e.scalar_tensor_tensor(out=tmS3[:, R_K3:R_K3 + 1],
                  in0=tm[:, 1:2], scalar=-1.0, in1=tm[:, 0:1], op0=ALU.mult, op1=ALU.mult),
                  rd=("tmT",), wr=("tmS",))
                V(lambda e: e.tensor_copy(out=tmS3[:, R_K4:R_K4 + 1], in_=tm[:, 2:3]),
                  rd=("tmT",), wr=("tmS",))
                mmg([lambda e: e.transpose(pb[4][0:8, 384:512], tmS3[:], P["identf"][:])],
                    rd=("tmS",), wr=("s9",))
                A_(lambda e, b=b: e.activation(out=rows3[0:8, 128 * b:128 * (b + 1)],
                   in_=pb[4][0:8, 384:512], func=AF.Copy), rd=("s9",), wr=("rows",))
                mmg([(lambda e, b=b, ri=ri, rk=rk: e.matmul(
                        pb[4][0:8, 128 * ri:128 * (ri + 1)], P["sel"][:, 8 * rk:8 * rk + 8],
                        rows3[0:8, 128 * b:128 * (b + 1)], start=True, stop=True))
                     for ri, rk in enumerate((R_K2, R_K3, R_K4))],
                    rd=("rows",), wr=("s8",))
                A_(lambda e, b=b: e.activation(
                    out=reps3[:].rearrange("p (k t) -> p k t", k=3)[
                        :, :, 128 * b:128 * (b + 1)],
                    in_=pb[4][0:8, 0:384].rearrange("p (k t) -> p k t", t=128),
                    func=AF.Copy), rd=("s8",), wr=("reps",))
            mmg([lambda e: e.matmul(pb[3][:, 0:TT], P["onesb"][0:1, 0:128],
                 reps3[0:1, 2 * TT:3 * TT], start=True, stop=True)],
                rd=("reps",), wr=("s6r",))
            mmg([lambda e: e.matmul(pb[3][:, 256:256 + TT], P["onesb"][0:1, 0:128],
                 reps3[0:1, 0:TT], start=True, stop=True)],
                rd=("reps",), wr=("s7",))
            mmg([lambda e: e.matmul(pb[5][:, 256:256 + TT], P["onesb"][0:1, 0:128],
                 reps3[0:1, TT:2 * TT], start=True, stop=True)],
                rd=("reps",), wr=("s10b",))
            for kc in range(NCH):
                V(lambda e, kc=kc, h2v=h2v: e.scalar_tensor_tensor(
                    out=h2v[:, kc, :], in0=x1v[:, kc, :], scalar=fm("B")[:, kc:kc + 1],
                    in1=pb[3][:, 256:256 + TT], op0=ALU.mult, op1=ALU.mult),
                    rd=("x1", "s7"), wr=(h2k,))
                V(lambda e, kc=kc, h2v=h2v: e.scalar_tensor_tensor(
                    out=h2v[:, kc, :], in0=pb[5][:, 256:256 + TT],
                    scalar=fm("B")[:, kc:kc + 1], in1=h2v[:, kc, :],
                    op0=ALU.mult, op1=ALU.add),
                    rd=("s10b", h2k), wr=(h2k,))
                V(lambda e, kc=kc, h2v=h2v: e.tensor_scalar(
                    out=h2v[:, kc, :], in0=h2v[:, kc, :], scalar1=1.0,
                    scalar2=fm("sb_")[:, kc:kc + 1], op0=ALU.mult, op1=ALU.add),
                    rd=(h2k,), wr=(h2k,))
                V(lambda e, kc=kc, h2v=h2v: e.scalar_tensor_tensor(
                    out=hnv[:, kc, :], in0=h2v[:, kc, :],
                    scalar=fm("gamma1")[:, kc:kc + 1], in1=pb[3][:, 0:TT],
                    op0=ALU.mult, op1=ALU.mult), rd=(h2k, "s6r"), wr=("hn",))

        def proj3(t, b):
            h2v = h2v_0 if t % 2 == 0 else h2v_1
            h2k = "h2a" if t % 2 == 0 else "h2b"
            for hf in range(2):
                bank = pb[0] if hf == 0 else pb[1]
                slot = "p0" if hf == 0 else "p1"
                mmg([(lambda e, b=b, hf=hf, kc=kc, bank=bank: e.matmul(
                        bank[:, 0:512], hnv[:, kc, 128 * b:128 * (b + 1)],
                        wqv[:, kc, 512 * hf:512 * (hf + 1)],
                        start=(kc == 0), stop=False)) for kc in range(NCH)] + [
                     (lambda e, b=b, hf=hf, bank=bank: e.matmul(
                        bank[:, 0:512], rows3[:, 128 * b:128 * (b + 1)],
                        corr_fbq3[:, 512 * hf:512 * (hf + 1)],
                        start=False, stop=True))],
                    rd=("hn", "rows", "wq3", "cfb"), wr=(slot,))
            for hf in range(2):
                bank = pb[0] if hf == 0 else pb[1]
                slot = "p0" if hf == 0 else "p1"
                A_(lambda e, b=b, hf=hf, bank=bank: e.activation(
                    out=qtm3[:, D * b + 512 * hf:D * b + 512 * (hf + 1)],
                    in_=bank[:, 0:512], func=AF.Identity, scale=np.float32(0.125)),
                    rd=(slot,), wr=("q3_%d" % b,))
            if b == 0:
                mmg([(lambda e, kc=kc: e.matmul(
                        pb[6][0:HD, 0:TT], wdv[:, kc, :], hnv[:, kc, :],
                        start=(kc == 0), stop=(kc == NCH - 1))) for kc in range(NCH)],
                    rd=("hn", "wd3"), wr=("s12",))
                A_(lambda e: e.activation(out=z_bf[:], in_=pb[6][0:HD, 0:TT],
                   func=AF.Identity, bias=P["fm_fbdown"][:]), rd=("s12",), wr=("z3",))
            for hf in range(2):
                for q2 in range(2):
                    bank = pb[0] if q2 == 0 else pb[1]
                    slot = "p0" if q2 == 0 else "p1"
                    mmg([(lambda e, b=b, hf=hf, q2=q2, bank=bank: e.matmul(
                            bank[:, 0:512], z_bf[:, 128 * b:128 * (b + 1)],
                            fWu_sb[:, 1024 * hf + 512 * q2:1024 * hf + 512 * (q2 + 1)],
                            start=True, stop=False)),
                         (lambda e, b=b, hf=hf, q2=q2, bank=bank: e.matmul(
                            bank[:, 0:512], rows3[:, 128 * b:128 * (b + 1)],
                            corr_bup3[:, 1024 * hf + 512 * q2:
                                       1024 * hf + 512 * (q2 + 1)],
                            start=False, stop=True))],
                        rd=("z3", "rows", "wu3", "cbu"), wr=(slot,))
                    A_(lambda e, b=b, hf=hf, q2=q2, bank=bank: e.activation(
                        out=kvtm[:, 2 * D * b + 1024 * hf + 512 * q2:
                                 2 * D * b + 1024 * hf + 512 * (q2 + 1)],
                        in_=bank[:, 0:512], func=AF.Copy), rd=(slot,), wr=("kv3_%d" % b,))

        def att_b(t, b):
            aotmc = aotm if t % 2 == 0 else aotmB
            aok = "aotm0" if t % 2 == 0 else "aotm1"
            qb = qtm3[:, D * b:D * (b + 1)].rearrange("p (h d) -> p h d", d=HD)
            kvb = kvtm[:, 2 * D * b:2 * D * (b + 1)]
            av = att3[:].rearrange("p (h g) -> p h g", g=H)
            for g in range(H):
                pr0 = (prod3, prod3g, prod3b, prod3gb)[g % 4]
                pk0 = ("prod", "prodg", "prodb", "prodgb")[g % 4]
                eng0 = V if g % 2 == 0 else G_
                eng0(lambda e, g=g, qb=qb, kvb=kvb, pr0=pr0: e.tensor_tensor(
                    out=pr0[:].rearrange("p (h d) -> p h d", d=HD), in0=qb,
                    in1=kvb[:, 128 * g:128 * g + HD].unsqueeze(1).broadcast_to(
                        [128, H, HD]), op=ALU.mult),
                    rd=("q3_%d" % b, "kv3_%d" % b), wr=(pk0,))
                V(lambda e, g=g, pr0=pr0: e.reduce_sum(
                    out=att3[:].rearrange("p (h g) -> p h g", g=H)[:, :, g:g + 1],
                    in_=pr0[:].rearrange("p (h d) -> p h d", d=HD), axis=AXL.X),
                    rd=(pk0,), wr=("att",))
            V(lambda e, av=av: e.reduce_max(out=s163[:, 0:16], in_=av, axis=AXL.X),
              rd=("att",), wr=("s163",))
            V(lambda e, av=av: e.tensor_tensor(
                out=att3[:].rearrange("p (h g) -> p h g", g=H), in0=av,
                in1=s163[:, 0:16].unsqueeze(2).broadcast_to([128, H, H]),
                op=ALU.subtract), rd=("att", "s163"), wr=("att",))
            A_(lambda e: e.activation(out=att3[:], in_=att3[:], func=AF.Exp),
               rd=("att",), wr=("att",))
            V(lambda e, av=av: e.reduce_sum(out=s163[:, 16:32], in_=av, axis=AXL.X),
              rd=("att",), wr=("s163",))
            V(lambda e: e.reciprocal(out=s163[:, 32:48], in_=s163[:, 16:32]),
              rd=("s163",), wr=("s163",))
            V(lambda e, av=av: e.tensor_tensor(
                out=attw[:].rearrange("p (h g) -> p h g", g=H), in0=av,
                in1=s163[:, 32:48].unsqueeze(2).broadcast_to([128, H, H]),
                op=ALU.mult), rd=("att", "s163"), wr=("attw",))
            for g in range(H):
                eng = V if g < 9 else G_
                pr = prod3 if g < 9 else prod3g
                acc = av3 if g < 9 else av3g
                pk = "prod" if g < 9 else "prodg"
                ak = "av" if g < 9 else "avg"
                if g in (0, 9):
                    eng(lambda e, g=g, kvb=kvb, acc=acc: e.tensor_tensor(
                        out=acc[:].rearrange("p (h d) -> p h d", d=HD),
                        in0=attw[:].rearrange("p (h g) -> p h g", g=H)[
                            :, :, g:g + 1].broadcast_to([128, H, HD]),
                        in1=kvb[:, 128 * g + HD:128 * (g + 1)].unsqueeze(1)
                        .broadcast_to([128, H, HD]), op=ALU.mult),
                        rd=("attw", "kv3_%d" % b), wr=(ak,))
                else:
                    eng(lambda e, g=g, kvb=kvb, pr=pr: e.tensor_tensor(
                        out=pr[:].rearrange("p (h d) -> p h d", d=HD),
                        in0=attw[:].rearrange("p (h g) -> p h g", g=H)[
                            :, :, g:g + 1].broadcast_to([128, H, HD]),
                        in1=kvb[:, 128 * g + HD:128 * (g + 1)].unsqueeze(1)
                        .broadcast_to([128, H, HD]), op=ALU.mult),
                        rd=("attw", "kv3_%d" % b), wr=(pk,))
                    eng(lambda e, pr=pr, acc=acc: e.tensor_add(
                        out=acc[:], in0=acc[:], in1=pr[:]), rd=(pk, ak), wr=(ak,))
            V(lambda e, b=b, aotmc=aotmc: e.tensor_add(
              out=aotmc[:, D * b:D * (b + 1)],
              in0=av3[:], in1=av3g[:]), rd=("av", "avg"), wr=(aok,))

        def tail3(t):
            h2v = h2v_0 if t % 2 == 0 else h2v_1
            h2k = "h2a" if t % 2 == 0 else "h2b"
            aotmc = aotm if t % 2 == 0 else aotmB
            aok = "aotm0" if t % 2 == 0 else "aotm1"
            for b in range(2):
                for jq in range(2):
                    slot = "t2a" if jq == 0 else "t2b"
                    off = 0 if jq == 0 else 512
                    mmg([(lambda e, b=b, j=4 * jq + q, off=off, q=q: e.transpose(
                            pb[2][:].bitcast(BF16)[:, off + 128 * q:off + 128 * (q + 1)],
                            aotmc[:, D * b + 128 * j:D * b + 128 * (j + 1)], P["identb"][:]))
                         for q in range(4)],
                        rd=(aok,), wr=(slot,))
                    A_(lambda e, b=b, jq=jq, off=off: e.activation(
                        out=aofv[:, 4 * jq:4 * jq + 4, 128 * b:128 * (b + 1)],
                        in_=pb[2][:].bitcast(BF16)[:, off:off + 512].rearrange(
                            "p (c t) -> p c t", t=128),
                        func=AF.Copy), rd=(slot,), wr=("aofm",))

            for oc in range(NCH):
                bank = pb[6] if oc % 2 == 0 else pb[7]
                slot = "s12" if oc % 2 == 0 else "s13"
                mmg([(lambda e, oc=oc, kc=kc, bank=bank: e.matmul(
                        bank[:, 0:TT], wov[:, kc, 128 * oc:128 * (oc + 1)],
                        aofv[:, kc, :], start=(kc == 0), stop=(kc == NCH - 1)))
                     for kc in range(NCH)],
                    rd=("aofm", "wo3"), wr=(slot,))
                V(lambda e, oc=oc, bank=bank, h2v=h2v: e.scalar_tensor_tensor(
                    out=xf1v[:, oc, :], in0=bank[:, 0:TT],
                    scalar=fm("f_bo")[:, oc:oc + 1], in1=h2v[:, oc, :],
                    op0=ALU.add, op1=ALU.add), rd=(slot, h2k), wr=("xf1",))
            op("sync", lambda e, t=t: e.dma_start(out=xf1_s[t], in_=xf1t[:]),
               waits=c.dep_waits(("xf1",), ()), incs=[(sds, 16)])
            c.register(("xf1",), (), sds)

        NT3 = NTILES or NT
        for t in range(NT3):
            head3(t)
            proj3(t, 0)
            if t > 0:
                att_b(t - 1, 1)
            proj3(t, 1)
            if t > 0:
                tail3(t - 1)
            att_b(t, 0)
        att_b(NT3 - 1, 1)
        tail3(NT3 - 1)
        op("sync", lambda e: e.nop(), waits=[w(sds)])
        op("vector", lambda e: e.nop(), waits=[w(sv)])
        op("scalar", lambda e: e.nop(), waits=[w(sa)])
        op("tensor", lambda e: e.nop(), waits=[w(spe)])
        op("gpsimd", lambda e: e.nop(), waits=[w(sg), w(sdg)])


    if phases >= 3:
        with nc.Block() as blk:
            ctx.walk(blk, p3)
    p3s.close()

    p4s = contextlib.ExitStack()
    TT4 = 512
    NT4 = T // TT4
    x14 = sb("x14", [128, 2 * NCH * TT], F32, p4s)
    x14B = sb("x14B", [128, 2 * NCH * TT], F32, p4s)
    xf14 = sb("xf14", [128, 2 * NCH * TT], F32, p4s)
    xf14B = sb("xf14B", [128, 2 * NCH * TT], F32, p4s)
    xfsq = sb("xfsq", [128, NCH * TT], BF16, p4s)
    min_bf = sb("min_bf", [128, 2 * NCH * TT], BF16, p4s)
    he_bf = sb("he_bf", [128, NCH * TT4], BF16, p4s)
    he_bf2 = sb("he_bf2", [128, NCH * TT4], BF16, p4s)
    gb8 = sb("gb8", [128, 8 * TT4], BF16, p4s)
    we_sb = sb("we_sb", [128, NCH * D], BF16, p4s)
    we_sb2 = sb("we_sb2", [128, NCH * D], BF16, p4s)
    fWg_sb = sb("fWg_sb", [128, NCH * 8], BF16, p4s)
    glog = sb("glog", [8, TT4], F32, p4s)
    gtm4 = sb("gtm4", [128, 8], F32, p4s)
    gfm4 = sb("gfm4", [8, TT4], BF16, p4s)
    g16 = sb("g16", [128, 4], F32, p4s)
    tmS4 = sb("tmS4", [128, 8], F32, p4s)
    rows4 = sb("rows4", [8, TT4], BF16, p4s)
    reps4 = sb("reps4", [8, TT4], BF16, p4s)
    zt4 = sb("zt4", [128, TT], F32, p4s)
    outfm = sb("outfm", [128, 2 * NCH * TT], F32, p4s)
    outtm = sb("outtm", [128, D], F32, p4s)

    def p4(c):
        op, w = c.op, c.w
        spe, sv, sa = S["pe4"], S["v4"], S["a4"]
        sdh, sds, sdg = S["dh4"], S["ds4"], S["dg4"]
        c.auto_tensor = ()
        c.self_sems = {"vector": sv, "scalar": sa, "gpsimd": S["g4"]}
        c.res = {}
        c.bankmap = {
            "s4": "BK2", "s4b": "BK2", "s4c": "BK2", "s6": "BK3", "s7": "BK3",
            "so4": "BK4", "so5": "BK5"}
        V = lambda emit, rd=(), wr=(), ww=(): op("vector", emit, waits=list(ww),
                                                incs=[(sv, 1)], reads=rd, writes=wr)
        A_ = lambda emit, rd=(), wr=(), ww=(): op("scalar", emit, waits=list(ww),
                                                 incs=[(sa, 1)], reads=rd, writes=wr)

        def mmg(mms, rd=(), wr=(), extra=()):
            ws = c.dep_waits(rd, wr) + list(extra)
            for i, mm in enumerate(mms):
                op("tensor", mm, waits=ws if i == 0 else (),
                   incs=[(spe, 1)] if i == len(mms) - 1 else ())
            c.register(rd, wr, spe)

        TT4 = 512
        NT4 = T // TT4
        xq4 = xfsq[:].rearrange("p (c t) -> p c t", t=TT)
        mvk = min_bf[:].rearrange("p (f c t) -> p c f t", c=NCH, t=TT)
        hev_0 = he_bf[:].rearrange("p (c t) -> p c t", t=TT4)
        hev_1 = he_bf2[:].rearrange("p (c t) -> p c t", t=TT4)
        gbv = gb8[:].rearrange("p (e t) -> p e t", t=TT4)
        ofm = outfm[:].rearrange("p (f c t) -> p f c t", c=NCH, t=TT)
        wgv = fWg_sb[:].rearrange("p (c n) -> p c n", n=8)

        op("gpsimd", lambda e: e.dma_start(
            out=fWg_sb[:].rearrange("p (c n) -> p c n", n=8),
            in_=fWg_h.rearrange("(c p) n -> p c n", p=128)), incs=[(S["dwg"], 16)])
        c.register((), ("wg4",), S["dwg"])
        op("vector", lambda e: e.memset(tmS4[:], 0.0), incs=[(sv, 1)])

        for t in range(min(NTILES or NT4, NT4)):
            x14c = x14 if t % 2 == 0 else x14B
            xf14c = xf14 if t % 2 == 0 else xf14B
            xk4 = "x14_%d" % (t % 2)
            xfk4 = "xf14_%d" % (t % 2)
            xf4 = xf14c[:].rearrange("p (f c t) -> p f c t", c=NCH, t=TT)
            x14v = x14c[:].rearrange("p (f c t) -> p f c t", c=NCH, t=TT)
            op("sync", lambda e, t=t, x14c=x14c: e.dma_start(
                out=x14c[:].rearrange("p (f n) -> p f n", f=2),
                in_=x1_s[2 * t:2 * t + 2].rearrange("f p n -> p f n")),
                waits=c.dep_waits((), (xk4,)), incs=[(sdh, 16)])
            c.register((), (xk4,), sdh)
            op("sync", lambda e, t=t, xf14c=xf14c: e.dma_start(
                out=xf14c[:].rearrange("p (f n) -> p f n", f=2),
                in_=xf1_s[2 * t:2 * t + 2].rearrange("f p n -> p f n")),
                waits=c.dep_waits((), (xfk4,)), incs=[(sdh, 16)])
            c.register((), (xfk4,), sdh)
            for f in range(2):
                A_(lambda e, f=f, xf14c=xf14c: e.activation(
                    out=xfsq[:], in_=xf14c[:, f * NCH * TT:(f + 1) * NCH * TT],
                    func=AF.Square), rd=(xfk4,), wr=("xfsq",))
                for b in range(2):
                    bb = 2 * f + b
                    mmg([(lambda e, b=b, kc=kc: e.matmul(
                            pb[2][:, 0:2], xq4[:, kc, 128 * b:128 * (b + 1)],
                            P["sv1"][:, 2 * kc:2 * kc + 2],
                            start=(kc == 0), stop=(kc == NCH - 1))) for kc in range(NCH)],
                        rd=("xfsq",), wr=("s4",))
                    V(lambda e: e.tensor_scalar_mul(out=g16[:, 0:1], in0=pb[2][:, 0:1],
                      scalar1=INVD), rd=("s4",), wr=("g16",))
                    A_(lambda e: e.activation(out=g16[:, 1:2], in_=g16[:, 0:1], func=AF.Ln,
                       bias=np.float32(EPSR)), rd=("g16",), wr=("g16",))
                    A_(lambda e: e.activation(out=tmS4[:, 0:1], in_=g16[:, 1:2],
                       func=AF.Exp, scale=np.float32(-0.5)), rd=("g16",), wr=("tmS",))
                    mmg([lambda e: e.transpose(pb[2][0:8, 128:256], tmS4[:],
                         P["identf"][:])], rd=("tmS",), wr=("s4b",))
                    A_(lambda e, bb=bb: e.activation(out=rows4[0:8, 128 * bb:128 * (bb + 1)],
                       in_=pb[2][0:8, 128:256], func=AF.Copy), rd=("s4b",), wr=("rows",))
                    mmg([(lambda e, bb=bb: e.matmul(
                            pb[2][0:8, 256:384], P["sel"][:, 0:8],
                            rows4[0:8, 128 * bb:128 * (bb + 1)], start=True, stop=True))],
                        rd=("rows",), wr=("s4c",))
                    A_(lambda e, bb=bb: e.activation(out=reps4[0:8, 128 * bb:128 * (bb + 1)],
                       in_=pb[2][0:8, 256:384], func=AF.Copy), rd=("s4c",), wr=("reps",))
            for f in range(2):
                for b in range(2):
                    bb = 2 * f + b
                    mmg([(lambda e, bb=bb: e.matmul(
                            pb[3][:, 0:128], P["onesb"][0:1, 0:128],
                            reps4[0:1, 128 * bb:128 * (bb + 1)], start=True, stop=True))],
                        rd=("reps",), wr=("s6",))
                    for kc in range(NCH):
                        V(lambda e, f=f, b=b, kc=kc: e.scalar_tensor_tensor(
                            out=min_bf[:].rearrange("p (f c t) -> p f c t", c=NCH, t=TT)[
                                :, f, kc, 128 * b:128 * (b + 1)],
                            in0=xf4[:, f, kc, 128 * b:128 * (b + 1)],
                            scalar=fm("gamma2")[:, kc:kc + 1], in1=pb[3][:, 0:128],
                            op0=ALU.mult, op1=ALU.mult), rd=(xfk4, "s6"), wr=("min",))
            for f in range(2):
                mmg([(lambda e, f=f, kc=kc: e.matmul(
                        pb[3][0:8, TT * f:TT * (f + 1)], wgv[:, kc, :],
                        min_bf[:].rearrange("p (f c t) -> p f c t", c=NCH, t=TT)[:, f, kc, :],
                        start=(kc == 0), stop=(kc == NCH - 1))) for kc in range(NCH)],
                    rd=("min", "wg4"), wr=("s6" if f == 0 else "s7",))
            A_(lambda e: e.activation(out=glog[:], in_=pb[3][0:8, 0:TT4], func=AF.Identity,
               bias=P["fm_fbg"][:]), rd=("s6", "s7"), wr=("glog",))
            for bb in range(4):
                mmg([(lambda e, bb=bb: e.transpose(
                        pb[2][:, 0:8], glog[:, 128 * bb:128 * (bb + 1)],
                        P["identf"][0:8, 0:8]))], rd=("glog",), wr=("s4",))
                V(lambda e: e.tensor_copy(out=gtm4[:], in_=pb[2][:, 0:8]),
                  rd=("s4",), wr=("gtm4",))
                V(lambda e: e.reduce_max(out=g16[:, 0:1], in_=gtm4[:], axis=AXL.X),
                  rd=("gtm4",), wr=("g16",))
                V(lambda e: e.tensor_scalar_mul(out=g16[:, 1:2], in0=g16[:, 0:1],
                  scalar1=-1.0), rd=("g16",), wr=("g16",))
                A_(lambda e: e.activation(out=gtm4[:], in_=gtm4[:], func=AF.Exp,
                   bias=g16[:, 1:2]), rd=("gtm4", "g16"), wr=("gtm4",))
                V(lambda e: e.reduce_sum(out=g16[:, 2:3], in_=gtm4[:], axis=AXL.X),
                  rd=("gtm4",), wr=("g16",))
                V(lambda e: e.reciprocal(out=g16[:, 3:4], in_=g16[:, 2:3]),
                  rd=("g16",), wr=("g16",))
                V(lambda e: e.tensor_scalar_mul(out=gtm4[:], in0=gtm4[:],
                  scalar1=g16[:, 3:4]), rd=("gtm4", "g16"), wr=("gtm4",))
                mmg([(lambda e, bb=bb: e.transpose(pb[2][0:8, 128:256], gtm4[:],
                     P["identf"][:]))], rd=("gtm4",), wr=("s4b",))
                A_(lambda e, bb=bb: e.activation(out=gfm4[:, 128 * bb:128 * (bb + 1)],
                   in_=pb[2][0:8, 128:256], func=AF.Copy), rd=("s4b",), wr=("gfm",))
            for e_ in range(8):
                mmg([(lambda e, x=e_: e.matmul(
                        pb[3][:, 0:TT4], P["selw"][:, 128 * x:128 * (x + 1)], gfm4[0:8, :],
                        start=True, stop=True))], rd=("gfm",), wr=("s6",))
                A_(lambda e, x=e_: e.activation(out=gbv[:, x, :], in_=pb[3][:, 0:TT4],
                   func=AF.Copy), rd=("s6",), wr=("gb8",))

            weneed = {}
            op("gpsimd", lambda e: e.dma_start(out=we_sb[:], in_=webf_s[0]),
               waits=c.dep_waits((), ("we0",)), incs=[(sdg, 16)])
            c.register((), ("we0",), sdg)
            weneed[0] = (sdg, sdg.c)
            for e_ in range(8):
                buf = we_sb if e_ % 2 == 0 else we_sb2
                nbuf = we_sb2 if e_ % 2 == 0 else we_sb
                bk = "we0" if e_ % 2 == 0 else "we1"
                nbk = "we1" if e_ % 2 == 0 else "we0"
                hev = hev_0 if e_ % 2 == 0 else hev_1
                hek = "he0" if e_ % 2 == 0 else "he1"
                wevv = buf[:].rearrange("p (c n) -> p c n", n=D)
                for kc in range(NCH):
                    V(lambda e, x=e_, kc=kc, hev=hev: e.tensor_tensor(
                        out=hev[:, kc, :].rearrange("p (f t) -> p f t", t=TT),
                        in0=mvk[:, kc, :, :],
                        in1=gbv[:, x, :].rearrange("p (f t) -> p f t", t=TT),
                        op=ALU.mult), rd=("min", "gb8"), wr=(hek,))
                if e_ < 7:
                    nsem = S["dg4b"] if e_ % 2 == 0 else sdg
                    op("gpsimd", lambda e, x=e_: e.dma_start(out=nbuf[:], in_=webf_s[x + 1]),
                       waits=c.dep_waits((), (nbk,)), incs=[(nsem, 16)])
                    c.register((), (nbk,), nsem)
                    weneed[e_ + 1] = (nsem, nsem.c)
                ws0 = c.dep_waits((hek, bk), ()) + [weneed[e_]]
                for oc in range(NCH):
                    for kc in range(NCH):
                        op("tensor", lambda e, x=e_, oc=oc, kc=kc, wevv=wevv,
                           hev=hev: e.matmul(
                            pb[oc][:, 0:512], wevv[:, kc, 128 * oc:128 * (oc + 1)],
                            hev[:, kc, :],
                            start=(x == 0 and kc == 0), stop=(x == 7 and kc == NCH - 1)),
                            waits=ws0 if (oc == 0 and kc == 0) else (),
                            incs=[(spe, 1)] if (oc == NCH - 1 and kc == NCH - 1) else [])
                c.register((hek, bk), ("moe",) if e_ == 7 else (), spe)
            for oc in range(NCH):
                for f in range(2):
                    V(lambda e, oc=oc, f=f, xf4=xf4: e.tensor_add(
                        out=zt4[:], in0=pb[oc][:, 256 * f:256 * (f + 1)],
                        in1=xf4[:, f, oc, :]), rd=("moe", xfk4), wr=("zt4",))
                    V(lambda e, oc=oc, f=f, x14v=x14v: e.scalar_tensor_tensor(
                        out=ofm[:, f, oc, :], in0=zt4[:], scalar=fm("gb")[:, oc:oc + 1],
                        in1=x14v[:, f, oc, :], op0=ALU.mult, op1=ALU.add),
                        rd=("zt4", xk4), wr=("ofm",))
            for bb in range(4):
                f, b = bb // 2, bb % 2
                for jq in range(2):
                    slot = "so4" if jq == 0 else "so5"
                    bank_off = 0 if jq == 0 else 0
                    pbx = pb[4] if jq == 0 else pb[5]
                    mmg([(lambda e, f=f, b=b, j=4 * jq + q, pbx=pbx, q=q: e.transpose(
                            pbx[:, 128 * q:128 * (q + 1)],
                            ofm[:, f, j, 128 * b:128 * (b + 1)], P["identf"][:]))
                         for q in range(4)],
                        rd=("ofm",), wr=(slot,))
                    A_(lambda e, jq=jq, pbx=pbx: e.activation(
                        out=outtm[:, 512 * jq:512 * (jq + 1)], in_=pbx[:, 0:512],
                        func=AF.Copy), rd=(slot,), wr=("outtm",))
                op("sync", lambda e, t=t, bb=bb: e.dma_start(
                    out=out_h[TT4 * t + 128 * bb:TT4 * t + 128 * (bb + 1), :],
                    in_=outtm[:]), waits=c.dep_waits(("outtm",), ()), incs=[(sds, 16)])
                c.register(("outtm",), (), sds)
        op("gpsimd", lambda e: e.nop(), waits=[w(sdg), (S["dg4b"], S["dg4b"].c)])
        op("vector", lambda e: e.nop(), waits=[w(sv)])
        op("scalar", lambda e: e.nop(), waits=[w(sa)])
        op("tensor", lambda e: e.nop(), waits=[w(spe)])
        op("sync", lambda e: e.nop(), waits=[w(sds)])


    if phases >= 4:
        with nc.Block() as blk:
            ctx.walk(blk, p4)
    p4s.close()
    big.close()
    return nc, out_h.name


def make_shared(inputs):
    g = lambda k: np.ascontiguousarray(np.asarray(inputs[k], np.float32))
    wp, bp = permute_w_in(g("L0_attn_w_in"), g("L0_attn_b_in"))
    woutp = np.zeros((NPD, D), np.float32)
    woutp[:NHID] = g("L0_attn_w_out")
    CN = host_consts()
    return dict(
        wmod=g("L0_wmod"),
        wq=g("L0_attn_wq"), wk=g("L0_attn_wk"), wv=g("L0_attn_wv"),
        wg=g("L0_attn_wg"), wo=g("L0_attn_wo"),
        winp=wp, binp=bp, woutp=woutp,
        fWq=g("L0_ffn_Wq"), fWdown=g("L0_ffn_Wdown"), fWup=g("L0_ffn_Wup"),
        fWo=g("L0_ffn_Wo"), fWg=g("L0_ffn_Wg"), fWe=g("L0_ffn_We"),
        fm_w1=fmv(g("L0_attn_norm1_w")), fm_b1=fmv(g("L0_attn_norm1_b")),
        fm_w2=fmv(g("L0_attn_norm2_w")),
        b2bf=fmv(g("L0_attn_norm2_b")),
        fm_bq=fmv(g("L0_attn_bq")), fm_bk=fmv(g("L0_attn_bk")),
        fm_bo=fmv(g("L0_attn_bo")), fm_b_out=fmv(g("L0_attn_b_out")),
        fm_gamma1=fmv(g("L0_ffn_gamma1")), fm_gamma2=fmv(g("L0_ffn_gamma2")),
        fm_f_bo=fmv(g("L0_ffn_bo")),
        fm_f_bdown=g("L0_ffn_bdown").reshape(HD, 1),
        fm_f_bg=g("L0_ffn_bg").reshape(8, 1),
        row_bv=g("L0_attn_bv").reshape(1, D), row_bg=g("L0_attn_bg").reshape(1, D),
        row_fbq=g("L0_ffn_bq").reshape(1, D), row_bup=g("L0_ffn_bup").reshape(1, 2 * D),
        row_bq=g("L0_attn_bq").reshape(1, D), row_bk=g("L0_attn_bk").reshape(1, D),
        gnw_rep=np.ascontiguousarray(np.broadcast_to(g("L0_attn_gn_w")[None, :], (128, D))),
        gnb_rep=np.ascontiguousarray(np.broadcast_to(g("L0_attn_gn_b")[None, :], (128, D))),
        **CN,
    )


def kernel(**inputs):
    shared = make_shared(inputs)
    in_maps = []
    for b in range(8):
        m = dict(shared)
        m["x"] = np.ascontiguousarray(np.asarray(inputs["L0_attn_x"][b], np.float32))
        m["cvec"] = fmv(np.asarray(inputs["L0_c"][b], np.float32))
        in_maps.append(m)
    nc = bass.Bass("TRN2", target_bir_lowering=False)
    nc, out_name = build(nc)
    res = run_bass_kernel_spmd(nc, in_maps, core_ids=list(range(8)))
    out = np.stack([np.asarray(r["out"], np.float32) for r in res.results], axis=0)
    return out
